# version 02
# baseline (speedup 1.0000x reference)
import math
from contextlib import ExitStack

import numpy as np
import ml_dtypes

import concourse.bass as bass
import concourse.mybir as mybir
from concourse.bass_utils import run_bass_kernel_spmd

F32 = mybir.dt.float32
BF16 = mybir.dt.bfloat16
I32 = mybir.dt.int32
U32 = mybir.dt.uint32
AF = mybir.ActivationFunctionType
ALU = mybir.AluOpType
AX = mybir.AxisListType

NCORE = 8
B = 4
T = 4096
TH = 2048
D = 1024
BF = ml_dtypes.bfloat16

_prog_cache = {}
DEBUG_A1 = False


def cdiv(a, b):
    return (a + b - 1) // b


class P:

    def __init__(self, name):
        self.nc = bass.Bass("TRN2", target_bir_lowering=False)
        self.name = name
        self.stack = ExitStack()
        self.block = None
        self.n = {}
        self.sem = {}
        self._uid = 0
        self.pre = ""
        self.io_map = {}
        self.fused = False
        self.started = False
        self._frames = []
        self._co = None

    def uid(self):
        self._uid += 1
        return self._uid

    def start(self, sems=("din", "gin", "mm", "v", "sc", "gp", "dout", "zd", "zg", "cc")):
        if self.started:
            return
        self.started = True
        for s in sems:
            self.sem[s] = self.stack.enter_context(self.nc.semaphore(s))
            self.n[s] = 0
        self.block = self.stack.enter_context(self.nc.Block())

    def push_frame(self):
        self._frames.append(ExitStack())

    def pop_frame(self):
        self._frames.pop().close()

    def sb(self, name, shape, dtype=F32, scope=None):
        st = scope if scope is not None else (self._frames[-1] if self._frames else self.stack)
        return st.enter_context(self.nc.sbuf_tensor(f"{name}{self.uid()}", shape, dtype))

    def ps(self, name, shape, dtype=F32, scope=None):
        st = scope if scope is not None else (self._frames[-1] if self._frames else self.stack)
        return st.enter_context(self.nc.psum_tensor(f"{name}{self.uid()}", shape, dtype))

    def din(self, name, shape, dtype=F32):
        nm = self.pre + name
        if nm in self.io_map:
            return self.io_map[nm]
        return self.nc.dram_tensor(nm, shape, dtype, kind="ExternalInput").ap()

    def dout(self, name, shape, dtype=F32):
        nm = self.pre + name
        if nm in self.io_map:
            return self.io_map[nm]
        return self.nc.dram_tensor(nm, shape, dtype, kind="ExternalOutput").ap()

    def dint(self, name, shape, dtype=F32):
        t = self.nc.dram_tensor(name, shape, dtype, kind="Internal")
        return t.ap()

    def sec(self, eng, fn):
        getattr(self.block, eng)(fn)

    def inc(self, instr, semname, k=1):
        instr.then_inc(self.sem[semname], k)
        self.n[semname] += k
        return self.n[semname]

    def wait(self, env, semname, val=None):
        v = self.n[semname] if val is None else val
        if v > 0:
            env.wait_ge(self.sem[semname], v)

    def close(self):
        if self.fused:
            return
        self.stack.close()


def _ap(x):
    return x[:, :] if not isinstance(x, bass.AP) else x


def dma_in(p, dst_sb, src_ap, cast=False, sem=None):
    dst_sb = _ap(dst_sb)
    src_ap = _ap(src_ap)
    if cast:
        sem = sem or "gin"
        def _(g):
            p.inc(g.dma_start(out=dst_sb, in_=src_ap), sem, 16)
        p.sec("gpsimd", _)
    else:
        sem = sem or "din"
        def _(s):
            p.inc(s.dma_start(out=dst_sb, in_=src_ap), sem, 16)
        p.sec("sync", _)


def load_fm(p, dst_sb, dram_ap, rows, cols, dst_col_stride=None, cast=True, sem=None,
            row0=0, col0=0):
    cstride = dst_col_stride if dst_col_stride is not None else cols
    nk = cdiv(rows, 128)
    for kt in range(nk):
        r = min(128, rows - kt * 128)
        dma_in(p, dst_sb[:r, kt * cstride: kt * cstride + cols],
               dram_ap[row0 + kt * 128: row0 + kt * 128 + r, col0: col0 + cols],
               cast=cast, sem=sem)


class Pool:

    def __init__(self, p, nb, width=512, parts=128):
        self.p = p
        self.tiles = [p.ps("pool", [parts, width], F32) for _ in range(nb)]
        self.freed = [None] * nb
        self.rr = 0
        self.nb = nb


def emit_mm(p, pool, jobs, evacs, mm_sem="mm", wait_before=()):
    ename, esem = evacs
    ebase = p.n[esem]
    plan = []

    def tsec(t):
        for wsem, wval in wait_before:
            p.wait(t, wsem, wval)
        for i, (mm_list, m, w, evac_fn) in enumerate(jobs):
            j = pool.rr % pool.nb
            pool.rr += 1
            if pool.freed[j] is not None:
                fsem, fval = pool.freed[j]
                t.wait_ge(p.sem[fsem], fval)
            tile = pool.tiles[j]
            for ki, (lh, rh) in enumerate(mm_list):
                ins = t.matmul(tile[:m, :w], lh, rh, start=(ki == 0), stop=(ki == len(mm_list) - 1))
            cnt = p.inc(ins, mm_sem)
            pred = ebase + i + 1
            pool.freed[j] = (esem, pred)
            plan.append((cnt, tile, evac_fn, pred))
    p.sec("tensor", tsec)

    def esec(env):
        for cnt, tile, evac_fn, pred in plan:
            env.wait_ge(p.sem[mm_sem], cnt)
            evac_fn(env, tile, pred)
            assert p.n[esem] == pred, f"evac must inc {esem} exactly once ({p.n[esem]} != {pred})"
    p.sec(ename, esec)


def emit_postln(p, s_sb, out_sb, Tt, co, eps=1e-5, ready=(), cstride=None, col0=0):
    cs = cstride if cstride is not None else Tt

    def sv(mt, c0, w):
        return s_sb[:, mt * cs + col0 + c0: mt * cs + col0 + c0 + w]

    scope = ExitStack()
    sbf = p.sb("lnsbf", [128, 8 * 512], BF16, scope)
    sqf = p.sb("lnsqf", [128, 8 * 512], BF16, scope)
    st_mu = p.sb("lnmu", [1, 512], F32, scope)
    st_rs = p.sb("lnrs", [1, 512], F32, scope)
    st_t2 = p.sb("lnt2", [1, 512], F32, scope)
    ps_a = p.ps("lnpa", [128, 512], F32, scope)
    ps_b = p.ps("lnpb", [128, 512], F32, scope)
    ps_mu = p.ps("lnpmu", [128, 512], F32, scope)
    ps_rs = p.ps("lnprs", [128, 512], F32, scope)
    NG = cdiv(Tt, 512)
    mm_prev = 0

    for ng in range(NG):
        c0 = ng * 512
        w = min(512, Tt - c0)

        def vcast(v, c0=c0, w=w, ng=ng, mm_prev=mm_prev):
            if ng == 0:
                for sem, val in ready:
                    p.wait(v, sem, val)
            else:
                v.wait_ge(p.sem["mm"], mm_prev)
            for mt in range(8):
                p.inc(v.tensor_copy(sbf[:, mt * 512: mt * 512 + w], sv(mt, c0, w)), "v")
        p.sec("vector", vcast)
        v_cast = p.n["v"]

        def ssq(sc, c0=c0, w=w, ng=ng, mm_prev=mm_prev):
            if ng == 0:
                for sem, val in ready:
                    p.wait(sc, sem, val)
            else:
                sc.wait_ge(p.sem["mm"], mm_prev)
            for mt in range(8):
                p.inc(sc.activation(sqf[:, mt * 512: mt * 512 + w], sv(mt, c0, w), AF.Square), "sc")
        p.sec("scalar", ssq)
        sc_sq = p.n["sc"]

        def tstats(t, c0=c0, w=w, ng=ng):
            t.wait_ge(p.sem["v"], v_cast)
            t.wait_ge(p.sem["sc"], sc_sq)
            if ng > 0:
                p.wait(t, "v")
            for mt in range(8):
                t.matmul(ps_a[0:1, :w], co["ones_bf"][:, :], sbf[:, mt * 512: mt * 512 + w],
                         start=(mt == 0), stop=(mt == 7))
            for mt in range(8):
                ins = t.matmul(ps_b[0:1, :w], co["ones_bf"][:, :], sqf[:, mt * 512: mt * 512 + w],
                               start=(mt == 0), stop=(mt == 7))
            p.inc(ins, "mm")
        p.sec("tensor", tstats)
        mm_stats = p.n["mm"]

        def vstat(v, w=w):
            v.wait_ge(p.sem["mm"], mm_stats)
            v.tensor_scalar(st_mu[0:1, :w], ps_a[0:1, :w], 1.0 / 1024, None, ALU.mult)
            v.tensor_scalar(st_t2[0:1, :w], ps_b[0:1, :w], 1.0 / 1024, float(eps), ALU.mult, ALU.add)
            v.tensor_tensor(st_rs[0:1, :w], st_mu[0:1, :w], st_mu[0:1, :w], ALU.mult)
            ins = v.tensor_tensor(st_t2[0:1, :w], st_t2[0:1, :w], st_rs[0:1, :w], ALU.subtract)
            p.inc(ins, "v")
        p.sec("vector", vstat)
        v_var = p.n["v"]

        def srstd(sc, w=w):
            sc.wait_ge(p.sem["v"], v_var)
            sc.activation(st_t2[0:1, :w], st_t2[0:1, :w], AF.Ln)
            ins = sc.activation(st_rs[0:1, :w], st_t2[0:1, :w], AF.Exp, scale=-0.5)
            p.inc(ins, "sc")
        p.sec("scalar", srstd)
        sc_rstd = p.n["sc"]

        def tbc(t, w=w):
            t.wait_ge(p.sem["sc"], sc_rstd)
            t.matmul(ps_mu[:, :w], co["onesrow"][:, :], st_mu[0:1, :w], start=True, stop=True)
            ins = t.matmul(ps_rs[:, :w], co["onesrow"][:, :], st_rs[0:1, :w], start=True, stop=True)
            p.inc(ins, "mm")
        p.sec("tensor", tbc)
        mm_prev = p.n["mm"]
        mm_bc = mm_prev

        def vnorm(v, c0=c0, w=w):
            v.wait_ge(p.sem["mm"], mm_bc)
            for mt in range(8):
                v.tensor_tensor(out_sb[:, mt * Tt + c0: mt * Tt + c0 + w],
                                sv(mt, c0, w), ps_mu[:, :w], ALU.subtract)
                ins = v.tensor_tensor(out_sb[:, mt * Tt + c0: mt * Tt + c0 + w],
                                      out_sb[:, mt * Tt + c0: mt * Tt + c0 + w], ps_rs[:, :w], ALU.mult)
            p.inc(ins, "v")
        p.sec("vector", vnorm)
    scope.close()


def softplus_evac(p, env_s, psum_ap, out_ap, tmp1, tmp2, bias_col):
    env_s.activation(tmp1, psum_ap, AF.Abs, bias=bias_col)
    env_s.activation(tmp2, tmp1, AF.Exp, scale=-1.0)
    return env_s.activation(tmp1, tmp2, AF.Ln, bias=1.0)


def host_consts():
    c = {}
    c["ident_bf"] = np.eye(128, dtype=BF)
    c["ident_f"] = np.eye(128, dtype=np.float32)
    c["ones_bf"] = np.ones((128, 1), dtype=np.float32)
    c["onesrow"] = np.ones((1, 128), dtype=np.float32)
    c["onesrow_bf"] = np.ones((1, 128), dtype=BF)
    return c


def emit_consts(p, co_np):
    if p._co is not None:
        return p._co
    co = {}
    ident_d = p.din("c_ident", [128, 128], BF16)
    identf_d = p.din("c_identf", [128, 128], F32)
    ones_d = p.din("c_ones", [128, 1], F32)
    onesrow_d = p.din("c_onesrow", [1, 128], F32)
    co["ident"] = p.sb("ident", [128, 128], BF16, scope=p.stack)
    co["identf"] = p.sb("identf", [128, 128], F32, scope=p.stack)
    co["ones_bf"] = p.sb("onesb", [128, 1], BF16, scope=p.stack)
    co["onesrow"] = p.sb("onesr", [1, 128], F32, scope=p.stack)
    co["onesrow_bf"] = p.sb("onesrb", [1, 128], BF16, scope=p.stack)
    dma_in(p, co["ident"], ident_d)
    onesrb_d = p.din("c_onesrb", [1, 128], BF16)
    dma_in(p, co["onesrow_bf"], onesrb_d)
    dma_in(p, co["identf"], identf_d)
    dma_in(p, co["ones_bf"], ones_d, cast=True)
    dma_in(p, co["onesrow"], onesrow_d)
    p._co = co
    return co


def const_inmap(co_np):
    return {"c_ident": co_np["ident_bf"], "c_identf": co_np["ident_f"],
            "c_ones": co_np["ones_bf"], "c_onesrow": co_np["onesrow"],
            "c_onesrb": co_np["onesrow_bf"]}


def build_A1(co_np, p=None):
    if p is None:
        p = P("A1")
    TX = 2176
    xT_d = p.din("xT", [1024, TX])
    win_d = p.din("WinT", [1024, 4096])
    cw_d = p.din("convw", [128, 64])
    cb_d = p.din("convb", [128, 16])
    wx_d = p.din("WxT", [2048, 96])
    u_d = p.dout("u", [2048, TH], BF16)
    xin_dbg = p.dout("xin_dbg", [2048, 2176], BF16) if DEBUG_A1 else None
    xc_dbg = p.dout("xc_dbg", [128, TH]) if DEBUG_A1 else None
    wb_dbg = p.dout("wb_dbg", [1024, 4096], BF16) if DEBUG_A1 else None
    z_d = p.dout("z", [2048, TH], BF16)
    dbc_d = p.dout("dbc", [96, TH])

    p.start()
    co = emit_consts(p, co_np)
    cw = p.sb("cw", [128, 64])
    cb = p.sb("cb", [128, 16])
    dma_in(p, cw, cw_d)
    dma_in(p, cb, cb_d)

    xin = p.sb("xin", [128, 16 * TX], BF16)
    zst = p.sb("zst", [128, 2 * TH], BF16)
    pool = Pool(p, 4)
    mm_scope = ExitStack()
    xtb = p.sb("xtb", [128, 8 * TX], BF16, mm_scope)
    wb = p.sb("wb", [128, 8 * 4096], BF16, mm_scope)
    load_fm(p, xtb, xT_d, 1024, TX)
    load_fm(p, wb, win_d, 1024, 4096)
    gin_all = p.n["gin"]

    jobs = []
    zplan = []
    for mt in range(32):
        is_z = mt >= 16
        ngs = [(128 + i * 512, 512) for i in range(4)] if is_z else \
              [(0, 512), (512, 512), (1024, 512), (1536, 512), (2048, 128)]
        for gi, (c0, w) in enumerate(ngs):
            mm_list = [(wb[:, k * 4096 + mt * 128: k * 4096 + (mt + 1) * 128],
                        xtb[:, k * TX + c0: k * TX + c0 + w]) for k in range(8)]
            if not is_z:
                def ev(env, ps, pred, mt=mt, c0=c0, w=w):
                    ins = env.tensor_copy(xin[:, mt * TX + c0: mt * TX + c0 + w], ps[:, :w])
                    p.inc(ins, "v")
            else:
                zt = mt - 16
                buf = zt % 2
                def ev(env, ps, pred, zt=zt, buf=buf, c0=c0, w=w, gi=gi):
                    if gi == 0 and zt >= 2:
                        p.wait(env, "zd", 16 * (zt - 1))
                    ins = env.tensor_copy(zst[:, buf * TH + (c0 - 128): buf * TH + (c0 - 128) + w], ps[:, :w])
                    cnt = p.inc(ins, "v")
                    if gi == 3:
                        zplan.append((cnt, zt, buf))
            jobs.append((mm_list, 128, w, ev))
    if DEBUG_A1:
        def wdma(s):
            p.wait(s, "gin", gin_all)
            for kt in range(8):
                p.inc(s.dma_start(out=wb_dbg[kt * 128:(kt + 1) * 128, :], in_=wb[:, kt * 4096:(kt + 1) * 4096]), "dout", 16)
        p.sec("sync", wdma)
        wb_dump_dout = p.n["dout"]

    emit_mm(p, pool, jobs, ("vector", "v"), wait_before=[("gin", gin_all)] + ([("dout", wb_dump_dout)] if DEBUG_A1 else []))

    def zdma(s):
        for cnt, zt, buf in zplan:
            s.wait_ge(p.sem["v"], cnt)
            p.inc(s.dma_start(out=z_d[zt * 128:(zt + 1) * 128, :], in_=zst[:, buf * TH:(buf + 1) * TH]), "zd", 16)
    p.sec("sync", zdma)
    v_xz = p.n["v"]
    mm_scope.close()

    if DEBUG_A1:
        def xdma(s):
            s.wait_ge(p.sem["v"], v_xz)
            for et in range(16):
                p.inc(s.dma_start(out=xin_dbg[et * 128:(et + 1) * 128, :], in_=xin[:, et * TX:(et + 1) * TX]), "dout", 16)
        p.sec("sync", xdma)

    ubuf = p.sb("ubuf", [128, 16 * TH], BF16)
    acc = p.sb("acc", [128, 2 * TH])
    sc_prev_silu = p.n["sc"]
    for et in range(16):
        def vconv(v, et=et, sc_prev_silu=sc_prev_silu):
            if et == 0:
                v.wait_ge(p.sem["v"], v_xz)
                p.wait(v, "din")
                if DEBUG_A1:
                    p.wait(v, "dout", wb_dump_dout)
            else:
                v.wait_ge(p.sem["sc"], sc_prev_silu)
            base = et * TX
            v.tensor_scalar(acc[:, :TH], xin[:, base + 125: base + 125 + TH],
                            cw[:, et * 4: et * 4 + 1], cb[:, et: et + 1], ALU.mult, ALU.add)
            v.scalar_tensor_tensor(acc[:, TH:], xin[:, base + 126: base + 126 + TH],
                                   cw[:, et * 4 + 1: et * 4 + 2], acc[:, :TH], ALU.mult, ALU.add)
            v.scalar_tensor_tensor(acc[:, :TH], xin[:, base + 127: base + 127 + TH],
                                   cw[:, et * 4 + 2: et * 4 + 3], acc[:, TH:], ALU.mult, ALU.add)
            ins = v.scalar_tensor_tensor(acc[:, TH:], xin[:, base + 128: base + 128 + TH],
                                         cw[:, et * 4 + 3: et * 4 + 4], acc[:, :TH], ALU.mult, ALU.add)
            p.inc(ins, "v")
        p.sec("vector", vconv)
        v_c = p.n["v"]
        if DEBUG_A1 and et == 0:
            def dacc(s, v_c=v_c):
                s.wait_ge(p.sem["v"], v_c)
                p.inc(s.dma_start(out=xc_dbg, in_=acc[:, TH:]), "dout", 16)
            p.sec("sync", dacc)

        def ssilu(sc, et=et):
            sc.wait_ge(p.sem["v"], v_c)
            ins = sc.activation(ubuf[:, et * TH:(et + 1) * TH], acc[:, TH:], AF.Silu)
            p.inc(ins, "sc")
        p.sec("scalar", ssilu)
        sc_prev_silu = p.n["sc"]

    sc_u = p.n["sc"]

    def udma(s):
        s.wait_ge(p.sem["sc"], sc_u)
        for et in range(16):
            p.inc(s.dma_start(out=u_d[et * 128:(et + 1) * 128, :], in_=ubuf[:, et * TH:(et + 1) * TH]), "dout", 16)
    p.sec("sync", udma)

    wxb = p.sb("wxb", [128, 16 * 96], BF16)
    def _gate(g):
        g.wait_ge(p.sem["v"], v_xz)
    p.sec("gpsimd", _gate)
    load_fm(p, wxb, wx_d, 2048, 96)
    gin2 = p.n["gin"]
    dbc_sb = p.sb("dbcsb", [96, TH])
    jobs = []
    for gi in range(4):
        c0 = gi * 512
        mm_list = [(wxb[:, k * 96:(k + 1) * 96], ubuf[:, k * TH + c0: k * TH + c0 + 512]) for k in range(16)]
        def ev(env, ps, pred, c0=c0):
            ins = env.tensor_copy(dbc_sb[:, c0:c0 + 512], ps[:96, :512])
            p.inc(ins, "v")
        jobs.append((mm_list, 96, 512, ev))
    emit_mm(p, pool, jobs, ("vector", "v"),
            wait_before=[("gin", gin2), ("sc", sc_u)])
    v_dbc = p.n["v"]

    def ddma(s):
        s.wait_ge(p.sem["v"], v_dbc)
        p.inc(s.dma_start(out=dbc_d, in_=dbc_sb[:96, :]), "dout", 16)
        p.wait(s, "dout")
    p.sec("sync", ddma)
    p.close()
    return p.nc


def host_A1_inmaps(inputs, co_np):
    x = inputs["L0_attn_x"]
    WinT = np.ascontiguousarray(inputs["L0_attn_W_in"].T)
    convw = inputs["L0_attn_conv_w"].reshape(16, 128, 4).transpose(1, 0, 2).reshape(128, 64)
    convb = inputs["L0_attn_conv_b"].reshape(16, 128).T.copy()
    WxT = np.ascontiguousarray(inputs["L0_attn_W_xproj"].T)
    base = const_inmap(co_np)
    maps = []
    for c in range(NCORE):
        b, i = c // 2, c % 2
        t0 = i * TH
        xs = np.zeros((2176, 1024), dtype=np.float32)
        lo = max(0, t0 - 128)
        xs[128 - (t0 - lo):128 + TH] = x[b, lo:t0 + TH]
        m = dict(base)
        m.update(xT=np.ascontiguousarray(xs.T), WinT=WinT, convw=convw, convb=convb, WxT=WxT)
        maps.append(m)
    return maps




def build_A2(co_np, p=None, fused_out=False):
    if p is None:
        p = P("A2")
    dbc_d = p.din("dbc", [96, T])
    u_d = p.din("u", [1024, T], BF16)
    wdt_d = p.din("WdtT", [64, 1024])
    bdt_d = p.din("bdt", [128, 8])
    al_d = p.din("AL", [128, 128])
    pd_d = p.din("PD4", [4 * 128, 128], BF16)
    pb_d = p.din("PBq", [4 * 96, 128], BF16)
    pc_d = p.din("PCq", [4 * 96, 128], BF16)
    sel_d = p.din("SEL", [128, 32], BF16)
    ys_d = p.dout("ys", [2048, TH] if fused_out else [1024, T], BF16)

    p.start()
    dbc = p.sb("dbc", [96, T], BF16)
    wdt = p.sb("wdt", [64, 1024], BF16)
    bdt = p.sb("bdt", [128, 8])
    alog = p.sb("alog", [128, 128])
    aneg = p.sb("aneg", [128, 128])
    pd4 = p.sb("pd4", [128, 4 * 128], BF16)
    pbq = p.sb("pbq", [96, 4 * 128], BF16)
    pcq = p.sb("pcq", [96, 4 * 128], BF16)
    sel = p.sb("sel", [128, 32], BF16)
    dtb = p.sb("dtb", [128, 8 * T], BF16)
    ut = p.sb("ut", [128, 2 * T], BF16)
    dtu = p.sb("dtu", [128, T], BF16)
    dub = p.sb("dub", [128, 512], BF16)
    beta = p.sb("beta", [128, T], BF16)
    hbuf = p.sb("hbuf", [128, T], BF16)
    hc = [p.sb(f"hc{q}", [128, T], BF16) for q in range(4)]
    ystg = p.sb("ystg", [32, 2 * T], BF16)
    spt = p.sb("spt", [128, 512])

    dma_in(p, dbc, dbc_d, cast=True)
    dma_in(p, wdt, wdt_d, cast=True)
    dma_in(p, bdt, bdt_d)
    dma_in(p, alog, al_d)
    for gm in range(4):
        dma_in(p, pd4[:, gm * 128:(gm + 1) * 128], pd_d[gm * 128:(gm + 1) * 128, :])
        dma_in(p, pbq[:, gm * 128:(gm + 1) * 128], pb_d[gm * 96:(gm + 1) * 96, :])
        dma_in(p, pcq[:, gm * 128:(gm + 1) * 128], pc_d[gm * 96:(gm + 1) * 96, :])
    dma_in(p, sel, sel_d)
    gin0, din0 = p.n["gin"], p.n["din"]

    def sa(sc):
        p.wait(sc, "din", din0)
        p.inc(sc.activation(aneg[:, :], alog[:, :], AF.Exp), "sc")
    p.sec("scalar", sa)
    sc0 = p.n["sc"]

    def va(v):
        v.wait_ge(p.sem["sc"], sc0)
        p.inc(v.tensor_scalar(aneg[:, :], aneg[:, :], -1.0, None, ALU.mult), "v")
    p.sec("vector", va)
    v_aneg = p.n["v"]

    pool = Pool(p, 2)
    jobs = []
    for mt in range(8):
        for tg in range(8):
            c0 = tg * 512
            mm_list = [(wdt[:, mt * 128:(mt + 1) * 128], dbc[0:64, c0:c0 + 512])]

            def ev(env, ps, pred, mt=mt, c0=c0):
                env.activation(spt[:, :], ps[:, :512], AF.Exp, bias=bdt[:, mt:mt + 1])
                ins = env.activation(dtb[:, mt * T + c0: mt * T + c0 + 512], spt[:, :], AF.Ln, bias=1.0)
                p.inc(ins, "sc")
            jobs.append((mm_list, 128, 512, ev))
    emit_mm(p, pool, jobs, ("scalar", "sc"), wait_before=[("gin", gin0), ("din", din0)])
    sc_dt = p.n["sc"]

    ps_dtb = p.ps("psdtb", [128, 512])
    ps_da = p.ps("psda", [128, 512])
    ps_dub = p.ps("psdub", [128, 512])
    ps_bb = p.ps("psbb", [128, 512])
    ps_cb = p.ps("pscb", [128, 512])
    ps_y = [pool.tiles[0][0:32, 0:512], pool.tiles[1][0:32, 0:512]]

    guard = {"dtb": None, "da": None, "dub": None, "bb": None, "cb": None,
             "y0": pool.freed[0], "y1": pool.freed[1]}
    dtu_ready = 0

    NTG = 8
    for g in range(32):
        mt = g // 4
        gm = g % 4
        if g % 4 == 0:
            ub = (mt % 2)
            def lu(gq, mt=mt, ub=ub):
                if mt >= 2:
                    p.wait(gq, "v", None)
                p.inc(gq.dma_start(out=ut[:, ub * T:(ub + 1) * T], in_=u_d[mt * 128:(mt + 1) * 128, :]), "gin", 16)
            p.sec("gpsimd", lu)
            gin_u = p.n["gin"]

            def vdtu(v, mt=mt, ub=ub):
                v.wait_ge(p.sem["gin"], gin_u)
                v.wait_ge(p.sem["sc"], sc_dt)
                if mt >= 1:
                    p.wait(v, "mm")
                ins = v.tensor_tensor(dtu[:, :], dtb[:, mt * T:(mt + 1) * T], ut[:, ub * T:(ub + 1) * T], ALU.mult)
                p.inc(ins, "v")
            p.sec("vector", vdtu)
            dtu_ready = p.n["v"]

        for q in range(4):
            acol = aneg[:, (g * 4 + q):(g * 4 + q) + 1]
            for tg in range(NTG):
                c0 = tg * 512
                def tmm(t, g=g, q=q, tg=tg, c0=c0, mt=mt, gm=gm, dtu_ready=dtu_ready):
                    if tg == 0 and q == 0 and g == 0:
                        p.wait(t, "v", v_aneg)
                        t.wait_ge(p.sem["sc"], sc_dt)
                    if tg == 0 and q == 0:
                        t.wait_ge(p.sem["v"], dtu_ready)
                    for nm, ps_ in (("dtb", ps_dtb), ("dub", ps_dub), ("bb", ps_bb), ("cb", ps_cb)):
                        if guard[nm] is not None:
                            t.wait_ge(p.sem[guard[nm][0]], guard[nm][1])
                    t.matmul(ps_dtb[:, :], pd4[:, gm * 128:(gm + 1) * 128], dtb[:, mt * T + c0: mt * T + c0 + 512], start=True, stop=True)
                    t.matmul(ps_dub[:, :], pd4[:, gm * 128:(gm + 1) * 128], dtu[:, c0:c0 + 512], start=True, stop=True)
                    t.matmul(ps_bb[:, :], pbq[:, q * 128:(q + 1) * 128], dbc[:, c0:c0 + 512], start=True, stop=True)
                    ins = t.matmul(ps_cb[:, :], pcq[:, q * 128:(q + 1) * 128], dbc[:, c0:c0 + 512], start=True, stop=True)
                    p.inc(ins, "mm")
                p.sec("tensor", tmm)
                mm_bc = p.n["mm"]

                def sda(sc, mm_bc=mm_bc, acol=acol):
                    sc.wait_ge(p.sem["mm"], mm_bc)
                    if guard["da"] is not None:
                        sc.wait_ge(p.sem[guard["da"][0]], guard["da"][1])
                    ins = sc.activation(ps_da[:, :], ps_dtb[:, :], AF.Exp, scale=acol)
                    p.inc(ins, "sc")
                p.sec("scalar", sda)
                sc_da = p.n["sc"]
                guard["dtb"] = ("sc", sc_da)

                def vsc(v, g=g, q=q, tg=tg, c0=c0, mm_bc=mm_bc, sc_da=sc_da):
                    v.wait_ge(p.sem["mm"], mm_bc)
                    i1 = p.inc(v.tensor_copy(dub[:, :], ps_dub[:, :]), "v")
                    v.wait_ge(p.sem["v"], i1)
                    i2 = p.inc(v.tensor_tensor(beta[:, c0:c0 + 512], dub[:, :], ps_bb[:, :], ALU.mult), "v")
                    v.wait_ge(p.sem["v"], i2)
                    v.wait_ge(p.sem["sc"], sc_da)
                    init = 0.0 if tg == 0 else hbuf[:, c0 - 1: c0]
                    i3 = p.inc(v.tensor_tensor_scan(hbuf[:, c0:c0 + 512], ps_da[:, :], beta[:, c0:c0 + 512], init, ALU.mult, ALU.add), "v")
                    v.wait_ge(p.sem["v"], i3)
                    ins = v.tensor_tensor(hc[q][:, c0:c0 + 512], hbuf[:, c0:c0 + 512], ps_cb[:, :], ALU.mult)
                    p.inc(ins, "v")
                p.sec("vector", vsc)
                vv = p.n["v"]
                guard["dub"] = ("v", vv)
                guard["bb"] = ("v", vv)
                guard["cb"] = ("v", vv)
                guard["da"] = ("v", vv)

        yb = g % 2
        for tg in range(NTG):
            c0 = tg * 512
            pyi = tg % 2
            def tsel(t, g=g, tg=tg, c0=c0, pyi=pyi):
                gk = "y%d" % pyi
                if guard[gk] is not None:
                    t.wait_ge(p.sem[guard[gk][0]], guard[gk][1])
                for q in range(4):
                    ins = t.matmul(ps_y[pyi][:, :], sel[:, :], hc[q][:, c0:c0 + 512], start=(q == 0), stop=(q == 3))
                p.inc(ins, "mm")
            p.sec("tensor", tsel)
            mm_y = p.n["mm"]

            def vy(v, g=g, tg=tg, c0=c0, pyi=pyi, yb=yb, mm_y=mm_y):
                v.wait_ge(p.sem["mm"], mm_y)
                if tg == 0 and g >= 2:
                    p.wait(v, "zd", (32 if fused_out else 16) * (g - 1))
                ins = v.tensor_copy(ystg[:, yb * T + c0: yb * T + c0 + 512], ps_y[pyi][:, :])
                p.inc(ins, "v")
            p.sec("vector", vy)
            guard["y%d" % pyi] = ("v", p.n["v"])
        v_y = p.n["v"]

        def dy(s, g=g, yb=yb, v_y=v_y):
            s.wait_ge(p.sem["v"], v_y)
            if fused_out:
                p.inc(s.dma_start(out=ys_d[g * 32:(g + 1) * 32, :], in_=ystg[:32, yb * T: yb * T + TH]), "zd", 16)
                p.inc(s.dma_start(out=ys_d[1024 + g * 32:1024 + (g + 1) * 32, :],
                                  in_=ystg[:32, yb * T + TH:(yb + 1) * T]), "zd", 16)
            else:
                p.inc(s.dma_start(out=ys_d[g * 32:(g + 1) * 32, :], in_=ystg[:32, yb * T:(yb + 1) * T]), "zd", 16)
        p.sec("sync", dy)

    def fin(s):
        p.wait(s, "zd")
    p.sec("sync", fin)
    p.close()
    return p.nc


def a2_consts():
    PD4 = np.zeros((4, 128, 128), dtype=BF)
    for gm in range(4):
        for pp in range(128):
            PD4[gm, gm * 32 + pp // 4, pp] = 1
    PB = np.zeros((4, 96, 128), dtype=BF)
    PC = np.zeros((4, 96, 128), dtype=BF)
    for q in range(4):
        for pp in range(128):
            PB[q, 64 + q * 4 + pp % 4, pp] = 1
            PC[q, 80 + q * 4 + pp % 4, pp] = 1
    SEL = np.zeros((128, 32), dtype=BF)
    for pp in range(128):
        SEL[pp, pp // 4] = 1
    return PD4.reshape(512, 128), PB.reshape(384, 128), PC.reshape(384, 128), SEL


def host_A2_inmaps(inputs, co_np, a1_results):
    PD4, PB, PC, SEL = a2_consts()
    WdtT = np.ascontiguousarray(inputs["L0_attn_W_dt"].T)
    b_dt = inputs["L0_attn_b_dt"]
    A_log = inputs["L0_attn_A_log"]
    maps = []
    for c in range(NCORE):
        b, j = c // 2, c % 2
        ch0 = j * 1024
        dbc_full = np.concatenate([np.asarray(a1_results[2 * b]["dbc"]),
                                   np.asarray(a1_results[2 * b + 1]["dbc"])], axis=1)
        u0 = np.asarray(a1_results[2 * b]["u"])
        u1 = np.asarray(a1_results[2 * b + 1]["u"])
        u_half = np.concatenate([u0[ch0:ch0 + 1024], u1[ch0:ch0 + 1024]], axis=1)
        bdt = b_dt[ch0:ch0 + 1024].reshape(8, 128).T.copy()
        AL = np.zeros((128, 128), dtype=np.float32)
        for g in range(32):
            for q in range(4):
                for pp in range(128):
                    AL[pp, g * 4 + q] = A_log[ch0 + g * 32 + pp // 4, q * 4 + pp % 4]
        m = dict(dbc=np.ascontiguousarray(dbc_full), u=np.ascontiguousarray(u_half),
                 WdtT=np.ascontiguousarray(WdtT[:, ch0:ch0 + 1024]), bdt=bdt, AL=AL,
                 PD4=PD4, PBq=PB, PCq=PC, SEL=SEL)
        maps.append(m)
    return maps


def kernel(**inputs):
    raise NotImplementedError("assembled later")


def build_A3(co_np, p=None):
    if p is None:
        p = P("A3")
    ys_d = p.din("ysl", [2048, TH], BF16)
    u_d = p.din("u", [2048, TH], BF16)
    z_d = p.din("z", [2048, TH], BF16)
    dsk_d = p.din("Dsk", [128, 16])
    wout_d = p.din("WoutT", [2048, 1024])
    xres_d = p.din("xres", [1024, TH])
    x1_d = p.dout("x1", [1024, TH])

    p.start()
    co = emit_consts(p, co_np)
    dsk = p.sb("dsk", [128, 16])
    dma_in(p, dsk, dsk_d)
    xres = p.sb("xres", [128, 8 * TH])
    load_fm(p, xres, xres_d, 1024, TH, cast=False)
    din0 = p.n["din"]

    sc1 = ExitStack()
    ybuf = p.sb("ybuf", [128, 16 * TH], BF16, sc1)
    yst = p.sb("yst", [128, 6 * TH], BF16, sc1)
    tmp = p.sb("tmp", [128, 2 * TH], BF16, sc1)
    szt = p.sb("szt", [128, 2 * TH], BF16, sc1)
    wob = p.sb("wob", [128, 16 * 1024], BF16, sc1)
    load_fm(p, wob, wout_d, 2048, 1024)
    gin_w = p.n["gin"]

    vready = 0
    for et in range(16):
        bb = et % 2
        def ld(g, et=et, bb=bb, vready=vready):
            if et >= 2:
                p.wait(g, "v", vready)
                p.wait(g, "sc")
            g.dma_start(out=yst[:, (bb * 3) * TH:(bb * 3 + 1) * TH], in_=ys_d[et * 128:(et + 1) * 128, :]).then_inc(p.sem["gin"], 16)
            g.dma_start(out=yst[:, (bb * 3 + 1) * TH:(bb * 3 + 2) * TH], in_=u_d[et * 128:(et + 1) * 128, :]).then_inc(p.sem["gin"], 16)
            g.dma_start(out=yst[:, (bb * 3 + 2) * TH:(bb * 3 + 3) * TH], in_=z_d[et * 128:(et + 1) * 128, :]).then_inc(p.sem["gin"], 16)
            p.n["gin"] += 48
        p.sec("gpsimd", ld)
        gin_t = p.n["gin"]

        def vt(v, et=et, bb=bb):
            v.wait_ge(p.sem["gin"], gin_t)
            p.wait(v, "din", din0)
            ins = v.scalar_tensor_tensor(tmp[:, bb * TH:(bb + 1) * TH], yst[:, (bb * 3 + 1) * TH:(bb * 3 + 2) * TH],
                                         dsk[:, et:et + 1], yst[:, (bb * 3) * TH:(bb * 3 + 1) * TH], ALU.mult, ALU.add)
            p.inc(ins, "v")
        p.sec("vector", vt)
        v_tmp = p.n["v"]

        def st(sc, et=et, bb=bb):
            sc.wait_ge(p.sem["gin"], gin_t)
            ins = sc.activation(szt[:, bb * TH:(bb + 1) * TH], yst[:, (bb * 3 + 2) * TH:(bb * 3 + 3) * TH], AF.Silu)
            p.inc(ins, "sc")
        p.sec("scalar", st)
        sc_z = p.n["sc"]

        def vm(v, et=et, bb=bb, sc_z=sc_z):
            v.wait_ge(p.sem["sc"], sc_z)
            ins = v.tensor_tensor(ybuf[:, et * TH:(et + 1) * TH], tmp[:, bb * TH:(bb + 1) * TH],
                                  szt[:, bb * TH:(bb + 1) * TH], ALU.mult)
            p.inc(ins, "v")
        p.sec("vector", vm)
        vready = p.n["v"]
    v_y = p.n["v"]

    pool = Pool(p, 4)
    jobs = []
    for mt in range(8):
        for ng in range(4):
            c0 = ng * 512
            mm_list = [(wob[:, k * 1024 + mt * 128: k * 1024 + (mt + 1) * 128],
                        ybuf[:, k * TH + c0: k * TH + c0 + 512]) for k in range(16)]

            def ev(env, ps, pred, mt=mt, c0=c0):
                ins = env.tensor_tensor(xres[:, mt * TH + c0: mt * TH + c0 + 512],
                                        ps[:, :512], xres[:, mt * TH + c0: mt * TH + c0 + 512], ALU.add)
                p.inc(ins, "v")
            jobs.append((mm_list, 128, 512, ev))
    emit_mm(p, pool, jobs, ("vector", "v"), wait_before=[("v", v_y), ("gin", gin_w), ("din", din0)])
    v_s = p.n["v"]
    sc1.close()

    emit_postln(p, xres, xres, TH, co, ready=[("v", v_s)])
    v_ln = p.n["v"]

    def dx(s):
        s.wait_ge(p.sem["v"], v_ln)
        for mt in range(8):
            p.inc(s.dma_start(out=x1_d[mt * 128:(mt + 1) * 128, :], in_=xres[:, mt * TH:(mt + 1) * TH]), "dout", 16)
        p.wait(s, "dout")
    p.sec("sync", dx)
    p.close()
    return p.nc


def host_A3_inmaps(inputs, co_np, a1, a2):
    WoutT = np.ascontiguousarray(inputs["L0_attn_W_out"].T)
    Dsk = inputs["L0_attn_D_skip"].reshape(16, 128).T.copy()
    x = inputs["L0_attn_x"]
    base = const_inmap(co_np)
    maps = []
    for c in range(NCORE):
        b, i = c // 2, c % 2
        t0 = i * TH
        ysl = np.concatenate([np.asarray(a2[2 * b]["ys"])[:, t0:t0 + TH],
                              np.asarray(a2[2 * b + 1]["ys"])[:, t0:t0 + TH]], axis=0)
        m = dict(base)
        m.update(ysl=np.ascontiguousarray(ysl), u=np.asarray(a1[c]["u"]), z=np.asarray(a1[c]["z"]),
                 Dsk=Dsk, WoutT=WoutT, xres=np.ascontiguousarray(x[b, t0:t0 + TH].T))
        maps.append(m)
    return maps


def build_A4(co_np, p=None, split_in=False):
    if p is None:
        p = P("A4")
    TX = TH + 1
    if split_in:
        x1o_d = p.din("x1own", [1024, TH])
        pc_d = p.din("pcol", [1024, 1])
    else:
        xh_d = p.din("xh", [1024, TX])
    mur_d = p.din("mur", [128, 8])
    muk_d = p.din("muk", [128, 8])
    wr_d = p.din("WrT", [1024, 1024])
    wk_d = p.din("WkT", [1024, 1024])
    wv_d = p.din("WvT", [1024, 1024])
    x2_d = p.dout("x2", [1024, TH])

    p.start()
    co = emit_consts(p, co_np)
    mur = p.sb("mur", [128, 8])
    muk = p.sb("muk", [128, 8])
    dma_in(p, mur, mur_d)
    dma_in(p, muk, muk_d)
    xh = p.sb("xh", [128, 8 * TX])
    if split_in:
        for kt in range(8):
            dma_in(p, xh[:, kt * TX + 1: kt * TX + 1 + TH], x1o_d[kt * 128:(kt + 1) * 128, :], cast=False)
            dma_in(p, xh[:, kt * TX: kt * TX + 1], pc_d[kt * 128:(kt + 1) * 128, 0:1], cast=False)
    else:
        load_fm(p, xh, xh_d, 1024, TX, cast=False)
    din0 = p.n["din"]

    sc1 = ExitStack()
    X = p.sb("X", [128, 8 * TH], BF16, sc1)
    rbuf = p.sb("rbuf", [128, 8 * TH], BF16, sc1)
    kabuf = p.sb("kabuf", [128, 8 * TH], BF16, sc1)
    wbuf = p.sb("wbuf", [128, 8 * 1024], BF16, sc1)
    dsc = p.sb("dsc", [128, TH], F32, sc1)
    ssc = p.sb("ssc", [128, 2 * 512], F32, sc1)
    pool = Pool(p, 4)

    def mk_mix(mu, which_v):
        for mt in range(8):
            def vm(v, mt=mt, mu=mu):
                if mt == 0:
                    p.wait(v, "din", din0)
                    p.wait(v, "mm")
                v.tensor_tensor(dsc[:, :], xh[:, mt * TX + 1: mt * TX + 1 + TH],
                                xh[:, mt * TX: mt * TX + TH], ALU.subtract)
                ins = v.scalar_tensor_tensor(X[:, mt * TH:(mt + 1) * TH], dsc[:, :],
                                             mu[:, mt:mt + 1], xh[:, mt * TX: mt * TX + TH], ALU.mult, ALU.add)
                p.inc(ins, "v")
            p.sec("vector", vm)

    def load_w(wd):
        def lw(g):
            p.wait(g, "mm")
            for kt in range(8):
                p.inc(g.dma_start(out=wbuf[:, kt * 1024:(kt + 1) * 1024], in_=wd[kt * 128:(kt + 1) * 128, :]), "gin", 16)
        p.sec("gpsimd", lw)
        return p.n["gin"]

    mk_mix(mur, None)
    v_xr = p.n["v"]
    g_wr = load_w(wr_d)
    jobs = []
    for mt in range(8):
        for ng in range(4):
            c0 = ng * 512
            mm_list = [(wbuf[:, k * 1024 + mt * 128: k * 1024 + (mt + 1) * 128],
                        X[:, k * TH + c0: k * TH + c0 + 512]) for k in range(8)]
            def ev(env, ps, pred, mt=mt, c0=c0):
                ins = env.activation(rbuf[:, mt * TH + c0: mt * TH + c0 + 512], ps[:, :512], AF.Sigmoid)
                p.inc(ins, "sc")
            jobs.append((mm_list, 128, 512, ev))
    emit_mm(p, pool, jobs, ("scalar", "sc"), wait_before=[("v", v_xr), ("gin", g_wr)])

    mk_mix(muk, None)
    v_xk = p.n["v"]
    g_wk = load_w(wk_d)
    jobs = []
    for mt in range(8):
        for ng in range(4):
            c0 = ng * 512
            mm_list = [(wbuf[:, k * 1024 + mt * 128: k * 1024 + (mt + 1) * 128],
                        X[:, k * TH + c0: k * TH + c0 + 512]) for k in range(8)]
            def ev(env, ps, pred, mt=mt, c0=c0):
                env.activation(ssc[:, :512], ps[:, :512], AF.Relu)
                ins = env.activation(kabuf[:, mt * TH + c0: mt * TH + c0 + 512], ssc[:, :512], AF.Square)
                p.inc(ins, "sc")
            jobs.append((mm_list, 128, 512, ev))
    emit_mm(p, pool, jobs, ("scalar", "sc"), wait_before=[("v", v_xk), ("gin", g_wk)])
    sc_ka = p.n["sc"]

    g_wv = load_w(wv_d)
    jobs = []
    for mt in range(8):
        for ng in range(4):
            c0 = ng * 512
            mm_list = [(wbuf[:, k * 1024 + mt * 128: k * 1024 + (mt + 1) * 128],
                        kabuf[:, k * TH + c0: k * TH + c0 + 512]) for k in range(8)]
            def ev(env, ps, pred, mt=mt, c0=c0):
                env.tensor_tensor(ssc[:, 512:1024], ps[:, :512], rbuf[:, mt * TH + c0: mt * TH + c0 + 512], ALU.mult)
                ins = env.tensor_tensor(xh[:, mt * TX + 1 + c0: mt * TX + 1 + c0 + 512],
                                        ssc[:, 512:1024], xh[:, mt * TX + 1 + c0: mt * TX + 1 + c0 + 512], ALU.add)
                p.inc(ins, "v")
            jobs.append((mm_list, 128, 512, ev))
    emit_mm(p, pool, jobs, ("vector", "v"), wait_before=[("sc", sc_ka), ("gin", g_wv)])
    v_s = p.n["v"]
    sc1.close()

    xout = p.sb("xout", [128, 8 * TH])
    emit_postln(p, xh, xout, TH, co, ready=[("v", v_s)], cstride=TX, col0=1)
    v_ln = p.n["v"]

    def dx(s):
        s.wait_ge(p.sem["v"], v_ln)
        for mt in range(8):
            p.inc(s.dma_start(out=x2_d[mt * 128:(mt + 1) * 128, :], in_=xout[:, mt * TH:(mt + 1) * TH]), "dout", 16)
        p.wait(s, "dout")
    p.sec("sync", dx)
    p.close()
    return p.nc


def host_A4_inmaps(inputs, co_np, x1_parts):
    WrT = np.ascontiguousarray(inputs["L0_ffn_W_r"].T)
    WkT = np.ascontiguousarray(inputs["L0_ffn_W_k"].T)
    WvT = np.ascontiguousarray(inputs["L0_ffn_W_v"].T)
    mur = inputs["L0_ffn_mu_r"].reshape(8, 128).T.copy()
    muk = inputs["L0_ffn_mu_k"].reshape(8, 128).T.copy()
    base = const_inmap(co_np)
    maps = []
    for c in range(NCORE):
        b, i = c // 2, c % 2
        xh = np.zeros((1024, TH + 1), dtype=np.float32)
        xh[:, 1:] = x1_parts[c]
        if i == 1:
            xh[:, 0] = x1_parts[2 * b][:, -1]
        m = dict(base)
        m.update(xh=xh, mur=mur, muk=muk, WrT=WrT, WkT=WkT, WvT=WvT)
        maps.append(m)
    return maps


def build_attn(co_np, biased=True, name="A5", npairs=4, nts=32, phases=9, dbg=False, p=None, fused_out=False):
    if p is None:
        p = P(name)
    HP = npairs
    W = 512
    xf_d = p.din("xf", [1024, T])
    wq_d = p.din("WQ", [1024, W])
    wk_d = p.din("WK", [1024, W])
    wv_d = p.din("WV", [1024, W])
    bq_d = p.din("bQ", [128, HP])
    bk_d = p.din("bK", [128, HP])
    bv_d = p.din("bV", [128, HP])
    msk_d = p.din("MSK", [4 * 128, 512], BF16)
    z_d = p.dout("z", [2 * W, TH] if fused_out else [W, T], BF16)
    if dbg:
        dbg_rows_d = p.dout("dbg_rows", [1, 2 * T], BF16)
        dbg_mhc_d = p.dout("dbg_mhc", [128, 128])
        dbg_ds_d = p.dout("dbg_ds", [128, 16])
        dbg_psb_d = p.dout("dbg_psb", [128, T], BF16)

    p.start()
    co = emit_consts(p, co_np)
    bq = p.sb("bq", [128, HP]); bk = p.sb("bk", [128, HP]); bv = p.sb("bv", [128, HP])
    dma_in(p, bq, bq_d); dma_in(p, bk, bk_d); dma_in(p, bv, bv_d)
    msk = p.sb("msk", [128, 4 * 512], BF16)
    for mI in range(4):
        dma_in(p, msk[:, mI * 512:(mI + 1) * 512], msk_d[mI * 128:(mI + 1) * 128, :])
    zout = p.sb("zout", [128, T], BF16)
    Psb = p.sb("Psb", [128, T], BF16)
    q2 = p.sb("q2", [128, T], BF16)
    rowsc = p.sb("rowsc", [1, 2 * T], BF16)
    mx1 = p.sb("mx1", [1, 4])
    mhc = p.sb("mhc", [128, 128])
    dsums = p.sb("dsums", [128, 16])
    PTsb = p.sb("PTsb", [128, 512], BF16)
    zsc = p.sb("zsc", [128, 128])
    rrow = p.sb("rrow", [1, 128])

    big = ExitStack()
    xf = p.sb("xf", [128, 8 * T], BF16, big)
    wqb = p.sb("wqb", [128, 8 * W], BF16, big)
    wkb = p.sb("wkb", [128, 8 * W], BF16, big)
    wvb = p.sb("wvb", [128, 8 * W], BF16, big)
    qp = p.sb("qp", [128, T], BF16, big)
    kp = p.sb("kp", [128, T], BF16, big)
    vp = p.sb("vp", [128, T], BF16, big)
    vT = p.sb("vT", [128, 2 * T], BF16, big)

    load_fm(p, xf, xf_d, 1024, T)
    load_fm(p, wqb, wq_d, 1024, W)
    load_fm(p, wkb, wk_d, 1024, W)
    load_fm(p, wvb, wv_d, 1024, W)
    gin0, din0 = p.n["gin"], p.n["din"]

    pool = Pool(p, 3)
    ps_pt = p.ps("pspt", [128, 512], BF16)
    ps_z = p.ps("psz", [128, 128])
    ps_sm = p.ps("pssm", [128, 512])

    NT = nts
    esem = "v"
    zdma_n = 0
    mm_psb_free = 0
    v_misc = 0
    v_ptevac = 0
    v_zevac = 0
    sc_track = {}

    for hp in range(HP):
        jobs = []
        for pi, (wb, bcol, dst) in enumerate(((wqb, bq, qp), (wkb, bk, kp), (wvb, bv, vp))):
            for ng in range(8):
                c0 = ng * 512
                mm_list = [(wb[:, k * W + hp * 128: k * W + (hp + 1) * 128],
                            xf[:, k * T + c0: k * T + c0 + 512]) for k in range(8)]
                if biased:
                    def ev(env, ps, pred, bcol=bcol, dst=dst, c0=c0, hp=hp):
                        ins = env.tensor_scalar(dst[:, c0:c0 + 512], ps[:, :512], bcol[:, hp:hp + 1], None, ALU.add)
                        p.inc(ins, "v")
                else:
                    def ev(env, ps, pred, dst=dst, c0=c0):
                        ins = env.tensor_copy(dst[:, c0:c0 + 512], ps[:, :512])
                        p.inc(ins, "v")
                jobs.append((mm_list, 128, 512, ev))
        wb4 = [("gin", gin0), ("din", din0)]
        if hp > 0:
            wb4 += [("zd", zdma_n), ("mm", mm_psb_free), ("v", v_zevac)]
        emit_mm(p, pool, jobs, ("vector", "v"), wait_before=wb4)
        sc_qkv = p.n["sc"]; v_qkv = p.n["v"]

        for hl in range(2):
            if phases < 2:
                continue
            h = 2 * hp + hl
            hb = hl * 64

            for grp in range(4):
                def tvt(t, grp=grp, hb=hb, hl=hl):
                    if grp == 0 and hl == 0:
                        p.wait(t, "v", v_qkv)
                    if v_ptevac > 0:
                        t.wait_ge(p.sem["v"], v_ptevac)
                    for j in range(8):
                        kt = grp * 8 + j
                        ins = t.transpose(ps_pt[:, j * 64: j * 64 + 64],
                                          vp[hb:hb + 64, kt * 128:(kt + 1) * 128], co["ident"][hb:hb + 64, hb:hb + 64])
                    p.inc(ins, "mm")
                p.sec("tensor", tvt)
                mmv = p.n["mm"]

                def vvt(v, grp=grp, hl=hl, mmv=mmv):
                    v.wait_ge(p.sem["mm"], mmv)
                    ins = v.tensor_copy(vT[:, hl * T + grp * 512:hl * T + (grp + 1) * 512], ps_pt[:, :])
                    p.inc(ins, "v")
                p.sec("vector", vvt)
                v_ptevac = p.n["v"]

            if phases < 3:
                continue
            def sq2(sc, hb=hb, vcur=0):
                sc.wait_ge(p.sem["v"], v_qkv)
                p.wait(sc, "v")
                ins = sc.activation(q2[hb:hb + 64, :], qp[hb:hb + 64, :], AF.Square)
                p.inc(ins, "sc")
            p.sec("scalar", sq2)
            sc_q2 = p.n["sc"]

            for ng in range(8):
                def tqn(t, ng=ng, vcur=None):
                    t.wait_ge(p.sem["sc"], sc_q2)
                    p.wait(t, "v")
                    ins = t.matmul(ps_sm[0:1, :512], co["ones_bf"][hb:hb + 64, :],
                                   q2[hb:hb + 64, ng * 512:(ng + 1) * 512], start=True, stop=True)
                    p.inc(ins, "mm")
                p.sec("tensor", tqn)
                mmq = p.n["mm"]

                def vqn(v, ng=ng, mmq=mmq):
                    v.wait_ge(p.sem["mm"], mmq)
                    ins = v.tensor_copy(rowsc[0:1, ng * 512:(ng + 1) * 512], ps_sm[0:1, :512])
                    p.inc(ins, "v")
                p.sec("vector", vqn)
                v_misc = p.n["v"]

            def sk2(sc, hb=hb):
                p.wait(sc, "v", v_misc)
                ins = sc.activation(q2[hb:hb + 64, :], kp[hb:hb + 64, :], AF.Square)
                p.inc(ins, "sc")
            p.sec("scalar", sk2)
            sc_k2 = p.n["sc"]

            for ng in range(8):
                def tkn(t, ng=ng):
                    t.wait_ge(p.sem["sc"], sc_k2)
                    p.wait(t, "v")
                    ins = t.matmul(ps_sm[0:1, :512], co["ones_bf"][hb:hb + 64, :],
                                   q2[hb:hb + 64, ng * 512:(ng + 1) * 512], start=True, stop=True)
                    p.inc(ins, "mm")
                p.sec("tensor", tkn)
                mmk = p.n["mm"]

                def vkn(v, ng=ng, mmk=mmk):
                    v.wait_ge(p.sem["mm"], mmk)
                    ins = v.tensor_copy(rowsc[0:1, T + ng * 512: T + (ng + 1) * 512], ps_sm[0:1, :512])
                    p.inc(ins, "v")
                p.sec("vector", vkn)
                v_misc = p.n["v"]

            def vmh(v):
                p.wait(v, "v")
                i1 = p.inc(v.tensor_reduce(mx1[0:1, 0:1], rowsc[0:1, T:2 * T], AX.X, ALU.max), "v")
                v.wait_ge(p.sem["v"], i1)
                ins = v.tensor_scalar(rowsc[0:1, 0:T], rowsc[0:1, 0:T],
                                      mx1[0:1, 0:1], None, ALU.mult)
                p.inc(ins, "v")
            p.sec("vector", vmh)
            v_mh = p.n["v"]

            def smh(sc):
                sc.wait_ge(p.sem["v"], v_mh)
                ins = sc.activation(rowsc[0:1, 0:T], rowsc[0:1, 0:T], AF.Sqrt)
                p.inc(ins, "sc")
            p.sec("scalar", smh)
            sc_mh = p.n["sc"]

            def tmh(t):
                t.wait_ge(p.sem["sc"], sc_mh)
                if v_ptevac:
                    t.wait_ge(p.sem["v"], v_ptevac)
                for ts in range(NT):
                    ins = t.transpose(ps_pt[:, 2 * ts:2 * ts + 1], rowsc[0:1, ts * 128:(ts + 1) * 128],
                                      co["ident"][0:1, 0:1])
                p.inc(ins, "mm")
            p.sec("tensor", tmh)
            mm_mh = p.n["mm"]

            def vmh2(v, mm_mh=mm_mh):
                v.wait_ge(p.sem["mm"], mm_mh)
                i1 = p.inc(v.tensor_copy(mhc[:, 0:64], ps_pt[:, 0:64]), "v")
                v.wait_ge(p.sem["v"], i1)
                ins = v.tensor_scalar(mhc[:, 64:128], mhc[:, 0:64], -0.125, None, ALU.mult)
                p.inc(ins, "v")
            p.sec("vector", vmh2)
            v_misc = p.n["v"]
            v_ptevac = p.n["v"]

            if phases < 4:
                continue
            for ts in range(NT):
                nkg = ts // 4 + 1
                mI = ts % 4
                splan = []

                scbase = p.n["sc"]
                def tS(t, ts=ts, nkg=nkg, hb=hb, scbase=scbase):
                    if ts == 0:
                        t.wait_ge(p.sem["v"], v_misc)
                    for kg in range(nkg):
                        j = pool.rr % pool.nb
                        pool.rr += 1
                        if pool.freed[j] is not None:
                            t.wait_ge(p.sem[pool.freed[j][0]], pool.freed[j][1])
                        pool.freed[j] = ("sc", scbase + kg + 1)
                        ins = t.matmul(pool.tiles[j][:, :512], qp[hb:hb + 64, ts * 128:(ts + 1) * 128],
                                       kp[hb:hb + 64, kg * 512:(kg + 1) * 512], start=True, stop=True)
                        cnt = p.inc(ins, "mm")
                        splan.append((kg, j, cnt))
                p.sec("tensor", tS)

                def sexp(sc, ts=ts, nkg=nkg):
                    sc.wait_ge(p.sem["mm"], mm_psb_free)
                    sc.wait_ge(p.sem["v"], v_zevac) if v_zevac else None
                    for kg, j, cnt in splan:
                        sc.wait_ge(p.sem["mm"], cnt)
                        kwargs = {}
                        if kg < nkg - 1:
                            kwargs["accum_out"] = dsums[:, kg:kg + 1]
                        ins = sc.activation(Psb[:, kg * 512:(kg + 1) * 512], pool.tiles[j][:, :512],
                                            AF.Exp, scale=0.125, bias=mhc[:, 64 + 2 * ts:65 + 2 * ts], **kwargs)
                        cnt2 = p.inc(ins, "sc")
                p.sec("scalar", sexp)
                sc_exp = p.n["sc"]
                if phases < 6:
                    mm_psb_free = p.n["mm"]
                    continue

                def vdiag(v, ts=ts, nkg=nkg, mI=mI, sc_exp=sc_exp):
                    v.wait_ge(p.sem["sc"], sc_exp)
                    dg0 = (nkg - 1) * 512
                    i1 = p.inc(v.tensor_tensor(Psb[:, dg0:dg0 + 512], Psb[:, dg0:dg0 + 512],
                                               msk[:, mI * 512:(mI + 1) * 512], ALU.mult), "v")
                    v.wait_ge(p.sem["v"], i1)
                    i2 = p.inc(v.tensor_reduce(dsums[:, nkg - 1:nkg], Psb[:, dg0:dg0 + 512], AX.X, ALU.add), "v")
                    v.wait_ge(p.sem["v"], i2)
                    i3 = p.inc(v.tensor_reduce(dsums[:, 15:16], dsums[:, 0:nkg], AX.X, ALU.add), "v")
                    v.wait_ge(p.sem["v"], i3)
                    ins = v.reciprocal(dsums[:, 14:15], dsums[:, 15:16])
                    p.inc(ins, "v")
                p.sec("vector", vdiag)
                v_diag = p.n["v"]
                if phases < 7:
                    mm_psb_free = p.n["mm"]
                    v_zevac = p.n["v"]
                    continue

                def trr(t, v_diag=v_diag):
                    t.wait_ge(p.sem["v"], v_diag)
                    ins = t.transpose(ps_sm[:1, 0:128], dsums[:, 14:15], co["identf"][:, :])
                    p.inc(ins, "mm")
                p.sec("tensor", trr)
                mm_rr = p.n["mm"]

                def vrr(v, mm_rr=mm_rr):
                    v.wait_ge(p.sem["mm"], mm_rr)
                    ins = v.tensor_copy(rrow[0:1, :], ps_sm[0:1, 0:128])
                    p.inc(ins, "v")
                p.sec("vector", vrr)
                v_rr = p.n["v"]

                def tdb(t, v_rr=v_rr, hb=hb):
                    t.wait_ge(p.sem["v"], v_rr)
                    ins = t.matmul(ps_sm[hb:hb + 64, 128:256], co["onesrow"][0:1, 0:64], rrow[0:1, :],
                                   start=True, stop=True)
                    p.inc(ins, "mm")
                p.sec("tensor", tdb)
                mm_db = p.n["mm"]
                if phases < 8:
                    mm_psb_free = p.n["mm"]
                    v_zevac = p.n["v"]
                    continue

                for kg in range(nkg):
                    def tpt(t, kg=kg, ts=ts, v_diag=v_diag):
                        if kg == nkg - 1:
                            t.wait_ge(p.sem["v"], v_diag)
                        if v_ptevac:
                            t.wait_ge(p.sem["v"], v_ptevac)
                        for j in range(4):
                            ins = t.transpose(ps_pt[:, j * 128:(j + 1) * 128],
                                              Psb[:, kg * 512 + j * 128: kg * 512 + (j + 1) * 128],
                                              co["ident"][:, :])
                        p.inc(ins, "mm")
                    p.sec("tensor", tpt)
                    mm_pt = p.n["mm"]

                    def vpt(v, mm_pt=mm_pt):
                        v.wait_ge(p.sem["mm"], mm_pt)
                        ins = v.tensor_copy(PTsb[:, :], ps_pt[:, :])
                        p.inc(ins, "v")
                    p.sec("vector", vpt)
                    v_ptevac = p.n["v"]

                    def tav(t, kg=kg, ts=ts, hl=hl, v_ptevac=v_ptevac):
                        t.wait_ge(p.sem["v"], v_ptevac)
                        if kg == 0 and v_zevac:
                            t.wait_ge(p.sem["v"], v_zevac)
                        hb2 = hl * 64
                        for j in range(4):
                            kt = kg * 4 + j
                            ins = t.matmul(ps_z[hb2:hb2 + 64, :], vT[:, hl * T + kt * 64: hl * T + (kt + 1) * 64],
                                           PTsb[:, j * 128:(j + 1) * 128],
                                           start=(kg == 0 and j == 0), stop=(kg == nkg - 1 and j == 3))
                        if kg == nkg - 1:
                            p.inc(ins, "mm")
                    p.sec("tensor", tav)
                mm_av = p.n["mm"]
                mm_psb_free = mm_av
                if phases < 9:
                    v_zevac = p.n["v"]
                    continue

                def vz(v, ts=ts, hp=hp, hl=hl, mm_av=mm_av):
                    v.wait_ge(p.sem["mm"], mm_av)
                    hb2 = hl * 64
                    i1 = p.inc(v.tensor_copy(zsc[hb2:hb2 + 64, :], ps_z[hb2:hb2 + 64, :]), "v")
                    v.wait_ge(p.sem["v"], i1)
                    if ts == 0 and hl == 0 and hp > 0:
                        p.wait(v, "zd", (32 if fused_out else 16) * hp)
                    ins = v.tensor_tensor(zout[hl * 64:hl * 64 + 64, ts * 128:(ts + 1) * 128],
                                          zsc[hb2:hb2 + 64, :], ps_sm[hb2:hb2 + 64, 128:256], ALU.mult)
                    p.inc(ins, "v")
                p.sec("vector", vz)
                v_zevac = p.n["v"]

        if dbg and hp == 0:
            def ddbg(s, v_zevac=v_zevac):
                s.wait_ge(p.sem["v"], v_zevac)
                p.inc(s.dma_start(out=dbg_rows_d, in_=rowsc[0:1, :]), "dout", 16)
                p.inc(s.dma_start(out=dbg_mhc_d, in_=mhc[:, :]), "dout", 16)
                p.inc(s.dma_start(out=dbg_ds_d, in_=dsums[:, :]), "dout", 16)
                p.inc(s.dma_start(out=dbg_psb_d, in_=Psb[:, :]), "dout", 16)
            p.sec("sync", ddbg)

        def dz(s, hp=hp, v_zevac=v_zevac):
            s.wait_ge(p.sem["v"], v_zevac)
            if fused_out:
                p.inc(s.dma_start(out=z_d[hp * 128:(hp + 1) * 128, :], in_=zout[:, 0:TH]), "zd", 16)
                p.inc(s.dma_start(out=z_d[W + hp * 128: W + (hp + 1) * 128, :], in_=zout[:, TH:T]), "zd", 16)
            else:
                p.inc(s.dma_start(out=z_d[hp * 128:(hp + 1) * 128, :], in_=zout[:, :]), "zd", 16)
        p.sec("sync", dz)
        zdma_n = p.n["zd"]

    def fin(s):
        p.wait(s, "zd")
    p.sec("sync", fin)
    big.close()
    p.close()
    return p.nc


def moe_e8():
    E8 = np.zeros((8, 8 * 128), dtype=BF)
    for e in range(8):
        E8[e, e * 128:(e + 1) * 128] = 1
    return E8


def attn_masks():
    MSK = np.zeros((4, 128, 512), dtype=BF)
    for m in range(4):
        for pp in range(128):
            MSK[m, pp, :m * 128 + pp + 1] = 1
    return MSK.reshape(4 * 128, 512)


def host_A5_inmaps(inputs, co_np, x2_parts):
    MSK = attn_masks()
    WQ = inputs["L1_attn_W_Q"]; WK = inputs["L1_attn_W_K"]; WV = inputs["L1_attn_W_V"]
    bQ = inputs["L1_attn_b_Q"]; bK = inputs["L1_attn_b_K"]; bV = inputs["L1_attn_b_V"]
    base = const_inmap(co_np)
    maps = []
    for c in range(NCORE):
        b, j = c // 2, c % 2
        h0 = j * 8
        xf = np.concatenate([x2_parts[2 * b], x2_parts[2 * b + 1]], axis=1)
        def wslice(Wt):
            return np.concatenate([Wt[h] for h in range(h0, h0 + 8)], axis=1)
        def bslice(bt):
            return bt[h0:h0 + 8].reshape(4, 128).T.copy()
        m = dict(base)
        m.update(xf=np.ascontiguousarray(xf), WQ=np.ascontiguousarray(wslice(WQ)),
                 WK=np.ascontiguousarray(wslice(WK)), WV=np.ascontiguousarray(wslice(WV)),
                 bQ=bslice(bQ), bK=bslice(bK), bV=bslice(bV), MSK=MSK)
        maps.append(m)
    return maps


def build_A6(co_np, nexp=8, p=None):
    if p is None:
        p = P("A6")
    z_d = p.din("zsl", [1024, TH], BF16)
    wo_d = p.din("WO", [1024, 1024])
    bo_d = p.din("bO", [128, 8])
    x2_d = p.din("x2", [1024, TH])
    rmsw_d = p.din("rmsw", [128, 8])
    gw_d = p.din("gwT", [1024, 8])
    w1_d = p.din("W1all", [8 * 1024, 4096])
    b1_d = p.din("b1c", [128, 8 * 32])
    w2_d = p.din("W2all", [8 * 4096, 1024])
    b2_d = p.din("b2c", [128, 8 * 8])
    e8_d = p.din("E8", [8, 8 * 128], BF16)
    x4_d = p.dout("x4", [1024, TH])
    x3_d = p.dint(p.pre + "x3scr", [1024, TH], F32)
    acc_d = p.dint(p.pre + "accscr", [1024, TH], F32)

    p.start()
    co = emit_consts(p, co_np)
    bo = p.sb("bo", [128, 8]); rmsw = p.sb("rmsw", [128, 8])
    b1c = p.sb("b1c", [128, 8 * 32]); b2c = p.sb("b2c", [128, 8 * 8])
    dma_in(p, bo, bo_d); dma_in(p, rmsw, rmsw_d); dma_in(p, b1c, b1_d); dma_in(p, b2c, b2_d)
    xn = p.sb("xn", [128, 8 * TH], BF16)
    lgf = p.sb("lgf", [8, TH])
    lgtile = p.sb("lgtile", [128, 8])
    mx8 = p.sb("mx8", [128, 8])
    et8 = p.sb("et8", [128, 8])
    idx8 = p.sb("idx8", [128, 8], U32)
    mT8 = p.sb("mT8", [128, 8])
    scol = p.sb("scol", [128, 2])
    maskR = p.sb("maskR", [8, TH], BF16)
    e8c = p.sb("e8c", [8, 8 * 128], BF16)
    dma_in(p, e8c, e8_d)
    scoreR = p.sb("scoreR", [1, TH], BF16)
    rmsrow = p.sb("rmsrow", [1, TH])
    pool = Pool(p, 4)

    sc1 = ExitStack()
    x2b = p.sb("x2b", [128, 8 * TH], F32, sc1)
    zb = p.sb("zb", [128, 8 * TH], BF16, sc1)
    wob = p.sb("wob", [128, 8 * 1024], BF16, sc1)
    gwb = p.sb("gwb", [128, 8 * 8], F32, sc1)
    gw2 = p.sb("gw2", [128, 8 * 8], F32, sc1)
    rstc = p.sb("rstc", [128, 1], F32, sc1)
    load_fm(p, x2b, x2_d, 1024, TH, cast=False)
    load_fm(p, zb, z_d, 1024, TH)
    load_fm(p, wob, wo_d, 1024, 1024)
    load_fm(p, gwb, gw_d, 1024, 8, cast=False)
    gin0, din0 = p.n["gin"], p.n["din"]

    jobs = []
    for mt in range(8):
        for ng in range(4):
            c0 = ng * 512
            mm_list = [(wob[:, k * 1024 + mt * 128: k * 1024 + (mt + 1) * 128],
                        zb[:, k * TH + c0: k * TH + c0 + 512]) for k in range(8)]
            def ev(env, ps, pred, mt=mt, c0=c0):
                i1 = env.tensor_scalar(ps[:, :512], ps[:, :512], bo[:, mt:mt + 1], None, ALU.add)
                ins = env.tensor_tensor(x2b[:, mt * TH + c0: mt * TH + c0 + 512], ps[:, :512],
                                        x2b[:, mt * TH + c0: mt * TH + c0 + 512], ALU.add)
                p.inc(ins, "v")
            jobs.append((mm_list, 128, 512, ev))
    emit_mm(p, pool, jobs, ("vector", "v"), wait_before=[("gin", gin0), ("din", din0)])
    v_s = p.n["v"]
    emit_postln(p, x2b, x2b, TH, co, ready=[("v", v_s)])
    v_x3 = p.n["v"]

    lnsc = ExitStack()
    sqf = p.sb("rsq", [128, 8 * 512], BF16, lnsc)
    r_t2 = p.sb("rt2", [1, 512], F32, lnsc)
    ps_a = p.ps("rpa", [128, 512], F32, lnsc)
    ps_mu = p.ps("rpmu", [128, 512], F32, lnsc)
    mm_prev2 = 0
    for ng in range(4):
        c0 = ng * 512
        def rsq(sc, c0=c0, ng=ng, mm_prev2=mm_prev2):
            if ng == 0:
                sc.wait_ge(p.sem["v"], v_x3)
            else:
                sc.wait_ge(p.sem["mm"], mm_prev2)
            for mt in range(8):
                p.inc(sc.activation(sqf[:, mt * 512:(mt + 1) * 512], x2b[:, mt * TH + c0: mt * TH + c0 + 512], AF.Square), "sc")
        p.sec("scalar", rsq)
        sc_sq = p.n["sc"]

        def rst(t, c0=c0, ng=ng):
            t.wait_ge(p.sem["sc"], sc_sq)
            if ng > 0:
                p.wait(t, "v")
            for mt in range(8):
                ins = t.matmul(ps_a[0:1, :512], co["ones_bf"][:, :], sqf[:, mt * 512:(mt + 1) * 512],
                               start=(mt == 0), stop=(mt == 7))
            p.inc(ins, "mm")
        p.sec("tensor", rst)
        mm_st = p.n["mm"]

        def rvs(v, c0=c0, mm_st=mm_st):
            v.wait_ge(p.sem["mm"], mm_st)
            ins = v.tensor_scalar(r_t2[0:1, :512], ps_a[0:1, :512], 1.0 / 1024, 1e-6, ALU.mult, ALU.add)
            p.inc(ins, "v")
        p.sec("vector", rvs)
        v_t2 = p.n["v"]

        def rss(sc, c0=c0, v_t2=v_t2):
            sc.wait_ge(p.sem["v"], v_t2)
            i1 = p.inc(sc.activation(r_t2[0:1, :512], r_t2[0:1, :512], AF.Ln), "sc")
            sc.wait_ge(p.sem["sc"], i1)
            ins = sc.activation(rmsrow[0:1, c0:c0 + 512], r_t2[0:1, :512], AF.Exp, scale=-0.5)
            p.inc(ins, "sc")
        p.sec("scalar", rss)
        sc_rms = p.n["sc"]

        def rbc(t, c0=c0, sc_rms=sc_rms):
            t.wait_ge(p.sem["sc"], sc_rms)
            ins = t.matmul(ps_mu[:, :512], co["onesrow"][:, :], rmsrow[0:1, c0:c0 + 512], start=True, stop=True)
            p.inc(ins, "mm")
        p.sec("tensor", rbc)
        mm_prev2 = p.n["mm"]
        mm_bc = mm_prev2

        def rxn(v, c0=c0, mm_bc=mm_bc):
            v.wait_ge(p.sem["mm"], mm_bc)
            for mt in range(8):
                ins = v.scalar_tensor_tensor(xn[:, mt * TH + c0: mt * TH + c0 + 512],
                                             x2b[:, mt * TH + c0: mt * TH + c0 + 512],
                                             rmsw[:, mt:mt + 1], ps_mu[:, :512], ALU.mult, ALU.mult)
                p.inc(ins, "v")
        p.sec("vector", rxn)
    v_xn = p.n["v"]
    lnsc.close()

    def dx3(s):
        s.wait_ge(p.sem["v"], v_xn)
        for mt in range(8):
            p.inc(s.dma_start(out=x3_d[mt * 128:(mt + 1) * 128, :], in_=x2b[:, mt * TH:(mt + 1) * TH]), "zd", 16)
    p.sec("sync", dx3)
    zd_x3 = p.n["zd"]

    gsc = ExitStack()
    ps_g = p.ps("psg", [128, 512], F32, gsc)
    def vgw2(v):
        p.wait(v, "din", din0)
        for k in range(8):
            p.inc(v.tensor_scalar(gw2[:, k * 8:(k + 1) * 8], gwb[:, k * 8:(k + 1) * 8],
                                  rmsw[:, k:k + 1], None, ALU.mult), "v")
    p.sec("vector", vgw2)
    for ng in range(4):
        def tgate(t, ng=ng):
            p.wait(t, "v")
            for k in range(8):
                ins = t.matmul(ps_g[0:8, 0:512], gw2[:, k * 8:(k + 1) * 8],
                               x2b[:, k * TH + ng * 512: k * TH + ng * 512 + 512],
                               start=(k == 0), stop=(k == 7))
            p.inc(ins, "mm")
        p.sec("tensor", tgate)
        mm_g = p.n["mm"]

        def vge(v, ng=ng, mm_g=mm_g):
            v.wait_ge(p.sem["mm"], mm_g)
            ins = v.tensor_copy(lgf[0:8, ng * 512:(ng + 1) * 512], ps_g[0:8, 0:512])
            p.inc(ins, "v")
        p.sec("vector", vge)
    v_lg = p.n["v"]

    for tt in range(16):
        def tlt(t, tt=tt):
            t.wait_ge(p.sem["v"], p.n["v"])
            t.transpose(ps_g[:, 0:8], lgf[0:8, tt * 128:(tt + 1) * 128], co["identf"][0:8, 0:8])
            ins = t.transpose(ps_g[:, 8:9], rmsrow[0:1, tt * 128:(tt + 1) * 128], co["identf"][0:1, 0:1])
            p.inc(ins, "mm")
        p.sec("tensor", tlt)
        mm_lt = p.n["mm"]

        def vgt(v, tt=tt, mm_lt=mm_lt):
            v.wait_ge(p.sem["mm"], mm_lt)
            ia = p.inc(v.tensor_copy(rstc[:, 0:1], ps_g[:, 8:9]), "v")
            v.wait_ge(p.sem["v"], ia)
            i0 = p.inc(v.tensor_scalar(lgtile[:, :], ps_g[:, 0:8], rstc[:, 0:1], None, ALU.mult), "v")
            v.wait_ge(p.sem["v"], i0)
            i1 = p.inc(v.max(mx8[:, :], lgtile[:, :]), "v")
            v.wait_ge(p.sem["v"], i1)
            i2 = p.inc(v.max_index(idx8[:, :], mx8[:, :], lgtile[:, :]), "v")
            v.wait_ge(p.sem["v"], i2)
            ins = v.tensor_scalar(scol[:, 0:1], mx8[:, 0:1], -1.0, None, ALU.mult)
            p.inc(ins, "v")
        p.sec("vector", vgt)
        v_gt = p.n["v"]

        def sgt(sc, tt=tt, v_gt=v_gt):
            sc.wait_ge(p.sem["v"], v_gt)
            ins = sc.activation(et8[:, :], lgtile[:, :], AF.Exp, bias=scol[:, 0:1])
            p.inc(ins, "sc")
        p.sec("scalar", sgt)
        sc_e8 = p.n["sc"]

        def vms(v, tt=tt, sc_e8=sc_e8):
            v.wait_ge(p.sem["sc"], sc_e8)
            i1 = p.inc(v.tensor_reduce(scol[:, 1:2], et8[:, :], AX.X, ALU.add), "v")
            v.wait_ge(p.sem["v"], i1)
            i2 = p.inc(v.reciprocal(scol[:, 0:1], scol[:, 1:2]), "v")
            v.wait_ge(p.sem["v"], i2)
            for e in range(8):
                ins = v.tensor_scalar(mT8[:, e:e + 1], idx8[:, 0:1], float(e), None, ALU.is_equal)
            p.inc(ins, "v")
        p.sec("vector", vms)
        v_ms = p.n["v"]

        def tmt(t, tt=tt, v_ms=v_ms):
            t.wait_ge(p.sem["v"], v_ms)
            t.transpose(ps_g[0:8, 128:256], mT8[:, :], co["identf"][:, :])
            ins = t.transpose(ps_g[0:1, 256:384], scol[:, 0:1], co["identf"][:, :])
            p.inc(ins, "mm")
        p.sec("tensor", tmt)
        mm_mt = p.n["mm"]

        def vmr(v, tt=tt, mm_mt=mm_mt):
            v.wait_ge(p.sem["mm"], mm_mt)
            v.tensor_copy(maskR[0:8, tt * 128:(tt + 1) * 128], ps_g[0:8, 128:256])
            ins = v.tensor_copy(scoreR[0:1, tt * 128:(tt + 1) * 128], ps_g[0:1, 256:384])
            p.inc(ins, "v")
        p.sec("vector", vmr)
    v_gate = p.n["v"]
    gsc.close()
    sc1.close()

    ex = ExitStack()
    w1b = p.sb("w1b", [128, 8 * 4096], BF16, ex)
    hb = p.sb("hb", [128, 32 * 1024], BF16, ex)
    w2b = p.sb("w2b", [128, 32 * 128], BF16, ex)
    mkb = p.sb("mkb", [128, TH], BF16, ex)
    tmpe = p.sb("tmpe", [128, 1024], F32, ex)
    ps_mk = p.ps("psmk", [128, 512], F32, ex)
    zd_region = {}
    for e in range(nexp):
        def lw1(g, e=e):
            p.wait(g, "mm")
            p.wait(g, "v", v_xn)
            for kt in range(8):
                p.inc(g.dma_start(out=w1b[:, kt * 4096:(kt + 1) * 4096],
                                  in_=w1_d[e * 1024 + kt * 128: e * 1024 + (kt + 1) * 128, :]), "gin", 16)
        p.sec("gpsimd", lw1)
        gin_w1 = p.n["gin"]

        for ng in range(4):
            def tmk(t, e=e, ng=ng):
                if e == 0 and ng == 0:
                    t.wait_ge(p.sem["v"], v_gate)
                p.wait(t, "v")
                ins = t.matmul(ps_mk[:, :512], e8c[0:8, e * 128:(e + 1) * 128],
                               maskR[0:8, ng * 512:(ng + 1) * 512], start=True, stop=True)
                p.inc(ins, "mm")
            p.sec("tensor", tmk)
            mm_mk = p.n["mm"]

            def vmk(v, ng=ng, mm_mk=mm_mk):
                v.wait_ge(p.sem["mm"], mm_mk)
                ins = v.tensor_copy(mkb[:, ng * 512:(ng + 1) * 512], ps_mk[:, :512])
                p.inc(ins, "v")
            p.sec("vector", vmk)
        v_mk = p.n["v"]

        for half in range(2):
            tc0 = half * 1024
            jobs = []
            for ft in range(32):
                for ng in range(2):
                    c0 = tc0 + ng * 512
                    mm_list = [(w1b[:, k * 4096 + ft * 128: k * 4096 + (ft + 1) * 128],
                                xn[:, k * TH + c0: k * TH + c0 + 512]) for k in range(8)]
                    def ev(env, ps, pred, e=e, ft=ft, ng=ng):
                        ins = env.activation(hb[:, ft * 1024 + ng * 512: ft * 1024 + (ng + 1) * 512],
                                             ps[:, :512], AF.Silu, bias=b1c[:, e * 32 + ft: e * 32 + ft + 1])
                        p.inc(ins, "sc")
                    jobs.append((mm_list, 128, 512, ev))
            emit_mm(p, pool, jobs, ("scalar", "sc"), wait_before=[("gin", gin_w1), ("v", v_mk)])
            sc_h = p.n["sc"]

            for dt_ in range(8):
                def lw2b(g, e=e, dt_=dt_):
                    p.wait(g, "mm")
                    p.inc(g.dma_start(out=w2b[:, :].rearrange("p (a m) -> p a m", a=32),
                                      in_=w2_d[e * 4096:(e + 1) * 4096, dt_ * 128:(dt_ + 1) * 128]
                                      .rearrange("(a p) m -> p a m", p=128)), "gin", 16)
                p.sec("gpsimd", lw2b)
                gin_w2 = p.n["gin"]

                jobs = []
                for ng in range(2):
                    c0l = ng * 512
                    mm_list = [(w2b[:, kk * 128:(kk + 1) * 128],
                                hb[:, kk * 1024 + c0l: kk * 1024 + c0l + 512]) for kk in range(32)]
                    def ev(env, ps, pred, e=e, dt_=dt_, ng=ng, half=half, c0l=c0l):
                        ins = env.scalar_tensor_tensor(tmpe[:, ng * 512:(ng + 1) * 512], ps[:, :512],
                                                       b2c[:, e * 8 + dt_: e * 8 + dt_ + 1],
                                                       mkb[:, half * 1024 + c0l: half * 1024 + c0l + 512],
                                                       ALU.add, ALU.mult)
                        p.inc(ins, "v")
                    jobs.append((mm_list, 128, 512, ev))
                prev_dma = zd_region.get(("tmpe",), 0)
                emit_mm(p, pool, jobs, ("vector", "v"),
                        wait_before=[("gin", gin_w2), ("sc", sc_h)] + ([("zg", prev_dma)] if prev_dma else []))
                v_tmp = p.n["v"]

                def dacc(g, e=e, dt_=dt_, half=half, v_tmp=v_tmp):
                    g.wait_ge(p.sem["v"], v_tmp)
                    rkey = (dt_, half)
                    if rkey in zd_region:
                        g.wait_ge(p.sem["zg"], zd_region[rkey])
                    ins = g.dma_start(out=acc_d[dt_ * 128:(dt_ + 1) * 128, half * 1024:(half + 1) * 1024],
                                      in_=tmpe[:, :],
                                      **({} if e == 0 else {"accum_op": ALU.add}))
                    p.inc(ins, "zg", 16)
                    zd_region[rkey] = p.n["zg"]
                    zd_region[("tmpe",)] = p.n["zg"]
                p.sec("gpsimd", dacc)
    zd_all = p.n["zg"]
    ex.close()

    fin = ExitStack()
    x3b = p.sb("x3b", [128, 8 * TH], F32, fin)
    accs = p.sb("accs", [128, TH], F32, fin)
    scb = p.sb("scb", [128, TH], BF16, fin)
    ps_sc = pool.tiles[0]
    def lx3(s):
        p.wait(s, "zg", zd_all)
        for mt in range(8):
            p.inc(s.dma_start(out=x3b[:, mt * TH:(mt + 1) * TH], in_=x3_d[mt * 128:(mt + 1) * 128, :]), "din", 16)
    p.sec("sync", lx3)
    din_x3 = p.n["din"]

    for ng in range(4):
        def tsc(t, ng=ng):
            p.wait(t, "v")
            ins = t.matmul(ps_sc[:, :512], co["onesrow_bf"][:, :], scoreR[0:1, ng * 512:(ng + 1) * 512],
                           start=True, stop=True)
            p.inc(ins, "mm")
        p.sec("tensor", tsc)
        mm_sc = p.n["mm"]

        def vsc2(v, ng=ng, mm_sc=mm_sc):
            v.wait_ge(p.sem["mm"], mm_sc)
            ins = v.tensor_copy(scb[:, ng * 512:(ng + 1) * 512], ps_sc[:, :512])
            p.inc(ins, "v")
        p.sec("vector", vsc2)
    v_scb = p.n["v"]

    for mt in range(8):
        def lacc(g, mt=mt):
            if mt >= 1:
                p.wait(g, "v")
            else:
                p.wait(g, "zg", zd_all)
                p.wait(g, "v")
            p.inc(g.dma_start(out=accs[:, :], in_=acc_d[mt * 128:(mt + 1) * 128, :]), "gin", 16)
        p.sec("gpsimd", lacc)
        gin_acc = p.n["gin"]

        def vfin(v, mt=mt, gin_acc=gin_acc):
            v.wait_ge(p.sem["gin"], gin_acc)
            v.wait_ge(p.sem["v"], v_scb)
            v.wait_ge(p.sem["din"], din_x3)
            i1 = p.inc(v.tensor_tensor(accs[:, :], accs[:, :], scb[:, :], ALU.mult), "v")
            v.wait_ge(p.sem["v"], i1)
            ins = v.tensor_tensor(x3b[:, mt * TH:(mt + 1) * TH], x3b[:, mt * TH:(mt + 1) * TH], accs[:, :], ALU.add)
            p.inc(ins, "v")
        p.sec("vector", vfin)
    v_fin = p.n["v"]

    emit_postln(p, x3b, x3b, TH, co, ready=[("v", v_fin)])
    v_ln2 = p.n["v"]

    def dx4(s):
        s.wait_ge(p.sem["v"], v_ln2)
        for mt in range(8):
            p.inc(s.dma_start(out=x4_d[mt * 128:(mt + 1) * 128, :], in_=x3b[:, mt * TH:(mt + 1) * TH]), "dout", 16)
        p.wait(s, "dout")
    p.sec("sync", dx4)
    fin.close()
    p.close()
    return p.nc


def host_A6_inmaps(inputs, co_np, z_parts, x2_parts):
    WO = np.ascontiguousarray(inputs["L1_attn_W_O"].reshape(1024, 1024))
    bO = inputs["L1_attn_b_O"].reshape(8, 128).T.copy()
    rmsw = inputs["L1_ffn_rms_w"].reshape(8, 128).T.copy()
    gwT = np.ascontiguousarray(inputs["L1_ffn_gate_w"].T)
    W1 = np.ascontiguousarray(inputs["L1_ffn_W1"].reshape(8 * 1024, 4096))
    W2 = np.ascontiguousarray(inputs["L1_ffn_W2"].reshape(8 * 4096, 1024))
    b1c = np.ascontiguousarray(inputs["L1_ffn_b1"].reshape(8, 32, 128).transpose(2, 0, 1).reshape(128, 8 * 32))
    b2c = np.ascontiguousarray(inputs["L1_ffn_b2"].reshape(8, 8, 128).transpose(2, 0, 1).reshape(128, 8 * 8))
    base = const_inmap(co_np)
    maps = []
    for c in range(NCORE):
        b, i = c // 2, c % 2
        t0 = i * TH
        zsl = np.concatenate([np.asarray(z_parts[2 * b])[:, t0:t0 + TH],
                              np.asarray(z_parts[2 * b + 1])[:, t0:t0 + TH]], axis=0)
        m = dict(base)
        m.update(zsl=np.ascontiguousarray(zsl), WO=WO, bO=bO, x2=x2_parts[c], rmsw=rmsw,
                 gwT=gwT, W1all=W1, b1c=b1c, W2all=W2, b2c=b2c, E8=moe_e8())
        maps.append(m)
    return maps


def build_A7(co_np, p=None):
    if p is None:
        p = P("A7")
    x_d = p.din("x", [1024, TH])
    wk_d = p.din("Wk", [1024, 512]); wv_d = p.din("Wv", [1024, 512])
    wr_d = p.din("Wr", [1024, 512]); wg_d = p.din("Wg", [1024, 512])
    wsel_d = p.din("Wsel", [1024, 160])
    dtw_d = p.din("dtW", [32, 512])
    dtb_d = p.din("dtbc", [128, 4])
    kk_d = p.dout("k", [512, TH], BF16)
    vv_d = p.dout("v", [512, TH], BF16)
    rr_d = p.dout("r", [512, TH], BF16)
    gg_d = p.dout("g", [512, TH], BF16)
    dd_d = p.dout("delta", [512, TH], BF16)
    bs_d = p.dout("bscale", [1, TH])

    p.start()
    co = emit_consts(p, co_np)
    dtbc = p.sb("dtbc", [128, 4])
    dma_in(p, dtbc, dtb_d)
    xb = p.sb("xb", [128, 8 * TH], BF16)
    load_fm(p, xb, x_d, 1024, TH)
    wb = p.sb("wb", [128, 8 * 512], BF16)
    wsb = p.sb("wsb", [128, 8 * 160], BF16)
    dtwb = p.sb("dtwb", [96, 512], BF16)
    dma_in(p, dtwb[64:96, :], dtw_d, cast=True)
    load_fm(p, wsb, wsel_d, 1024, 160)
    projb = p.sb("projb", [96, TH], BF16)
    outb = p.sb("outb", [128, 4 * TH], BF16)
    bsr = p.sb("bsr", [1, TH])
    pool = Pool(p, 4)
    gin0, din0 = p.n["gin"], p.n["din"]

    jobs = []
    for ng in range(4):
        c0 = ng * 512
        mm_list = [(wsb[:, k * 160: k * 160 + 96], xb[:, k * TH + c0: k * TH + c0 + 512]) for k in range(8)]
        def ev(env, ps, pred, c0=c0):
            ins = env.tensor_copy(projb[:, c0:c0 + 512], ps[:96, :512])
            p.inc(ins, "v")
        jobs.append((mm_list, 96, 512, ev))
    emit_mm(p, pool, jobs, ("vector", "v"), wait_before=[("gin", gin0), ("din", din0)])
    v_proj = p.n["v"]

    ps_b = p.ps("psb", [128, 512])
    for ng in range(4):
        c0 = ng * 512
        def tbs(t, c0=c0, ng=ng):
            if ng == 0:
                t.wait_ge(p.sem["v"], v_proj)
            else:
                p.wait(t, "v")
            ins = t.matmul(ps_b[0:1, :512], co["ones_bf"][0:64, :], projb[0:64, c0:c0 + 512],
                           start=True, stop=True)
            p.inc(ins, "mm")
        p.sec("tensor", tbs)
        mm_b = p.n["mm"]

        def vbs(v, c0=c0, mm_b=mm_b):
            v.wait_ge(p.sem["mm"], mm_b)
            ins = v.tensor_scalar(bsr[0:1, c0:c0 + 512], ps_b[0:1, :512], 1.0 / 64, None, ALU.mult)
            p.inc(ins, "v")
        p.sec("vector", vbs)
    v_bs = p.n["v"]

    spt2 = p.sb("spt2", [128, 512])
    jobs = []
    for mt in range(4):
        for ng in range(4):
            c0 = ng * 512
            mm_list = [(dtwb[64:96, mt * 128:(mt + 1) * 128], projb[64:96, c0:c0 + 512])]
            def ev(env, ps, pred, mt=mt, c0=c0):
                i1 = env.activation(spt2[:, :], ps[:, :512], AF.Exp, bias=dtbc[:, mt:mt + 1])
                ins = env.activation(outb[:, mt * TH + c0: mt * TH + c0 + 512], spt2[:, :], AF.Ln, bias=1.0)
                p.inc(ins, "sc")
            jobs.append((mm_list, 128, 512, ev))
    emit_mm(p, pool, jobs, ("scalar", "sc"), wait_before=[("v", v_proj)])
    sc_dt = p.n["sc"]

    def ddma(s):
        s.wait_ge(p.sem["sc"], sc_dt)
        for mt in range(4):
            p.inc(s.dma_start(out=dd_d[mt * 128:(mt + 1) * 128, :], in_=outb[:, mt * TH:(mt + 1) * TH]), "dout", 16)
        s.wait_ge(p.sem["v"], v_bs)
        p.inc(s.dma_start(out=bs_d, in_=bsr[0:1, :]), "dout", 16)
    p.sec("sync", ddma)
    d_prev = p.n["dout"]

    for wi, (wd, od) in enumerate(((wk_d, kk_d), (wv_d, vv_d), (wr_d, rr_d), (wg_d, gg_d))):
        def lw(g, wd=wd, wi=wi):
            p.wait(g, "mm")
            for kt in range(8):
                p.inc(g.dma_start(out=wb[:, kt * 512:(kt + 1) * 512], in_=wd[kt * 128:(kt + 1) * 128, :]), "gin", 16)
        p.sec("gpsimd", lw)
        gin_w = p.n["gin"]

        jobs = []
        for mt in range(4):
            for ng in range(4):
                c0 = ng * 512
                mm_list = [(wb[:, k * 512 + mt * 128: k * 512 + (mt + 1) * 128],
                            xb[:, k * TH + c0: k * TH + c0 + 512]) for k in range(8)]
                def ev(env, ps, pred, mt=mt, c0=c0, wi=wi):
                    if wi >= 1:
                        p.wait(env, "dout", d_prev + 16 * 4 * (wi - 1))
                    ins = env.tensor_copy(outb[:, mt * TH + c0: mt * TH + c0 + 512], ps[:, :512])
                    p.inc(ins, "v")
                jobs.append((mm_list, 128, 512, ev))
        emit_mm(p, pool, jobs, ("vector", "v"), wait_before=[("gin", gin_w)])
        v_w = p.n["v"]

        def odma(s, od=od, v_w=v_w):
            s.wait_ge(p.sem["v"], v_w)
            for mt in range(4):
                p.inc(s.dma_start(out=od[mt * 128:(mt + 1) * 128, :], in_=outb[:, mt * TH:(mt + 1) * TH]), "dout", 16)
        p.sec("sync", odma)
    def fin(s):
        p.wait(s, "dout")
    p.sec("sync", fin)
    p.close()
    return p.nc


def host_A7_inmaps(inputs, co_np, x4_parts):
    dtbc = inputs["L2_attn_dt_b"].reshape(4, 128).T.copy()
    base = const_inmap(co_np)
    maps = []
    for c in range(NCORE):
        m = dict(base)
        Wsel_perm = np.ascontiguousarray(np.concatenate(
            [inputs["L2_attn_W_sel"][:, 32:96], inputs["L2_attn_W_sel"][:, 0:32],
             inputs["L2_attn_W_sel"][:, 96:160]], axis=1))
        m.update(x=x4_parts[c], Wk=inputs["L2_attn_Wk"], Wv=inputs["L2_attn_Wv"],
                 Wr=inputs["L2_attn_Wr"], Wg=inputs["L2_attn_Wg"], Wsel=Wsel_perm,
                 dtW=inputs["L2_attn_dt_W"], dtbc=dtbc)
        maps.append(m)
    return maps


def build_A9(co_np, p=None):
    if p is None:
        p = P("A9")
    ys_d = p.din("ysl", [512, TH], BF16)
    wo_d = p.din("Wo", [512, 1024])
    x4_d = p.din("x4", [1024, TH])
    w1_d = p.din("W1", [1024, 4096])
    b1_d = p.din("b1c", [128, 32])
    w2_d = p.din("W2", [4096, 1024])
    b2_d = p.din("b2c", [128, 8])
    x6_d = p.dout("x6", [1024, TH])

    p.start()
    co = emit_consts(p, co_np)
    b1c = p.sb("b1c", [128, 32]); b2c = p.sb("b2c", [128, 8])
    dma_in(p, b1c, b1_d); dma_in(p, b2c, b2_d)
    xr = p.sb("xr", [128, 8 * TH], F32)
    load_fm(p, xr, x4_d, 1024, TH, cast=False)
    din0 = p.n["din"]
    pool = Pool(p, 4)

    s1 = ExitStack()
    ysb = p.sb("ysb", [128, 4 * TH], BF16, s1)
    wob = p.sb("wob", [128, 4 * 1024], BF16, s1)
    load_fm(p, ysb, ys_d, 512, TH)
    load_fm(p, wob, wo_d, 512, 1024)
    gin0 = p.n["gin"]

    jobs = []
    for mt in range(8):
        for ng in range(4):
            c0 = ng * 512
            mm_list = [(wob[:, k * 1024 + mt * 128: k * 1024 + (mt + 1) * 128],
                        ysb[:, k * TH + c0: k * TH + c0 + 512]) for k in range(4)]
            def ev(env, ps, pred, mt=mt, c0=c0):
                ins = env.tensor_tensor(xr[:, mt * TH + c0: mt * TH + c0 + 512], ps[:, :512],
                                        xr[:, mt * TH + c0: mt * TH + c0 + 512], ALU.add)
                p.inc(ins, "v")
            jobs.append((mm_list, 128, 512, ev))
    emit_mm(p, pool, jobs, ("vector", "v"), wait_before=[("gin", gin0), ("din", din0)])
    v_s = p.n["v"]
    emit_postln(p, xr, xr, TH, co, ready=[("v", v_s)])
    v_x5 = p.n["v"]
    s1.close()

    s2 = ExitStack()
    x5b = p.sb("x5b", [128, 8 * TH], BF16, s2)
    w1b = p.sb("w1b", [128, 8 * 4096], BF16, s2)
    hbb = p.sb("hbb", [128, 32 * 512], BF16, s2)
    w2b = p.sb("w2b", [128, 32 * 128], BF16, s2)

    def vx5(v):
        v.wait_ge(p.sem["v"], v_x5)
        for mt in range(8):
            p.inc(v.tensor_copy(x5b[:, mt * TH:(mt + 1) * TH], xr[:, mt * TH:(mt + 1) * TH]), "v")
    p.sec("vector", vx5)
    v_xb = p.n["v"]

    def lw1(g):
        p.wait(g, "mm")
        for kt in range(8):
            p.inc(g.dma_start(out=w1b[:, kt * 4096:(kt + 1) * 4096], in_=w1_d[kt * 128:(kt + 1) * 128, :]), "gin", 16)
    p.sec("gpsimd", lw1)
    gin_w1 = p.n["gin"]

    for qt in range(4):
        tc0 = qt * 512
        jobs = []
        for ft in range(32):
            mm_list = [(w1b[:, k * 4096 + ft * 128: k * 4096 + (ft + 1) * 128],
                        x5b[:, k * TH + tc0: k * TH + tc0 + 512]) for k in range(8)]
            def ev(env, ps, pred, ft=ft):
                ins = env.activation(hbb[:, ft * 512:(ft + 1) * 512],
                                     ps[:, :512], AF.Relu, bias=b1c[:, ft: ft + 1])
                p.inc(ins, "sc")
            jobs.append((mm_list, 128, 512, ev))
        emit_mm(p, pool, jobs, ("scalar", "sc"), wait_before=[("gin", gin_w1), ("v", v_xb)])
        sc_h = p.n["sc"]

        for dt_ in range(8):
            def lw2(g, dt_=dt_):
                p.wait(g, "mm")
                p.inc(g.dma_start(out=w2b[:, :].rearrange("p (a m) -> p a m", a=32),
                                  in_=w2_d[:, dt_ * 128:(dt_ + 1) * 128].rearrange("(a p) m -> p a m", p=128)), "gin", 16)
            p.sec("gpsimd", lw2)
            gin_w2 = p.n["gin"]

            jobs = []
            mm_list = [(w2b[:, kk * 128:(kk + 1) * 128],
                        hbb[:, kk * 512: kk * 512 + 512]) for kk in range(32)]
            def ev(env, ps, pred, dt_=dt_, tc0=tc0):
                env.tensor_scalar(ps[:, :512], ps[:, :512], b2c[:, dt_:dt_ + 1], None, ALU.add)
                ins = env.tensor_tensor(xr[:, dt_ * TH + tc0: dt_ * TH + tc0 + 512], ps[:, :512],
                                        xr[:, dt_ * TH + tc0: dt_ * TH + tc0 + 512], ALU.add)
                p.inc(ins, "v")
            jobs.append((mm_list, 128, 512, ev))
            emit_mm(p, pool, jobs, ("vector", "v"), wait_before=[("gin", gin_w2), ("sc", sc_h)])
    v_s6 = p.n["v"]
    s2.close()

    emit_postln(p, xr, xr, TH, co, ready=[("v", v_s6)])
    v_ln = p.n["v"]

    def dx(s):
        s.wait_ge(p.sem["v"], v_ln)
        for mt in range(8):
            p.inc(s.dma_start(out=x6_d[mt * 128:(mt + 1) * 128, :], in_=xr[:, mt * TH:(mt + 1) * TH]), "dout", 16)
        p.wait(s, "dout")
    p.sec("sync", dx)
    p.close()
    return p.nc


def host_A9_inmaps(inputs, co_np, ys_parts, x4_parts):
    b1c = inputs["L2_ffn_b1"].reshape(32, 128).T.copy()
    b2c = inputs["L2_ffn_b2"].reshape(8, 128).T.copy()
    base = const_inmap(co_np)
    maps = []
    for c in range(NCORE):
        b, i = c // 2, c % 2
        t0 = i * TH
        ysl = np.concatenate([np.asarray(ys_parts[2 * b])[:, t0:t0 + TH],
                              np.asarray(ys_parts[2 * b + 1])[:, t0:t0 + TH]], axis=0)
        m = dict(base)
        m.update(ysl=np.ascontiguousarray(ysl), Wo=inputs["L2_attn_Wo"], x4=x4_parts[c],
                 W1=inputs["L2_ffn_W1"], b1c=b1c, W2=inputs["L2_ffn_W2"], b2c=b2c)
        maps.append(m)
    return maps


def build_A8(co_np, p=None, fused_out=False):
    if p is None:
        p = P("A8")
    NH = 4
    k_d = p.din("kf", [NH * 64, T], BF16)
    v_d = p.din("vf", [NH * 64, T], BF16)
    r_d = p.din("rf", [NH * 64, T], BF16)
    g_d = p.din("gf", [NH * 64, T], BF16)
    dl_d = p.din("df", [NH * 64, T], BF16)
    bs_d = p.din("bscale", [1, T])
    bdc_d = p.din("bdc", [128, 2])
    tfc_d = p.din("tfc", [128, 2])
    m16_d = p.din("MSK16", [16, 128], mybir.dt.uint8)
    ys_d = p.dout("ys", [2 * NH * 64, TH] if fused_out else [NH * 64, T], BF16)

    p.start()
    co = emit_consts(p, co_np)
    bdc = p.sb("bdc", [128, 2]); tfc = p.sb("tfc", [128, 2])
    m16 = p.sb("m16", [16, 128], mybir.dt.uint8)
    dma_in(p, bdc, bdc_d); dma_in(p, tfc, tfc_d); dma_in(p, m16, m16_d)
    bsr = p.sb("bsr", [1, T])
    dma_in(p, bsr, bs_d)
    ones64 = p.sb("ones64", [64, 512], BF16)
    zer64 = p.sb("zer64", [64, 64])
    zer16 = p.sb("zer16", [16, 128], BF16)
    bscB = p.sb("bscB", [64, T], BF16)
    kf = p.sb("kf", [64, T], BF16)
    vf = p.sb("vf", [64, T], BF16)
    rf = p.sb("rf", [64, T], BF16)
    gf = p.sb("gf", [64, T], BF16)
    df = p.sb("df", [64, T], BF16)
    Lb = p.sb("Lb", [64, T], F32)
    Efl = p.sb("Efl", [64, T], F32)
    mfl = p.sb("mfl", [64, T], F32)
    Erow = p.sb("Erow", [64, 256], F32)
    Mrow = p.sb("Mrow", [64, 256], F32)
    lam = p.sb("lam", [64, 256], F32)
    fsc = p.sb("fsc", [64, T], F32)
    ktil = p.sb("ktil", [64, T], BF16)
    kap = p.sb("kap", [64, T], BF16)
    kin = p.sb("kin", [64, T], BF16)
    kpr = p.sb("kpr", [64, T], BF16)
    EProw = p.sb("EProw", [64, 256], F32)
    vtl = p.sb("vtl", [64, T], BF16)
    wkv = p.sb("wkv", [64, T], F32)
    kTg = p.sb("kTg", [16, 512], BF16)
    vTg = p.sb("vTg", [16, 512], BF16)
    Pm = p.sb("Pm", [16, 128], BF16)
    Ub = p.sb("Ub", [64, 512])
    Sg = p.sb("Sg", [64, 512], BF16)
    Scar = p.sb("Scar", [64, 64], BF16)

    ps_tr = p.ps("pstr", [16, 512], BF16)
    ps_u = p.ps("psu", [64, 512], F32)
    ps_p = p.ps("psp", [16, 128], F32)
    ps_w = p.ps("psw", [64, 128], F32)
    ps_bc = p.ps("psbc", [64, 512], F32)

    def ginit(g):
        g.memset(ones64[:, :], 1.0)
        g.memset(zer64[:, :], 0.0)
        g.memset(Scar[:, :], 0.0)
        ins = g.memset(zer16[:, :], 0.0)
        p.inc(ins, "gp")
    p.sec("gpsimd", ginit)
    gp0 = p.n["gp"]
    din0, gin0 = p.n["din"], p.n["gin"]

    for ng in range(8):
        def tb(t, ng=ng):
            if ng == 0:
                p.wait(t, "din", din0)
            else:
                p.wait(t, "v")
            ins = t.matmul(ps_bc[:, :512], co["onesrow"][0:1, 0:64], bsr[0:1, ng * 512:(ng + 1) * 512],
                           start=True, stop=True)
            p.inc(ins, "mm")
        p.sec("tensor", tb)
        mm_b = p.n["mm"]

        def vb(v, ng=ng, mm_b=mm_b):
            v.wait_ge(p.sem["mm"], mm_b)
            ins = v.tensor_copy(bscB[:, ng * 512:(ng + 1) * 512], ps_bc[:, :512])
            p.inc(ins, "v")
        p.sec("vector", vb)
    v_bsc = p.n["v"]

    NG8 = T // 512
    for h in range(NH):
        hcol = h // 2
        hrow = (h % 2) * 64
        def lh(g, h=h):
            p.wait(g, "v")
            p.wait(g, "sc")
            p.wait(g, "zd")
            for src, dst in ((k_d, kf), (v_d, vf), (r_d, rf), (g_d, gf), (dl_d, df)):
                p.inc(g.dma_start(out=dst[:, :], in_=src[h * 64:(h + 1) * 64, :]), "gin", 16)
        p.sec("gpsimd", lh)
        gin_h = p.n["gin"]

        def va(v, h=h, hcol=hcol, hrow=hrow):
            v.wait_ge(p.sem["gin"], gin_h)
            p.wait(v, "gp", gp0)
            i1 = p.inc(v.tensor_scalar(fsc[:, :], df[:, :], bdc[hrow:hrow + 64, hcol:hcol + 1], None, ALU.mult), "v")
            v.wait_ge(p.sem["v"], i1)
            last = None
            for cg in range(8):
                c0 = cg * 512
                init = 0.0 if cg == 0 else Lb[:, c0 - 1: c0]
                i2 = p.inc(v.tensor_tensor_scan(Lb[:, c0:c0 + 512], ones64[:, :], fsc[:, c0:c0 + 512],
                                                init, ALU.mult, ALU.add), "v")
                v.wait_ge(p.sem["v"], i2)
            Lv = Lb[:, :].rearrange("p (c s) -> p c s", s=16)
            i3 = p.inc(v.tensor_copy(Erow[:, :], Lv[:, :, 15]), "v")
            v.wait_ge(p.sem["v"], i3)
            v.tensor_scalar(lam[:, 0:1], Erow[:, 0:1], -1.0, None, ALU.mult)
            i4 = p.inc(v.tensor_tensor(lam[:, 1:256], Erow[:, 0:255], Erow[:, 1:256], ALU.subtract), "v")
            v.wait_ge(p.sem["v"], i4)
            v.tensor_scalar(Mrow[:, 0:1], Erow[:, 0:1], 0.5, None, ALU.mult)
            i5 = p.inc(v.tensor_tensor(Mrow[:, 1:256], Erow[:, 0:255], Erow[:, 1:256], ALU.add), "v")
            v.wait_ge(p.sem["v"], i5)
            i6 = p.inc(v.tensor_scalar(Mrow[:, 1:256], Mrow[:, 1:256], 0.5, None, ALU.mult), "v")
            v.wait_ge(p.sem["v"], i6)
            v.memset(EProw[:, 0:1], 0.0)
            i7 = p.inc(v.tensor_copy(EProw[:, 1:256], Erow[:, 0:255]), "v")
            v.wait_ge(p.sem["v"], i7)
            Ev = Efl[:, :].rearrange("p (c s) -> p c s", s=16)
            Mv = mfl[:, :].rearrange("p (c s) -> p c s", s=16)
            for sI in range(16):
                v.tensor_copy(Ev[:, :, sI], Erow[:, :])
                ins = v.tensor_copy(Mv[:, :, sI], Mrow[:, :])
            p.inc(ins, "v")
        p.sec("vector", va)
        v_L = p.n["v"]

        def sl(sc, h=h):
            sc.wait_ge(p.sem["v"], v_L)
            ins = sc.activation(lam[:, :], lam[:, :], AF.Exp)
            p.inc(ins, "sc")
        p.sec("scalar", sl)
        sc_lam = p.n["sc"]

        def vargs(v, h=h):
            v.wait_ge(p.sem["v"], v_L)
            ins = v.tensor_tensor(fsc[:, :], mfl[:, :], Lb[:, :], ALU.subtract)
            p.inc(ins, "v")
        p.sec("vector", vargs)
        v_a1 = p.n["v"]

        def sexps(sc, h=h):
            sc.wait_ge(p.sem["v"], v_a1)
            i1 = p.inc(sc.activation(ktil[:, :], fsc[:, :], AF.Exp), "sc")
            ins = sc.activation(kap[:, :], fsc[:, :], AF.Exp, scale=-1.0)
            p.inc(ins, "sc")
        p.sec("scalar", sexps)
        sc_e1 = p.n["sc"]

        def varg2(v, h=h):
            v.wait_ge(p.sem["sc"], sc_e1)
            i1 = p.inc(v.tensor_tensor(fsc[:, :], Lb[:, :], Efl[:, :], ALU.subtract), "v")
            v.wait_ge(p.sem["v"], i1)
            EPv = Efl[:, :].rearrange("p (c s) -> p c s", s=16)
            for sI in range(16):
                ins = v.tensor_copy(EPv[:, :, sI], EProw[:, :])
            i2 = p.inc(ins, "v")
            v.wait_ge(p.sem["v"], i2)
            ins = v.tensor_tensor(Efl[:, :], Efl[:, :], Lb[:, :], ALU.subtract)
            p.inc(ins, "v")
        p.sec("vector", varg2)
        v_a2 = p.n["v"]

        def sexp2(sc, h=h):
            sc.wait_ge(p.sem["v"], v_a2)
            i1 = p.inc(sc.activation(fsc[:, :], fsc[:, :], AF.Exp), "sc")
            ins = sc.activation(kin[:, :], Efl[:, :], AF.Exp)
            p.inc(ins, "sc")
        p.sec("scalar", sexp2)
        sc_e2 = p.n["sc"]

        def vfac(v, h=h):
            v.wait_ge(p.sem["sc"], sc_e2)
            v.wait_ge(p.sem["v"], v_bsc)
            p.inc(v.tensor_tensor(ktil[:, :], ktil[:, :], kf[:, :], ALU.mult), "v")
            p.inc(v.tensor_tensor(kap[:, :], kap[:, :], kf[:, :], ALU.mult), "v")
            p.inc(v.tensor_tensor(kin[:, :], kin[:, :], kf[:, :], ALU.mult), "v")
            p.inc(v.tensor_tensor(kpr[:, :], fsc[:, :], kf[:, :], ALU.mult), "v")
            ins = v.tensor_tensor(vtl[:, :], vf[:, :], bscB[:, :], ALU.mult)
            p.inc(ins, "v")
        p.sec("vector", vfac)
        v_fac = p.n["v"]

        for g8 in range(32):
            t0 = g8 * 128
            def ttr(t, g8=g8, t0=t0):
                if g8 == 0:
                    t.wait_ge(p.sem["v"], v_fac)
                p.wait(t, "v")
                for c8 in range(8):
                    c0 = t0 + c8 * 16
                    ins = t.transpose(ps_tr[:, c8 * 64:(c8 + 1) * 64], kpr[:, c0:c0 + 16], co["ident"][0:64, 0:64])
                p.inc(ins, "mm")
            p.sec("tensor", ttr)
            mm_t1 = p.n["mm"]

            def vtr(v, mm_t1=mm_t1):
                v.wait_ge(p.sem["mm"], mm_t1)
                ins = v.tensor_copy(kTg[:, :], ps_tr[:, :])
                p.inc(ins, "v")
            p.sec("vector", vtr)
            v_t1 = p.n["v"]

            def ttr2(t, g8=g8, t0=t0, v_t1=v_t1):
                t.wait_ge(p.sem["v"], v_t1)
                for c8 in range(8):
                    c0 = t0 + c8 * 16
                    ins = t.transpose(ps_tr[:, c8 * 64:(c8 + 1) * 64], vtl[:, c0:c0 + 16], co["ident"][0:64, 0:64])
                p.inc(ins, "mm")
            p.sec("tensor", ttr2)
            mm_t2 = p.n["mm"]

            def vtr2(v, mm_t2=mm_t2):
                v.wait_ge(p.sem["mm"], mm_t2)
                ins = v.tensor_copy(vTg[:, :], ps_tr[:, :])
                p.inc(ins, "v")
            p.sec("vector", vtr2)
            v_t2 = p.n["v"]

            def tu(t, g8=g8, t0=t0, v_t2=v_t2):
                t.wait_ge(p.sem["v"], v_t2)
                for c8 in range(8):
                    t.matmul(ps_u[:, c8 * 64:(c8 + 1) * 64], kTg[:, c8 * 64:(c8 + 1) * 64],
                             vTg[:, c8 * 64:(c8 + 1) * 64], start=True, stop=True)
                for c8 in range(8):
                    c0 = t0 + c8 * 16
                    ins = t.matmul(ps_p[:, c8 * 16:(c8 + 1) * 16], kap[:, c0:c0 + 16],
                                   ktil[:, c0:c0 + 16], start=True, stop=True)
                p.inc(ins, "mm")
            p.sec("tensor", tu)
            mm_u = p.n["mm"]

            def vu(v, g8=g8, mm_u=mm_u):
                v.wait_ge(p.sem["mm"], mm_u)
                i1 = p.inc(v.tensor_copy(Ub[:, :], ps_u[:, :]), "v")
                i2 = p.inc(v.select(Pm[:, :], m16[:, :], ps_p[:, :], zer16[:, :]), "v")
                v.wait_ge(p.sem["v"], i1)
                for c8 in range(8):
                    cI = g8 * 8 + c8
                    prev = Scar[:, :] if c8 == 0 else Sg[:, (c8 - 1) * 64: c8 * 64]
                    i3 = p.inc(v.scalar_tensor_tensor(Sg[:, c8 * 64:(c8 + 1) * 64], prev,
                                                      lam[:, cI:cI + 1], Ub[:, c8 * 64:(c8 + 1) * 64],
                                                      ALU.mult, ALU.add), "v")
                    v.wait_ge(p.sem["v"], i3)
            p.sec("vector", vu)
            v_ch = p.n["v"]

            def tw(t, g8=g8, t0=t0, v_ch=v_ch):
                t.wait_ge(p.sem["v"], v_ch)
                for c8 in range(8):
                    c0 = t0 + c8 * 16
                    Sprev = Scar[:, :] if c8 == 0 else Sg[:, (c8 - 1) * 64: c8 * 64]
                    t.matmul(ps_w[:, c8 * 16:(c8 + 1) * 16], Sprev, kin[:, c0:c0 + 16], start=True, stop=False)
                    ins = t.matmul(ps_w[:, c8 * 16:(c8 + 1) * 16], vTg[:, c8 * 64:(c8 + 1) * 64],
                                   Pm[:, c8 * 16:(c8 + 1) * 16], start=False, stop=True)
                p.inc(ins, "mm")
            p.sec("tensor", tw)
            mm_w = p.n["mm"]

            def vw(v, g8=g8, t0=t0, mm_w=mm_w):
                v.wait_ge(p.sem["mm"], mm_w)
                i1 = p.inc(v.tensor_copy(wkv[:, t0:t0 + 128], ps_w[:, :]), "v")
                ins = v.tensor_copy(Scar[:, :], Sg[:, 7 * 64: 8 * 64])
                p.inc(ins, "v")
            p.sec("vector", vw)

        v_wkv = p.n["v"]

        def sep(sc, h=h):
            sc.wait_ge(p.sem["v"], v_wkv)
            i1 = p.inc(sc.activation(rf[:, :], rf[:, :], AF.Sigmoid), "sc")
            ins = sc.activation(gf[:, :], gf[:, :], AF.Sigmoid)
            p.inc(ins, "sc")
        p.sec("scalar", sep)
        sc_sig = p.n["sc"]

        def vep(v, h=h, hcol=hcol, hrow=hrow, sc_sig=sc_sig):
            v.wait_ge(p.sem["sc"], sc_sig)
            i1 = p.inc(v.tensor_scalar(fsc[:, :], kf[:, :], tfc[hrow:hrow + 64, hcol:hcol + 1], None, ALU.mult), "v")
            v.wait_ge(p.sem["v"], i1)
            i2 = p.inc(v.tensor_tensor(fsc[:, :], fsc[:, :], vf[:, :], ALU.mult), "v")
            v.wait_ge(p.sem["v"], i2)
            i3 = p.inc(v.tensor_tensor(wkv[:, :], wkv[:, :], fsc[:, :], ALU.add), "v")
            v.wait_ge(p.sem["v"], i3)
            i4 = p.inc(v.tensor_tensor(wkv[:, :], wkv[:, :], rf[:, :], ALU.mult), "v")
            v.wait_ge(p.sem["v"], i4)
            ins = v.tensor_tensor(ktil[:, :], wkv[:, :], gf[:, :], ALU.mult)
            p.inc(ins, "v")
        p.sec("vector", vep)
        v_ys = p.n["v"]

        def dys(s, h=h, v_ys=v_ys):
            s.wait_ge(p.sem["v"], v_ys)
            if fused_out:
                p.inc(s.dma_start(out=ys_d[h * 64:(h + 1) * 64, :], in_=ktil[:, 0:TH]), "zd", 16)
                p.inc(s.dma_start(out=ys_d[256 + h * 64: 256 + (h + 1) * 64, :], in_=ktil[:, TH:T]), "zd", 16)
            else:
                p.inc(s.dma_start(out=ys_d[h * 64:(h + 1) * 64, :], in_=ktil[:, :]), "zd", 16)
        p.sec("sync", dys)

        def gz(g, h=h):
            p.wait(g, "mm")
            ins = g.memset(Scar[:, :], 0.0)
            p.inc(ins, "gp")
        p.sec("gpsimd", gz)
        gp0 = p.n["gp"]

    def fin(s):
        p.wait(s, "zd")
    p.sec("sync", fin)
    p.close()
    return p.nc


def host_A8_inmaps(inputs, co_np, l2parts):
    MSK16 = np.zeros((16, 128), dtype=np.uint8)
    for pp in range(16):
        for c in range(8):
            MSK16[pp, c * 16 + pp: c * 16 + 16] = 1
    bd = inputs["L2_attn_base_decay"]
    tf = inputs["L2_attn_time_first"]
    base = const_inmap(co_np)
    maps = []
    for c in range(NCORE):
        b, j = c // 2, c % 2
        h0 = j * 4
        def cat(key):
            return np.ascontiguousarray(np.concatenate([np.asarray(l2parts[2 * b][key])[h0 * 64:(h0 + 4) * 64, :],
                                                        np.asarray(l2parts[2 * b + 1][key])[h0 * 64:(h0 + 4) * 64, :]], axis=1))
        bdc = np.zeros((128, 2), np.float32)
        tfc = np.zeros((128, 2), np.float32)
        for hl in range(4):
            bdc[(hl % 2) * 64:(hl % 2) * 64 + 64, hl // 2] = bd[h0 + hl]
            tfc[(hl % 2) * 64:(hl % 2) * 64 + 64, hl // 2] = tf[h0 + hl]
        bs = np.concatenate([np.asarray(l2parts[2 * b]["bscale"]), np.asarray(l2parts[2 * b + 1]["bscale"])], axis=1)
        m = dict(base)
        m.update(kf=cat("k"), vf=cat("v"), rf=cat("r"), gf=cat("g"), df=cat("delta"),
                 bscale=np.ascontiguousarray(bs), bdc=bdc, tfc=tfc, MSK16=MSK16)
        maps.append(m)
    return maps


def host_A10_inmaps(inputs, co_np, x6_parts):
    MSK = attn_masks()
    WqT = np.ascontiguousarray(inputs["L3_attn_Wq"].T)
    WkT = np.ascontiguousarray(inputs["L3_attn_Wk"].T)
    WvT = np.ascontiguousarray(inputs["L3_attn_Wv"].T)
    zb = np.zeros((128, 4), np.float32)
    base = const_inmap(co_np)
    maps = []
    for c in range(NCORE):
        b, j = c // 2, c % 2
        h0 = j * 8
        xf = np.concatenate([x6_parts[2 * b], x6_parts[2 * b + 1]], axis=1)
        m = dict(base)
        m.update(xf=np.ascontiguousarray(xf),
                 WQ=np.ascontiguousarray(WqT[:, h0 * 64:(h0 + 8) * 64]),
                 WK=np.ascontiguousarray(WkT[:, h0 * 64:(h0 + 8) * 64]),
                 WV=np.ascontiguousarray(WvT[:, h0 * 64:(h0 + 8) * 64]),
                 bQ=zb, bK=zb, bV=zb, MSK=MSK)
        maps.append(m)
    return maps


def build_A11(co_np, p=None):
    if p is None:
        p = P("A11")
    z_d = p.din("zsl", [1024, TH], BF16)
    wo_d = p.din("WoT", [1024, 1024])
    x6_d = p.din("x6", [1024, TH])
    wr_d = p.din("wR", [1024, 1024])
    wi_d = p.din("wI", [1024, 1024])
    fr_d = p.din("FR", [1024, 1024])
    fi_d = p.din("FI", [1024, 1024])
    gr_d = p.din("GR", [1024, 1024])
    gin_d = p.din("GIn", [1024, 1024])
    bias_d = p.din("biasc", [128, 8])
    x8_d = p.dout("x8", [1024, TH])

    p.start()
    co = emit_consts(p, co_np)
    biasc = p.sb("biasc", [128, 8])
    dma_in(p, biasc, bias_d)
    xr = p.sb("xr", [128, 8 * TH], F32)
    load_fm(p, xr, x6_d, 1024, TH, cast=False)
    din0 = p.n["din"]
    T1R = p.sb("T1R", [128, 8 * 1024], BF16)
    T1I = p.sb("T1I", [128, 8 * 1024], BF16)
    Mb = p.sb("Mb", [128, 8 * 1024], BF16)
    pool = Pool(p, 4)

    s1 = ExitStack()
    zb = p.sb("zb", [128, 8 * TH], BF16, s1)
    wob = p.sb("wob", [128, 8 * 1024], BF16, s1)
    load_fm(p, zb, z_d, 1024, TH)
    load_fm(p, wob, wo_d, 1024, 1024)
    gin1 = p.n["gin"]
    jobs = []
    for mt in range(8):
        for ng in range(4):
            c0 = ng * 512
            mm_list = [(wob[:, k * 1024 + mt * 128: k * 1024 + (mt + 1) * 128],
                        zb[:, k * TH + c0: k * TH + c0 + 512]) for k in range(8)]
            def ev(env, ps, pred, mt=mt, c0=c0):
                ins = env.tensor_tensor(xr[:, mt * TH + c0: mt * TH + c0 + 512], ps[:, :512],
                                        xr[:, mt * TH + c0: mt * TH + c0 + 512], ALU.add)
                p.inc(ins, "v")
            jobs.append((mm_list, 128, 512, ev))
    emit_mm(p, pool, jobs, ("vector", "v"), wait_before=[("gin", gin1), ("din", din0)])
    v_s = p.n["v"]
    emit_postln(p, xr, xr, TH, co, ready=[("v", v_s)])
    v_x7 = p.n["v"]
    s1.close()

    s2 = ExitStack()
    wRb = p.sb("wRb", [128, 8 * 1024], BF16, s2)
    wIb = p.sb("wIb", [128, 8 * 1024], BF16, s2)
    FRb = p.sb("FRb", [128, 8 * 1024], BF16, s2)
    FIb = p.sb("FIb", [128, 8 * 1024], BF16, s2)
    tA = p.sb("tA", [128, 512], F32, s2)
    def _g2(g):
        g.wait_ge(p.sem["v"], v_x7)
    p.sec("gpsimd", _g2)
    load_fm(p, wRb, wr_d, 1024, 1024)
    load_fm(p, wIb, wi_d, 1024, 1024)
    load_fm(p, FRb, fr_d, 1024, 1024)
    load_fm(p, FIb, fi_d, 1024, 1024)
    gin2 = p.n["gin"]

    for part in range(2):
        dstbuf = T1R if part == 0 else T1I
        jobs = []
        for mt in range(8):
            for ng in range(2):
                c0 = ng * 512
                if part == 0:
                    mmA = [(wRb[:, k * 1024 + mt * 128: k * 1024 + (mt + 1) * 128],
                            FRb[:, k * 1024 + c0: k * 1024 + c0 + 512]) for k in range(8)]
                    mmB = [(wIb[:, k * 1024 + mt * 128: k * 1024 + (mt + 1) * 128],
                            FIb[:, k * 1024 + c0: k * 1024 + c0 + 512]) for k in range(8)]
                else:
                    mmA = [(wRb[:, k * 1024 + mt * 128: k * 1024 + (mt + 1) * 128],
                            FIb[:, k * 1024 + c0: k * 1024 + c0 + 512]) for k in range(8)]
                    mmB = [(wIb[:, k * 1024 + mt * 128: k * 1024 + (mt + 1) * 128],
                            FRb[:, k * 1024 + c0: k * 1024 + c0 + 512]) for k in range(8)]
                def evA(env, ps, pred, mt=mt, c0=c0):
                    ins = env.tensor_copy(tA[:, :], ps[:, :512])
                    p.inc(ins, "v")
                jobs.append((mmA, 128, 512, evA))
                def evB(env, ps, pred, mt=mt, c0=c0, part=part, dstbuf=dstbuf):
                    op = ALU.subtract if part == 0 else ALU.add
                    ins = env.tensor_tensor(dstbuf[:, mt * 1024 + c0: mt * 1024 + c0 + 512],
                                            tA[:, :], ps[:, :512], op)
                    p.inc(ins, "v")
                jobs.append((mmB, 128, 512, evB))
        emit_mm(p, pool, jobs, ("vector", "v"), wait_before=[("gin", gin2)])
    v_t1 = p.n["v"]
    s2.close()

    s3 = ExitStack()
    GRb = p.sb("GRb", [128, 8 * 1024], BF16, s3)
    GInb = p.sb("GInb", [128, 8 * 1024], BF16, s3)
    def _g3(g):
        g.wait_ge(p.sem["v"], v_t1)
    p.sec("gpsimd", _g3)
    load_fm(p, GRb, gr_d, 1024, 1024)
    load_fm(p, GInb, gin_d, 1024, 1024)
    gin3 = p.n["gin"]
    jobs = []
    for mt in range(8):
        for ng in range(2):
            c0 = ng * 512
            mm_list = [(T1R[:, k * 1024 + mt * 128: k * 1024 + (mt + 1) * 128],
                        GRb[:, k * 1024 + c0: k * 1024 + c0 + 512]) for k in range(8)] + \
                      [(T1I[:, k * 1024 + mt * 128: k * 1024 + (mt + 1) * 128],
                        GInb[:, k * 1024 + c0: k * 1024 + c0 + 512]) for k in range(8)]
            def ev(env, ps, pred, mt=mt, c0=c0):
                ins = env.tensor_copy(Mb[:, mt * 1024 + c0: mt * 1024 + c0 + 512], ps[:, :512])
                p.inc(ins, "v")
            jobs.append((mm_list, 128, 512, ev))
    emit_mm(p, pool, jobs, ("vector", "v"), wait_before=[("gin", gin3), ("v", v_t1)])
    v_m = p.n["v"]
    s3.close()

    s4 = ExitStack()
    x7b = p.sb("x7b", [128, 8 * TH], BF16, s4)
    def vx7(v):
        v.wait_ge(p.sem["v"], v_m)
        for mt in range(8):
            p.inc(v.tensor_copy(x7b[:, mt * TH:(mt + 1) * TH], xr[:, mt * TH:(mt + 1) * TH]), "v")
    p.sec("vector", vx7)
    v_x7b = p.n["v"]

    jobs = []
    for mt in range(8):
        for ng in range(4):
            c0 = ng * 512
            mm_list = [(Mb[:, k * 1024 + mt * 128: k * 1024 + (mt + 1) * 128],
                        x7b[:, k * TH + c0: k * TH + c0 + 512]) for k in range(8)]
            def ev(env, ps, pred, mt=mt, c0=c0):
                env.tensor_scalar(ps[:, :512], ps[:, :512], biasc[:, mt:mt + 1], None, ALU.add)
                ins = env.tensor_tensor(xr[:, mt * TH + c0: mt * TH + c0 + 512], ps[:, :512],
                                        xr[:, mt * TH + c0: mt * TH + c0 + 512], ALU.add)
                p.inc(ins, "v")
            jobs.append((mm_list, 128, 512, ev))
    emit_mm(p, pool, jobs, ("vector", "v"), wait_before=[("v", v_x7b)])
    v_s8 = p.n["v"]
    emit_postln(p, xr, xr, TH, co, ready=[("v", v_s8)])
    v_ln = p.n["v"]
    s4.close()

    def dx(s):
        s.wait_ge(p.sem["v"], v_ln)
        for mt in range(8):
            p.inc(s.dma_start(out=x8_d[mt * 128:(mt + 1) * 128, :], in_=xr[:, mt * TH:(mt + 1) * TH]), "dout", 16)
        p.wait(s, "dout")
    p.sec("sync", dx)
    p.close()
    return p.nc


def dft_consts():
    D = 1024
    jk = np.outer(np.arange(D), np.arange(D)) % D
    ang = 2.0 * np.pi * jk / D
    FR = np.cos(ang).astype(np.float32)
    FI = (-np.sin(ang)).astype(np.float32)
    GR = (np.cos(ang) / D).astype(np.float32)
    GIn = (-np.sin(ang) / D).astype(np.float32)
    return FR, FI, GR, GIn


def host_A11_inmaps(inputs, co_np, z_parts, x6_parts):
    FR, FI, GR, GIn = dft_consts()
    WoT = np.ascontiguousarray(inputs["L3_attn_Wo"].T)
    biasc = inputs["L3_ffn_bias"].reshape(8, 128).T.copy()
    base = const_inmap(co_np)
    maps = []
    for c in range(NCORE):
        b, i = c // 2, c % 2
        t0 = i * TH
        zsl = np.concatenate([np.asarray(z_parts[2 * b])[:, t0:t0 + TH],
                              np.asarray(z_parts[2 * b + 1])[:, t0:t0 + TH]], axis=0)
        m = dict(base)
        m.update(zsl=np.ascontiguousarray(zsl), WoT=WoT, x6=x6_parts[c],
                 wR=inputs["L3_ffn_w_freq_real"], wI=inputs["L3_ffn_w_freq_imag"],
                 FR=FR, FI=FI, GR=GR, GIn=GIn, biasc=biasc)
        maps.append(m)
    return maps


def _prog(name, builder):
    if name not in _prog_cache:
        _prog_cache[name] = builder()
    return _prog_cache[name]


def kernel(**inputs):
    return kernel_fused(**inputs)


def kernel_unfused(**inputs):
    inputs = {k: np.ascontiguousarray(np.asarray(v, dtype=np.float32)) for k, v in inputs.items()}
    co_np = host_consts()
    cores = list(range(NCORE))

    r1 = run_bass_kernel_spmd(_prog("A1", lambda: build_A1(co_np)), host_A1_inmaps(inputs, co_np), core_ids=cores).results
    r2 = run_bass_kernel_spmd(_prog("A2", lambda: build_A2(co_np)), host_A2_inmaps(inputs, co_np, r1), core_ids=cores).results
    r3 = run_bass_kernel_spmd(_prog("A3", lambda: build_A3(co_np)), host_A3_inmaps(inputs, co_np, r1, r2), core_ids=cores).results
    x1p = [np.asarray(r3[c]["x1"], np.float32) for c in cores]
    r4 = run_bass_kernel_spmd(_prog("A4", lambda: build_A4(co_np)), host_A4_inmaps(inputs, co_np, x1p), core_ids=cores).results
    x2p = [np.asarray(r4[c]["x2"], np.float32) for c in cores]
    r5 = run_bass_kernel_spmd(_prog("A5", lambda: build_attn(co_np, biased=True, name="A5")),
                              host_A5_inmaps(inputs, co_np, x2p), core_ids=cores).results
    zp = [np.asarray(r5[c]["z"]) for c in cores]
    r6 = run_bass_kernel_spmd(_prog("A6", lambda: build_A6(co_np)), host_A6_inmaps(inputs, co_np, zp, x2p), core_ids=cores).results
    x4p = [np.asarray(r6[c]["x4"], np.float32) for c in cores]
    r7 = run_bass_kernel_spmd(_prog("A7", lambda: build_A7(co_np)), host_A7_inmaps(inputs, co_np, x4p), core_ids=cores).results
    r8 = run_bass_kernel_spmd(_prog("A8", lambda: build_A8(co_np)), host_A8_inmaps(inputs, co_np, r7), core_ids=cores).results
    ysp = [np.asarray(r8[c]["ys"]) for c in cores]
    r9 = run_bass_kernel_spmd(_prog("A9", lambda: build_A9(co_np)), host_A9_inmaps(inputs, co_np, ysp, x4p), core_ids=cores).results
    x6p = [np.asarray(r9[c]["x6"], np.float32) for c in cores]
    r10 = run_bass_kernel_spmd(_prog("A10", lambda: build_attn(co_np, biased=True, name="A10")),
                               host_A10_inmaps(inputs, co_np, x6p), core_ids=cores).results
    z3p = [np.asarray(r10[c]["z"]) for c in cores]
    r11 = run_bass_kernel_spmd(_prog("A11", lambda: build_A11(co_np)), host_A11_inmaps(inputs, co_np, z3p, x6p), core_ids=cores).results

    out = np.zeros((B, T, D), dtype=np.float32)
    for c in cores:
        b, i = c // 2, c % 2
        out[b, i * TH:(i + 1) * TH] = np.asarray(r11[c]["x8"], np.float32).T
    return out


RG2 = [[0, 1], [2, 3], [4, 5], [6, 7]]


def _barrier(p):
    sems = ["din", "gin", "mm", "v", "sc", "zd", "zg", "dout", "cc"]
    def gb(g):
        for sm in sems:
            p.wait(g, sm)
    p.sec("gpsimd", gb)
    def sb_(s):
        for sm in sems:
            p.wait(s, sm)
    p.sec("sync", sb_)


def _coll(p, kind, in_ap, out_ap, op=ALU.bypass):
    def g(g):
        for sm in ("dout", "din", "zd", "zg"):
            p.wait(g, sm)
        ins = g.collective_compute(kind, op, replica_groups=RG2,
                                   ins=[in_ap.opt()], outs=[out_ap.opt()])
        p.inc(ins, "cc", 1)
        g.wait_ge(p.sem["cc"], p.n["cc"])
    p.sec("gpsimd", g)


def _dd(p, dst_ap, src_ap, slow=False):
    def s(s):
        p.wait(s, "cc")
        p.inc(s.dma_start(out=dst_ap, in_=src_ap, allow_slow_non_contiguous=slow),
              "din", 16)
        p.wait(s, "din")
    p.sec("sync", s)


def emit_xchg(p, src, ybuf, dst, S, N, dt, m0, m1):
    tps = S // 128
    sc = ExitStack()
    nsl = 2
    ts = [p.sb("xgt", [128, N], dt, scope=sc) for _ in range(nsl)]
    y0s = [p.sb("xgy0", [128, N], dt, scope=sc) for _ in range(nsl)]
    y1s = [p.sb("xgy1", [128, N], dt, scope=sc) for _ in range(nsl)]
    tg = [0] * nsl
    y0g = [0] * nsl
    y1g = [0] * nsl
    for r in range(2 * tps):
        sl = r % nsl
        j, rr = r // tps, r % tps
        def ld(sy, r=r, sl=sl, g=tg[sl]):
            if g:
                sy.wait_ge(p.sem["v"], g)
            p.inc(sy.dma_start(out=ts[sl][:, :], in_=src[r * 128:(r + 1) * 128, :]),
                  "din", 16)
        p.sec("sync", ld)
        dv = p.n["din"]
        def v0(v, sl=sl, dv=dv, g=y0g[sl]):
            v.wait_ge(p.sem["din"], dv)
            if g:
                v.wait_ge(p.sem["dout"], g)
            p.inc(v.tensor_scalar(y0s[sl][:, :], ts[sl][:, :], m0[:, :], None, ALU.mult),
                  "v", 1)
        p.sec("vector", v0)
        vv0 = p.n["v"]
        def s0(sy, sl=sl, vv=vv0, row=(2 * j) * S + rr * 128):
            sy.wait_ge(p.sem["v"], vv)
            p.inc(sy.dma_start(out=ybuf[row:row + 128, :], in_=y0s[sl][:, :]), "dout", 16)
        p.sec("sync", s0)
        y0g[sl] = p.n["dout"]
        def v1(v, sl=sl, g=y1g[sl]):
            if g:
                v.wait_ge(p.sem["dout"], g)
            p.inc(v.tensor_scalar(y1s[sl][:, :], ts[sl][:, :], m1[:, :], None, ALU.mult),
                  "v", 1)
        p.sec("vector", v1)
        tg[sl] = p.n["v"]
        vv1 = p.n["v"]
        def s1(sy, sl=sl, vv=vv1, row=(2 * j + 1) * S + rr * 128):
            sy.wait_ge(p.sem["v"], vv)
            p.inc(sy.dma_start(out=ybuf[row:row + 128, :], in_=y1s[sl][:, :]), "dout", 16)
        p.sec("sync", s1)
        y1g[sl] = p.n["dout"]
    sc.close()
    _coll(p, "ReduceScatter", ybuf, dst, op=ALU.add)


def build_fused(co_np):
    p = P("FUSED")
    p.fused = True
    p.start()
    emit_consts(p, co_np)

    d = p.dint
    u_i = d("i_u", [2048, TH], BF16)
    z_i = d("i_z", [2048, TH], BF16)
    dbc_i = d("i_dbc", [96, TH], F32)
    y_u = d("y_u", [4096, TH], BF16)
    ucc = d("c_u", [2048, TH], BF16)
    dbcg = d("c_dbc", [192, TH], F32)
    uh_i = d("i_uh", [1024, T], BF16)
    dbcf_i = d("i_dbcf", [96, T], F32)
    ys2_i = d("i_ys2", [2048, TH], BF16)
    y_ys2 = d("y_ys2", [4096, TH], BF16)
    ysl3_i = d("c_ysl3", [2048, TH], BF16)
    x1_i = d("i_x1", [1024, TH], F32)
    hal_in = d("i_halin", [1024, 1], F32)
    halg = d("c_halg", [2048, 1], F32)
    pcol_d = d("i_pcol", [1024, 1], F32)
    x2_i = d("i_x2", [1024, TH], F32)
    x2g = d("c_x2g", [2048, TH], F32)
    xf5_i = d("i_xf5", [1024, T], F32)
    z5_i = d("i_z5", [1024, TH], BF16)
    y_z5 = d("y_z5", [2048, TH], BF16)
    zsl6_i = d("c_zsl6", [1024, TH], BF16)
    x4_i = d("i_x4", [1024, TH], F32)
    l2names = ["k", "v", "r", "g", "delta"]
    l2_i = {nm: d(f"i_l2{nm}", [512, TH], BF16) for nm in l2names}
    y_l2 = {nm: d(f"y_l2{nm}", [1024, TH], BF16) for nm in l2names}
    l2cc = {nm: d(f"c_l2{nm}", [512, TH], BF16) for nm in l2names}
    l2f = {nm: d(f"i_l2f{nm}", [256, T], BF16) for nm in l2names}
    bs_i = d("i_bs", [1, TH], F32)
    bsg = d("c_bsg", [2, TH], F32)
    bsf_i = d("i_bsf", [1, T], F32)
    ys8_i = d("i_ys8", [512, TH], BF16)
    y_ys8 = d("y_ys8", [1024, TH], BF16)
    ysl9_i = d("c_ysl9", [512, TH], BF16)
    x6_i = d("i_x6", [1024, TH], F32)
    x6g = d("c_x6g", [2048, TH], F32)
    xf10_i = d("i_xf10", [1024, T], F32)
    z10_i = d("i_z10", [1024, TH], BF16)
    y_z10 = d("y_z10", [2048, TH], BF16)
    zsl11_i = d("c_zsl11", [1024, TH], BF16)

    m0h = p.din("G_m0", [128, 1], F32)
    m1h = p.din("G_m1", [128, 1], F32)
    m0b = p.sb("m0b", [128, 1], F32)
    m1b = p.sb("m1b", [128, 1], F32)
    m1f = m1b
    dma_in(p, m0b, m0h, cast=False)
    dma_in(p, m1b, m1h, cast=False)
    def vw(v):
        p.wait(v, "gin")
        p.wait(v, "din")
    p.sec("vector", vw)

    def stage(pre, io, fn):
        _barrier(p)
        p.pre = pre
        p.io_map = {pre + k: v for k, v in io.items()}
        p.push_frame()
        fn()
        p.pop_frame()
        p.pre = ""
        p.io_map = {}
        _barrier(p)

    stage("A1_", {"u": u_i, "z": z_i, "dbc": dbc_i}, lambda: build_A1(co_np, p=p))
    emit_xchg(p, u_i, y_u, ucc, 1024, TH, BF16, m0b, m1b)
    _dd(p, uh_i[:, 0:TH], ucc[0:1024, :])
    _dd(p, uh_i[:, TH:T], ucc[1024:2048, :])
    _coll(p, "AllGather", dbc_i, dbcg)
    _dd(p, dbcf_i[:, 0:TH], dbcg[0:96, :])
    _dd(p, dbcf_i[:, TH:T], dbcg[96:192, :])

    stage("A2_", {"dbc": dbcf_i, "u": uh_i, "ys": ys2_i},
          lambda: build_A2(co_np, p=p, fused_out=True))
    emit_xchg(p, ys2_i, y_ys2, ysl3_i, 1024, TH, BF16, m0b, m1b)

    stage("A3_", {"ysl": ysl3_i, "u": u_i, "z": z_i, "x1": x1_i},
          lambda: build_A3(co_np, p=p))
    _dd(p, hal_in, x1_i[:, TH - 1:TH], slow=True)
    _barrier(p)
    _coll(p, "AllGather", hal_in, halg)
    hsc = ExitStack()
    ht = p.sb("halt", [128, 8], F32, scope=hsc)
    hy = p.sb("haly", [128, 8], F32, scope=hsc)
    def hl(sy):
        p.wait(sy, "cc")
        p.inc(sy.dma_start(out=ht[:, :], in_=halg[0:1024, :]), "din", 16)
    p.sec("sync", hl)
    hdv = p.n["din"]
    def hv(v, hdv=hdv):
        v.wait_ge(p.sem["din"], hdv)
        p.inc(v.tensor_scalar(hy[:, :], ht[:, :], m1f[:, :], None, ALU.mult), "v", 1)
    p.sec("vector", hv)
    hvv = p.n["v"]
    def hs(sy, hvv=hvv):
        sy.wait_ge(p.sem["v"], hvv)
        p.inc(sy.dma_start(out=pcol_d, in_=hy[:, :]), "dout", 16)
    p.sec("sync", hs)
    hsc.close()

    stage("A4_", {"x1own": x1_i, "pcol": pcol_d, "x2": x2_i},
          lambda: build_A4(co_np, p=p, split_in=True))
    _coll(p, "AllGather", x2_i, x2g)
    _dd(p, xf5_i[:, 0:TH], x2g[0:1024, :])
    _dd(p, xf5_i[:, TH:T], x2g[1024:2048, :])

    stage("A5_", {"xf": xf5_i, "z": z5_i},
          lambda: build_attn(co_np, biased=True, name="A5", p=p, fused_out=True))
    emit_xchg(p, z5_i, y_z5, zsl6_i, 512, TH, BF16, m0b, m1b)

    stage("A6_", {"zsl": zsl6_i, "x2": x2_i, "x4": x4_i},
          lambda: build_A6(co_np, p=p))

    stage("A7_", dict({nm: l2_i[nm] for nm in l2names}, x=x4_i, bscale=bs_i),
          lambda: build_A7(co_np, p=p))
    for nm in l2names:
        emit_xchg(p, l2_i[nm], y_l2[nm], l2cc[nm], 256, TH, BF16, m0b, m1b)
        _dd(p, l2f[nm][:, 0:TH], l2cc[nm][0:256, :])
        _dd(p, l2f[nm][:, TH:T], l2cc[nm][256:512, :])
    _coll(p, "AllGather", bs_i, bsg)
    _dd(p, bsf_i[0:1, 0:TH], bsg[0:1, :])
    _dd(p, bsf_i[0:1, TH:T], bsg[1:2, :])

    stage("A8_", {"kf": l2f["k"], "vf": l2f["v"], "rf": l2f["r"], "gf": l2f["g"],
                  "df": l2f["delta"], "bscale": bsf_i, "ys": ys8_i},
          lambda: build_A8(co_np, p=p, fused_out=True))
    emit_xchg(p, ys8_i, y_ys8, ysl9_i, 256, TH, BF16, m0b, m1b)

    stage("A9_", {"ysl": ysl9_i, "x4": x4_i, "x6": x6_i},
          lambda: build_A9(co_np, p=p))
    _coll(p, "AllGather", x6_i, x6g)
    _dd(p, xf10_i[:, 0:TH], x6g[0:1024, :])
    _dd(p, xf10_i[:, TH:T], x6g[1024:2048, :])

    stage("A10_", {"xf": xf10_i, "z": z10_i},
          lambda: build_attn(co_np, biased=True, name="A10", p=p, fused_out=True))
    emit_xchg(p, z10_i, y_z10, zsl11_i, 512, TH, BF16, m0b, m1b)

    stage("A11_", {"zsl": zsl11_i, "x6": x6_i},
          lambda: build_A11(co_np, p=p))

    def fin(s):
        for sm in ("dout", "zd", "zg", "din", "gin"):
            p.wait(s, sm)
    p.sec("sync", fin)
    p.stack.close()
    return p.nc


_FUSE_KEYS = {
    "A1": ["xT", "WinT", "convw", "convb", "WxT"],
    "A2": ["WdtT", "bdt", "AL", "PD4", "PBq", "PCq", "SEL"],
    "A3": ["Dsk", "WoutT", "xres"],
    "A4": ["mur", "muk", "WrT", "WkT", "WvT"],
    "A5": ["WQ", "WK", "WV", "bQ", "bK", "bV", "MSK"],
    "A6": ["WO", "bO", "rmsw", "gwT", "W1all", "b1c", "W2all", "b2c", "E8"],
    "A7": ["Wk", "Wv", "Wr", "Wg", "Wsel", "dtW", "dtbc"],
    "A8": ["bdc", "tfc", "MSK16"],
    "A9": ["Wo", "W1", "b1c", "W2", "b2c"],
    "A10": ["WQ", "WK", "WV", "bQ", "bK", "bV", "MSK"],
    "A11": ["WoT", "wR", "wI", "FR", "FI", "GR", "GIn", "biasc"],
}


def host_fused_inmaps(inputs, co_np):
    dz8 = [np.zeros((1024, TH), np.float32) for _ in range(NCORE)]
    dzb = [{"k": np.zeros((512, TH), BF), "v": np.zeros((512, TH), BF), "r": np.zeros((512, TH), BF),
            "g": np.zeros((512, TH), BF), "delta": np.zeros((512, TH), BF),
            "bscale": np.zeros((1, TH), np.float32), "u": np.zeros((2048, TH), BF),
            "z": np.zeros((2048, TH), BF), "dbc": np.zeros((96, TH), np.float32),
            "ys": np.zeros((1024, TH), BF), "x1": dz8[0]} for _ in range(NCORE)]
    per_stage = {
        "A1": host_A1_inmaps(inputs, co_np),
        "A2": host_A2_inmaps(inputs, co_np, dzb),
        "A3": host_A3_inmaps(inputs, co_np, dzb, dzb),
        "A4": host_A4_inmaps(inputs, co_np, dz8),
        "A5": host_A5_inmaps(inputs, co_np, dz8),
        "A6": host_A6_inmaps(inputs, co_np, [a["ys"] for a in dzb], dz8),
        "A7": host_A7_inmaps(inputs, co_np, dz8),
        "A8": host_A8_inmaps(inputs, co_np, dzb),
        "A9": host_A9_inmaps(inputs, co_np, [a["ys"][:512] for a in dzb], dz8),
        "A10": host_A10_inmaps(inputs, co_np, dz8),
        "A11": host_A11_inmaps(inputs, co_np, [a["ys"] for a in dzb], dz8),
    }
    base = const_inmap(co_np)
    maps = []
    for c in range(NCORE):
        m = dict(base)
        i = c % 2
        m["G_m0"] = np.full((128, 1), 1.0 - i, np.float32)
        m["G_m1"] = np.full((128, 1), float(i), np.float32)
        for st_, keys in _FUSE_KEYS.items():
            sm = per_stage[st_][c]
            for k in keys:
                m[st_ + "_" + k] = sm[k]
        maps.append(m)
    return maps


def kernel_fused(**inputs):
    inputs = {k: np.ascontiguousarray(np.asarray(v, dtype=np.float32)) for k, v in inputs.items()}
    co_np = host_consts()
    nc = _prog("FUSED", lambda: build_fused(co_np))
    maps = host_fused_inmaps(inputs, co_np)
    res = run_bass_kernel_spmd(nc, maps, core_ids=list(range(NCORE))).results
    out = np.zeros((B, T, D), dtype=np.float32)
    for c in range(NCORE):
        b, i = c // 2, c % 2
        out[b, i * TH:(i + 1) * TH] = np.asarray(res[c]["A11_x8"], np.float32).T
    return out


# version 04
# speedup vs baseline: 1.3120x; 1.3120x over previous
import math
from contextlib import ExitStack

import numpy as np
import ml_dtypes

import concourse.bass as bass
import concourse.mybir as mybir
from concourse.bass_utils import run_bass_kernel_spmd

F32 = mybir.dt.float32
BF16 = mybir.dt.bfloat16
I32 = mybir.dt.int32
U32 = mybir.dt.uint32
AF = mybir.ActivationFunctionType
ALU = mybir.AluOpType
AX = mybir.AxisListType

NCORE = 8
B = 4
T = 4096
TH = 2048
D = 1024
BF = ml_dtypes.bfloat16

_prog_cache = {}
DEBUG_A1 = False


def cdiv(a, b):
    return (a + b - 1) // b


class P:

    def __init__(self, name):
        self.nc = bass.Bass("TRN2", target_bir_lowering=False)
        self.name = name
        self.stack = ExitStack()
        self.block = None
        self.n = {}
        self.sem = {}
        self._uid = 0
        self.pre = ""
        self.io_map = {}
        self.fused = False
        self.started = False
        self._frames = []
        self._co = None

    def uid(self):
        self._uid += 1
        return self._uid

    def start(self, sems=("din", "gin", "mm", "v", "sc", "gp", "dout", "zd", "zg", "cc")):
        if self.started:
            return
        self.started = True
        for s in sems:
            self.sem[s] = self.stack.enter_context(self.nc.semaphore(s))
            self.n[s] = 0
        self.block = self.stack.enter_context(self.nc.Block())

    def push_frame(self):
        self._frames.append(ExitStack())

    def pop_frame(self):
        self._frames.pop().close()

    def sb(self, name, shape, dtype=F32, scope=None):
        st = scope if scope is not None else (self._frames[-1] if self._frames else self.stack)
        return st.enter_context(self.nc.sbuf_tensor(f"{name}{self.uid()}", shape, dtype))

    def ps(self, name, shape, dtype=F32, scope=None):
        st = scope if scope is not None else (self._frames[-1] if self._frames else self.stack)
        return st.enter_context(self.nc.psum_tensor(f"{name}{self.uid()}", shape, dtype))

    def din(self, name, shape, dtype=F32):
        nm = self.pre + name
        if nm in self.io_map:
            return self.io_map[nm]
        return self.nc.dram_tensor(nm, shape, dtype, kind="ExternalInput").ap()

    def dout(self, name, shape, dtype=F32):
        nm = self.pre + name
        if nm in self.io_map:
            return self.io_map[nm]
        return self.nc.dram_tensor(nm, shape, dtype, kind="ExternalOutput").ap()

    def dint(self, name, shape, dtype=F32):
        t = self.nc.dram_tensor(name, shape, dtype, kind="Internal")
        return t.ap()

    def sec(self, eng, fn):
        getattr(self.block, eng)(fn)

    def inc(self, instr, semname, k=1):
        instr.then_inc(self.sem[semname], k)
        self.n[semname] += k
        return self.n[semname]

    def wait(self, env, semname, val=None):
        v = self.n[semname] if val is None else val
        if v > 0:
            env.wait_ge(self.sem[semname], v)

    def close(self):
        if self.fused:
            return
        self.stack.close()


def _ap(x):
    return x[:, :] if not isinstance(x, bass.AP) else x


def dma_in(p, dst_sb, src_ap, cast=False, sem=None):
    dst_sb = _ap(dst_sb)
    src_ap = _ap(src_ap)
    if cast:
        sem = sem or "gin"
        def _(g):
            p.inc(g.dma_start(out=dst_sb, in_=src_ap), sem, 16)
        p.sec("gpsimd", _)
    else:
        sem = sem or "din"
        def _(s):
            p.inc(s.dma_start(out=dst_sb, in_=src_ap), sem, 16)
        p.sec("sync", _)


def load_fm(p, dst_sb, dram_ap, rows, cols, dst_col_stride=None, cast=True, sem=None,
            row0=0, col0=0):
    cstride = dst_col_stride if dst_col_stride is not None else cols
    nk = cdiv(rows, 128)
    for kt in range(nk):
        r = min(128, rows - kt * 128)
        dma_in(p, dst_sb[:r, kt * cstride: kt * cstride + cols],
               dram_ap[row0 + kt * 128: row0 + kt * 128 + r, col0: col0 + cols],
               cast=cast, sem=sem)


class Pool:

    def __init__(self, p, nb, width=512, parts=128):
        self.p = p
        self.tiles = [p.ps("pool", [parts, width], F32) for _ in range(nb)]
        self.freed = [None] * nb
        self.rr = 0
        self.nb = nb


def emit_mm(p, pool, jobs, evacs, mm_sem="mm", wait_before=()):
    ename, esem = evacs
    ebase = p.n[esem]
    plan = []

    def tsec(t):
        for wsem, wval in wait_before:
            p.wait(t, wsem, wval)
        for i, (mm_list, m, w, evac_fn) in enumerate(jobs):
            j = pool.rr % pool.nb
            pool.rr += 1
            if pool.freed[j] is not None:
                fsem, fval = pool.freed[j]
                t.wait_ge(p.sem[fsem], fval)
            tile = pool.tiles[j]
            for ki, (lh, rh) in enumerate(mm_list):
                ins = t.matmul(tile[:m, :w], lh, rh, start=(ki == 0), stop=(ki == len(mm_list) - 1))
            cnt = p.inc(ins, mm_sem)
            pred = ebase + i + 1
            pool.freed[j] = (esem, pred)
            plan.append((cnt, tile, evac_fn, pred))
    p.sec("tensor", tsec)

    def esec(env):
        for cnt, tile, evac_fn, pred in plan:
            env.wait_ge(p.sem[mm_sem], cnt)
            evac_fn(env, tile, pred)
            assert p.n[esem] == pred, f"evac must inc {esem} exactly once ({p.n[esem]} != {pred})"
    p.sec(ename, esec)


def emit_postln(p, s_sb, out_sb, Tt, co, eps=1e-5, ready=(), cstride=None, col0=0):
    cs = cstride if cstride is not None else Tt

    def sv(mt, c0, w):
        return s_sb[:, mt * cs + col0 + c0: mt * cs + col0 + c0 + w]

    scope = ExitStack()
    sbf = p.sb("lnsbf", [128, 8 * 512], BF16, scope)
    sqf = p.sb("lnsqf", [128, 8 * 512], BF16, scope)
    st_mu = p.sb("lnmu", [1, 512], F32, scope)
    st_rs = p.sb("lnrs", [1, 512], F32, scope)
    st_t2 = p.sb("lnt2", [1, 512], F32, scope)
    ps_a = p.ps("lnpa", [128, 512], F32, scope)
    ps_b = p.ps("lnpb", [128, 512], F32, scope)
    ps_mu = p.ps("lnpmu", [128, 512], F32, scope)
    ps_rs = p.ps("lnprs", [128, 512], F32, scope)
    NG = cdiv(Tt, 512)
    mm_prev = 0

    for ng in range(NG):
        c0 = ng * 512
        w = min(512, Tt - c0)

        def vcast(v, c0=c0, w=w, ng=ng, mm_prev=mm_prev):
            if ng == 0:
                for sem, val in ready:
                    p.wait(v, sem, val)
            else:
                v.wait_ge(p.sem["mm"], mm_prev)
            for mt in range(8):
                p.inc(v.tensor_copy(sbf[:, mt * 512: mt * 512 + w], sv(mt, c0, w)), "v")
        p.sec("vector", vcast)
        v_cast = p.n["v"]

        def ssq(sc, c0=c0, w=w, ng=ng, mm_prev=mm_prev):
            if ng == 0:
                for sem, val in ready:
                    p.wait(sc, sem, val)
            else:
                sc.wait_ge(p.sem["mm"], mm_prev)
            for mt in range(8):
                p.inc(sc.activation(sqf[:, mt * 512: mt * 512 + w], sv(mt, c0, w), AF.Square), "sc")
        p.sec("scalar", ssq)
        sc_sq = p.n["sc"]

        def tstats(t, c0=c0, w=w, ng=ng):
            t.wait_ge(p.sem["v"], v_cast)
            t.wait_ge(p.sem["sc"], sc_sq)
            if ng > 0:
                p.wait(t, "v")
            for mt in range(8):
                t.matmul(ps_a[0:1, :w], co["ones_bf"][:, :], sbf[:, mt * 512: mt * 512 + w],
                         start=(mt == 0), stop=(mt == 7))
            for mt in range(8):
                ins = t.matmul(ps_b[0:1, :w], co["ones_bf"][:, :], sqf[:, mt * 512: mt * 512 + w],
                               start=(mt == 0), stop=(mt == 7))
            p.inc(ins, "mm")
        p.sec("tensor", tstats)
        mm_stats = p.n["mm"]

        def vstat(v, w=w):
            v.wait_ge(p.sem["mm"], mm_stats)
            v.tensor_scalar(st_mu[0:1, :w], ps_a[0:1, :w], 1.0 / 1024, None, ALU.mult)
            v.tensor_scalar(st_t2[0:1, :w], ps_b[0:1, :w], 1.0 / 1024, float(eps), ALU.mult, ALU.add)
            v.tensor_tensor(st_rs[0:1, :w], st_mu[0:1, :w], st_mu[0:1, :w], ALU.mult)
            ins = v.tensor_tensor(st_t2[0:1, :w], st_t2[0:1, :w], st_rs[0:1, :w], ALU.subtract)
            p.inc(ins, "v")
        p.sec("vector", vstat)
        v_var = p.n["v"]

        def srstd(sc, w=w):
            sc.wait_ge(p.sem["v"], v_var)
            sc.activation(st_t2[0:1, :w], st_t2[0:1, :w], AF.Ln)
            ins = sc.activation(st_rs[0:1, :w], st_t2[0:1, :w], AF.Exp, scale=-0.5)
            p.inc(ins, "sc")
        p.sec("scalar", srstd)
        sc_rstd = p.n["sc"]

        def tbc(t, w=w):
            t.wait_ge(p.sem["sc"], sc_rstd)
            t.matmul(ps_mu[:, :w], co["onesrow"][:, :], st_mu[0:1, :w], start=True, stop=True)
            ins = t.matmul(ps_rs[:, :w], co["onesrow"][:, :], st_rs[0:1, :w], start=True, stop=True)
            p.inc(ins, "mm")
        p.sec("tensor", tbc)
        mm_prev = p.n["mm"]
        mm_bc = mm_prev

        def vnorm(v, c0=c0, w=w):
            v.wait_ge(p.sem["mm"], mm_bc)
            for mt in range(8):
                v.tensor_tensor(out_sb[:, mt * Tt + c0: mt * Tt + c0 + w],
                                sv(mt, c0, w), ps_mu[:, :w], ALU.subtract)
                ins = v.tensor_tensor(out_sb[:, mt * Tt + c0: mt * Tt + c0 + w],
                                      out_sb[:, mt * Tt + c0: mt * Tt + c0 + w], ps_rs[:, :w], ALU.mult)
            p.inc(ins, "v")
        p.sec("vector", vnorm)
    scope.close()


def softplus_evac(p, env_s, psum_ap, out_ap, tmp1, tmp2, bias_col):
    env_s.activation(tmp1, psum_ap, AF.Abs, bias=bias_col)
    env_s.activation(tmp2, tmp1, AF.Exp, scale=-1.0)
    return env_s.activation(tmp1, tmp2, AF.Ln, bias=1.0)


def host_consts():
    c = {}
    c["ident_bf"] = np.eye(128, dtype=BF)
    c["ident_f"] = np.eye(128, dtype=np.float32)
    c["ones_bf"] = np.ones((128, 1), dtype=np.float32)
    c["onesrow"] = np.ones((1, 128), dtype=np.float32)
    c["onesrow_bf"] = np.ones((1, 128), dtype=BF)
    return c


def emit_consts(p, co_np):
    if p._co is not None:
        return p._co
    co = {}
    ident_d = p.din("c_ident", [128, 128], BF16)
    identf_d = p.din("c_identf", [128, 128], F32)
    ones_d = p.din("c_ones", [128, 1], F32)
    onesrow_d = p.din("c_onesrow", [1, 128], F32)
    co["ident"] = p.sb("ident", [128, 128], BF16, scope=p.stack)
    co["identf"] = p.sb("identf", [128, 128], F32, scope=p.stack)
    co["ones_bf"] = p.sb("onesb", [128, 1], BF16, scope=p.stack)
    co["onesrow"] = p.sb("onesr", [1, 128], F32, scope=p.stack)
    co["onesrow_bf"] = p.sb("onesrb", [1, 128], BF16, scope=p.stack)
    dma_in(p, co["ident"], ident_d)
    onesrb_d = p.din("c_onesrb", [1, 128], BF16)
    dma_in(p, co["onesrow_bf"], onesrb_d)
    dma_in(p, co["identf"], identf_d)
    dma_in(p, co["ones_bf"], ones_d, cast=True)
    dma_in(p, co["onesrow"], onesrow_d)
    p._co = co
    return co


def const_inmap(co_np):
    return {"c_ident": co_np["ident_bf"], "c_identf": co_np["ident_f"],
            "c_ones": co_np["ones_bf"], "c_onesrow": co_np["onesrow"],
            "c_onesrb": co_np["onesrow_bf"]}


def build_A1(co_np, p=None):
    if p is None:
        p = P("A1")
    TX = 2176
    xT_d = p.din("xT", [1024, TX])
    win_d = p.din("WinT", [1024, 4096])
    cw_d = p.din("convw", [128, 64])
    cb_d = p.din("convb", [128, 16])
    wx_d = p.din("WxT", [2048, 96])
    u_d = p.dout("u", [2048, TH], BF16)
    xin_dbg = p.dout("xin_dbg", [2048, 2176], BF16) if DEBUG_A1 else None
    xc_dbg = p.dout("xc_dbg", [128, TH]) if DEBUG_A1 else None
    wb_dbg = p.dout("wb_dbg", [1024, 4096], BF16) if DEBUG_A1 else None
    z_d = p.dout("z", [2048, TH], BF16)
    dbc_d = p.dout("dbc", [96, TH])

    p.start()
    co = emit_consts(p, co_np)
    cw = p.sb("cw", [128, 64])
    cb = p.sb("cb", [128, 16])
    dma_in(p, cw, cw_d)
    dma_in(p, cb, cb_d)

    xin = p.sb("xin", [128, 16 * TX], BF16)
    zst = p.sb("zst", [128, 2 * TH], BF16)
    pool = Pool(p, 4)
    mm_scope = ExitStack()
    xtb = p.sb("xtb", [128, 8 * TX], BF16, mm_scope)
    wb = p.sb("wb", [128, 8 * 4096], BF16, mm_scope)
    load_fm(p, xtb, xT_d, 1024, TX)
    load_fm(p, wb, win_d, 1024, 4096)
    gin_all = p.n["gin"]

    jobs = []
    zplan = []
    for mt in range(32):
        is_z = mt >= 16
        ngs = [(128 + i * 512, 512) for i in range(4)] if is_z else \
              [(0, 512), (512, 512), (1024, 512), (1536, 512), (2048, 128)]
        for gi, (c0, w) in enumerate(ngs):
            mm_list = [(wb[:, k * 4096 + mt * 128: k * 4096 + (mt + 1) * 128],
                        xtb[:, k * TX + c0: k * TX + c0 + w]) for k in range(8)]
            if not is_z:
                def ev(env, ps, pred, mt=mt, c0=c0, w=w):
                    ins = env.tensor_copy(xin[:, mt * TX + c0: mt * TX + c0 + w], ps[:, :w])
                    p.inc(ins, "v")
            else:
                zt = mt - 16
                buf = zt % 2
                def ev(env, ps, pred, zt=zt, buf=buf, c0=c0, w=w, gi=gi):
                    if gi == 0 and zt >= 2:
                        p.wait(env, "zd", 16 * (zt - 1))
                    ins = env.tensor_copy(zst[:, buf * TH + (c0 - 128): buf * TH + (c0 - 128) + w], ps[:, :w])
                    cnt = p.inc(ins, "v")
                    if gi == 3:
                        zplan.append((cnt, zt, buf))
            jobs.append((mm_list, 128, w, ev))
    if DEBUG_A1:
        def wdma(s):
            p.wait(s, "gin", gin_all)
            for kt in range(8):
                p.inc(s.dma_start(out=wb_dbg[kt * 128:(kt + 1) * 128, :], in_=wb[:, kt * 4096:(kt + 1) * 4096]), "dout", 16)
        p.sec("sync", wdma)
        wb_dump_dout = p.n["dout"]

    emit_mm(p, pool, jobs, ("vector", "v"), wait_before=[("gin", gin_all)] + ([("dout", wb_dump_dout)] if DEBUG_A1 else []))

    def zdma(s):
        for cnt, zt, buf in zplan:
            s.wait_ge(p.sem["v"], cnt)
            p.inc(s.dma_start(out=z_d[zt * 128:(zt + 1) * 128, :], in_=zst[:, buf * TH:(buf + 1) * TH]), "zd", 16)
    p.sec("sync", zdma)
    v_xz = p.n["v"]
    mm_scope.close()

    if DEBUG_A1:
        def xdma(s):
            s.wait_ge(p.sem["v"], v_xz)
            for et in range(16):
                p.inc(s.dma_start(out=xin_dbg[et * 128:(et + 1) * 128, :], in_=xin[:, et * TX:(et + 1) * TX]), "dout", 16)
        p.sec("sync", xdma)

    ubuf = p.sb("ubuf", [128, 16 * TH], BF16)
    acc = p.sb("acc", [128, 2 * TH])
    sc_prev_silu = p.n["sc"]
    for et in range(16):
        def vconv(v, et=et, sc_prev_silu=sc_prev_silu):
            if et == 0:
                v.wait_ge(p.sem["v"], v_xz)
                p.wait(v, "din")
                if DEBUG_A1:
                    p.wait(v, "dout", wb_dump_dout)
            else:
                v.wait_ge(p.sem["sc"], sc_prev_silu)
            base = et * TX
            v.tensor_scalar(acc[:, :TH], xin[:, base + 125: base + 125 + TH],
                            cw[:, et * 4: et * 4 + 1], cb[:, et: et + 1], ALU.mult, ALU.add)
            v.scalar_tensor_tensor(acc[:, TH:], xin[:, base + 126: base + 126 + TH],
                                   cw[:, et * 4 + 1: et * 4 + 2], acc[:, :TH], ALU.mult, ALU.add)
            v.scalar_tensor_tensor(acc[:, :TH], xin[:, base + 127: base + 127 + TH],
                                   cw[:, et * 4 + 2: et * 4 + 3], acc[:, TH:], ALU.mult, ALU.add)
            ins = v.scalar_tensor_tensor(acc[:, TH:], xin[:, base + 128: base + 128 + TH],
                                         cw[:, et * 4 + 3: et * 4 + 4], acc[:, :TH], ALU.mult, ALU.add)
            p.inc(ins, "v")
        p.sec("vector", vconv)
        v_c = p.n["v"]
        if DEBUG_A1 and et == 0:
            def dacc(s, v_c=v_c):
                s.wait_ge(p.sem["v"], v_c)
                p.inc(s.dma_start(out=xc_dbg, in_=acc[:, TH:]), "dout", 16)
            p.sec("sync", dacc)

        def ssilu(sc, et=et):
            sc.wait_ge(p.sem["v"], v_c)
            ins = sc.activation(ubuf[:, et * TH:(et + 1) * TH], acc[:, TH:], AF.Silu)
            p.inc(ins, "sc")
        p.sec("scalar", ssilu)
        sc_prev_silu = p.n["sc"]

    sc_u = p.n["sc"]

    def udma(s):
        s.wait_ge(p.sem["sc"], sc_u)
        for et in range(16):
            p.inc(s.dma_start(out=u_d[et * 128:(et + 1) * 128, :], in_=ubuf[:, et * TH:(et + 1) * TH]), "dout", 16)
    p.sec("sync", udma)

    wxb = p.sb("wxb", [128, 16 * 96], BF16)
    def _gate(g):
        g.wait_ge(p.sem["v"], v_xz)
    p.sec("gpsimd", _gate)
    load_fm(p, wxb, wx_d, 2048, 96)
    gin2 = p.n["gin"]
    dbc_sb = p.sb("dbcsb", [96, TH])
    jobs = []
    for gi in range(4):
        c0 = gi * 512
        mm_list = [(wxb[:, k * 96:(k + 1) * 96], ubuf[:, k * TH + c0: k * TH + c0 + 512]) for k in range(16)]
        def ev(env, ps, pred, c0=c0):
            ins = env.tensor_copy(dbc_sb[:, c0:c0 + 512], ps[:96, :512])
            p.inc(ins, "v")
        jobs.append((mm_list, 96, 512, ev))
    emit_mm(p, pool, jobs, ("vector", "v"),
            wait_before=[("gin", gin2), ("sc", sc_u)])
    v_dbc = p.n["v"]

    def ddma(s):
        s.wait_ge(p.sem["v"], v_dbc)
        p.inc(s.dma_start(out=dbc_d, in_=dbc_sb[:96, :]), "dout", 16)
        p.wait(s, "dout")
    p.sec("sync", ddma)
    p.close()
    return p.nc


def host_A1_inmaps(inputs, co_np):
    x = inputs["L0_attn_x"]
    WinT = np.ascontiguousarray(inputs["L0_attn_W_in"].T)
    convw = inputs["L0_attn_conv_w"].reshape(16, 128, 4).transpose(1, 0, 2).reshape(128, 64)
    convb = inputs["L0_attn_conv_b"].reshape(16, 128).T.copy()
    WxT = np.ascontiguousarray(inputs["L0_attn_W_xproj"].T)
    base = const_inmap(co_np)
    maps = []
    for c in range(NCORE):
        b, i = c // 2, c % 2
        t0 = i * TH
        xs = np.zeros((2176, 1024), dtype=np.float32)
        lo = max(0, t0 - 128)
        xs[128 - (t0 - lo):128 + TH] = x[b, lo:t0 + TH]
        m = dict(base)
        m.update(xT=np.ascontiguousarray(xs.T), WinT=WinT, convw=convw, convb=convb, WxT=WxT)
        maps.append(m)
    return maps




def build_A2(co_np, p=None, fused_out=False):
    if p is None:
        p = P("A2")
    dbc_d = p.din("dbc", [96, T])
    u_d = p.din("u", [1024, T], BF16)
    wdt_d = p.din("WdtT", [64, 1024])
    bdt_d = p.din("bdt", [128, 8])
    al_d = p.din("AL", [128, 128])
    pd_d = p.din("PD4", [4 * 128, 128], BF16)
    pb_d = p.din("PBq", [4 * 96, 128], BF16)
    pc_d = p.din("PCq", [4 * 96, 128], BF16)
    sel_d = p.din("SEL", [128, 32], BF16)
    ys_d = p.dout("ys", [2048, TH] if fused_out else [1024, T], BF16)

    p.start()
    dbc = p.sb("dbc", [96, T], BF16)
    wdt = p.sb("wdt", [64, 1024], BF16)
    bdt = p.sb("bdt", [128, 8])
    alog = p.sb("alog", [128, 128])
    aneg = p.sb("aneg", [128, 128])
    pd4 = p.sb("pd4", [128, 4 * 128], BF16)
    pbq = p.sb("pbq", [96, 4 * 128], BF16)
    pcq = p.sb("pcq", [96, 4 * 128], BF16)
    sel = p.sb("sel", [128, 32], BF16)
    dtb = p.sb("dtb", [128, 8 * T], BF16)
    ut = p.sb("ut", [128, 2 * T], BF16)
    dtu = p.sb("dtu", [128, T], BF16)
    dub = p.sb("dub", [128, 512], BF16)
    beta = p.sb("beta", [128, T], BF16)
    hbuf = p.sb("hbuf", [128, T], BF16)
    hc = [p.sb(f"hc{q}", [128, T], BF16) for q in range(4)]
    ystg = p.sb("ystg", [32, 2 * T], BF16)
    spt = p.sb("spt", [128, 512])

    dma_in(p, dbc, dbc_d, cast=True)
    dma_in(p, wdt, wdt_d, cast=True)
    dma_in(p, bdt, bdt_d)
    dma_in(p, alog, al_d)
    for gm in range(4):
        dma_in(p, pd4[:, gm * 128:(gm + 1) * 128], pd_d[gm * 128:(gm + 1) * 128, :])
        dma_in(p, pbq[:, gm * 128:(gm + 1) * 128], pb_d[gm * 96:(gm + 1) * 96, :])
        dma_in(p, pcq[:, gm * 128:(gm + 1) * 128], pc_d[gm * 96:(gm + 1) * 96, :])
    dma_in(p, sel, sel_d)
    gin0, din0 = p.n["gin"], p.n["din"]

    def sa(sc):
        p.wait(sc, "din", din0)
        p.inc(sc.activation(aneg[:, :], alog[:, :], AF.Exp), "sc")
    p.sec("scalar", sa)
    sc0 = p.n["sc"]

    def va(v):
        v.wait_ge(p.sem["sc"], sc0)
        p.inc(v.tensor_scalar(aneg[:, :], aneg[:, :], -1.0, None, ALU.mult), "v")
    p.sec("vector", va)
    v_aneg = p.n["v"]

    pool = Pool(p, 2)
    jobs = []
    for mt in range(8):
        for tg in range(8):
            c0 = tg * 512
            mm_list = [(wdt[:, mt * 128:(mt + 1) * 128], dbc[0:64, c0:c0 + 512])]

            def ev(env, ps, pred, mt=mt, c0=c0):
                env.activation(spt[:, :], ps[:, :512], AF.Exp, bias=bdt[:, mt:mt + 1])
                ins = env.activation(dtb[:, mt * T + c0: mt * T + c0 + 512], spt[:, :], AF.Ln, bias=1.0)
                p.inc(ins, "sc")
            jobs.append((mm_list, 128, 512, ev))
    emit_mm(p, pool, jobs, ("scalar", "sc"), wait_before=[("gin", gin0), ("din", din0)])
    sc_dt = p.n["sc"]

    ps_dtb = p.ps("psdtb", [128, 512])
    ps_da = p.ps("psda", [128, 512])
    ps_dub = p.ps("psdub", [128, 512])
    ps_bb = p.ps("psbb", [128, 512])
    ps_cb2 = [p.ps("pscb", [128, 512]), p.ps("pscb2", [128, 512])]
    ps_y = [pool.tiles[0][0:32, 0:512], pool.tiles[1][0:32, 0:512]]

    guard = {"dtb": None, "da": None, "dub": None, "bb": None, "cb0": None,
             "cb1": None, "y0": pool.freed[0], "y1": pool.freed[1]}
    dtu_ready = 0

    NTG = 8
    for g in range(32):
        mt = g // 4
        gm = g % 4
        if g % 4 == 0:
            ub = (mt % 2)
            def lu(gq, mt=mt, ub=ub):
                if mt >= 2:
                    p.wait(gq, "v", None)
                p.inc(gq.dma_start(out=ut[:, ub * T:(ub + 1) * T], in_=u_d[mt * 128:(mt + 1) * 128, :]), "gin", 16)
            p.sec("gpsimd", lu)
            gin_u = p.n["gin"]

            def vdtu(v, mt=mt, ub=ub):
                v.wait_ge(p.sem["gin"], gin_u)
                v.wait_ge(p.sem["sc"], sc_dt)
                if mt >= 1:
                    p.wait(v, "mm")
                ins = v.tensor_tensor(dtu[:, :], dtb[:, mt * T:(mt + 1) * T], ut[:, ub * T:(ub + 1) * T], ALU.mult)
                p.inc(ins, "v")
            p.sec("vector", vdtu)
            dtu_ready = p.n["v"]

        for q in range(4):
            acol = aneg[:, (g * 4 + q):(g * 4 + q) + 1]
            for tg in range(NTG):
                c0 = tg * 512
                par = ((g * 4 + q) * NTG + tg) % 2
                ps_cb = ps_cb2[par]
                def tmm(t, g=g, q=q, tg=tg, c0=c0, mt=mt, gm=gm, dtu_ready=dtu_ready,
                        par=par, ps_cb=ps_cb):
                    if tg == 0 and q == 0 and g == 0:
                        p.wait(t, "v", v_aneg)
                        t.wait_ge(p.sem["sc"], sc_dt)
                    if tg == 0 and q == 0:
                        t.wait_ge(p.sem["v"], dtu_ready)
                    for nm in ("dtb", "dub", "bb", "cb%d" % par):
                        if guard[nm] is not None:
                            t.wait_ge(p.sem[guard[nm][0]], guard[nm][1])
                    t.matmul(ps_dtb[:, :], pd4[:, gm * 128:(gm + 1) * 128], dtb[:, mt * T + c0: mt * T + c0 + 512], start=True, stop=True)
                    t.matmul(ps_dub[:, :], pd4[:, gm * 128:(gm + 1) * 128], dtu[:, c0:c0 + 512], start=True, stop=True)
                    t.matmul(ps_bb[:, :], pbq[:, q * 128:(q + 1) * 128], dbc[:, c0:c0 + 512], start=True, stop=True)
                    ins = t.matmul(ps_cb[:, :], pcq[:, q * 128:(q + 1) * 128], dbc[:, c0:c0 + 512], start=True, stop=True)
                    p.inc(ins, "mm")
                p.sec("tensor", tmm)
                mm_bc = p.n["mm"]

                def sda(sc, mm_bc=mm_bc, acol=acol):
                    sc.wait_ge(p.sem["mm"], mm_bc)
                    if guard["da"] is not None:
                        sc.wait_ge(p.sem[guard["da"][0]], guard["da"][1])
                    ins = sc.activation(ps_da[:, :], ps_dtb[:, :], AF.Exp, scale=acol)
                    p.inc(ins, "sc")
                p.sec("scalar", sda)
                sc_da = p.n["sc"]
                guard["dtb"] = ("sc", sc_da)

                def vsc(v, g=g, q=q, tg=tg, c0=c0, mm_bc=mm_bc, sc_da=sc_da, ps_cb=ps_cb):
                    v.wait_ge(p.sem["mm"], mm_bc)
                    i1 = p.inc(v.tensor_copy(dub[:, :], ps_dub[:, :]), "v")
                    v.wait_ge(p.sem["v"], i1)
                    i2 = p.inc(v.tensor_tensor(beta[:, c0:c0 + 512], dub[:, :], ps_bb[:, :], ALU.mult), "v")
                    v.wait_ge(p.sem["v"], i2)
                    v.wait_ge(p.sem["sc"], sc_da)
                    init = 0.0 if tg == 0 else hbuf[:, c0 - 1: c0]
                    i3 = p.inc(v.tensor_tensor_scan(hbuf[:, c0:c0 + 512], ps_da[:, :], beta[:, c0:c0 + 512], init, ALU.mult, ALU.add), "v")
                    v.wait_ge(p.sem["v"], i3)
                    ins = v.tensor_tensor(hc[q][:, c0:c0 + 512], hbuf[:, c0:c0 + 512], ps_cb[:, :], ALU.mult)
                    p.inc(ins, "v")
                p.sec("vector", vsc)
                vv = p.n["v"]
                guard["dub"] = ("v", vv - 3)
                guard["bb"] = ("v", vv - 2)
                guard["da"] = ("v", vv - 1)
                guard["cb%d" % par] = ("v", vv)

        yb = g % 2
        for tg in range(NTG):
            c0 = tg * 512
            pyi = tg % 2
            def tsel(t, g=g, tg=tg, c0=c0, pyi=pyi):
                gk = "y%d" % pyi
                if guard[gk] is not None:
                    t.wait_ge(p.sem[guard[gk][0]], guard[gk][1])
                for q in range(4):
                    ins = t.matmul(ps_y[pyi][:, :], sel[:, :], hc[q][:, c0:c0 + 512], start=(q == 0), stop=(q == 3))
                p.inc(ins, "mm")
            p.sec("tensor", tsel)
            mm_y = p.n["mm"]

            def vy(v, g=g, tg=tg, c0=c0, pyi=pyi, yb=yb, mm_y=mm_y):
                v.wait_ge(p.sem["mm"], mm_y)
                if tg == 0 and g >= 2:
                    p.wait(v, "zd", (32 if fused_out else 16) * (g - 1))
                ins = v.tensor_copy(ystg[:, yb * T + c0: yb * T + c0 + 512], ps_y[pyi][:, :])
                p.inc(ins, "v")
            p.sec("vector", vy)
            guard["y%d" % pyi] = ("v", p.n["v"])
        v_y = p.n["v"]

        def dy(s, g=g, yb=yb, v_y=v_y):
            s.wait_ge(p.sem["v"], v_y)
            if fused_out:
                p.inc(s.dma_start(out=ys_d[g * 32:(g + 1) * 32, :], in_=ystg[:32, yb * T: yb * T + TH]), "zd", 16)
                p.inc(s.dma_start(out=ys_d[1024 + g * 32:1024 + (g + 1) * 32, :],
                                  in_=ystg[:32, yb * T + TH:(yb + 1) * T]), "zd", 16)
            else:
                p.inc(s.dma_start(out=ys_d[g * 32:(g + 1) * 32, :], in_=ystg[:32, yb * T:(yb + 1) * T]), "zd", 16)
        p.sec("sync", dy)

    def fin(s):
        p.wait(s, "zd")
    p.sec("sync", fin)
    p.close()
    return p.nc


def a2_consts():
    PD4 = np.zeros((4, 128, 128), dtype=BF)
    for gm in range(4):
        for pp in range(128):
            PD4[gm, gm * 32 + pp // 4, pp] = 1
    PB = np.zeros((4, 96, 128), dtype=BF)
    PC = np.zeros((4, 96, 128), dtype=BF)
    for q in range(4):
        for pp in range(128):
            PB[q, 64 + q * 4 + pp % 4, pp] = 1
            PC[q, 80 + q * 4 + pp % 4, pp] = 1
    SEL = np.zeros((128, 32), dtype=BF)
    for pp in range(128):
        SEL[pp, pp // 4] = 1
    return PD4.reshape(512, 128), PB.reshape(384, 128), PC.reshape(384, 128), SEL


def host_A2_inmaps(inputs, co_np, a1_results):
    PD4, PB, PC, SEL = a2_consts()
    WdtT = np.ascontiguousarray(inputs["L0_attn_W_dt"].T)
    b_dt = inputs["L0_attn_b_dt"]
    A_log = inputs["L0_attn_A_log"]
    maps = []
    for c in range(NCORE):
        b, j = c // 2, c % 2
        ch0 = j * 1024
        dbc_full = np.concatenate([np.asarray(a1_results[2 * b]["dbc"]),
                                   np.asarray(a1_results[2 * b + 1]["dbc"])], axis=1)
        u0 = np.asarray(a1_results[2 * b]["u"])
        u1 = np.asarray(a1_results[2 * b + 1]["u"])
        u_half = np.concatenate([u0[ch0:ch0 + 1024], u1[ch0:ch0 + 1024]], axis=1)
        bdt = b_dt[ch0:ch0 + 1024].reshape(8, 128).T.copy()
        AL = np.zeros((128, 128), dtype=np.float32)
        for g in range(32):
            for q in range(4):
                for pp in range(128):
                    AL[pp, g * 4 + q] = A_log[ch0 + g * 32 + pp // 4, q * 4 + pp % 4]
        m = dict(dbc=np.ascontiguousarray(dbc_full), u=np.ascontiguousarray(u_half),
                 WdtT=np.ascontiguousarray(WdtT[:, ch0:ch0 + 1024]), bdt=bdt, AL=AL,
                 PD4=PD4, PBq=PB, PCq=PC, SEL=SEL)
        maps.append(m)
    return maps


def kernel(**inputs):
    raise NotImplementedError("assembled later")


def build_A3(co_np, p=None):
    if p is None:
        p = P("A3")
    ys_d = p.din("ysl", [2048, TH], BF16)
    u_d = p.din("u", [2048, TH], BF16)
    z_d = p.din("z", [2048, TH], BF16)
    dsk_d = p.din("Dsk", [128, 16])
    wout_d = p.din("WoutT", [2048, 1024])
    xres_d = p.din("xres", [1024, TH])
    x1_d = p.dout("x1", [1024, TH])

    p.start()
    co = emit_consts(p, co_np)
    dsk = p.sb("dsk", [128, 16])
    dma_in(p, dsk, dsk_d)
    xres = p.sb("xres", [128, 8 * TH])
    load_fm(p, xres, xres_d, 1024, TH, cast=False)
    din0 = p.n["din"]

    sc1 = ExitStack()
    ybuf = p.sb("ybuf", [128, 16 * TH], BF16, sc1)
    yst = p.sb("yst", [128, 6 * TH], BF16, sc1)
    tmp = p.sb("tmp", [128, 2 * TH], BF16, sc1)
    szt = p.sb("szt", [128, 2 * TH], BF16, sc1)
    wob = p.sb("wob", [128, 16 * 1024], BF16, sc1)
    load_fm(p, wob, wout_d, 2048, 1024)
    gin_w = p.n["gin"]

    vready = 0
    for et in range(16):
        bb = et % 2
        def ld(g, et=et, bb=bb, vready=vready):
            if et >= 2:
                p.wait(g, "v", vready)
                p.wait(g, "sc")
            g.dma_start(out=yst[:, (bb * 3) * TH:(bb * 3 + 1) * TH], in_=ys_d[et * 128:(et + 1) * 128, :]).then_inc(p.sem["gin"], 16)
            g.dma_start(out=yst[:, (bb * 3 + 1) * TH:(bb * 3 + 2) * TH], in_=u_d[et * 128:(et + 1) * 128, :]).then_inc(p.sem["gin"], 16)
            g.dma_start(out=yst[:, (bb * 3 + 2) * TH:(bb * 3 + 3) * TH], in_=z_d[et * 128:(et + 1) * 128, :]).then_inc(p.sem["gin"], 16)
            p.n["gin"] += 48
        p.sec("gpsimd", ld)
        gin_t = p.n["gin"]

        def vt(v, et=et, bb=bb):
            v.wait_ge(p.sem["gin"], gin_t)
            p.wait(v, "din", din0)
            ins = v.scalar_tensor_tensor(tmp[:, bb * TH:(bb + 1) * TH], yst[:, (bb * 3 + 1) * TH:(bb * 3 + 2) * TH],
                                         dsk[:, et:et + 1], yst[:, (bb * 3) * TH:(bb * 3 + 1) * TH], ALU.mult, ALU.add)
            p.inc(ins, "v")
        p.sec("vector", vt)
        v_tmp = p.n["v"]

        def st(sc, et=et, bb=bb):
            sc.wait_ge(p.sem["gin"], gin_t)
            ins = sc.activation(szt[:, bb * TH:(bb + 1) * TH], yst[:, (bb * 3 + 2) * TH:(bb * 3 + 3) * TH], AF.Silu)
            p.inc(ins, "sc")
        p.sec("scalar", st)
        sc_z = p.n["sc"]

        def vm(v, et=et, bb=bb, sc_z=sc_z):
            v.wait_ge(p.sem["sc"], sc_z)
            ins = v.tensor_tensor(ybuf[:, et * TH:(et + 1) * TH], tmp[:, bb * TH:(bb + 1) * TH],
                                  szt[:, bb * TH:(bb + 1) * TH], ALU.mult)
            p.inc(ins, "v")
        p.sec("vector", vm)
        vready = p.n["v"]
    v_y = p.n["v"]

    pool = Pool(p, 4)
    jobs = []
    for mt in range(8):
        for ng in range(4):
            c0 = ng * 512
            mm_list = [(wob[:, k * 1024 + mt * 128: k * 1024 + (mt + 1) * 128],
                        ybuf[:, k * TH + c0: k * TH + c0 + 512]) for k in range(16)]

            def ev(env, ps, pred, mt=mt, c0=c0):
                ins = env.tensor_tensor(xres[:, mt * TH + c0: mt * TH + c0 + 512],
                                        ps[:, :512], xres[:, mt * TH + c0: mt * TH + c0 + 512], ALU.add)
                p.inc(ins, "v")
            jobs.append((mm_list, 128, 512, ev))
    emit_mm(p, pool, jobs, ("vector", "v"), wait_before=[("v", v_y), ("gin", gin_w), ("din", din0)])
    v_s = p.n["v"]
    sc1.close()

    emit_postln(p, xres, xres, TH, co, ready=[("v", v_s)])
    v_ln = p.n["v"]

    def dx(s):
        s.wait_ge(p.sem["v"], v_ln)
        for mt in range(8):
            p.inc(s.dma_start(out=x1_d[mt * 128:(mt + 1) * 128, :], in_=xres[:, mt * TH:(mt + 1) * TH]), "dout", 16)
        p.wait(s, "dout")
    p.sec("sync", dx)
    p.close()
    return p.nc


def host_A3_inmaps(inputs, co_np, a1, a2):
    WoutT = np.ascontiguousarray(inputs["L0_attn_W_out"].T)
    Dsk = inputs["L0_attn_D_skip"].reshape(16, 128).T.copy()
    x = inputs["L0_attn_x"]
    base = const_inmap(co_np)
    maps = []
    for c in range(NCORE):
        b, i = c // 2, c % 2
        t0 = i * TH
        ysl = np.concatenate([np.asarray(a2[2 * b]["ys"])[:, t0:t0 + TH],
                              np.asarray(a2[2 * b + 1]["ys"])[:, t0:t0 + TH]], axis=0)
        m = dict(base)
        m.update(ysl=np.ascontiguousarray(ysl), u=np.asarray(a1[c]["u"]), z=np.asarray(a1[c]["z"]),
                 Dsk=Dsk, WoutT=WoutT, xres=np.ascontiguousarray(x[b, t0:t0 + TH].T))
        maps.append(m)
    return maps


def build_A4(co_np, p=None, split_in=False):
    if p is None:
        p = P("A4")
    TX = TH + 1
    if split_in:
        x1o_d = p.din("x1own", [1024, TH])
        pc_d = p.din("pcol", [1024, 1])
    else:
        xh_d = p.din("xh", [1024, TX])
    mur_d = p.din("mur", [128, 8])
    muk_d = p.din("muk", [128, 8])
    wr_d = p.din("WrT", [1024, 1024])
    wk_d = p.din("WkT", [1024, 1024])
    wv_d = p.din("WvT", [1024, 1024])
    x2_d = p.dout("x2", [1024, TH])

    p.start()
    co = emit_consts(p, co_np)
    mur = p.sb("mur", [128, 8])
    muk = p.sb("muk", [128, 8])
    dma_in(p, mur, mur_d)
    dma_in(p, muk, muk_d)
    xh = p.sb("xh", [128, 8 * TX])
    if split_in:
        for kt in range(8):
            dma_in(p, xh[:, kt * TX + 1: kt * TX + 1 + TH], x1o_d[kt * 128:(kt + 1) * 128, :], cast=False)
            dma_in(p, xh[:, kt * TX: kt * TX + 1], pc_d[kt * 128:(kt + 1) * 128, 0:1], cast=False)
    else:
        load_fm(p, xh, xh_d, 1024, TX, cast=False)
    din0 = p.n["din"]

    sc1 = ExitStack()
    X = p.sb("X", [128, 8 * TH], BF16, sc1)
    rbuf = p.sb("rbuf", [128, 8 * TH], BF16, sc1)
    kabuf = p.sb("kabuf", [128, 8 * TH], BF16, sc1)
    wbuf = p.sb("wbuf", [128, 8 * 1024], BF16, sc1)
    dsc = p.sb("dsc", [128, TH], F32, sc1)
    ssc = p.sb("ssc", [128, 2 * 512], F32, sc1)
    pool = Pool(p, 4)

    def mk_mix(mu, which_v):
        for mt in range(8):
            def vm(v, mt=mt, mu=mu):
                if mt == 0:
                    p.wait(v, "din", din0)
                    p.wait(v, "mm")
                v.tensor_tensor(dsc[:, :], xh[:, mt * TX + 1: mt * TX + 1 + TH],
                                xh[:, mt * TX: mt * TX + TH], ALU.subtract)
                ins = v.scalar_tensor_tensor(X[:, mt * TH:(mt + 1) * TH], dsc[:, :],
                                             mu[:, mt:mt + 1], xh[:, mt * TX: mt * TX + TH], ALU.mult, ALU.add)
                p.inc(ins, "v")
            p.sec("vector", vm)

    def load_w(wd):
        def lw(g):
            p.wait(g, "mm")
            for kt in range(8):
                p.inc(g.dma_start(out=wbuf[:, kt * 1024:(kt + 1) * 1024], in_=wd[kt * 128:(kt + 1) * 128, :]), "gin", 16)
        p.sec("gpsimd", lw)
        return p.n["gin"]

    mk_mix(mur, None)
    v_xr = p.n["v"]
    g_wr = load_w(wr_d)
    jobs = []
    for mt in range(8):
        for ng in range(4):
            c0 = ng * 512
            mm_list = [(wbuf[:, k * 1024 + mt * 128: k * 1024 + (mt + 1) * 128],
                        X[:, k * TH + c0: k * TH + c0 + 512]) for k in range(8)]
            def ev(env, ps, pred, mt=mt, c0=c0):
                ins = env.activation(rbuf[:, mt * TH + c0: mt * TH + c0 + 512], ps[:, :512], AF.Sigmoid)
                p.inc(ins, "sc")
            jobs.append((mm_list, 128, 512, ev))
    emit_mm(p, pool, jobs, ("scalar", "sc"), wait_before=[("v", v_xr), ("gin", g_wr)])

    mk_mix(muk, None)
    v_xk = p.n["v"]
    g_wk = load_w(wk_d)
    jobs = []
    for mt in range(8):
        for ng in range(4):
            c0 = ng * 512
            mm_list = [(wbuf[:, k * 1024 + mt * 128: k * 1024 + (mt + 1) * 128],
                        X[:, k * TH + c0: k * TH + c0 + 512]) for k in range(8)]
            def ev(env, ps, pred, mt=mt, c0=c0):
                env.activation(ssc[:, :512], ps[:, :512], AF.Relu)
                ins = env.activation(kabuf[:, mt * TH + c0: mt * TH + c0 + 512], ssc[:, :512], AF.Square)
                p.inc(ins, "sc")
            jobs.append((mm_list, 128, 512, ev))
    emit_mm(p, pool, jobs, ("scalar", "sc"), wait_before=[("v", v_xk), ("gin", g_wk)])
    sc_ka = p.n["sc"]

    g_wv = load_w(wv_d)
    jobs = []
    for mt in range(8):
        for ng in range(4):
            c0 = ng * 512
            mm_list = [(wbuf[:, k * 1024 + mt * 128: k * 1024 + (mt + 1) * 128],
                        kabuf[:, k * TH + c0: k * TH + c0 + 512]) for k in range(8)]
            def ev(env, ps, pred, mt=mt, c0=c0):
                env.tensor_tensor(ssc[:, 512:1024], ps[:, :512], rbuf[:, mt * TH + c0: mt * TH + c0 + 512], ALU.mult)
                ins = env.tensor_tensor(xh[:, mt * TX + 1 + c0: mt * TX + 1 + c0 + 512],
                                        ssc[:, 512:1024], xh[:, mt * TX + 1 + c0: mt * TX + 1 + c0 + 512], ALU.add)
                p.inc(ins, "v")
            jobs.append((mm_list, 128, 512, ev))
    emit_mm(p, pool, jobs, ("vector", "v"), wait_before=[("sc", sc_ka), ("gin", g_wv)])
    v_s = p.n["v"]
    sc1.close()

    xout = p.sb("xout", [128, 8 * TH])
    emit_postln(p, xh, xout, TH, co, ready=[("v", v_s)], cstride=TX, col0=1)
    v_ln = p.n["v"]

    def dx(s):
        s.wait_ge(p.sem["v"], v_ln)
        for mt in range(8):
            p.inc(s.dma_start(out=x2_d[mt * 128:(mt + 1) * 128, :], in_=xout[:, mt * TH:(mt + 1) * TH]), "dout", 16)
        p.wait(s, "dout")
    p.sec("sync", dx)
    p.close()
    return p.nc


def host_A4_inmaps(inputs, co_np, x1_parts):
    WrT = np.ascontiguousarray(inputs["L0_ffn_W_r"].T)
    WkT = np.ascontiguousarray(inputs["L0_ffn_W_k"].T)
    WvT = np.ascontiguousarray(inputs["L0_ffn_W_v"].T)
    mur = inputs["L0_ffn_mu_r"].reshape(8, 128).T.copy()
    muk = inputs["L0_ffn_mu_k"].reshape(8, 128).T.copy()
    base = const_inmap(co_np)
    maps = []
    for c in range(NCORE):
        b, i = c // 2, c % 2
        xh = np.zeros((1024, TH + 1), dtype=np.float32)
        xh[:, 1:] = x1_parts[c]
        if i == 1:
            xh[:, 0] = x1_parts[2 * b][:, -1]
        m = dict(base)
        m.update(xh=xh, mur=mur, muk=muk, WrT=WrT, WkT=WkT, WvT=WvT)
        maps.append(m)
    return maps


def build_attn(co_np, biased=True, name="A5", npairs=4, nts=32, phases=9, dbg=False, p=None, fused_out=False):
    if p is None:
        p = P(name)
    HP = npairs
    W = 512
    xf_d = p.din("xf", [1024, T])
    wq_d = p.din("WQ", [1024, W])
    wk_d = p.din("WK", [1024, W])
    wv_d = p.din("WV", [1024, W])
    bq_d = p.din("bQ", [128, HP])
    bk_d = p.din("bK", [128, HP])
    bv_d = p.din("bV", [128, HP])
    msk_d = p.din("MSK", [4 * 128, 512], BF16)
    z_d = p.dout("z", [2 * W, TH] if fused_out else [W, T], BF16)
    if dbg:
        dbg_rows_d = p.dout("dbg_rows", [1, 2 * T], BF16)
        dbg_mhc_d = p.dout("dbg_mhc", [128, 128])
        dbg_ds_d = p.dout("dbg_ds", [128, 16])
        dbg_psb_d = p.dout("dbg_psb", [128, T], BF16)

    p.start()
    co = emit_consts(p, co_np)
    bq = p.sb("bq", [128, HP]); bk = p.sb("bk", [128, HP]); bv = p.sb("bv", [128, HP])
    dma_in(p, bq, bq_d); dma_in(p, bk, bk_d); dma_in(p, bv, bv_d)
    msk = p.sb("msk", [128, 4 * 512], BF16)
    for mI in range(4):
        dma_in(p, msk[:, mI * 512:(mI + 1) * 512], msk_d[mI * 128:(mI + 1) * 128, :])
    zout = p.sb("zout", [128, T], BF16)
    Psb = p.sb("Psb", [128, T], BF16)
    q2 = p.sb("q2", [128, T], BF16)
    rowsc = p.sb("rowsc", [1, 2 * T], BF16)
    mx1 = p.sb("mx1", [1, 4])
    mhc = p.sb("mhc", [128, 128])
    dsums = p.sb("dsums", [128, 16])
    PTsb = p.sb("PTsb", [128, 512], BF16)
    zsc = p.sb("zsc", [128, 128])
    rrow = p.sb("rrow", [1, 128])

    big = ExitStack()
    xf = p.sb("xf", [128, 8 * T], BF16, big)
    wqb = p.sb("wqb", [128, 8 * W], BF16, big)
    wkb = p.sb("wkb", [128, 8 * W], BF16, big)
    wvb = p.sb("wvb", [128, 8 * W], BF16, big)
    qp = p.sb("qp", [128, T], BF16, big)
    kp = p.sb("kp", [128, T], BF16, big)
    vp = p.sb("vp", [128, T], BF16, big)
    vT = p.sb("vT", [128, 2 * T], BF16, big)

    load_fm(p, xf, xf_d, 1024, T)
    load_fm(p, wqb, wq_d, 1024, W)
    load_fm(p, wkb, wk_d, 1024, W)
    load_fm(p, wvb, wv_d, 1024, W)
    gin0, din0 = p.n["gin"], p.n["din"]

    pool = Pool(p, 3)
    ps_pt = p.ps("pspt", [128, 512], BF16)
    ps_z = p.ps("psz", [128, 128])
    ps_sm = p.ps("pssm", [128, 512])

    NT = nts
    esem = "v"
    zdma_n = 0
    mm_psb_free = 0
    v_misc = 0
    v_ptevac = 0
    v_zevac = 0
    sc_track = {}

    for hp in range(HP):
        jobs = []
        for pi, (wb, bcol, dst) in enumerate(((wqb, bq, qp), (wkb, bk, kp), (wvb, bv, vp))):
            for ng in range(8):
                c0 = ng * 512
                mm_list = [(wb[:, k * W + hp * 128: k * W + (hp + 1) * 128],
                            xf[:, k * T + c0: k * T + c0 + 512]) for k in range(8)]
                if biased:
                    def ev(env, ps, pred, bcol=bcol, dst=dst, c0=c0, hp=hp):
                        ins = env.tensor_scalar(dst[:, c0:c0 + 512], ps[:, :512], bcol[:, hp:hp + 1], None, ALU.add)
                        p.inc(ins, "v")
                else:
                    def ev(env, ps, pred, dst=dst, c0=c0):
                        ins = env.tensor_copy(dst[:, c0:c0 + 512], ps[:, :512])
                        p.inc(ins, "v")
                jobs.append((mm_list, 128, 512, ev))
        wb4 = [("gin", gin0), ("din", din0)]
        if hp > 0:
            wb4 += [("zd", zdma_n), ("mm", mm_psb_free), ("v", v_zevac)]
        emit_mm(p, pool, jobs, ("vector", "v"), wait_before=wb4)
        sc_qkv = p.n["sc"]; v_qkv = p.n["v"]

        for hl in range(2):
            if phases < 2:
                continue
            h = 2 * hp + hl
            hb = hl * 64

            for grp in range(4):
                def tvt(t, grp=grp, hb=hb, hl=hl):
                    if grp == 0 and hl == 0:
                        p.wait(t, "v", v_qkv)
                    if v_ptevac > 0:
                        t.wait_ge(p.sem["v"], v_ptevac)
                    for j in range(8):
                        kt = grp * 8 + j
                        ins = t.transpose(ps_pt[:, j * 64: j * 64 + 64],
                                          vp[hb:hb + 64, kt * 128:(kt + 1) * 128], co["ident"][hb:hb + 64, hb:hb + 64])
                    p.inc(ins, "mm")
                p.sec("tensor", tvt)
                mmv = p.n["mm"]

                def vvt(v, grp=grp, hl=hl, mmv=mmv):
                    v.wait_ge(p.sem["mm"], mmv)
                    ins = v.tensor_copy(vT[:, hl * T + grp * 512:hl * T + (grp + 1) * 512], ps_pt[:, :])
                    p.inc(ins, "v")
                p.sec("vector", vvt)
                v_ptevac = p.n["v"]

            if phases < 3:
                continue
            def sq2(sc, hb=hb, vcur=0):
                sc.wait_ge(p.sem["v"], v_qkv)
                p.wait(sc, "v")
                ins = sc.activation(q2[hb:hb + 64, :], qp[hb:hb + 64, :], AF.Square)
                p.inc(ins, "sc")
            p.sec("scalar", sq2)
            sc_q2 = p.n["sc"]

            for ng in range(8):
                def tqn(t, ng=ng, vcur=None):
                    t.wait_ge(p.sem["sc"], sc_q2)
                    p.wait(t, "v")
                    ins = t.matmul(ps_sm[0:1, :512], co["ones_bf"][hb:hb + 64, :],
                                   q2[hb:hb + 64, ng * 512:(ng + 1) * 512], start=True, stop=True)
                    p.inc(ins, "mm")
                p.sec("tensor", tqn)
                mmq = p.n["mm"]

                def vqn(v, ng=ng, mmq=mmq):
                    v.wait_ge(p.sem["mm"], mmq)
                    ins = v.tensor_copy(rowsc[0:1, ng * 512:(ng + 1) * 512], ps_sm[0:1, :512])
                    p.inc(ins, "v")
                p.sec("vector", vqn)
                v_misc = p.n["v"]

            def sk2(sc, hb=hb):
                p.wait(sc, "v", v_misc)
                ins = sc.activation(q2[hb:hb + 64, :], kp[hb:hb + 64, :], AF.Square)
                p.inc(ins, "sc")
            p.sec("scalar", sk2)
            sc_k2 = p.n["sc"]

            for ng in range(8):
                def tkn(t, ng=ng):
                    t.wait_ge(p.sem["sc"], sc_k2)
                    p.wait(t, "v")
                    ins = t.matmul(ps_sm[0:1, :512], co["ones_bf"][hb:hb + 64, :],
                                   q2[hb:hb + 64, ng * 512:(ng + 1) * 512], start=True, stop=True)
                    p.inc(ins, "mm")
                p.sec("tensor", tkn)
                mmk = p.n["mm"]

                def vkn(v, ng=ng, mmk=mmk):
                    v.wait_ge(p.sem["mm"], mmk)
                    ins = v.tensor_copy(rowsc[0:1, T + ng * 512: T + (ng + 1) * 512], ps_sm[0:1, :512])
                    p.inc(ins, "v")
                p.sec("vector", vkn)
                v_misc = p.n["v"]

            def vmh(v):
                p.wait(v, "v")
                i1 = p.inc(v.tensor_reduce(mx1[0:1, 0:1], rowsc[0:1, T:2 * T], AX.X, ALU.max), "v")
                v.wait_ge(p.sem["v"], i1)
                ins = v.tensor_scalar(rowsc[0:1, 0:T], rowsc[0:1, 0:T],
                                      mx1[0:1, 0:1], None, ALU.mult)
                p.inc(ins, "v")
            p.sec("vector", vmh)
            v_mh = p.n["v"]

            def smh(sc):
                sc.wait_ge(p.sem["v"], v_mh)
                ins = sc.activation(rowsc[0:1, 0:T], rowsc[0:1, 0:T], AF.Sqrt)
                p.inc(ins, "sc")
            p.sec("scalar", smh)
            sc_mh = p.n["sc"]

            def tmh(t):
                t.wait_ge(p.sem["sc"], sc_mh)
                if v_ptevac:
                    t.wait_ge(p.sem["v"], v_ptevac)
                for ts in range(NT):
                    ins = t.transpose(ps_pt[:, 2 * ts:2 * ts + 1], rowsc[0:1, ts * 128:(ts + 1) * 128],
                                      co["ident"][0:1, 0:1])
                p.inc(ins, "mm")
            p.sec("tensor", tmh)
            mm_mh = p.n["mm"]

            def vmh2(v, mm_mh=mm_mh):
                v.wait_ge(p.sem["mm"], mm_mh)
                i1 = p.inc(v.tensor_copy(mhc[:, 0:64], ps_pt[:, 0:64]), "v")
                v.wait_ge(p.sem["v"], i1)
                ins = v.tensor_scalar(mhc[:, 64:128], mhc[:, 0:64], -0.125, None, ALU.mult)
                p.inc(ins, "v")
            p.sec("vector", vmh2)
            v_misc = p.n["v"]
            v_ptevac = p.n["v"]

            if phases < 4:
                continue
            for ts in range(NT):
                nkg = ts // 4 + 1
                mI = ts % 4
                splan = []

                scbase = p.n["sc"]
                def tS(t, ts=ts, nkg=nkg, hb=hb, scbase=scbase):
                    if ts == 0:
                        t.wait_ge(p.sem["v"], v_misc)
                    for kg in range(nkg):
                        j = pool.rr % pool.nb
                        pool.rr += 1
                        if pool.freed[j] is not None:
                            t.wait_ge(p.sem[pool.freed[j][0]], pool.freed[j][1])
                        pool.freed[j] = ("sc", scbase + kg + 1)
                        ins = t.matmul(pool.tiles[j][:, :512], qp[hb:hb + 64, ts * 128:(ts + 1) * 128],
                                       kp[hb:hb + 64, kg * 512:(kg + 1) * 512], start=True, stop=True)
                        cnt = p.inc(ins, "mm")
                        splan.append((kg, j, cnt))
                p.sec("tensor", tS)

                def sexp(sc, ts=ts, nkg=nkg):
                    sc.wait_ge(p.sem["mm"], mm_psb_free)
                    sc.wait_ge(p.sem["v"], v_zevac) if v_zevac else None
                    for kg, j, cnt in splan:
                        sc.wait_ge(p.sem["mm"], cnt)
                        kwargs = {}
                        if kg < nkg - 1:
                            kwargs["accum_out"] = dsums[:, kg:kg + 1]
                        ins = sc.activation(Psb[:, kg * 512:(kg + 1) * 512], pool.tiles[j][:, :512],
                                            AF.Exp, scale=0.125, bias=mhc[:, 64 + 2 * ts:65 + 2 * ts], **kwargs)
                        cnt2 = p.inc(ins, "sc")
                p.sec("scalar", sexp)
                sc_exp = p.n["sc"]
                if phases < 6:
                    mm_psb_free = p.n["mm"]
                    continue

                def vdiag(v, ts=ts, nkg=nkg, mI=mI, sc_exp=sc_exp):
                    v.wait_ge(p.sem["sc"], sc_exp)
                    dg0 = (nkg - 1) * 512
                    i1 = p.inc(v.tensor_tensor(Psb[:, dg0:dg0 + 512], Psb[:, dg0:dg0 + 512],
                                               msk[:, mI * 512:(mI + 1) * 512], ALU.mult), "v")
                    v.wait_ge(p.sem["v"], i1)
                    i2 = p.inc(v.tensor_reduce(dsums[:, nkg - 1:nkg], Psb[:, dg0:dg0 + 512], AX.X, ALU.add), "v")
                    v.wait_ge(p.sem["v"], i2)
                    i3 = p.inc(v.tensor_reduce(dsums[:, 15:16], dsums[:, 0:nkg], AX.X, ALU.add), "v")
                    v.wait_ge(p.sem["v"], i3)
                    ins = v.reciprocal(dsums[:, 14:15], dsums[:, 15:16])
                    p.inc(ins, "v")
                p.sec("vector", vdiag)
                v_diag = p.n["v"]
                if phases < 7:
                    mm_psb_free = p.n["mm"]
                    v_zevac = p.n["v"]
                    continue

                def trr(t, v_diag=v_diag):
                    t.wait_ge(p.sem["v"], v_diag)
                    ins = t.transpose(ps_sm[:1, 0:128], dsums[:, 14:15], co["identf"][:, :])
                    p.inc(ins, "mm")
                p.sec("tensor", trr)
                mm_rr = p.n["mm"]

                def vrr(v, mm_rr=mm_rr):
                    v.wait_ge(p.sem["mm"], mm_rr)
                    ins = v.tensor_copy(rrow[0:1, :], ps_sm[0:1, 0:128])
                    p.inc(ins, "v")
                p.sec("vector", vrr)
                v_rr = p.n["v"]

                def tdb(t, v_rr=v_rr, hb=hb):
                    t.wait_ge(p.sem["v"], v_rr)
                    ins = t.matmul(ps_sm[hb:hb + 64, 128:256], co["onesrow"][0:1, 0:64], rrow[0:1, :],
                                   start=True, stop=True)
                    p.inc(ins, "mm")
                p.sec("tensor", tdb)
                mm_db = p.n["mm"]
                if phases < 8:
                    mm_psb_free = p.n["mm"]
                    v_zevac = p.n["v"]
                    continue

                for kg in range(nkg):
                    def tpt(t, kg=kg, ts=ts, v_diag=v_diag):
                        if kg == nkg - 1:
                            t.wait_ge(p.sem["v"], v_diag)
                        if v_ptevac:
                            t.wait_ge(p.sem["v"], v_ptevac)
                        for j in range(4):
                            ins = t.transpose(ps_pt[:, j * 128:(j + 1) * 128],
                                              Psb[:, kg * 512 + j * 128: kg * 512 + (j + 1) * 128],
                                              co["ident"][:, :])
                        p.inc(ins, "mm")
                    p.sec("tensor", tpt)
                    mm_pt = p.n["mm"]

                    def vpt(v, mm_pt=mm_pt):
                        v.wait_ge(p.sem["mm"], mm_pt)
                        ins = v.tensor_copy(PTsb[:, :], ps_pt[:, :])
                        p.inc(ins, "v")
                    p.sec("vector", vpt)
                    v_ptevac = p.n["v"]

                    def tav(t, kg=kg, ts=ts, hl=hl, v_ptevac=v_ptevac):
                        t.wait_ge(p.sem["v"], v_ptevac)
                        if kg == 0 and v_zevac:
                            t.wait_ge(p.sem["v"], v_zevac)
                        hb2 = hl * 64
                        for j in range(4):
                            kt = kg * 4 + j
                            ins = t.matmul(ps_z[hb2:hb2 + 64, :], vT[:, hl * T + kt * 64: hl * T + (kt + 1) * 64],
                                           PTsb[:, j * 128:(j + 1) * 128],
                                           start=(kg == 0 and j == 0), stop=(kg == nkg - 1 and j == 3))
                        if kg == nkg - 1:
                            p.inc(ins, "mm")
                    p.sec("tensor", tav)
                mm_av = p.n["mm"]
                mm_psb_free = mm_av
                if phases < 9:
                    v_zevac = p.n["v"]
                    continue

                def vz(v, ts=ts, hp=hp, hl=hl, mm_av=mm_av):
                    v.wait_ge(p.sem["mm"], mm_av)
                    hb2 = hl * 64
                    i1 = p.inc(v.tensor_copy(zsc[hb2:hb2 + 64, :], ps_z[hb2:hb2 + 64, :]), "v")
                    v.wait_ge(p.sem["v"], i1)
                    if ts == 0 and hl == 0 and hp > 0:
                        p.wait(v, "zd", (32 if fused_out else 16) * hp)
                    ins = v.tensor_tensor(zout[hl * 64:hl * 64 + 64, ts * 128:(ts + 1) * 128],
                                          zsc[hb2:hb2 + 64, :], ps_sm[hb2:hb2 + 64, 128:256], ALU.mult)
                    p.inc(ins, "v")
                p.sec("vector", vz)
                v_zevac = p.n["v"]

        if dbg and hp == 0:
            def ddbg(s, v_zevac=v_zevac):
                s.wait_ge(p.sem["v"], v_zevac)
                p.inc(s.dma_start(out=dbg_rows_d, in_=rowsc[0:1, :]), "dout", 16)
                p.inc(s.dma_start(out=dbg_mhc_d, in_=mhc[:, :]), "dout", 16)
                p.inc(s.dma_start(out=dbg_ds_d, in_=dsums[:, :]), "dout", 16)
                p.inc(s.dma_start(out=dbg_psb_d, in_=Psb[:, :]), "dout", 16)
            p.sec("sync", ddbg)

        def dz(s, hp=hp, v_zevac=v_zevac):
            s.wait_ge(p.sem["v"], v_zevac)
            if fused_out:
                p.inc(s.dma_start(out=z_d[hp * 128:(hp + 1) * 128, :], in_=zout[:, 0:TH]), "zd", 16)
                p.inc(s.dma_start(out=z_d[W + hp * 128: W + (hp + 1) * 128, :], in_=zout[:, TH:T]), "zd", 16)
            else:
                p.inc(s.dma_start(out=z_d[hp * 128:(hp + 1) * 128, :], in_=zout[:, :]), "zd", 16)
        p.sec("sync", dz)
        zdma_n = p.n["zd"]

    def fin(s):
        p.wait(s, "zd")
    p.sec("sync", fin)
    big.close()
    p.close()
    return p.nc


def moe_e8():
    E8 = np.zeros((8, 8 * 128), dtype=BF)
    for e in range(8):
        E8[e, e * 128:(e + 1) * 128] = 1
    return E8


def attn_masks():
    MSK = np.zeros((4, 128, 512), dtype=BF)
    for m in range(4):
        for pp in range(128):
            MSK[m, pp, :m * 128 + pp + 1] = 1
    return MSK.reshape(4 * 128, 512)


def host_A5_inmaps(inputs, co_np, x2_parts):
    MSK = attn_masks()
    WQ = inputs["L1_attn_W_Q"]; WK = inputs["L1_attn_W_K"]; WV = inputs["L1_attn_W_V"]
    bQ = inputs["L1_attn_b_Q"]; bK = inputs["L1_attn_b_K"]; bV = inputs["L1_attn_b_V"]
    base = const_inmap(co_np)
    maps = []
    for c in range(NCORE):
        b, j = c // 2, c % 2
        h0 = j * 8
        xf = np.concatenate([x2_parts[2 * b], x2_parts[2 * b + 1]], axis=1)
        def wslice(Wt):
            return np.concatenate([Wt[h] for h in range(h0, h0 + 8)], axis=1)
        def bslice(bt):
            return bt[h0:h0 + 8].reshape(4, 128).T.copy()
        m = dict(base)
        m.update(xf=np.ascontiguousarray(xf), WQ=np.ascontiguousarray(wslice(WQ)),
                 WK=np.ascontiguousarray(wslice(WK)), WV=np.ascontiguousarray(wslice(WV)),
                 bQ=bslice(bQ), bK=bslice(bK), bV=bslice(bV), MSK=MSK)
        maps.append(m)
    return maps


def build_A6(co_np, nexp=8, p=None):
    if p is None:
        p = P("A6")
    z_d = p.din("zsl", [1024, TH], BF16)
    wo_d = p.din("WO", [1024, 1024])
    bo_d = p.din("bO", [128, 8])
    x2_d = p.din("x2", [1024, TH])
    rmsw_d = p.din("rmsw", [128, 8])
    gw_d = p.din("gwT", [1024, 8])
    w1_d = p.din("W1all", [8 * 1024, 4096])
    b1_d = p.din("b1c", [128, 8 * 32])
    w2_d = p.din("W2all", [8 * 4096, 1024])
    b2_d = p.din("b2c", [128, 8 * 8])
    e8_d = p.din("E8", [8, 8 * 128], BF16)
    x4_d = p.dout("x4", [1024, TH])
    x3_d = p.dint(p.pre + "x3scr", [1024, TH], F32)
    acc_d = p.dint(p.pre + "accscr", [1024, TH], F32)

    p.start()
    co = emit_consts(p, co_np)
    bo = p.sb("bo", [128, 8]); rmsw = p.sb("rmsw", [128, 8])
    b1c = p.sb("b1c", [128, 8 * 32]); b2c = p.sb("b2c", [128, 8 * 8])
    dma_in(p, bo, bo_d); dma_in(p, rmsw, rmsw_d); dma_in(p, b1c, b1_d); dma_in(p, b2c, b2_d)
    xn = p.sb("xn", [128, 8 * TH], BF16)
    lgtile = p.sb("lgtile", [128, 8])
    mx8 = p.sb("mx8", [128, 8])
    et8 = p.sb("et8", [128, 8])
    idx8 = p.sb("idx8", [128, 8], U32)
    mT8 = p.sb("mT8", [128, 8])
    scol = p.sb("scol", [128, 2])
    maskR = p.sb("maskR", [8, TH], BF16)
    e8c = p.sb("e8c", [8, 8 * 128], BF16)
    dma_in(p, e8c, e8_d)
    scoreR = p.sb("scoreR", [1, TH], BF16)
    pool = Pool(p, 4)

    sc1 = ExitStack()
    x2b = p.sb("x2b", [128, 8 * TH], F32, sc1)
    zb = p.sb("zb", [128, 8 * TH], BF16, sc1)
    wob = p.sb("wob", [128, 8 * 1024], BF16, sc1)
    gwb = p.sb("gwb", [128, 8 * 8], F32, sc1)
    gw2 = p.sb("gw2", [128, 8 * 8], F32, sc1)
    rstc = p.sb("rstc", [128, 1], F32, sc1)
    gf = ExitStack()
    lgf = p.sb("lgf", [8, TH], scope=gf)
    rmsrow = p.sb("rmsrow", [1, TH], scope=gf)
    load_fm(p, x2b, x2_d, 1024, TH, cast=False)
    load_fm(p, zb, z_d, 1024, TH)
    load_fm(p, wob, wo_d, 1024, 1024)
    load_fm(p, gwb, gw_d, 1024, 8, cast=False)
    gin0, din0 = p.n["gin"], p.n["din"]

    jobs = []
    for mt in range(8):
        for ng in range(4):
            c0 = ng * 512
            mm_list = [(wob[:, k * 1024 + mt * 128: k * 1024 + (mt + 1) * 128],
                        zb[:, k * TH + c0: k * TH + c0 + 512]) for k in range(8)]
            def ev(env, ps, pred, mt=mt, c0=c0):
                i1 = env.tensor_scalar(ps[:, :512], ps[:, :512], bo[:, mt:mt + 1], None, ALU.add)
                ins = env.tensor_tensor(x2b[:, mt * TH + c0: mt * TH + c0 + 512], ps[:, :512],
                                        x2b[:, mt * TH + c0: mt * TH + c0 + 512], ALU.add)
                p.inc(ins, "v")
            jobs.append((mm_list, 128, 512, ev))
    emit_mm(p, pool, jobs, ("vector", "v"), wait_before=[("gin", gin0), ("din", din0)])
    v_s = p.n["v"]
    emit_postln(p, x2b, x2b, TH, co, ready=[("v", v_s)])
    v_x3 = p.n["v"]

    lnsc = ExitStack()
    sqf = p.sb("rsq", [128, 8 * 512], BF16, lnsc)
    r_t2 = p.sb("rt2", [1, 512], F32, lnsc)
    ps_a = p.ps("rpa", [128, 512], F32, lnsc)
    ps_mu = p.ps("rpmu", [128, 512], F32, lnsc)
    mm_prev2 = 0
    for ng in range(4):
        c0 = ng * 512
        def rsq(sc, c0=c0, ng=ng, mm_prev2=mm_prev2):
            if ng == 0:
                sc.wait_ge(p.sem["v"], v_x3)
            else:
                sc.wait_ge(p.sem["mm"], mm_prev2)
            for mt in range(8):
                p.inc(sc.activation(sqf[:, mt * 512:(mt + 1) * 512], x2b[:, mt * TH + c0: mt * TH + c0 + 512], AF.Square), "sc")
        p.sec("scalar", rsq)
        sc_sq = p.n["sc"]

        def rst(t, c0=c0, ng=ng):
            t.wait_ge(p.sem["sc"], sc_sq)
            if ng > 0:
                p.wait(t, "v")
            for mt in range(8):
                ins = t.matmul(ps_a[0:1, :512], co["ones_bf"][:, :], sqf[:, mt * 512:(mt + 1) * 512],
                               start=(mt == 0), stop=(mt == 7))
            p.inc(ins, "mm")
        p.sec("tensor", rst)
        mm_st = p.n["mm"]

        def rvs(v, c0=c0, mm_st=mm_st):
            v.wait_ge(p.sem["mm"], mm_st)
            ins = v.tensor_scalar(r_t2[0:1, :512], ps_a[0:1, :512], 1.0 / 1024, 1e-6, ALU.mult, ALU.add)
            p.inc(ins, "v")
        p.sec("vector", rvs)
        v_t2 = p.n["v"]

        def rss(sc, c0=c0, v_t2=v_t2):
            sc.wait_ge(p.sem["v"], v_t2)
            i1 = p.inc(sc.activation(r_t2[0:1, :512], r_t2[0:1, :512], AF.Ln), "sc")
            sc.wait_ge(p.sem["sc"], i1)
            ins = sc.activation(rmsrow[0:1, c0:c0 + 512], r_t2[0:1, :512], AF.Exp, scale=-0.5)
            p.inc(ins, "sc")
        p.sec("scalar", rss)
        sc_rms = p.n["sc"]

        def rbc(t, c0=c0, sc_rms=sc_rms):
            t.wait_ge(p.sem["sc"], sc_rms)
            ins = t.matmul(ps_mu[:, :512], co["onesrow"][:, :], rmsrow[0:1, c0:c0 + 512], start=True, stop=True)
            p.inc(ins, "mm")
        p.sec("tensor", rbc)
        mm_prev2 = p.n["mm"]
        mm_bc = mm_prev2

        def rxn(v, c0=c0, mm_bc=mm_bc):
            v.wait_ge(p.sem["mm"], mm_bc)
            for mt in range(8):
                ins = v.scalar_tensor_tensor(xn[:, mt * TH + c0: mt * TH + c0 + 512],
                                             x2b[:, mt * TH + c0: mt * TH + c0 + 512],
                                             rmsw[:, mt:mt + 1], ps_mu[:, :512], ALU.mult, ALU.mult)
                p.inc(ins, "v")
        p.sec("vector", rxn)
    v_xn = p.n["v"]
    lnsc.close()

    def dx3(s):
        s.wait_ge(p.sem["v"], v_xn)
        for mt in range(8):
            p.inc(s.dma_start(out=x3_d[mt * 128:(mt + 1) * 128, :], in_=x2b[:, mt * TH:(mt + 1) * TH]), "zd", 16)
    p.sec("sync", dx3)
    zd_x3 = p.n["zd"]

    gsc = ExitStack()
    ps_g = p.ps("psg", [128, 512], F32, gsc)
    def vgw2(v):
        p.wait(v, "din", din0)
        for k in range(8):
            p.inc(v.tensor_scalar(gw2[:, k * 8:(k + 1) * 8], gwb[:, k * 8:(k + 1) * 8],
                                  rmsw[:, k:k + 1], None, ALU.mult), "v")
    p.sec("vector", vgw2)
    for ng in range(4):
        def tgate(t, ng=ng):
            p.wait(t, "v")
            for k in range(8):
                ins = t.matmul(ps_g[0:8, 0:512], gw2[:, k * 8:(k + 1) * 8],
                               x2b[:, k * TH + ng * 512: k * TH + ng * 512 + 512],
                               start=(k == 0), stop=(k == 7))
            p.inc(ins, "mm")
        p.sec("tensor", tgate)
        mm_g = p.n["mm"]

        def vge(v, ng=ng, mm_g=mm_g):
            v.wait_ge(p.sem["mm"], mm_g)
            ins = v.tensor_copy(lgf[0:8, ng * 512:(ng + 1) * 512], ps_g[0:8, 0:512])
            p.inc(ins, "v")
        p.sec("vector", vge)
    v_lg = p.n["v"]

    for tt in range(16):
        def tlt(t, tt=tt):
            t.wait_ge(p.sem["v"], p.n["v"])
            t.transpose(ps_g[:, 0:8], lgf[0:8, tt * 128:(tt + 1) * 128], co["identf"][0:8, 0:8])
            ins = t.transpose(ps_g[:, 8:9], rmsrow[0:1, tt * 128:(tt + 1) * 128], co["identf"][0:1, 0:1])
            p.inc(ins, "mm")
        p.sec("tensor", tlt)
        mm_lt = p.n["mm"]

        def vgt(v, tt=tt, mm_lt=mm_lt):
            v.wait_ge(p.sem["mm"], mm_lt)
            ia = p.inc(v.tensor_copy(rstc[:, 0:1], ps_g[:, 8:9]), "v")
            v.wait_ge(p.sem["v"], ia)
            i0 = p.inc(v.tensor_scalar(lgtile[:, :], ps_g[:, 0:8], rstc[:, 0:1], None, ALU.mult), "v")
            v.wait_ge(p.sem["v"], i0)
            i1 = p.inc(v.max(mx8[:, :], lgtile[:, :]), "v")
            v.wait_ge(p.sem["v"], i1)
            i2 = p.inc(v.max_index(idx8[:, :], mx8[:, :], lgtile[:, :]), "v")
            v.wait_ge(p.sem["v"], i2)
            ins = v.tensor_scalar(scol[:, 0:1], mx8[:, 0:1], -1.0, None, ALU.mult)
            p.inc(ins, "v")
        p.sec("vector", vgt)
        v_gt = p.n["v"]

        def sgt(sc, tt=tt, v_gt=v_gt):
            sc.wait_ge(p.sem["v"], v_gt)
            ins = sc.activation(et8[:, :], lgtile[:, :], AF.Exp, bias=scol[:, 0:1])
            p.inc(ins, "sc")
        p.sec("scalar", sgt)
        sc_e8 = p.n["sc"]

        def vms(v, tt=tt, sc_e8=sc_e8):
            v.wait_ge(p.sem["sc"], sc_e8)
            i1 = p.inc(v.tensor_reduce(scol[:, 1:2], et8[:, :], AX.X, ALU.add), "v")
            v.wait_ge(p.sem["v"], i1)
            i2 = p.inc(v.reciprocal(scol[:, 0:1], scol[:, 1:2]), "v")
            v.wait_ge(p.sem["v"], i2)
            for e in range(8):
                ins = v.tensor_scalar(mT8[:, e:e + 1], idx8[:, 0:1], float(e), None, ALU.is_equal)
            p.inc(ins, "v")
        p.sec("vector", vms)
        v_ms = p.n["v"]

        def tmt(t, tt=tt, v_ms=v_ms):
            t.wait_ge(p.sem["v"], v_ms)
            t.transpose(ps_g[0:8, 128:256], mT8[:, :], co["identf"][:, :])
            ins = t.transpose(ps_g[0:1, 256:384], scol[:, 0:1], co["identf"][:, :])
            p.inc(ins, "mm")
        p.sec("tensor", tmt)
        mm_mt = p.n["mm"]

        def vmr(v, tt=tt, mm_mt=mm_mt):
            v.wait_ge(p.sem["mm"], mm_mt)
            v.tensor_copy(maskR[0:8, tt * 128:(tt + 1) * 128], ps_g[0:8, 128:256])
            ins = v.tensor_copy(scoreR[0:1, tt * 128:(tt + 1) * 128], ps_g[0:1, 256:384])
            p.inc(ins, "v")
        p.sec("vector", vmr)
    v_gate = p.n["v"]
    gsc.close()
    gf.close()
    sc1.close()

    ex = ExitStack()
    w1b = p.sb("w1b", [128, 8 * 4096], BF16, ex)
    hb = p.sb("hb", [128, 32 * 1024], BF16, ex)
    w2b = p.sb("w2b", [128, 2 * 32 * 128], BF16, ex)
    mkb = p.sb("mkb", [128, TH], BF16, ex)
    tmpe = p.sb("tmpe", [128, 2 * 1024], F32, ex)
    ps_mk = p.ps("psmk", [128, 512], F32, ex)
    zd_region = {}
    w1tally = {}
    w1free_at = {}
    w2free = {0: 0, 1: 0}
    w2tally = {}
    w2ctr = [0]

    def emit_lw1(e, first):
        def lw1(g, e=e, first=first):
            if first:
                p.wait(g, "mm")
                p.wait(g, "v", v_xn)
            else:
                g.wait_ge(p.sem["mm"], w1free_at[e])
            for kt in range(8):
                p.inc(g.dma_start(out=w1b[:, kt * 4096:(kt + 1) * 4096],
                                  in_=w1_d[e * 1024 + kt * 128: e * 1024 + (kt + 1) * 128, :]), "gin", 16)
        p.sec("gpsimd", lw1)
        w1tally[e] = p.n["gin"]

    def emit_lw2b(e, half, dt_):
        sl = w2ctr[0] % 2
        w2ctr[0] += 1
        def lw2b(g, e=e, dt_=dt_, sl=sl, guard=w2free[sl]):
            if guard:
                g.wait_ge(p.sem["mm"], guard)
            p.inc(g.dma_start(out=w2b[:, sl * 4096:(sl + 1) * 4096]
                              .rearrange("p (a m) -> p a m", a=32),
                              in_=w2_d[e * 4096:(e + 1) * 4096, dt_ * 128:(dt_ + 1) * 128]
                              .rearrange("(a p) m -> p a m", p=128)), "gin", 16)
        p.sec("gpsimd", lw2b)
        w2tally[(e, half, dt_)] = p.n["gin"]
        return sl

    emit_lw1(0, True)
    for e in range(nexp):
        for ng in range(4):
            def tmk(t, e=e, ng=ng):
                if e == 0 and ng == 0:
                    t.wait_ge(p.sem["v"], v_gate)
                p.wait(t, "v")
                ins = t.matmul(ps_mk[:, :512], e8c[0:8, e * 128:(e + 1) * 128],
                               maskR[0:8, ng * 512:(ng + 1) * 512], start=True, stop=True)
                p.inc(ins, "mm")
            p.sec("tensor", tmk)
            mm_mk = p.n["mm"]

            def vmk(v, ng=ng, mm_mk=mm_mk):
                v.wait_ge(p.sem["mm"], mm_mk)
                ins = v.tensor_copy(mkb[:, ng * 512:(ng + 1) * 512], ps_mk[:, :512])
                p.inc(ins, "v")
            p.sec("vector", vmk)
        v_mk = p.n["v"]

        for half in range(2):
            tc0 = half * 1024
            jobs = []
            for ft in range(32):
                for ng in range(2):
                    c0 = tc0 + ng * 512
                    mm_list = [(w1b[:, k * 4096 + ft * 128: k * 4096 + (ft + 1) * 128],
                                xn[:, k * TH + c0: k * TH + c0 + 512]) for k in range(8)]
                    def ev(env, ps, pred, e=e, ft=ft, ng=ng):
                        ins = env.activation(hb[:, ft * 1024 + ng * 512: ft * 1024 + (ng + 1) * 512],
                                             ps[:, :512], AF.Silu, bias=b1c[:, e * 32 + ft: e * 32 + ft + 1])
                        p.inc(ins, "sc")
                    jobs.append((mm_list, 128, 512, ev))
            emit_mm(p, pool, jobs, ("scalar", "sc"), wait_before=[("gin", w1tally[e]), ("v", v_mk)])
            sc_h = p.n["sc"]
            if half == 1:
                w1free_at[e + 1] = p.n["mm"]

            slots = {0: emit_lw2b(e, half, 0)}
            for dt_ in range(8):
                sl = slots[dt_]
                jobs = []
                for ng in range(2):
                    c0l = ng * 512
                    mm_list = [(w2b[:, sl * 4096 + kk * 128: sl * 4096 + (kk + 1) * 128],
                                hb[:, kk * 1024 + c0l: kk * 1024 + c0l + 512]) for kk in range(32)]
                    def ev(env, ps, pred, e=e, dt_=dt_, ng=ng, half=half, c0l=c0l, sl=sl):
                        ins = env.scalar_tensor_tensor(tmpe[:, sl * 1024 + ng * 512: sl * 1024 + (ng + 1) * 512],
                                                       ps[:, :512],
                                                       b2c[:, e * 8 + dt_: e * 8 + dt_ + 1],
                                                       mkb[:, half * 1024 + c0l: half * 1024 + c0l + 512],
                                                       ALU.add, ALU.mult)
                        p.inc(ins, "v")
                    jobs.append((mm_list, 128, 512, ev))
                prev_dma = zd_region.get(("tmpe", sl), 0)
                emit_mm(p, pool, jobs, ("vector", "v"),
                        wait_before=[("gin", w2tally[(e, half, dt_)]), ("sc", sc_h)]
                        + ([("zg", prev_dma)] if prev_dma else []))
                w2free[sl] = p.n["mm"]
                v_tmp = p.n["v"]

                if dt_ + 1 < 8:
                    slots[dt_ + 1] = emit_lw2b(e, half, dt_ + 1)
                if half == 1 and dt_ == 1 and e + 1 < nexp:
                    emit_lw1(e + 1, False)

                def dacc(g, e=e, dt_=dt_, half=half, sl=sl, v_tmp=v_tmp):
                    g.wait_ge(p.sem["v"], v_tmp)
                    rkey = (dt_, half)
                    if rkey in zd_region:
                        g.wait_ge(p.sem["zg"], zd_region[rkey])
                    ins = g.dma_start(out=acc_d[dt_ * 128:(dt_ + 1) * 128, half * 1024:(half + 1) * 1024],
                                      in_=tmpe[:, sl * 1024:(sl + 1) * 1024],
                                      **({} if e == 0 else {"accum_op": ALU.add}))
                    p.inc(ins, "zg", 16)
                    zd_region[rkey] = p.n["zg"]
                    zd_region[("tmpe", sl)] = p.n["zg"]
                p.sec("gpsimd", dacc)
    zd_all = p.n["zg"]
    ex.close()

    fin = ExitStack()
    x3b = p.sb("x3b", [128, 8 * TH], F32, fin)
    accs = p.sb("accs", [128, TH], F32, fin)
    scb = p.sb("scb", [128, TH], BF16, fin)
    ps_sc = pool.tiles[0]
    def lx3(s):
        p.wait(s, "zg", zd_all)
        for mt in range(8):
            p.inc(s.dma_start(out=x3b[:, mt * TH:(mt + 1) * TH], in_=x3_d[mt * 128:(mt + 1) * 128, :]), "din", 16)
    p.sec("sync", lx3)
    din_x3 = p.n["din"]

    for ng in range(4):
        def tsc(t, ng=ng):
            p.wait(t, "v")
            ins = t.matmul(ps_sc[:, :512], co["onesrow_bf"][:, :], scoreR[0:1, ng * 512:(ng + 1) * 512],
                           start=True, stop=True)
            p.inc(ins, "mm")
        p.sec("tensor", tsc)
        mm_sc = p.n["mm"]

        def vsc2(v, ng=ng, mm_sc=mm_sc):
            v.wait_ge(p.sem["mm"], mm_sc)
            ins = v.tensor_copy(scb[:, ng * 512:(ng + 1) * 512], ps_sc[:, :512])
            p.inc(ins, "v")
        p.sec("vector", vsc2)
    v_scb = p.n["v"]

    for mt in range(8):
        def lacc(g, mt=mt):
            if mt >= 1:
                p.wait(g, "v")
            else:
                p.wait(g, "zg", zd_all)
                p.wait(g, "v")
            p.inc(g.dma_start(out=accs[:, :], in_=acc_d[mt * 128:(mt + 1) * 128, :]), "gin", 16)
        p.sec("gpsimd", lacc)
        gin_acc = p.n["gin"]

        def vfin(v, mt=mt, gin_acc=gin_acc):
            v.wait_ge(p.sem["gin"], gin_acc)
            v.wait_ge(p.sem["v"], v_scb)
            v.wait_ge(p.sem["din"], din_x3)
            i1 = p.inc(v.tensor_tensor(accs[:, :], accs[:, :], scb[:, :], ALU.mult), "v")
            v.wait_ge(p.sem["v"], i1)
            ins = v.tensor_tensor(x3b[:, mt * TH:(mt + 1) * TH], x3b[:, mt * TH:(mt + 1) * TH], accs[:, :], ALU.add)
            p.inc(ins, "v")
        p.sec("vector", vfin)
    v_fin = p.n["v"]

    emit_postln(p, x3b, x3b, TH, co, ready=[("v", v_fin)])
    v_ln2 = p.n["v"]

    def dx4(s):
        s.wait_ge(p.sem["v"], v_ln2)
        for mt in range(8):
            p.inc(s.dma_start(out=x4_d[mt * 128:(mt + 1) * 128, :], in_=x3b[:, mt * TH:(mt + 1) * TH]), "dout", 16)
        p.wait(s, "dout")
    p.sec("sync", dx4)
    fin.close()
    p.close()
    return p.nc


def host_A6_inmaps(inputs, co_np, z_parts, x2_parts):
    WO = np.ascontiguousarray(inputs["L1_attn_W_O"].reshape(1024, 1024))
    bO = inputs["L1_attn_b_O"].reshape(8, 128).T.copy()
    rmsw = inputs["L1_ffn_rms_w"].reshape(8, 128).T.copy()
    gwT = np.ascontiguousarray(inputs["L1_ffn_gate_w"].T)
    W1 = np.ascontiguousarray(inputs["L1_ffn_W1"].reshape(8 * 1024, 4096))
    W2 = np.ascontiguousarray(inputs["L1_ffn_W2"].reshape(8 * 4096, 1024))
    b1c = np.ascontiguousarray(inputs["L1_ffn_b1"].reshape(8, 32, 128).transpose(2, 0, 1).reshape(128, 8 * 32))
    b2c = np.ascontiguousarray(inputs["L1_ffn_b2"].reshape(8, 8, 128).transpose(2, 0, 1).reshape(128, 8 * 8))
    base = const_inmap(co_np)
    maps = []
    for c in range(NCORE):
        b, i = c // 2, c % 2
        t0 = i * TH
        zsl = np.concatenate([np.asarray(z_parts[2 * b])[:, t0:t0 + TH],
                              np.asarray(z_parts[2 * b + 1])[:, t0:t0 + TH]], axis=0)
        m = dict(base)
        m.update(zsl=np.ascontiguousarray(zsl), WO=WO, bO=bO, x2=x2_parts[c], rmsw=rmsw,
                 gwT=gwT, W1all=W1, b1c=b1c, W2all=W2, b2c=b2c, E8=moe_e8())
        maps.append(m)
    return maps


def build_A7(co_np, p=None):
    if p is None:
        p = P("A7")
    x_d = p.din("x", [1024, TH])
    wk_d = p.din("Wk", [1024, 512]); wv_d = p.din("Wv", [1024, 512])
    wr_d = p.din("Wr", [1024, 512]); wg_d = p.din("Wg", [1024, 512])
    wsel_d = p.din("Wsel", [1024, 160])
    dtw_d = p.din("dtW", [32, 512])
    dtb_d = p.din("dtbc", [128, 4])
    kk_d = p.dout("k", [512, TH], BF16)
    vv_d = p.dout("v", [512, TH], BF16)
    rr_d = p.dout("r", [512, TH], BF16)
    gg_d = p.dout("g", [512, TH], BF16)
    dd_d = p.dout("delta", [512, TH], BF16)
    bs_d = p.dout("bscale", [1, TH])

    p.start()
    co = emit_consts(p, co_np)
    dtbc = p.sb("dtbc", [128, 4])
    dma_in(p, dtbc, dtb_d)
    xb = p.sb("xb", [128, 8 * TH], BF16)
    load_fm(p, xb, x_d, 1024, TH)
    wb = p.sb("wb", [128, 8 * 512], BF16)
    wsb = p.sb("wsb", [128, 8 * 160], BF16)
    dtwb = p.sb("dtwb", [96, 512], BF16)
    dma_in(p, dtwb[64:96, :], dtw_d, cast=True)
    load_fm(p, wsb, wsel_d, 1024, 160)
    projb = p.sb("projb", [96, TH], BF16)
    outb = p.sb("outb", [128, 4 * TH], BF16)
    bsr = p.sb("bsr", [1, TH])
    pool = Pool(p, 4)
    gin0, din0 = p.n["gin"], p.n["din"]

    jobs = []
    for ng in range(4):
        c0 = ng * 512
        mm_list = [(wsb[:, k * 160: k * 160 + 96], xb[:, k * TH + c0: k * TH + c0 + 512]) for k in range(8)]
        def ev(env, ps, pred, c0=c0):
            ins = env.tensor_copy(projb[:, c0:c0 + 512], ps[:96, :512])
            p.inc(ins, "v")
        jobs.append((mm_list, 96, 512, ev))
    emit_mm(p, pool, jobs, ("vector", "v"), wait_before=[("gin", gin0), ("din", din0)])
    v_proj = p.n["v"]

    ps_b = p.ps("psb", [128, 512])
    for ng in range(4):
        c0 = ng * 512
        def tbs(t, c0=c0, ng=ng):
            if ng == 0:
                t.wait_ge(p.sem["v"], v_proj)
            else:
                p.wait(t, "v")
            ins = t.matmul(ps_b[0:1, :512], co["ones_bf"][0:64, :], projb[0:64, c0:c0 + 512],
                           start=True, stop=True)
            p.inc(ins, "mm")
        p.sec("tensor", tbs)
        mm_b = p.n["mm"]

        def vbs(v, c0=c0, mm_b=mm_b):
            v.wait_ge(p.sem["mm"], mm_b)
            ins = v.tensor_scalar(bsr[0:1, c0:c0 + 512], ps_b[0:1, :512], 1.0 / 64, None, ALU.mult)
            p.inc(ins, "v")
        p.sec("vector", vbs)
    v_bs = p.n["v"]

    spt2 = p.sb("spt2", [128, 512])
    jobs = []
    for mt in range(4):
        for ng in range(4):
            c0 = ng * 512
            mm_list = [(dtwb[64:96, mt * 128:(mt + 1) * 128], projb[64:96, c0:c0 + 512])]
            def ev(env, ps, pred, mt=mt, c0=c0):
                i1 = env.activation(spt2[:, :], ps[:, :512], AF.Exp, bias=dtbc[:, mt:mt + 1])
                ins = env.activation(outb[:, mt * TH + c0: mt * TH + c0 + 512], spt2[:, :], AF.Ln, bias=1.0)
                p.inc(ins, "sc")
            jobs.append((mm_list, 128, 512, ev))
    emit_mm(p, pool, jobs, ("scalar", "sc"), wait_before=[("v", v_proj)])
    sc_dt = p.n["sc"]

    def ddma(s):
        s.wait_ge(p.sem["sc"], sc_dt)
        for mt in range(4):
            p.inc(s.dma_start(out=dd_d[mt * 128:(mt + 1) * 128, :], in_=outb[:, mt * TH:(mt + 1) * TH]), "dout", 16)
        s.wait_ge(p.sem["v"], v_bs)
        p.inc(s.dma_start(out=bs_d, in_=bsr[0:1, :]), "dout", 16)
    p.sec("sync", ddma)
    d_prev = p.n["dout"]

    for wi, (wd, od) in enumerate(((wk_d, kk_d), (wv_d, vv_d), (wr_d, rr_d), (wg_d, gg_d))):
        def lw(g, wd=wd, wi=wi):
            p.wait(g, "mm")
            for kt in range(8):
                p.inc(g.dma_start(out=wb[:, kt * 512:(kt + 1) * 512], in_=wd[kt * 128:(kt + 1) * 128, :]), "gin", 16)
        p.sec("gpsimd", lw)
        gin_w = p.n["gin"]

        jobs = []
        for mt in range(4):
            for ng in range(4):
                c0 = ng * 512
                mm_list = [(wb[:, k * 512 + mt * 128: k * 512 + (mt + 1) * 128],
                            xb[:, k * TH + c0: k * TH + c0 + 512]) for k in range(8)]
                def ev(env, ps, pred, mt=mt, c0=c0, wi=wi):
                    if wi >= 1:
                        p.wait(env, "dout", d_prev + 16 * 4 * (wi - 1))
                    ins = env.tensor_copy(outb[:, mt * TH + c0: mt * TH + c0 + 512], ps[:, :512])
                    p.inc(ins, "v")
                jobs.append((mm_list, 128, 512, ev))
        emit_mm(p, pool, jobs, ("vector", "v"), wait_before=[("gin", gin_w)])
        v_w = p.n["v"]

        def odma(s, od=od, v_w=v_w):
            s.wait_ge(p.sem["v"], v_w)
            for mt in range(4):
                p.inc(s.dma_start(out=od[mt * 128:(mt + 1) * 128, :], in_=outb[:, mt * TH:(mt + 1) * TH]), "dout", 16)
        p.sec("sync", odma)
    def fin(s):
        p.wait(s, "dout")
    p.sec("sync", fin)
    p.close()
    return p.nc


def host_A7_inmaps(inputs, co_np, x4_parts):
    dtbc = inputs["L2_attn_dt_b"].reshape(4, 128).T.copy()
    base = const_inmap(co_np)
    maps = []
    for c in range(NCORE):
        m = dict(base)
        Wsel_perm = np.ascontiguousarray(np.concatenate(
            [inputs["L2_attn_W_sel"][:, 32:96], inputs["L2_attn_W_sel"][:, 0:32],
             inputs["L2_attn_W_sel"][:, 96:160]], axis=1))
        m.update(x=x4_parts[c], Wk=inputs["L2_attn_Wk"], Wv=inputs["L2_attn_Wv"],
                 Wr=inputs["L2_attn_Wr"], Wg=inputs["L2_attn_Wg"], Wsel=Wsel_perm,
                 dtW=inputs["L2_attn_dt_W"], dtbc=dtbc)
        maps.append(m)
    return maps


def build_A9(co_np, p=None):
    if p is None:
        p = P("A9")
    ys_d = p.din("ysl", [512, TH], BF16)
    wo_d = p.din("Wo", [512, 1024])
    x4_d = p.din("x4", [1024, TH])
    w1_d = p.din("W1", [1024, 4096])
    b1_d = p.din("b1c", [128, 32])
    w2_d = p.din("W2", [4096, 1024])
    b2_d = p.din("b2c", [128, 8])
    x6_d = p.dout("x6", [1024, TH])

    p.start()
    co = emit_consts(p, co_np)
    b1c = p.sb("b1c", [128, 32]); b2c = p.sb("b2c", [128, 8])
    dma_in(p, b1c, b1_d); dma_in(p, b2c, b2_d)
    xr = p.sb("xr", [128, 8 * TH], F32)
    load_fm(p, xr, x4_d, 1024, TH, cast=False)
    din0 = p.n["din"]
    pool = Pool(p, 4)

    s1 = ExitStack()
    ysb = p.sb("ysb", [128, 4 * TH], BF16, s1)
    wob = p.sb("wob", [128, 4 * 1024], BF16, s1)
    load_fm(p, ysb, ys_d, 512, TH)
    load_fm(p, wob, wo_d, 512, 1024)
    gin0 = p.n["gin"]

    jobs = []
    for mt in range(8):
        for ng in range(4):
            c0 = ng * 512
            mm_list = [(wob[:, k * 1024 + mt * 128: k * 1024 + (mt + 1) * 128],
                        ysb[:, k * TH + c0: k * TH + c0 + 512]) for k in range(4)]
            def ev(env, ps, pred, mt=mt, c0=c0):
                ins = env.tensor_tensor(xr[:, mt * TH + c0: mt * TH + c0 + 512], ps[:, :512],
                                        xr[:, mt * TH + c0: mt * TH + c0 + 512], ALU.add)
                p.inc(ins, "v")
            jobs.append((mm_list, 128, 512, ev))
    emit_mm(p, pool, jobs, ("vector", "v"), wait_before=[("gin", gin0), ("din", din0)])
    v_s = p.n["v"]
    emit_postln(p, xr, xr, TH, co, ready=[("v", v_s)])
    v_x5 = p.n["v"]
    s1.close()

    s2 = ExitStack()
    x5b = p.sb("x5b", [128, 8 * TH], BF16, s2)
    w1b = p.sb("w1b", [128, 8 * 4096], BF16, s2)
    hbb = p.sb("hbb", [128, 32 * 512], BF16, s2)
    w2b = p.sb("w2b", [128, 32 * 128], BF16, s2)

    def vx5(v):
        v.wait_ge(p.sem["v"], v_x5)
        for mt in range(8):
            p.inc(v.tensor_copy(x5b[:, mt * TH:(mt + 1) * TH], xr[:, mt * TH:(mt + 1) * TH]), "v")
    p.sec("vector", vx5)
    v_xb = p.n["v"]

    def lw1(g):
        p.wait(g, "mm")
        for kt in range(8):
            p.inc(g.dma_start(out=w1b[:, kt * 4096:(kt + 1) * 4096], in_=w1_d[kt * 128:(kt + 1) * 128, :]), "gin", 16)
    p.sec("gpsimd", lw1)
    gin_w1 = p.n["gin"]

    for qt in range(4):
        tc0 = qt * 512
        jobs = []
        for ft in range(32):
            mm_list = [(w1b[:, k * 4096 + ft * 128: k * 4096 + (ft + 1) * 128],
                        x5b[:, k * TH + tc0: k * TH + tc0 + 512]) for k in range(8)]
            def ev(env, ps, pred, ft=ft):
                ins = env.activation(hbb[:, ft * 512:(ft + 1) * 512],
                                     ps[:, :512], AF.Relu, bias=b1c[:, ft: ft + 1])
                p.inc(ins, "sc")
            jobs.append((mm_list, 128, 512, ev))
        emit_mm(p, pool, jobs, ("scalar", "sc"), wait_before=[("gin", gin_w1), ("v", v_xb)])
        sc_h = p.n["sc"]

        for dt_ in range(8):
            def lw2(g, dt_=dt_):
                p.wait(g, "mm")
                p.inc(g.dma_start(out=w2b[:, :].rearrange("p (a m) -> p a m", a=32),
                                  in_=w2_d[:, dt_ * 128:(dt_ + 1) * 128].rearrange("(a p) m -> p a m", p=128)), "gin", 16)
            p.sec("gpsimd", lw2)
            gin_w2 = p.n["gin"]

            jobs = []
            mm_list = [(w2b[:, kk * 128:(kk + 1) * 128],
                        hbb[:, kk * 512: kk * 512 + 512]) for kk in range(32)]
            def ev(env, ps, pred, dt_=dt_, tc0=tc0):
                env.tensor_scalar(ps[:, :512], ps[:, :512], b2c[:, dt_:dt_ + 1], None, ALU.add)
                ins = env.tensor_tensor(xr[:, dt_ * TH + tc0: dt_ * TH + tc0 + 512], ps[:, :512],
                                        xr[:, dt_ * TH + tc0: dt_ * TH + tc0 + 512], ALU.add)
                p.inc(ins, "v")
            jobs.append((mm_list, 128, 512, ev))
            emit_mm(p, pool, jobs, ("vector", "v"), wait_before=[("gin", gin_w2), ("sc", sc_h)])
    v_s6 = p.n["v"]
    s2.close()

    emit_postln(p, xr, xr, TH, co, ready=[("v", v_s6)])
    v_ln = p.n["v"]

    def dx(s):
        s.wait_ge(p.sem["v"], v_ln)
        for mt in range(8):
            p.inc(s.dma_start(out=x6_d[mt * 128:(mt + 1) * 128, :], in_=xr[:, mt * TH:(mt + 1) * TH]), "dout", 16)
        p.wait(s, "dout")
    p.sec("sync", dx)
    p.close()
    return p.nc


def host_A9_inmaps(inputs, co_np, ys_parts, x4_parts):
    b1c = inputs["L2_ffn_b1"].reshape(32, 128).T.copy()
    b2c = inputs["L2_ffn_b2"].reshape(8, 128).T.copy()
    base = const_inmap(co_np)
    maps = []
    for c in range(NCORE):
        b, i = c // 2, c % 2
        t0 = i * TH
        ysl = np.concatenate([np.asarray(ys_parts[2 * b])[:, t0:t0 + TH],
                              np.asarray(ys_parts[2 * b + 1])[:, t0:t0 + TH]], axis=0)
        m = dict(base)
        m.update(ysl=np.ascontiguousarray(ysl), Wo=inputs["L2_attn_Wo"], x4=x4_parts[c],
                 W1=inputs["L2_ffn_W1"], b1c=b1c, W2=inputs["L2_ffn_W2"], b2c=b2c)
        maps.append(m)
    return maps


def build_A8(co_np, p=None, fused_out=False):
    if p is None:
        p = P("A8")
    NH = 4
    k_d = p.din("kf", [NH * 64, T], BF16)
    v_d = p.din("vf", [NH * 64, T], BF16)
    r_d = p.din("rf", [NH * 64, T], BF16)
    g_d = p.din("gf", [NH * 64, T], BF16)
    dl_d = p.din("df", [NH * 64, T], BF16)
    bs_d = p.din("bscale", [1, T])
    bdc_d = p.din("bdc", [128, 2])
    tfc_d = p.din("tfc", [128, 2])
    m16_d = p.din("MSK16", [16, 128], mybir.dt.uint8)
    ys_d = p.dout("ys", [2 * NH * 64, TH] if fused_out else [NH * 64, T], BF16)

    p.start()
    co = emit_consts(p, co_np)
    bdc = p.sb("bdc", [128, 2]); tfc = p.sb("tfc", [128, 2])
    m16 = p.sb("m16", [16, 128], mybir.dt.uint8)
    dma_in(p, bdc, bdc_d); dma_in(p, tfc, tfc_d); dma_in(p, m16, m16_d)
    bsr = p.sb("bsr", [1, T])
    dma_in(p, bsr, bs_d)
    ones64 = p.sb("ones64", [64, 512], BF16)
    zer64 = p.sb("zer64", [64, 64])
    zer16 = p.sb("zer16", [16, 128], BF16)
    bscB = p.sb("bscB", [64, T], BF16)
    kf = p.sb("kf", [64, T], BF16)
    vf = p.sb("vf", [64, T], BF16)
    rf = p.sb("rf", [64, T], BF16)
    gf = p.sb("gf", [64, T], BF16)
    df = p.sb("df", [64, T], BF16)
    Lb = p.sb("Lb", [64, T], F32)
    Efl = p.sb("Efl", [64, T], F32)
    mfl = p.sb("mfl", [64, T], F32)
    Erow = p.sb("Erow", [64, 256], F32)
    Mrow = p.sb("Mrow", [64, 256], F32)
    lam = p.sb("lam", [64, 256], F32)
    fsc = p.sb("fsc", [64, T], F32)
    ktil = p.sb("ktil", [64, T], BF16)
    kap = p.sb("kap", [64, T], BF16)
    kin = p.sb("kin", [64, T], BF16)
    kpr = p.sb("kpr", [64, T], BF16)
    EProw = p.sb("EProw", [64, 256], F32)
    vtl = p.sb("vtl", [64, T], BF16)
    wkv = p.sb("wkv", [64, T], F32)
    kTg = p.sb("kTg", [16, 512], BF16)
    vTg = p.sb("vTg", [16, 512], BF16)
    Pm = p.sb("Pm", [16, 128], BF16)
    Ub = p.sb("Ub", [64, 512])
    Sg = p.sb("Sg", [64, 512], BF16)
    Scar = p.sb("Scar", [64, 64], BF16)

    ps_tr = p.ps("pstr", [16, 512], BF16)
    ps_u = p.ps("psu", [64, 512], F32)
    ps_p = p.ps("psp", [16, 128], F32)
    ps_w = p.ps("psw", [64, 128], F32)
    ps_bc = p.ps("psbc", [64, 512], F32)

    def ginit(g):
        g.memset(ones64[:, :], 1.0)
        g.memset(zer64[:, :], 0.0)
        g.memset(Scar[:, :], 0.0)
        ins = g.memset(zer16[:, :], 0.0)
        p.inc(ins, "gp")
    p.sec("gpsimd", ginit)
    gp0 = p.n["gp"]
    din0, gin0 = p.n["din"], p.n["gin"]

    for ng in range(8):
        def tb(t, ng=ng):
            if ng == 0:
                p.wait(t, "din", din0)
            else:
                p.wait(t, "v")
            ins = t.matmul(ps_bc[:, :512], co["onesrow"][0:1, 0:64], bsr[0:1, ng * 512:(ng + 1) * 512],
                           start=True, stop=True)
            p.inc(ins, "mm")
        p.sec("tensor", tb)
        mm_b = p.n["mm"]

        def vb(v, ng=ng, mm_b=mm_b):
            v.wait_ge(p.sem["mm"], mm_b)
            ins = v.tensor_copy(bscB[:, ng * 512:(ng + 1) * 512], ps_bc[:, :512])
            p.inc(ins, "v")
        p.sec("vector", vb)
    v_bsc = p.n["v"]

    NG8 = T // 512
    for h in range(NH):
        hcol = h // 2
        hrow = (h % 2) * 64
        def lh(g, h=h):
            p.wait(g, "v")
            p.wait(g, "sc")
            p.wait(g, "zd")
            for src, dst in ((k_d, kf), (v_d, vf), (r_d, rf), (g_d, gf), (dl_d, df)):
                p.inc(g.dma_start(out=dst[:, :], in_=src[h * 64:(h + 1) * 64, :]), "gin", 16)
        p.sec("gpsimd", lh)
        gin_h = p.n["gin"]

        def va(v, h=h, hcol=hcol, hrow=hrow):
            v.wait_ge(p.sem["gin"], gin_h)
            p.wait(v, "gp", gp0)
            i1 = p.inc(v.tensor_scalar(fsc[:, :], df[:, :], bdc[hrow:hrow + 64, hcol:hcol + 1], None, ALU.mult), "v")
            v.wait_ge(p.sem["v"], i1)
            last = None
            for cg in range(8):
                c0 = cg * 512
                init = 0.0 if cg == 0 else Lb[:, c0 - 1: c0]
                i2 = p.inc(v.tensor_tensor_scan(Lb[:, c0:c0 + 512], ones64[:, :], fsc[:, c0:c0 + 512],
                                                init, ALU.mult, ALU.add), "v")
                v.wait_ge(p.sem["v"], i2)
            Lv = Lb[:, :].rearrange("p (c s) -> p c s", s=16)
            i3 = p.inc(v.tensor_copy(Erow[:, :], Lv[:, :, 15]), "v")
            v.wait_ge(p.sem["v"], i3)
            v.tensor_scalar(lam[:, 0:1], Erow[:, 0:1], -1.0, None, ALU.mult)
            i4 = p.inc(v.tensor_tensor(lam[:, 1:256], Erow[:, 0:255], Erow[:, 1:256], ALU.subtract), "v")
            v.wait_ge(p.sem["v"], i4)
            v.tensor_scalar(Mrow[:, 0:1], Erow[:, 0:1], 0.5, None, ALU.mult)
            i5 = p.inc(v.tensor_tensor(Mrow[:, 1:256], Erow[:, 0:255], Erow[:, 1:256], ALU.add), "v")
            v.wait_ge(p.sem["v"], i5)
            i6 = p.inc(v.tensor_scalar(Mrow[:, 1:256], Mrow[:, 1:256], 0.5, None, ALU.mult), "v")
            v.wait_ge(p.sem["v"], i6)
            v.memset(EProw[:, 0:1], 0.0)
            i7 = p.inc(v.tensor_copy(EProw[:, 1:256], Erow[:, 0:255]), "v")
            v.wait_ge(p.sem["v"], i7)
            Ev = Efl[:, :].rearrange("p (c s) -> p c s", s=16)
            Mv = mfl[:, :].rearrange("p (c s) -> p c s", s=16)
            for sI in range(16):
                v.tensor_copy(Ev[:, :, sI], Erow[:, :])
                ins = v.tensor_copy(Mv[:, :, sI], Mrow[:, :])
            p.inc(ins, "v")
        p.sec("vector", va)
        v_L = p.n["v"]

        def sl(sc, h=h):
            sc.wait_ge(p.sem["v"], v_L)
            ins = sc.activation(lam[:, :], lam[:, :], AF.Exp)
            p.inc(ins, "sc")
        p.sec("scalar", sl)
        sc_lam = p.n["sc"]

        def vargs(v, h=h):
            v.wait_ge(p.sem["v"], v_L)
            ins = v.tensor_tensor(fsc[:, :], mfl[:, :], Lb[:, :], ALU.subtract)
            p.inc(ins, "v")
        p.sec("vector", vargs)
        v_a1 = p.n["v"]

        def sexps(sc, h=h):
            sc.wait_ge(p.sem["v"], v_a1)
            i1 = p.inc(sc.activation(ktil[:, :], fsc[:, :], AF.Exp), "sc")
            ins = sc.activation(kap[:, :], fsc[:, :], AF.Exp, scale=-1.0)
            p.inc(ins, "sc")
        p.sec("scalar", sexps)
        sc_e1 = p.n["sc"]

        def varg2(v, h=h):
            v.wait_ge(p.sem["sc"], sc_e1)
            i1 = p.inc(v.tensor_tensor(fsc[:, :], Lb[:, :], Efl[:, :], ALU.subtract), "v")
            v.wait_ge(p.sem["v"], i1)
            EPv = Efl[:, :].rearrange("p (c s) -> p c s", s=16)
            for sI in range(16):
                ins = v.tensor_copy(EPv[:, :, sI], EProw[:, :])
            i2 = p.inc(ins, "v")
            v.wait_ge(p.sem["v"], i2)
            ins = v.tensor_tensor(Efl[:, :], Efl[:, :], Lb[:, :], ALU.subtract)
            p.inc(ins, "v")
        p.sec("vector", varg2)
        v_a2 = p.n["v"]

        def sexp2(sc, h=h):
            sc.wait_ge(p.sem["v"], v_a2)
            i1 = p.inc(sc.activation(fsc[:, :], fsc[:, :], AF.Exp), "sc")
            ins = sc.activation(kin[:, :], Efl[:, :], AF.Exp)
            p.inc(ins, "sc")
        p.sec("scalar", sexp2)
        sc_e2 = p.n["sc"]

        def vfac(v, h=h):
            v.wait_ge(p.sem["sc"], sc_e2)
            v.wait_ge(p.sem["v"], v_bsc)
            p.inc(v.tensor_tensor(ktil[:, :], ktil[:, :], kf[:, :], ALU.mult), "v")
            p.inc(v.tensor_tensor(kap[:, :], kap[:, :], kf[:, :], ALU.mult), "v")
            p.inc(v.tensor_tensor(kin[:, :], kin[:, :], kf[:, :], ALU.mult), "v")
            p.inc(v.tensor_tensor(kpr[:, :], fsc[:, :], kf[:, :], ALU.mult), "v")
            ins = v.tensor_tensor(vtl[:, :], vf[:, :], bscB[:, :], ALU.mult)
            p.inc(ins, "v")
        p.sec("vector", vfac)
        v_fac = p.n["v"]

        for g8 in range(32):
            t0 = g8 * 128
            def ttr(t, g8=g8, t0=t0):
                if g8 == 0:
                    t.wait_ge(p.sem["v"], v_fac)
                p.wait(t, "v")
                for c8 in range(8):
                    c0 = t0 + c8 * 16
                    ins = t.transpose(ps_tr[:, c8 * 64:(c8 + 1) * 64], kpr[:, c0:c0 + 16], co["ident"][0:64, 0:64])
                p.inc(ins, "mm")
            p.sec("tensor", ttr)
            mm_t1 = p.n["mm"]

            def vtr(v, mm_t1=mm_t1):
                v.wait_ge(p.sem["mm"], mm_t1)
                ins = v.tensor_copy(kTg[:, :], ps_tr[:, :])
                p.inc(ins, "v")
            p.sec("vector", vtr)
            v_t1 = p.n["v"]

            def ttr2(t, g8=g8, t0=t0, v_t1=v_t1):
                t.wait_ge(p.sem["v"], v_t1)
                for c8 in range(8):
                    c0 = t0 + c8 * 16
                    ins = t.transpose(ps_tr[:, c8 * 64:(c8 + 1) * 64], vtl[:, c0:c0 + 16], co["ident"][0:64, 0:64])
                p.inc(ins, "mm")
            p.sec("tensor", ttr2)
            mm_t2 = p.n["mm"]

            def vtr2(v, mm_t2=mm_t2):
                v.wait_ge(p.sem["mm"], mm_t2)
                ins = v.tensor_copy(vTg[:, :], ps_tr[:, :])
                p.inc(ins, "v")
            p.sec("vector", vtr2)
            v_t2 = p.n["v"]

            def tu(t, g8=g8, t0=t0, v_t2=v_t2):
                t.wait_ge(p.sem["v"], v_t2)
                for c8 in range(8):
                    t.matmul(ps_u[:, c8 * 64:(c8 + 1) * 64], kTg[:, c8 * 64:(c8 + 1) * 64],
                             vTg[:, c8 * 64:(c8 + 1) * 64], start=True, stop=True)
                for c8 in range(8):
                    c0 = t0 + c8 * 16
                    ins = t.matmul(ps_p[:, c8 * 16:(c8 + 1) * 16], kap[:, c0:c0 + 16],
                                   ktil[:, c0:c0 + 16], start=True, stop=True)
                p.inc(ins, "mm")
            p.sec("tensor", tu)
            mm_u = p.n["mm"]

            def vu(v, g8=g8, mm_u=mm_u):
                v.wait_ge(p.sem["mm"], mm_u)
                i1 = p.inc(v.tensor_copy(Ub[:, :], ps_u[:, :]), "v")
                i2 = p.inc(v.select(Pm[:, :], m16[:, :], ps_p[:, :], zer16[:, :]), "v")
                v.wait_ge(p.sem["v"], i1)
                for c8 in range(8):
                    cI = g8 * 8 + c8
                    prev = Scar[:, :] if c8 == 0 else Sg[:, (c8 - 1) * 64: c8 * 64]
                    i3 = p.inc(v.scalar_tensor_tensor(Sg[:, c8 * 64:(c8 + 1) * 64], prev,
                                                      lam[:, cI:cI + 1], Ub[:, c8 * 64:(c8 + 1) * 64],
                                                      ALU.mult, ALU.add), "v")
                    v.wait_ge(p.sem["v"], i3)
            p.sec("vector", vu)
            v_ch = p.n["v"]

            def tw(t, g8=g8, t0=t0, v_ch=v_ch):
                t.wait_ge(p.sem["v"], v_ch)
                for c8 in range(8):
                    c0 = t0 + c8 * 16
                    Sprev = Scar[:, :] if c8 == 0 else Sg[:, (c8 - 1) * 64: c8 * 64]
                    t.matmul(ps_w[:, c8 * 16:(c8 + 1) * 16], Sprev, kin[:, c0:c0 + 16], start=True, stop=False)
                    ins = t.matmul(ps_w[:, c8 * 16:(c8 + 1) * 16], vTg[:, c8 * 64:(c8 + 1) * 64],
                                   Pm[:, c8 * 16:(c8 + 1) * 16], start=False, stop=True)
                p.inc(ins, "mm")
            p.sec("tensor", tw)
            mm_w = p.n["mm"]

            def vw(v, g8=g8, t0=t0, mm_w=mm_w):
                v.wait_ge(p.sem["mm"], mm_w)
                i1 = p.inc(v.tensor_copy(wkv[:, t0:t0 + 128], ps_w[:, :]), "v")
                ins = v.tensor_copy(Scar[:, :], Sg[:, 7 * 64: 8 * 64])
                p.inc(ins, "v")
            p.sec("vector", vw)

        v_wkv = p.n["v"]

        def sep(sc, h=h):
            sc.wait_ge(p.sem["v"], v_wkv)
            i1 = p.inc(sc.activation(rf[:, :], rf[:, :], AF.Sigmoid), "sc")
            ins = sc.activation(gf[:, :], gf[:, :], AF.Sigmoid)
            p.inc(ins, "sc")
        p.sec("scalar", sep)
        sc_sig = p.n["sc"]

        def vep(v, h=h, hcol=hcol, hrow=hrow, sc_sig=sc_sig):
            v.wait_ge(p.sem["sc"], sc_sig)
            i1 = p.inc(v.tensor_scalar(fsc[:, :], kf[:, :], tfc[hrow:hrow + 64, hcol:hcol + 1], None, ALU.mult), "v")
            v.wait_ge(p.sem["v"], i1)
            i2 = p.inc(v.tensor_tensor(fsc[:, :], fsc[:, :], vf[:, :], ALU.mult), "v")
            v.wait_ge(p.sem["v"], i2)
            i3 = p.inc(v.tensor_tensor(wkv[:, :], wkv[:, :], fsc[:, :], ALU.add), "v")
            v.wait_ge(p.sem["v"], i3)
            i4 = p.inc(v.tensor_tensor(wkv[:, :], wkv[:, :], rf[:, :], ALU.mult), "v")
            v.wait_ge(p.sem["v"], i4)
            ins = v.tensor_tensor(ktil[:, :], wkv[:, :], gf[:, :], ALU.mult)
            p.inc(ins, "v")
        p.sec("vector", vep)
        v_ys = p.n["v"]

        def dys(s, h=h, v_ys=v_ys):
            s.wait_ge(p.sem["v"], v_ys)
            if fused_out:
                p.inc(s.dma_start(out=ys_d[h * 64:(h + 1) * 64, :], in_=ktil[:, 0:TH]), "zd", 16)
                p.inc(s.dma_start(out=ys_d[256 + h * 64: 256 + (h + 1) * 64, :], in_=ktil[:, TH:T]), "zd", 16)
            else:
                p.inc(s.dma_start(out=ys_d[h * 64:(h + 1) * 64, :], in_=ktil[:, :]), "zd", 16)
        p.sec("sync", dys)

        def gz(g, h=h):
            p.wait(g, "mm")
            ins = g.memset(Scar[:, :], 0.0)
            p.inc(ins, "gp")
        p.sec("gpsimd", gz)
        gp0 = p.n["gp"]

    def fin(s):
        p.wait(s, "zd")
    p.sec("sync", fin)
    p.close()
    return p.nc


def host_A8_inmaps(inputs, co_np, l2parts):
    MSK16 = np.zeros((16, 128), dtype=np.uint8)
    for pp in range(16):
        for c in range(8):
            MSK16[pp, c * 16 + pp: c * 16 + 16] = 1
    bd = inputs["L2_attn_base_decay"]
    tf = inputs["L2_attn_time_first"]
    base = const_inmap(co_np)
    maps = []
    for c in range(NCORE):
        b, j = c // 2, c % 2
        h0 = j * 4
        def cat(key):
            return np.ascontiguousarray(np.concatenate([np.asarray(l2parts[2 * b][key])[h0 * 64:(h0 + 4) * 64, :],
                                                        np.asarray(l2parts[2 * b + 1][key])[h0 * 64:(h0 + 4) * 64, :]], axis=1))
        bdc = np.zeros((128, 2), np.float32)
        tfc = np.zeros((128, 2), np.float32)
        for hl in range(4):
            bdc[(hl % 2) * 64:(hl % 2) * 64 + 64, hl // 2] = bd[h0 + hl]
            tfc[(hl % 2) * 64:(hl % 2) * 64 + 64, hl // 2] = tf[h0 + hl]
        bs = np.concatenate([np.asarray(l2parts[2 * b]["bscale"]), np.asarray(l2parts[2 * b + 1]["bscale"])], axis=1)
        m = dict(base)
        m.update(kf=cat("k"), vf=cat("v"), rf=cat("r"), gf=cat("g"), df=cat("delta"),
                 bscale=np.ascontiguousarray(bs), bdc=bdc, tfc=tfc, MSK16=MSK16)
        maps.append(m)
    return maps


def host_A10_inmaps(inputs, co_np, x6_parts):
    MSK = attn_masks()
    WqT = np.ascontiguousarray(inputs["L3_attn_Wq"].T)
    WkT = np.ascontiguousarray(inputs["L3_attn_Wk"].T)
    WvT = np.ascontiguousarray(inputs["L3_attn_Wv"].T)
    zb = np.zeros((128, 4), np.float32)
    base = const_inmap(co_np)
    maps = []
    for c in range(NCORE):
        b, j = c // 2, c % 2
        h0 = j * 8
        xf = np.concatenate([x6_parts[2 * b], x6_parts[2 * b + 1]], axis=1)
        m = dict(base)
        m.update(xf=np.ascontiguousarray(xf),
                 WQ=np.ascontiguousarray(WqT[:, h0 * 64:(h0 + 8) * 64]),
                 WK=np.ascontiguousarray(WkT[:, h0 * 64:(h0 + 8) * 64]),
                 WV=np.ascontiguousarray(WvT[:, h0 * 64:(h0 + 8) * 64]),
                 bQ=zb, bK=zb, bV=zb, MSK=MSK)
        maps.append(m)
    return maps


def build_A11(co_np, p=None):
    if p is None:
        p = P("A11")
    z_d = p.din("zsl", [1024, TH], BF16)
    wo_d = p.din("WoT", [1024, 1024])
    x6_d = p.din("x6", [1024, TH])
    wr_d = p.din("wR", [1024, 1024])
    wi_d = p.din("wI", [1024, 1024])
    fr_d = p.din("FR", [1024, 1024])
    fi_d = p.din("FI", [1024, 1024])
    gr_d = p.din("GR", [1024, 1024])
    gin_d = p.din("GIn", [1024, 1024])
    bias_d = p.din("biasc", [128, 8])
    x8_d = p.dout("x8", [1024, TH])

    p.start()
    co = emit_consts(p, co_np)
    biasc = p.sb("biasc", [128, 8])
    dma_in(p, biasc, bias_d)
    xr = p.sb("xr", [128, 8 * TH], F32)
    load_fm(p, xr, x6_d, 1024, TH, cast=False)
    din0 = p.n["din"]
    T1R = p.sb("T1R", [128, 8 * 1024], BF16)
    T1I = p.sb("T1I", [128, 8 * 1024], BF16)
    Mb = p.sb("Mb", [128, 8 * 1024], BF16)
    pool = Pool(p, 4)

    s1 = ExitStack()
    zb = p.sb("zb", [128, 8 * TH], BF16, s1)
    wob = p.sb("wob", [128, 8 * 1024], BF16, s1)
    load_fm(p, zb, z_d, 1024, TH)
    load_fm(p, wob, wo_d, 1024, 1024)
    gin1 = p.n["gin"]
    jobs = []
    for mt in range(8):
        for ng in range(4):
            c0 = ng * 512
            mm_list = [(wob[:, k * 1024 + mt * 128: k * 1024 + (mt + 1) * 128],
                        zb[:, k * TH + c0: k * TH + c0 + 512]) for k in range(8)]
            def ev(env, ps, pred, mt=mt, c0=c0):
                ins = env.tensor_tensor(xr[:, mt * TH + c0: mt * TH + c0 + 512], ps[:, :512],
                                        xr[:, mt * TH + c0: mt * TH + c0 + 512], ALU.add)
                p.inc(ins, "v")
            jobs.append((mm_list, 128, 512, ev))
    emit_mm(p, pool, jobs, ("vector", "v"), wait_before=[("gin", gin1), ("din", din0)])
    v_s = p.n["v"]
    emit_postln(p, xr, xr, TH, co, ready=[("v", v_s)])
    v_x7 = p.n["v"]
    s1.close()

    s2 = ExitStack()
    wRb = p.sb("wRb", [128, 8 * 1024], BF16, s2)
    wIb = p.sb("wIb", [128, 8 * 1024], BF16, s2)
    FRb = p.sb("FRb", [128, 8 * 1024], BF16, s2)
    FIb = p.sb("FIb", [128, 8 * 1024], BF16, s2)
    tA = p.sb("tA", [128, 512], F32, s2)
    def _g2(g):
        g.wait_ge(p.sem["v"], v_x7)
    p.sec("gpsimd", _g2)
    load_fm(p, wRb, wr_d, 1024, 1024)
    load_fm(p, wIb, wi_d, 1024, 1024)
    load_fm(p, FRb, fr_d, 1024, 1024)
    load_fm(p, FIb, fi_d, 1024, 1024)
    gin2 = p.n["gin"]

    for part in range(2):
        dstbuf = T1R if part == 0 else T1I
        jobs = []
        for mt in range(8):
            for ng in range(2):
                c0 = ng * 512
                if part == 0:
                    mmA = [(wRb[:, k * 1024 + mt * 128: k * 1024 + (mt + 1) * 128],
                            FRb[:, k * 1024 + c0: k * 1024 + c0 + 512]) for k in range(8)]
                    mmB = [(wIb[:, k * 1024 + mt * 128: k * 1024 + (mt + 1) * 128],
                            FIb[:, k * 1024 + c0: k * 1024 + c0 + 512]) for k in range(8)]
                else:
                    mmA = [(wRb[:, k * 1024 + mt * 128: k * 1024 + (mt + 1) * 128],
                            FIb[:, k * 1024 + c0: k * 1024 + c0 + 512]) for k in range(8)]
                    mmB = [(wIb[:, k * 1024 + mt * 128: k * 1024 + (mt + 1) * 128],
                            FRb[:, k * 1024 + c0: k * 1024 + c0 + 512]) for k in range(8)]
                def evA(env, ps, pred, mt=mt, c0=c0):
                    ins = env.tensor_copy(tA[:, :], ps[:, :512])
                    p.inc(ins, "v")
                jobs.append((mmA, 128, 512, evA))
                def evB(env, ps, pred, mt=mt, c0=c0, part=part, dstbuf=dstbuf):
                    op = ALU.subtract if part == 0 else ALU.add
                    ins = env.tensor_tensor(dstbuf[:, mt * 1024 + c0: mt * 1024 + c0 + 512],
                                            tA[:, :], ps[:, :512], op)
                    p.inc(ins, "v")
                jobs.append((mmB, 128, 512, evB))
        emit_mm(p, pool, jobs, ("vector", "v"), wait_before=[("gin", gin2)])
    v_t1 = p.n["v"]
    s2.close()

    s3 = ExitStack()
    GRb = p.sb("GRb", [128, 8 * 1024], BF16, s3)
    GInb = p.sb("GInb", [128, 8 * 1024], BF16, s3)
    def _g3(g):
        g.wait_ge(p.sem["v"], v_t1)
    p.sec("gpsimd", _g3)
    load_fm(p, GRb, gr_d, 1024, 1024)
    load_fm(p, GInb, gin_d, 1024, 1024)
    gin3 = p.n["gin"]
    jobs = []
    for mt in range(8):
        for ng in range(2):
            c0 = ng * 512
            mm_list = [(T1R[:, k * 1024 + mt * 128: k * 1024 + (mt + 1) * 128],
                        GRb[:, k * 1024 + c0: k * 1024 + c0 + 512]) for k in range(8)] + \
                      [(T1I[:, k * 1024 + mt * 128: k * 1024 + (mt + 1) * 128],
                        GInb[:, k * 1024 + c0: k * 1024 + c0 + 512]) for k in range(8)]
            def ev(env, ps, pred, mt=mt, c0=c0):
                ins = env.tensor_copy(Mb[:, mt * 1024 + c0: mt * 1024 + c0 + 512], ps[:, :512])
                p.inc(ins, "v")
            jobs.append((mm_list, 128, 512, ev))
    emit_mm(p, pool, jobs, ("vector", "v"), wait_before=[("gin", gin3), ("v", v_t1)])
    v_m = p.n["v"]
    s3.close()

    s4 = ExitStack()
    x7b = p.sb("x7b", [128, 8 * TH], BF16, s4)
    def vx7(v):
        v.wait_ge(p.sem["v"], v_m)
        for mt in range(8):
            p.inc(v.tensor_copy(x7b[:, mt * TH:(mt + 1) * TH], xr[:, mt * TH:(mt + 1) * TH]), "v")
    p.sec("vector", vx7)
    v_x7b = p.n["v"]

    jobs = []
    for mt in range(8):
        for ng in range(4):
            c0 = ng * 512
            mm_list = [(Mb[:, k * 1024 + mt * 128: k * 1024 + (mt + 1) * 128],
                        x7b[:, k * TH + c0: k * TH + c0 + 512]) for k in range(8)]
            def ev(env, ps, pred, mt=mt, c0=c0):
                env.tensor_scalar(ps[:, :512], ps[:, :512], biasc[:, mt:mt + 1], None, ALU.add)
                ins = env.tensor_tensor(xr[:, mt * TH + c0: mt * TH + c0 + 512], ps[:, :512],
                                        xr[:, mt * TH + c0: mt * TH + c0 + 512], ALU.add)
                p.inc(ins, "v")
            jobs.append((mm_list, 128, 512, ev))
    emit_mm(p, pool, jobs, ("vector", "v"), wait_before=[("v", v_x7b)])
    v_s8 = p.n["v"]
    emit_postln(p, xr, xr, TH, co, ready=[("v", v_s8)])
    v_ln = p.n["v"]
    s4.close()

    def dx(s):
        s.wait_ge(p.sem["v"], v_ln)
        for mt in range(8):
            p.inc(s.dma_start(out=x8_d[mt * 128:(mt + 1) * 128, :], in_=xr[:, mt * TH:(mt + 1) * TH]), "dout", 16)
        p.wait(s, "dout")
    p.sec("sync", dx)
    p.close()
    return p.nc


def dft_consts():
    D = 1024
    jk = np.outer(np.arange(D), np.arange(D)) % D
    ang = 2.0 * np.pi * jk / D
    FR = np.cos(ang).astype(np.float32)
    FI = (-np.sin(ang)).astype(np.float32)
    GR = (np.cos(ang) / D).astype(np.float32)
    GIn = (-np.sin(ang) / D).astype(np.float32)
    return FR, FI, GR, GIn


def host_A11_inmaps(inputs, co_np, z_parts, x6_parts):
    FR, FI, GR, GIn = dft_consts()
    WoT = np.ascontiguousarray(inputs["L3_attn_Wo"].T)
    biasc = inputs["L3_ffn_bias"].reshape(8, 128).T.copy()
    base = const_inmap(co_np)
    maps = []
    for c in range(NCORE):
        b, i = c // 2, c % 2
        t0 = i * TH
        zsl = np.concatenate([np.asarray(z_parts[2 * b])[:, t0:t0 + TH],
                              np.asarray(z_parts[2 * b + 1])[:, t0:t0 + TH]], axis=0)
        m = dict(base)
        m.update(zsl=np.ascontiguousarray(zsl), WoT=WoT, x6=x6_parts[c],
                 wR=inputs["L3_ffn_w_freq_real"], wI=inputs["L3_ffn_w_freq_imag"],
                 FR=FR, FI=FI, GR=GR, GIn=GIn, biasc=biasc)
        maps.append(m)
    return maps


def _prog(name, builder):
    if name not in _prog_cache:
        _prog_cache[name] = builder()
    return _prog_cache[name]


def kernel(**inputs):
    return kernel_fused(**inputs)


def kernel_unfused(**inputs):
    inputs = {k: np.ascontiguousarray(np.asarray(v, dtype=np.float32)) for k, v in inputs.items()}
    co_np = host_consts()
    cores = list(range(NCORE))

    r1 = run_bass_kernel_spmd(_prog("A1", lambda: build_A1(co_np)), host_A1_inmaps(inputs, co_np), core_ids=cores).results
    r2 = run_bass_kernel_spmd(_prog("A2", lambda: build_A2(co_np)), host_A2_inmaps(inputs, co_np, r1), core_ids=cores).results
    r3 = run_bass_kernel_spmd(_prog("A3", lambda: build_A3(co_np)), host_A3_inmaps(inputs, co_np, r1, r2), core_ids=cores).results
    x1p = [np.asarray(r3[c]["x1"], np.float32) for c in cores]
    r4 = run_bass_kernel_spmd(_prog("A4", lambda: build_A4(co_np)), host_A4_inmaps(inputs, co_np, x1p), core_ids=cores).results
    x2p = [np.asarray(r4[c]["x2"], np.float32) for c in cores]
    r5 = run_bass_kernel_spmd(_prog("A5", lambda: build_attn(co_np, biased=True, name="A5")),
                              host_A5_inmaps(inputs, co_np, x2p), core_ids=cores).results
    zp = [np.asarray(r5[c]["z"]) for c in cores]
    r6 = run_bass_kernel_spmd(_prog("A6", lambda: build_A6(co_np)), host_A6_inmaps(inputs, co_np, zp, x2p), core_ids=cores).results
    x4p = [np.asarray(r6[c]["x4"], np.float32) for c in cores]
    r7 = run_bass_kernel_spmd(_prog("A7", lambda: build_A7(co_np)), host_A7_inmaps(inputs, co_np, x4p), core_ids=cores).results
    r8 = run_bass_kernel_spmd(_prog("A8", lambda: build_A8(co_np)), host_A8_inmaps(inputs, co_np, r7), core_ids=cores).results
    ysp = [np.asarray(r8[c]["ys"]) for c in cores]
    r9 = run_bass_kernel_spmd(_prog("A9", lambda: build_A9(co_np)), host_A9_inmaps(inputs, co_np, ysp, x4p), core_ids=cores).results
    x6p = [np.asarray(r9[c]["x6"], np.float32) for c in cores]
    r10 = run_bass_kernel_spmd(_prog("A10", lambda: build_attn(co_np, biased=True, name="A10")),
                               host_A10_inmaps(inputs, co_np, x6p), core_ids=cores).results
    z3p = [np.asarray(r10[c]["z"]) for c in cores]
    r11 = run_bass_kernel_spmd(_prog("A11", lambda: build_A11(co_np)), host_A11_inmaps(inputs, co_np, z3p, x6p), core_ids=cores).results

    out = np.zeros((B, T, D), dtype=np.float32)
    for c in cores:
        b, i = c // 2, c % 2
        out[b, i * TH:(i + 1) * TH] = np.asarray(r11[c]["x8"], np.float32).T
    return out


RG2 = [[0, 1], [2, 3], [4, 5], [6, 7]]


def _barrier(p):
    sems = ["din", "gin", "mm", "v", "sc", "zd", "zg", "dout", "cc"]
    def gb(g):
        for sm in sems:
            p.wait(g, sm)
    p.sec("gpsimd", gb)
    def sb_(s):
        for sm in sems:
            p.wait(s, sm)
    p.sec("sync", sb_)


def _coll(p, kind, in_ap, out_ap, op=ALU.bypass):
    def g(g):
        for sm in ("dout", "din", "zd", "zg"):
            p.wait(g, sm)
        ins = g.collective_compute(kind, op, replica_groups=RG2,
                                   ins=[in_ap.opt()], outs=[out_ap.opt()])
        p.inc(ins, "cc", 1)
        g.wait_ge(p.sem["cc"], p.n["cc"])
    p.sec("gpsimd", g)


def _dd(p, dst_ap, src_ap, slow=False):
    def s(s):
        p.wait(s, "cc")
        p.inc(s.dma_start(out=dst_ap, in_=src_ap, allow_slow_non_contiguous=slow),
              "din", 16)
        p.wait(s, "din")
    p.sec("sync", s)


def emit_xchg(p, src, ybuf, dst, S, N, dt, m0, m1):
    tps = S // 128
    sc = ExitStack()
    nsl = 2
    ts = [p.sb("xgt", [128, N], dt, scope=sc) for _ in range(nsl)]
    y0s = [p.sb("xgy0", [128, N], dt, scope=sc) for _ in range(nsl)]
    y1s = [p.sb("xgy1", [128, N], dt, scope=sc) for _ in range(nsl)]
    tg = [0] * nsl
    y0g = [0] * nsl
    y1g = [0] * nsl
    for r in range(2 * tps):
        sl = r % nsl
        j, rr = r // tps, r % tps
        def ld(sy, r=r, sl=sl, g=tg[sl]):
            if g:
                sy.wait_ge(p.sem["v"], g)
            p.inc(sy.dma_start(out=ts[sl][:, :], in_=src[r * 128:(r + 1) * 128, :]),
                  "din", 16)
        p.sec("sync", ld)
        dv = p.n["din"]
        def v0(v, sl=sl, dv=dv, g=y0g[sl]):
            v.wait_ge(p.sem["din"], dv)
            if g:
                v.wait_ge(p.sem["dout"], g)
            p.inc(v.tensor_scalar(y0s[sl][:, :], ts[sl][:, :], m0[:, :], None, ALU.mult),
                  "v", 1)
        p.sec("vector", v0)
        vv0 = p.n["v"]
        def s0(sy, sl=sl, vv=vv0, row=(2 * j) * S + rr * 128):
            sy.wait_ge(p.sem["v"], vv)
            p.inc(sy.dma_start(out=ybuf[row:row + 128, :], in_=y0s[sl][:, :]), "dout", 16)
        p.sec("sync", s0)
        y0g[sl] = p.n["dout"]
        def v1(v, sl=sl, g=y1g[sl]):
            if g:
                v.wait_ge(p.sem["dout"], g)
            p.inc(v.tensor_scalar(y1s[sl][:, :], ts[sl][:, :], m1[:, :], None, ALU.mult),
                  "v", 1)
        p.sec("vector", v1)
        tg[sl] = p.n["v"]
        vv1 = p.n["v"]
        def s1(sy, sl=sl, vv=vv1, row=(2 * j + 1) * S + rr * 128):
            sy.wait_ge(p.sem["v"], vv)
            p.inc(sy.dma_start(out=ybuf[row:row + 128, :], in_=y1s[sl][:, :]), "dout", 16)
        p.sec("sync", s1)
        y1g[sl] = p.n["dout"]
    sc.close()
    _coll(p, "ReduceScatter", ybuf, dst, op=ALU.add)


def build_fused(co_np):
    p = P("FUSED")
    p.fused = True
    p.start()
    emit_consts(p, co_np)

    d = p.dint
    u_i = d("i_u", [2048, TH], BF16)
    z_i = d("i_z", [2048, TH], BF16)
    dbc_i = d("i_dbc", [96, TH], F32)
    y_u = d("y_u", [4096, TH], BF16)
    ucc = d("c_u", [2048, TH], BF16)
    dbcg = d("c_dbc", [192, TH], F32)
    uh_i = d("i_uh", [1024, T], BF16)
    dbcf_i = d("i_dbcf", [96, T], F32)
    ys2_i = d("i_ys2", [2048, TH], BF16)
    y_ys2 = d("y_ys2", [4096, TH], BF16)
    ysl3_i = d("c_ysl3", [2048, TH], BF16)
    x1_i = d("i_x1", [1024, TH], F32)
    hal_in = d("i_halin", [1024, 1], F32)
    halg = d("c_halg", [2048, 1], F32)
    pcol_d = d("i_pcol", [1024, 1], F32)
    x2_i = d("i_x2", [1024, TH], F32)
    x2g = d("c_x2g", [2048, TH], F32)
    xf5_i = d("i_xf5", [1024, T], F32)
    z5_i = d("i_z5", [1024, TH], BF16)
    y_z5 = d("y_z5", [2048, TH], BF16)
    zsl6_i = d("c_zsl6", [1024, TH], BF16)
    x4_i = d("i_x4", [1024, TH], F32)
    l2names = ["k", "v", "r", "g", "delta"]
    l2_i = {nm: d(f"i_l2{nm}", [512, TH], BF16) for nm in l2names}
    y_l2 = {nm: d(f"y_l2{nm}", [1024, TH], BF16) for nm in l2names}
    l2cc = {nm: d(f"c_l2{nm}", [512, TH], BF16) for nm in l2names}
    l2f = {nm: d(f"i_l2f{nm}", [256, T], BF16) for nm in l2names}
    bs_i = d("i_bs", [1, TH], F32)
    bsg = d("c_bsg", [2, TH], F32)
    bsf_i = d("i_bsf", [1, T], F32)
    ys8_i = d("i_ys8", [512, TH], BF16)
    y_ys8 = d("y_ys8", [1024, TH], BF16)
    ysl9_i = d("c_ysl9", [512, TH], BF16)
    x6_i = d("i_x6", [1024, TH], F32)
    x6g = d("c_x6g", [2048, TH], F32)
    xf10_i = d("i_xf10", [1024, T], F32)
    z10_i = d("i_z10", [1024, TH], BF16)
    y_z10 = d("y_z10", [2048, TH], BF16)
    zsl11_i = d("c_zsl11", [1024, TH], BF16)

    m0h = p.din("G_m0", [128, 1], F32)
    m1h = p.din("G_m1", [128, 1], F32)
    m0b = p.sb("m0b", [128, 1], F32)
    m1b = p.sb("m1b", [128, 1], F32)
    m1f = m1b
    dma_in(p, m0b, m0h, cast=False)
    dma_in(p, m1b, m1h, cast=False)
    def vw(v):
        p.wait(v, "gin")
        p.wait(v, "din")
    p.sec("vector", vw)

    def stage(pre, io, fn):
        _barrier(p)
        p.pre = pre
        p.io_map = {pre + k: v for k, v in io.items()}
        p.push_frame()
        with p.nc.named_scope(pre.rstrip("_")):
            fn()
        p.pop_frame()
        p.pre = ""
        p.io_map = {}
        _barrier(p)

    stage("A1_", {"u": u_i, "z": z_i, "dbc": dbc_i}, lambda: build_A1(co_np, p=p))
    emit_xchg(p, u_i, y_u, ucc, 1024, TH, BF16, m0b, m1b)
    _dd(p, uh_i[:, 0:TH], ucc[0:1024, :])
    _dd(p, uh_i[:, TH:T], ucc[1024:2048, :])
    _coll(p, "AllGather", dbc_i, dbcg)
    _dd(p, dbcf_i[:, 0:TH], dbcg[0:96, :])
    _dd(p, dbcf_i[:, TH:T], dbcg[96:192, :])

    stage("A2_", {"dbc": dbcf_i, "u": uh_i, "ys": ys2_i},
          lambda: build_A2(co_np, p=p, fused_out=True))
    emit_xchg(p, ys2_i, y_ys2, ysl3_i, 1024, TH, BF16, m0b, m1b)

    stage("A3_", {"ysl": ysl3_i, "u": u_i, "z": z_i, "x1": x1_i},
          lambda: build_A3(co_np, p=p))
    _dd(p, hal_in, x1_i[:, TH - 1:TH], slow=True)
    _barrier(p)
    _coll(p, "AllGather", hal_in, halg)
    hsc = ExitStack()
    ht = p.sb("halt", [128, 8], F32, scope=hsc)
    hy = p.sb("haly", [128, 8], F32, scope=hsc)
    def hl(sy):
        p.wait(sy, "cc")
        p.inc(sy.dma_start(out=ht[:, :], in_=halg[0:1024, :]), "din", 16)
    p.sec("sync", hl)
    hdv = p.n["din"]
    def hv(v, hdv=hdv):
        v.wait_ge(p.sem["din"], hdv)
        p.inc(v.tensor_scalar(hy[:, :], ht[:, :], m1f[:, :], None, ALU.mult), "v", 1)
    p.sec("vector", hv)
    hvv = p.n["v"]
    def hs(sy, hvv=hvv):
        sy.wait_ge(p.sem["v"], hvv)
        p.inc(sy.dma_start(out=pcol_d, in_=hy[:, :]), "dout", 16)
    p.sec("sync", hs)
    hsc.close()

    stage("A4_", {"x1own": x1_i, "pcol": pcol_d, "x2": x2_i},
          lambda: build_A4(co_np, p=p, split_in=True))
    _coll(p, "AllGather", x2_i, x2g)
    _dd(p, xf5_i[:, 0:TH], x2g[0:1024, :])
    _dd(p, xf5_i[:, TH:T], x2g[1024:2048, :])

    stage("A5_", {"xf": xf5_i, "z": z5_i},
          lambda: build_attn(co_np, biased=True, name="A5", p=p, fused_out=True))
    emit_xchg(p, z5_i, y_z5, zsl6_i, 512, TH, BF16, m0b, m1b)

    stage("A6_", {"zsl": zsl6_i, "x2": x2_i, "x4": x4_i},
          lambda: build_A6(co_np, p=p))

    stage("A7_", dict({nm: l2_i[nm] for nm in l2names}, x=x4_i, bscale=bs_i),
          lambda: build_A7(co_np, p=p))
    for nm in l2names:
        emit_xchg(p, l2_i[nm], y_l2[nm], l2cc[nm], 256, TH, BF16, m0b, m1b)
        _dd(p, l2f[nm][:, 0:TH], l2cc[nm][0:256, :])
        _dd(p, l2f[nm][:, TH:T], l2cc[nm][256:512, :])
    _coll(p, "AllGather", bs_i, bsg)
    _dd(p, bsf_i[0:1, 0:TH], bsg[0:1, :])
    _dd(p, bsf_i[0:1, TH:T], bsg[1:2, :])

    stage("A8_", {"kf": l2f["k"], "vf": l2f["v"], "rf": l2f["r"], "gf": l2f["g"],
                  "df": l2f["delta"], "bscale": bsf_i, "ys": ys8_i},
          lambda: build_A8(co_np, p=p, fused_out=True))
    emit_xchg(p, ys8_i, y_ys8, ysl9_i, 256, TH, BF16, m0b, m1b)

    stage("A9_", {"ysl": ysl9_i, "x4": x4_i, "x6": x6_i},
          lambda: build_A9(co_np, p=p))
    _coll(p, "AllGather", x6_i, x6g)
    _dd(p, xf10_i[:, 0:TH], x6g[0:1024, :])
    _dd(p, xf10_i[:, TH:T], x6g[1024:2048, :])

    stage("A10_", {"xf": xf10_i, "z": z10_i},
          lambda: build_attn(co_np, biased=True, name="A10", p=p, fused_out=True))
    emit_xchg(p, z10_i, y_z10, zsl11_i, 512, TH, BF16, m0b, m1b)

    stage("A11_", {"zsl": zsl11_i, "x6": x6_i},
          lambda: build_A11(co_np, p=p))

    def fin(s):
        for sm in ("dout", "zd", "zg", "din", "gin"):
            p.wait(s, sm)
    p.sec("sync", fin)
    p.stack.close()
    return p.nc


_FUSE_KEYS = {
    "A1": ["xT", "WinT", "convw", "convb", "WxT"],
    "A2": ["WdtT", "bdt", "AL", "PD4", "PBq", "PCq", "SEL"],
    "A3": ["Dsk", "WoutT", "xres"],
    "A4": ["mur", "muk", "WrT", "WkT", "WvT"],
    "A5": ["WQ", "WK", "WV", "bQ", "bK", "bV", "MSK"],
    "A6": ["WO", "bO", "rmsw", "gwT", "W1all", "b1c", "W2all", "b2c", "E8"],
    "A7": ["Wk", "Wv", "Wr", "Wg", "Wsel", "dtW", "dtbc"],
    "A8": ["bdc", "tfc", "MSK16"],
    "A9": ["Wo", "W1", "b1c", "W2", "b2c"],
    "A10": ["WQ", "WK", "WV", "bQ", "bK", "bV", "MSK"],
    "A11": ["WoT", "wR", "wI", "FR", "FI", "GR", "GIn", "biasc"],
}


def host_fused_inmaps(inputs, co_np):
    dz8 = [np.zeros((1024, TH), np.float32) for _ in range(NCORE)]
    dzb = [{"k": np.zeros((512, TH), BF), "v": np.zeros((512, TH), BF), "r": np.zeros((512, TH), BF),
            "g": np.zeros((512, TH), BF), "delta": np.zeros((512, TH), BF),
            "bscale": np.zeros((1, TH), np.float32), "u": np.zeros((2048, TH), BF),
            "z": np.zeros((2048, TH), BF), "dbc": np.zeros((96, TH), np.float32),
            "ys": np.zeros((1024, TH), BF), "x1": dz8[0]} for _ in range(NCORE)]
    per_stage = {
        "A1": host_A1_inmaps(inputs, co_np),
        "A2": host_A2_inmaps(inputs, co_np, dzb),
        "A3": host_A3_inmaps(inputs, co_np, dzb, dzb),
        "A4": host_A4_inmaps(inputs, co_np, dz8),
        "A5": host_A5_inmaps(inputs, co_np, dz8),
        "A6": host_A6_inmaps(inputs, co_np, [a["ys"] for a in dzb], dz8),
        "A7": host_A7_inmaps(inputs, co_np, dz8),
        "A8": host_A8_inmaps(inputs, co_np, dzb),
        "A9": host_A9_inmaps(inputs, co_np, [a["ys"][:512] for a in dzb], dz8),
        "A10": host_A10_inmaps(inputs, co_np, dz8),
        "A11": host_A11_inmaps(inputs, co_np, [a["ys"] for a in dzb], dz8),
    }
    base = const_inmap(co_np)
    maps = []
    for c in range(NCORE):
        m = dict(base)
        i = c % 2
        m["G_m0"] = np.full((128, 1), 1.0 - i, np.float32)
        m["G_m1"] = np.full((128, 1), float(i), np.float32)
        for st_, keys in _FUSE_KEYS.items():
            sm = per_stage[st_][c]
            for k in keys:
                m[st_ + "_" + k] = sm[k]
        maps.append(m)
    return maps


def kernel_fused(**inputs):
    inputs = {k: np.ascontiguousarray(np.asarray(v, dtype=np.float32)) for k, v in inputs.items()}
    co_np = host_consts()
    nc = _prog("FUSED", lambda: build_fused(co_np))
    maps = host_fused_inmaps(inputs, co_np)
    res = run_bass_kernel_spmd(nc, maps, core_ids=list(range(NCORE))).results
    out = np.zeros((B, T, D), dtype=np.float32)
    for c in range(NCORE):
        b, i = c // 2, c % 2
        out[b, i * TH:(i + 1) * TH] = np.asarray(res[c]["A11_x8"], np.float32).T
    return out


# version 05 (final)
# speedup vs baseline: 1.3329x; 1.0159x over previous
import math
from contextlib import ExitStack

import numpy as np
import ml_dtypes

import concourse.bass as bass
import concourse.mybir as mybir
from concourse.bass_utils import run_bass_kernel_spmd

F32 = mybir.dt.float32
BF16 = mybir.dt.bfloat16
I32 = mybir.dt.int32
U32 = mybir.dt.uint32
AF = mybir.ActivationFunctionType
ALU = mybir.AluOpType
AX = mybir.AxisListType

NCORE = 8
B = 4
T = 4096
TH = 2048
D = 1024
BF = ml_dtypes.bfloat16

_prog_cache = {}
DEBUG_A1 = False


def cdiv(a, b):
    return (a + b - 1) // b


class P:

    def __init__(self, name):
        self.nc = bass.Bass("TRN2", target_bir_lowering=False)
        self.name = name
        self.stack = ExitStack()
        self.block = None
        self.n = {}
        self.sem = {}
        self._uid = 0
        self.pre = ""
        self.io_map = {}
        self.fused = False
        self.started = False
        self._frames = []
        self._co = None

    def uid(self):
        self._uid += 1
        return self._uid

    def start(self, sems=("din", "gin", "mm", "v", "sc", "gp", "dout", "zd", "zg", "cc")):
        if self.started:
            return
        self.started = True
        for s in sems:
            self.sem[s] = self.stack.enter_context(self.nc.semaphore(s))
            self.n[s] = 0
        self.block = self.stack.enter_context(self.nc.Block())

    def push_frame(self):
        self._frames.append(ExitStack())

    def pop_frame(self):
        self._frames.pop().close()

    def sb(self, name, shape, dtype=F32, scope=None):
        st = scope if scope is not None else (self._frames[-1] if self._frames else self.stack)
        return st.enter_context(self.nc.sbuf_tensor(f"{name}{self.uid()}", shape, dtype))

    def ps(self, name, shape, dtype=F32, scope=None):
        st = scope if scope is not None else (self._frames[-1] if self._frames else self.stack)
        return st.enter_context(self.nc.psum_tensor(f"{name}{self.uid()}", shape, dtype))

    def din(self, name, shape, dtype=F32):
        nm = self.pre + name
        if nm in self.io_map:
            return self.io_map[nm]
        return self.nc.dram_tensor(nm, shape, dtype, kind="ExternalInput").ap()

    def dout(self, name, shape, dtype=F32):
        nm = self.pre + name
        if nm in self.io_map:
            return self.io_map[nm]
        return self.nc.dram_tensor(nm, shape, dtype, kind="ExternalOutput").ap()

    def dint(self, name, shape, dtype=F32):
        t = self.nc.dram_tensor(name, shape, dtype, kind="Internal")
        return t.ap()

    def sec(self, eng, fn):
        getattr(self.block, eng)(fn)

    def inc(self, instr, semname, k=1):
        instr.then_inc(self.sem[semname], k)
        self.n[semname] += k
        return self.n[semname]

    def wait(self, env, semname, val=None):
        v = self.n[semname] if val is None else val
        if v > 0:
            env.wait_ge(self.sem[semname], v)

    def close(self):
        if self.fused:
            return
        self.stack.close()


def _ap(x):
    return x[:, :] if not isinstance(x, bass.AP) else x


def dma_in(p, dst_sb, src_ap, cast=False, sem=None):
    dst_sb = _ap(dst_sb)
    src_ap = _ap(src_ap)
    if cast:
        sem = sem or "gin"
        def _(g):
            p.inc(g.dma_start(out=dst_sb, in_=src_ap), sem, 16)
        p.sec("gpsimd", _)
    else:
        sem = sem or "din"
        def _(s):
            p.inc(s.dma_start(out=dst_sb, in_=src_ap), sem, 16)
        p.sec("sync", _)


def load_fm(p, dst_sb, dram_ap, rows, cols, dst_col_stride=None, cast=True, sem=None,
            row0=0, col0=0):
    cstride = dst_col_stride if dst_col_stride is not None else cols
    nk = cdiv(rows, 128)
    for kt in range(nk):
        r = min(128, rows - kt * 128)
        dma_in(p, dst_sb[:r, kt * cstride: kt * cstride + cols],
               dram_ap[row0 + kt * 128: row0 + kt * 128 + r, col0: col0 + cols],
               cast=cast, sem=sem)


class Pool:

    def __init__(self, p, nb, width=512, parts=128):
        self.p = p
        self.tiles = [p.ps("pool", [parts, width], F32) for _ in range(nb)]
        self.freed = [None] * nb
        self.rr = 0
        self.nb = nb


def emit_mm(p, pool, jobs, evacs, mm_sem="mm", wait_before=()):
    ename, esem = evacs
    ebase = p.n[esem]
    plan = []

    def tsec(t):
        for wsem, wval in wait_before:
            p.wait(t, wsem, wval)
        for i, (mm_list, m, w, evac_fn) in enumerate(jobs):
            j = pool.rr % pool.nb
            pool.rr += 1
            if pool.freed[j] is not None:
                fsem, fval = pool.freed[j]
                t.wait_ge(p.sem[fsem], fval)
            tile = pool.tiles[j]
            for ki, (lh, rh) in enumerate(mm_list):
                ins = t.matmul(tile[:m, :w], lh, rh, start=(ki == 0), stop=(ki == len(mm_list) - 1))
            cnt = p.inc(ins, mm_sem)
            pred = ebase + i + 1
            pool.freed[j] = (esem, pred)
            plan.append((cnt, tile, evac_fn, pred))
    p.sec("tensor", tsec)

    def esec(env):
        for cnt, tile, evac_fn, pred in plan:
            env.wait_ge(p.sem[mm_sem], cnt)
            evac_fn(env, tile, pred)
            assert p.n[esem] == pred, f"evac must inc {esem} exactly once ({p.n[esem]} != {pred})"
    p.sec(ename, esec)


def emit_postln(p, s_sb, out_sb, Tt, co, eps=1e-5, ready=(), cstride=None, col0=0):
    cs = cstride if cstride is not None else Tt

    def sv(mt, c0, w):
        return s_sb[:, mt * cs + col0 + c0: mt * cs + col0 + c0 + w]

    scope = ExitStack()
    sbf = p.sb("lnsbf", [128, 8 * 512], BF16, scope)
    sqf = p.sb("lnsqf", [128, 8 * 512], BF16, scope)
    st_mu = p.sb("lnmu", [1, 512], F32, scope)
    st_rs = p.sb("lnrs", [1, 512], F32, scope)
    st_t2 = p.sb("lnt2", [1, 512], F32, scope)
    ps_a = p.ps("lnpa", [128, 512], F32, scope)
    ps_b = p.ps("lnpb", [128, 512], F32, scope)
    ps_mu = p.ps("lnpmu", [128, 512], F32, scope)
    ps_rs = p.ps("lnprs", [128, 512], F32, scope)
    NG = cdiv(Tt, 512)
    mm_prev = 0

    for ng in range(NG):
        c0 = ng * 512
        w = min(512, Tt - c0)

        def vcast(v, c0=c0, w=w, ng=ng, mm_prev=mm_prev):
            if ng == 0:
                for sem, val in ready:
                    p.wait(v, sem, val)
            else:
                v.wait_ge(p.sem["mm"], mm_prev)
            for mt in range(8):
                p.inc(v.tensor_copy(sbf[:, mt * 512: mt * 512 + w], sv(mt, c0, w)), "v")
        p.sec("vector", vcast)
        v_cast = p.n["v"]

        def ssq(sc, c0=c0, w=w, ng=ng, mm_prev=mm_prev):
            if ng == 0:
                for sem, val in ready:
                    p.wait(sc, sem, val)
            else:
                sc.wait_ge(p.sem["mm"], mm_prev)
            for mt in range(8):
                p.inc(sc.activation(sqf[:, mt * 512: mt * 512 + w], sv(mt, c0, w), AF.Square), "sc")
        p.sec("scalar", ssq)
        sc_sq = p.n["sc"]

        def tstats(t, c0=c0, w=w, ng=ng):
            t.wait_ge(p.sem["v"], v_cast)
            t.wait_ge(p.sem["sc"], sc_sq)
            if ng > 0:
                p.wait(t, "v")
            for mt in range(8):
                t.matmul(ps_a[0:1, :w], co["ones_bf"][:, :], sbf[:, mt * 512: mt * 512 + w],
                         start=(mt == 0), stop=(mt == 7))
            for mt in range(8):
                ins = t.matmul(ps_b[0:1, :w], co["ones_bf"][:, :], sqf[:, mt * 512: mt * 512 + w],
                               start=(mt == 0), stop=(mt == 7))
            p.inc(ins, "mm")
        p.sec("tensor", tstats)
        mm_stats = p.n["mm"]

        def vstat(v, w=w):
            v.wait_ge(p.sem["mm"], mm_stats)
            v.tensor_scalar(st_mu[0:1, :w], ps_a[0:1, :w], 1.0 / 1024, None, ALU.mult)
            v.tensor_scalar(st_t2[0:1, :w], ps_b[0:1, :w], 1.0 / 1024, float(eps), ALU.mult, ALU.add)
            v.tensor_tensor(st_rs[0:1, :w], st_mu[0:1, :w], st_mu[0:1, :w], ALU.mult)
            ins = v.tensor_tensor(st_t2[0:1, :w], st_t2[0:1, :w], st_rs[0:1, :w], ALU.subtract)
            p.inc(ins, "v")
        p.sec("vector", vstat)
        v_var = p.n["v"]

        def srstd(sc, w=w):
            sc.wait_ge(p.sem["v"], v_var)
            sc.activation(st_t2[0:1, :w], st_t2[0:1, :w], AF.Ln)
            ins = sc.activation(st_rs[0:1, :w], st_t2[0:1, :w], AF.Exp, scale=-0.5)
            p.inc(ins, "sc")
        p.sec("scalar", srstd)
        sc_rstd = p.n["sc"]

        def tbc(t, w=w):
            t.wait_ge(p.sem["sc"], sc_rstd)
            t.matmul(ps_mu[:, :w], co["onesrow"][:, :], st_mu[0:1, :w], start=True, stop=True)
            ins = t.matmul(ps_rs[:, :w], co["onesrow"][:, :], st_rs[0:1, :w], start=True, stop=True)
            p.inc(ins, "mm")
        p.sec("tensor", tbc)
        mm_prev = p.n["mm"]
        mm_bc = mm_prev

        def vnorm(v, c0=c0, w=w):
            v.wait_ge(p.sem["mm"], mm_bc)
            for mt in range(8):
                v.tensor_tensor(out_sb[:, mt * Tt + c0: mt * Tt + c0 + w],
                                sv(mt, c0, w), ps_mu[:, :w], ALU.subtract)
                ins = v.tensor_tensor(out_sb[:, mt * Tt + c0: mt * Tt + c0 + w],
                                      out_sb[:, mt * Tt + c0: mt * Tt + c0 + w], ps_rs[:, :w], ALU.mult)
            p.inc(ins, "v")
        p.sec("vector", vnorm)
    scope.close()


def softplus_evac(p, env_s, psum_ap, out_ap, tmp1, tmp2, bias_col):
    env_s.activation(tmp1, psum_ap, AF.Abs, bias=bias_col)
    env_s.activation(tmp2, tmp1, AF.Exp, scale=-1.0)
    return env_s.activation(tmp1, tmp2, AF.Ln, bias=1.0)


def host_consts():
    c = {}
    c["ident_bf"] = np.eye(128, dtype=BF)
    c["ident_f"] = np.eye(128, dtype=np.float32)
    c["ones_bf"] = np.ones((128, 1), dtype=np.float32)
    c["onesrow"] = np.ones((1, 128), dtype=np.float32)
    c["onesrow_bf"] = np.ones((1, 128), dtype=BF)
    return c


def emit_consts(p, co_np):
    if p._co is not None:
        return p._co
    co = {}
    ident_d = p.din("c_ident", [128, 128], BF16)
    identf_d = p.din("c_identf", [128, 128], F32)
    ones_d = p.din("c_ones", [128, 1], F32)
    onesrow_d = p.din("c_onesrow", [1, 128], F32)
    co["ident"] = p.sb("ident", [128, 128], BF16, scope=p.stack)
    co["identf"] = p.sb("identf", [128, 128], F32, scope=p.stack)
    co["ones_bf"] = p.sb("onesb", [128, 1], BF16, scope=p.stack)
    co["onesrow"] = p.sb("onesr", [1, 128], F32, scope=p.stack)
    co["onesrow_bf"] = p.sb("onesrb", [1, 128], BF16, scope=p.stack)
    dma_in(p, co["ident"], ident_d)
    onesrb_d = p.din("c_onesrb", [1, 128], BF16)
    dma_in(p, co["onesrow_bf"], onesrb_d)
    dma_in(p, co["identf"], identf_d)
    dma_in(p, co["ones_bf"], ones_d, cast=True)
    dma_in(p, co["onesrow"], onesrow_d)
    p._co = co
    return co


def const_inmap(co_np):
    return {"c_ident": co_np["ident_bf"], "c_identf": co_np["ident_f"],
            "c_ones": co_np["ones_bf"], "c_onesrow": co_np["onesrow"],
            "c_onesrb": co_np["onesrow_bf"]}


def build_A1(co_np, p=None):
    if p is None:
        p = P("A1")
    TX = 2176
    xT_d = p.din("xT", [1024, TX])
    win_d = p.din("WinT", [1024, 4096])
    cw_d = p.din("convw", [128, 64])
    cb_d = p.din("convb", [128, 16])
    wx_d = p.din("WxT", [2048, 96])
    u_d = p.dout("u", [2048, TH], BF16)
    xin_dbg = p.dout("xin_dbg", [2048, 2176], BF16) if DEBUG_A1 else None
    xc_dbg = p.dout("xc_dbg", [128, TH]) if DEBUG_A1 else None
    wb_dbg = p.dout("wb_dbg", [1024, 4096], BF16) if DEBUG_A1 else None
    z_d = p.dout("z", [2048, TH], BF16)
    dbc_d = p.dout("dbc", [96, TH])

    p.start()
    co = emit_consts(p, co_np)
    cw = p.sb("cw", [128, 64])
    cb = p.sb("cb", [128, 16])
    dma_in(p, cw, cw_d)
    dma_in(p, cb, cb_d)

    xin = p.sb("xin", [128, 16 * TX], BF16)
    zst = p.sb("zst", [128, 2 * TH], BF16)
    pool = Pool(p, 4)
    mm_scope = ExitStack()
    xtb = p.sb("xtb", [128, 8 * TX], BF16, mm_scope)
    wb = p.sb("wb", [128, 8 * 4096], BF16, mm_scope)
    load_fm(p, xtb, xT_d, 1024, TX)
    load_fm(p, wb, win_d, 1024, 4096)
    gin_all = p.n["gin"]

    jobs = []
    zplan = []
    for mt in range(32):
        is_z = mt >= 16
        ngs = [(128 + i * 512, 512) for i in range(4)] if is_z else \
              [(0, 512), (512, 512), (1024, 512), (1536, 512), (2048, 128)]
        for gi, (c0, w) in enumerate(ngs):
            mm_list = [(wb[:, k * 4096 + mt * 128: k * 4096 + (mt + 1) * 128],
                        xtb[:, k * TX + c0: k * TX + c0 + w]) for k in range(8)]
            if not is_z:
                def ev(env, ps, pred, mt=mt, c0=c0, w=w):
                    ins = env.tensor_copy(xin[:, mt * TX + c0: mt * TX + c0 + w], ps[:, :w])
                    p.inc(ins, "v")
            else:
                zt = mt - 16
                buf = zt % 2
                def ev(env, ps, pred, zt=zt, buf=buf, c0=c0, w=w, gi=gi):
                    if gi == 0 and zt >= 2:
                        p.wait(env, "zd", 16 * (zt - 1))
                    ins = env.tensor_copy(zst[:, buf * TH + (c0 - 128): buf * TH + (c0 - 128) + w], ps[:, :w])
                    cnt = p.inc(ins, "v")
                    if gi == 3:
                        zplan.append((cnt, zt, buf))
            jobs.append((mm_list, 128, w, ev))
    if DEBUG_A1:
        def wdma(s):
            p.wait(s, "gin", gin_all)
            for kt in range(8):
                p.inc(s.dma_start(out=wb_dbg[kt * 128:(kt + 1) * 128, :], in_=wb[:, kt * 4096:(kt + 1) * 4096]), "dout", 16)
        p.sec("sync", wdma)
        wb_dump_dout = p.n["dout"]

    emit_mm(p, pool, jobs, ("vector", "v"), wait_before=[("gin", gin_all)] + ([("dout", wb_dump_dout)] if DEBUG_A1 else []))

    def zdma(s):
        for cnt, zt, buf in zplan:
            s.wait_ge(p.sem["v"], cnt)
            p.inc(s.dma_start(out=z_d[zt * 128:(zt + 1) * 128, :], in_=zst[:, buf * TH:(buf + 1) * TH]), "zd", 16)
    p.sec("sync", zdma)
    v_xz = p.n["v"]
    mm_scope.close()

    if DEBUG_A1:
        def xdma(s):
            s.wait_ge(p.sem["v"], v_xz)
            for et in range(16):
                p.inc(s.dma_start(out=xin_dbg[et * 128:(et + 1) * 128, :], in_=xin[:, et * TX:(et + 1) * TX]), "dout", 16)
        p.sec("sync", xdma)

    ubuf = p.sb("ubuf", [128, 16 * TH], BF16)
    acc = p.sb("acc", [128, 2 * TH])
    sc_prev_silu = p.n["sc"]
    for et in range(16):
        def vconv(v, et=et, sc_prev_silu=sc_prev_silu):
            if et == 0:
                v.wait_ge(p.sem["v"], v_xz)
                p.wait(v, "din")
                if DEBUG_A1:
                    p.wait(v, "dout", wb_dump_dout)
            else:
                v.wait_ge(p.sem["sc"], sc_prev_silu)
            base = et * TX
            v.tensor_scalar(acc[:, :TH], xin[:, base + 125: base + 125 + TH],
                            cw[:, et * 4: et * 4 + 1], cb[:, et: et + 1], ALU.mult, ALU.add)
            v.scalar_tensor_tensor(acc[:, TH:], xin[:, base + 126: base + 126 + TH],
                                   cw[:, et * 4 + 1: et * 4 + 2], acc[:, :TH], ALU.mult, ALU.add)
            v.scalar_tensor_tensor(acc[:, :TH], xin[:, base + 127: base + 127 + TH],
                                   cw[:, et * 4 + 2: et * 4 + 3], acc[:, TH:], ALU.mult, ALU.add)
            ins = v.scalar_tensor_tensor(acc[:, TH:], xin[:, base + 128: base + 128 + TH],
                                         cw[:, et * 4 + 3: et * 4 + 4], acc[:, :TH], ALU.mult, ALU.add)
            p.inc(ins, "v")
        p.sec("vector", vconv)
        v_c = p.n["v"]
        if DEBUG_A1 and et == 0:
            def dacc(s, v_c=v_c):
                s.wait_ge(p.sem["v"], v_c)
                p.inc(s.dma_start(out=xc_dbg, in_=acc[:, TH:]), "dout", 16)
            p.sec("sync", dacc)

        def ssilu(sc, et=et):
            sc.wait_ge(p.sem["v"], v_c)
            ins = sc.activation(ubuf[:, et * TH:(et + 1) * TH], acc[:, TH:], AF.Silu)
            p.inc(ins, "sc")
        p.sec("scalar", ssilu)
        sc_prev_silu = p.n["sc"]

    sc_u = p.n["sc"]

    def udma(s):
        s.wait_ge(p.sem["sc"], sc_u)
        for et in range(16):
            p.inc(s.dma_start(out=u_d[et * 128:(et + 1) * 128, :], in_=ubuf[:, et * TH:(et + 1) * TH]), "dout", 16)
    p.sec("sync", udma)

    wxb = p.sb("wxb", [128, 16 * 96], BF16)
    def _gate(g):
        g.wait_ge(p.sem["v"], v_xz)
    p.sec("gpsimd", _gate)
    load_fm(p, wxb, wx_d, 2048, 96)
    gin2 = p.n["gin"]
    dbc_sb = p.sb("dbcsb", [96, TH])
    jobs = []
    for gi in range(4):
        c0 = gi * 512
        mm_list = [(wxb[:, k * 96:(k + 1) * 96], ubuf[:, k * TH + c0: k * TH + c0 + 512]) for k in range(16)]
        def ev(env, ps, pred, c0=c0):
            ins = env.tensor_copy(dbc_sb[:, c0:c0 + 512], ps[:96, :512])
            p.inc(ins, "v")
        jobs.append((mm_list, 96, 512, ev))
    emit_mm(p, pool, jobs, ("vector", "v"),
            wait_before=[("gin", gin2), ("sc", sc_u)])
    v_dbc = p.n["v"]

    def ddma(s):
        s.wait_ge(p.sem["v"], v_dbc)
        p.inc(s.dma_start(out=dbc_d, in_=dbc_sb[:96, :]), "dout", 16)
        p.wait(s, "dout")
    p.sec("sync", ddma)
    p.close()
    return p.nc


def host_A1_inmaps(inputs, co_np):
    x = inputs["L0_attn_x"]
    WinT = np.ascontiguousarray(inputs["L0_attn_W_in"].T)
    convw = inputs["L0_attn_conv_w"].reshape(16, 128, 4).transpose(1, 0, 2).reshape(128, 64)
    convb = inputs["L0_attn_conv_b"].reshape(16, 128).T.copy()
    WxT = np.ascontiguousarray(inputs["L0_attn_W_xproj"].T)
    base = const_inmap(co_np)
    maps = []
    for c in range(NCORE):
        b, i = c // 2, c % 2
        t0 = i * TH
        xs = np.zeros((2176, 1024), dtype=np.float32)
        lo = max(0, t0 - 128)
        xs[128 - (t0 - lo):128 + TH] = x[b, lo:t0 + TH]
        m = dict(base)
        m.update(xT=np.ascontiguousarray(xs.T), WinT=WinT, convw=convw, convb=convb, WxT=WxT)
        maps.append(m)
    return maps




def build_A2(co_np, p=None, fused_out=False):
    if p is None:
        p = P("A2")
    dbc_d = p.din("dbc", [96, T])
    u_d = p.din("u", [1024, T], BF16)
    wdt_d = p.din("WdtT", [64, 1024])
    bdt_d = p.din("bdt", [128, 8])
    al_d = p.din("AL", [128, 128])
    pd_d = p.din("PD4", [4 * 128, 128], BF16)
    pb_d = p.din("PBq", [4 * 96, 128], BF16)
    pc_d = p.din("PCq", [4 * 96, 128], BF16)
    sel_d = p.din("SEL", [128, 32], BF16)
    ys_d = p.dout("ys", [2048, TH] if fused_out else [1024, T], BF16)

    p.start()
    dbc = p.sb("dbc", [96, T], BF16)
    wdt = p.sb("wdt", [64, 1024], BF16)
    bdt = p.sb("bdt", [128, 8])
    alog = p.sb("alog", [128, 128])
    aneg = p.sb("aneg", [128, 128])
    pd4 = p.sb("pd4", [128, 4 * 128], BF16)
    pbq = p.sb("pbq", [96, 4 * 128], BF16)
    pcq = p.sb("pcq", [96, 4 * 128], BF16)
    sel = p.sb("sel", [128, 32], BF16)
    dtb = p.sb("dtb", [128, 8 * T], BF16)
    ut = p.sb("ut", [128, 2 * T], BF16)
    dtu = p.sb("dtu", [128, T], BF16)
    dub = p.sb("dub", [128, 512], BF16)
    beta = p.sb("beta", [128, T], BF16)
    hbuf = p.sb("hbuf", [128, T], BF16)
    hc = [p.sb(f"hc{q}", [128, T], BF16) for q in range(4)]
    ystg = p.sb("ystg", [32, 2 * T], BF16)
    spt = p.sb("spt", [128, 512])

    dma_in(p, dbc, dbc_d, cast=True)
    dma_in(p, wdt, wdt_d, cast=True)
    dma_in(p, bdt, bdt_d)
    dma_in(p, alog, al_d)
    for gm in range(4):
        dma_in(p, pd4[:, gm * 128:(gm + 1) * 128], pd_d[gm * 128:(gm + 1) * 128, :])
        dma_in(p, pbq[:, gm * 128:(gm + 1) * 128], pb_d[gm * 96:(gm + 1) * 96, :])
        dma_in(p, pcq[:, gm * 128:(gm + 1) * 128], pc_d[gm * 96:(gm + 1) * 96, :])
    dma_in(p, sel, sel_d)
    gin0, din0 = p.n["gin"], p.n["din"]

    def sa(sc):
        p.wait(sc, "din", din0)
        p.inc(sc.activation(aneg[:, :], alog[:, :], AF.Exp), "sc")
    p.sec("scalar", sa)
    sc0 = p.n["sc"]

    def va(v):
        v.wait_ge(p.sem["sc"], sc0)
        p.inc(v.tensor_scalar(aneg[:, :], aneg[:, :], -1.0, None, ALU.mult), "v")
    p.sec("vector", va)
    v_aneg = p.n["v"]

    pool = Pool(p, 2)
    jobs = []
    for mt in range(8):
        for tg in range(8):
            c0 = tg * 512
            mm_list = [(wdt[:, mt * 128:(mt + 1) * 128], dbc[0:64, c0:c0 + 512])]

            def ev(env, ps, pred, mt=mt, c0=c0):
                env.activation(spt[:, :], ps[:, :512], AF.Exp, bias=bdt[:, mt:mt + 1])
                ins = env.activation(dtb[:, mt * T + c0: mt * T + c0 + 512], spt[:, :], AF.Ln, bias=1.0)
                p.inc(ins, "sc")
            jobs.append((mm_list, 128, 512, ev))
    emit_mm(p, pool, jobs, ("scalar", "sc"), wait_before=[("gin", gin0), ("din", din0)])
    sc_dt = p.n["sc"]

    ps_dtb = p.ps("psdtb", [128, 512])
    ps_da = p.ps("psda", [128, 512])
    ps_dub = p.ps("psdub", [128, 512])
    ps_bb = p.ps("psbb", [128, 512])
    ps_cb2 = [p.ps("pscb", [128, 512]), p.ps("pscb2", [128, 512])]
    ps_y = [pool.tiles[0][0:32, 0:512], pool.tiles[1][0:32, 0:512]]

    guard = {"dtb": None, "da": None, "dub": None, "bb": None, "cb0": None,
             "cb1": None, "y0": pool.freed[0], "y1": pool.freed[1]}
    dtu_ready = 0

    NTG = 8
    for g in range(32):
        mt = g // 4
        gm = g % 4
        if g % 4 == 0:
            ub = (mt % 2)
            def lu(gq, mt=mt, ub=ub):
                if mt >= 2:
                    p.wait(gq, "v", None)
                p.inc(gq.dma_start(out=ut[:, ub * T:(ub + 1) * T], in_=u_d[mt * 128:(mt + 1) * 128, :]), "gin", 16)
            p.sec("gpsimd", lu)
            gin_u = p.n["gin"]

            def vdtu(v, mt=mt, ub=ub):
                v.wait_ge(p.sem["gin"], gin_u)
                v.wait_ge(p.sem["sc"], sc_dt)
                if mt >= 1:
                    p.wait(v, "mm")
                ins = v.tensor_tensor(dtu[:, :], dtb[:, mt * T:(mt + 1) * T], ut[:, ub * T:(ub + 1) * T], ALU.mult)
                p.inc(ins, "v")
            p.sec("vector", vdtu)
            dtu_ready = p.n["v"]

        for q in range(4):
            acol = aneg[:, (g * 4 + q):(g * 4 + q) + 1]
            for tg in range(NTG):
                c0 = tg * 512
                par = ((g * 4 + q) * NTG + tg) % 2
                ps_cb = ps_cb2[par]
                def tmm(t, g=g, q=q, tg=tg, c0=c0, mt=mt, gm=gm, dtu_ready=dtu_ready,
                        par=par, ps_cb=ps_cb):
                    if tg == 0 and q == 0 and g == 0:
                        p.wait(t, "v", v_aneg)
                        t.wait_ge(p.sem["sc"], sc_dt)
                    if tg == 0 and q == 0:
                        t.wait_ge(p.sem["v"], dtu_ready)
                    for nm in ("dtb", "dub", "bb", "cb%d" % par):
                        if guard[nm] is not None:
                            t.wait_ge(p.sem[guard[nm][0]], guard[nm][1])
                    t.matmul(ps_dtb[:, :], pd4[:, gm * 128:(gm + 1) * 128], dtb[:, mt * T + c0: mt * T + c0 + 512], start=True, stop=True)
                    t.matmul(ps_dub[:, :], pd4[:, gm * 128:(gm + 1) * 128], dtu[:, c0:c0 + 512], start=True, stop=True)
                    t.matmul(ps_bb[:, :], pbq[:, q * 128:(q + 1) * 128], dbc[:, c0:c0 + 512], start=True, stop=True)
                    ins = t.matmul(ps_cb[:, :], pcq[:, q * 128:(q + 1) * 128], dbc[:, c0:c0 + 512], start=True, stop=True)
                    p.inc(ins, "mm")
                p.sec("tensor", tmm)
                mm_bc = p.n["mm"]

                def sda(sc, mm_bc=mm_bc, acol=acol):
                    sc.wait_ge(p.sem["mm"], mm_bc)
                    if guard["da"] is not None:
                        sc.wait_ge(p.sem[guard["da"][0]], guard["da"][1])
                    ins = sc.activation(ps_da[:, :], ps_dtb[:, :], AF.Exp, scale=acol)
                    p.inc(ins, "sc")
                p.sec("scalar", sda)
                sc_da = p.n["sc"]
                guard["dtb"] = ("sc", sc_da)

                def vsc(v, g=g, q=q, tg=tg, c0=c0, mm_bc=mm_bc, sc_da=sc_da, ps_cb=ps_cb):
                    v.wait_ge(p.sem["mm"], mm_bc)
                    i1 = p.inc(v.tensor_copy(dub[:, :], ps_dub[:, :]), "v")
                    v.wait_ge(p.sem["v"], i1)
                    i2 = p.inc(v.tensor_tensor(beta[:, c0:c0 + 512], dub[:, :], ps_bb[:, :], ALU.mult), "v")
                    v.wait_ge(p.sem["v"], i2)
                    v.wait_ge(p.sem["sc"], sc_da)
                    init = 0.0 if tg == 0 else hbuf[:, c0 - 1: c0]
                    i3 = p.inc(v.tensor_tensor_scan(hbuf[:, c0:c0 + 512], ps_da[:, :], beta[:, c0:c0 + 512], init, ALU.mult, ALU.add), "v")
                    v.wait_ge(p.sem["v"], i3)
                    ins = v.tensor_tensor(hc[q][:, c0:c0 + 512], hbuf[:, c0:c0 + 512], ps_cb[:, :], ALU.mult)
                    p.inc(ins, "v")
                p.sec("vector", vsc)
                vv = p.n["v"]
                guard["dub"] = ("v", vv - 3)
                guard["bb"] = ("v", vv - 2)
                guard["da"] = ("v", vv - 1)
                guard["cb%d" % par] = ("v", vv)

        yb = g % 2
        for tg in range(NTG):
            c0 = tg * 512
            pyi = tg % 2
            def tsel(t, g=g, tg=tg, c0=c0, pyi=pyi):
                gk = "y%d" % pyi
                if guard[gk] is not None:
                    t.wait_ge(p.sem[guard[gk][0]], guard[gk][1])
                for q in range(4):
                    ins = t.matmul(ps_y[pyi][:, :], sel[:, :], hc[q][:, c0:c0 + 512], start=(q == 0), stop=(q == 3))
                p.inc(ins, "mm")
            p.sec("tensor", tsel)
            mm_y = p.n["mm"]

            def vy(v, g=g, tg=tg, c0=c0, pyi=pyi, yb=yb, mm_y=mm_y):
                v.wait_ge(p.sem["mm"], mm_y)
                if tg == 0 and g >= 2:
                    p.wait(v, "zd", (32 if fused_out else 16) * (g - 1))
                ins = v.tensor_copy(ystg[:, yb * T + c0: yb * T + c0 + 512], ps_y[pyi][:, :])
                p.inc(ins, "v")
            p.sec("vector", vy)
            guard["y%d" % pyi] = ("v", p.n["v"])
        v_y = p.n["v"]

        def dy(s, g=g, yb=yb, v_y=v_y):
            s.wait_ge(p.sem["v"], v_y)
            if fused_out:
                p.inc(s.dma_start(out=ys_d[g * 32:(g + 1) * 32, :], in_=ystg[:32, yb * T: yb * T + TH]), "zd", 16)
                p.inc(s.dma_start(out=ys_d[1024 + g * 32:1024 + (g + 1) * 32, :],
                                  in_=ystg[:32, yb * T + TH:(yb + 1) * T]), "zd", 16)
            else:
                p.inc(s.dma_start(out=ys_d[g * 32:(g + 1) * 32, :], in_=ystg[:32, yb * T:(yb + 1) * T]), "zd", 16)
        p.sec("sync", dy)

    def fin(s):
        p.wait(s, "zd")
    p.sec("sync", fin)
    p.close()
    return p.nc


def a2_consts():
    PD4 = np.zeros((4, 128, 128), dtype=BF)
    for gm in range(4):
        for pp in range(128):
            PD4[gm, gm * 32 + pp // 4, pp] = 1
    PB = np.zeros((4, 96, 128), dtype=BF)
    PC = np.zeros((4, 96, 128), dtype=BF)
    for q in range(4):
        for pp in range(128):
            PB[q, 64 + q * 4 + pp % 4, pp] = 1
            PC[q, 80 + q * 4 + pp % 4, pp] = 1
    SEL = np.zeros((128, 32), dtype=BF)
    for pp in range(128):
        SEL[pp, pp // 4] = 1
    return PD4.reshape(512, 128), PB.reshape(384, 128), PC.reshape(384, 128), SEL


def host_A2_inmaps(inputs, co_np, a1_results):
    PD4, PB, PC, SEL = a2_consts()
    WdtT = np.ascontiguousarray(inputs["L0_attn_W_dt"].T)
    b_dt = inputs["L0_attn_b_dt"]
    A_log = inputs["L0_attn_A_log"]
    maps = []
    for c in range(NCORE):
        b, j = c // 2, c % 2
        ch0 = j * 1024
        dbc_full = np.concatenate([np.asarray(a1_results[2 * b]["dbc"]),
                                   np.asarray(a1_results[2 * b + 1]["dbc"])], axis=1)
        u0 = np.asarray(a1_results[2 * b]["u"])
        u1 = np.asarray(a1_results[2 * b + 1]["u"])
        u_half = np.concatenate([u0[ch0:ch0 + 1024], u1[ch0:ch0 + 1024]], axis=1)
        bdt = b_dt[ch0:ch0 + 1024].reshape(8, 128).T.copy()
        AL = np.zeros((128, 128), dtype=np.float32)
        for g in range(32):
            for q in range(4):
                for pp in range(128):
                    AL[pp, g * 4 + q] = A_log[ch0 + g * 32 + pp // 4, q * 4 + pp % 4]
        m = dict(dbc=np.ascontiguousarray(dbc_full), u=np.ascontiguousarray(u_half),
                 WdtT=np.ascontiguousarray(WdtT[:, ch0:ch0 + 1024]), bdt=bdt, AL=AL,
                 PD4=PD4, PBq=PB, PCq=PC, SEL=SEL)
        maps.append(m)
    return maps


def kernel(**inputs):
    raise NotImplementedError("assembled later")


def build_A3(co_np, p=None):
    if p is None:
        p = P("A3")
    ys_d = p.din("ysl", [2048, TH], BF16)
    u_d = p.din("u", [2048, TH], BF16)
    z_d = p.din("z", [2048, TH], BF16)
    dsk_d = p.din("Dsk", [128, 16])
    wout_d = p.din("WoutT", [2048, 1024])
    xres_d = p.din("xres", [1024, TH])
    x1_d = p.dout("x1", [1024, TH])

    p.start()
    co = emit_consts(p, co_np)
    dsk = p.sb("dsk", [128, 16])
    dma_in(p, dsk, dsk_d)
    xres = p.sb("xres", [128, 8 * TH])
    load_fm(p, xres, xres_d, 1024, TH, cast=False)
    din0 = p.n["din"]

    sc1 = ExitStack()
    ybuf = p.sb("ybuf", [128, 16 * TH], BF16, sc1)
    yst = p.sb("yst", [128, 6 * TH], BF16, sc1)
    tmp = p.sb("tmp", [128, 2 * TH], BF16, sc1)
    szt = p.sb("szt", [128, 2 * TH], BF16, sc1)
    wob = p.sb("wob", [128, 16 * 1024], BF16, sc1)
    load_fm(p, wob, wout_d, 2048, 1024)
    gin_w = p.n["gin"]

    vready = 0
    for et in range(16):
        bb = et % 2
        def ld(g, et=et, bb=bb, vready=vready):
            if et >= 2:
                p.wait(g, "v", vready)
                p.wait(g, "sc")
            g.dma_start(out=yst[:, (bb * 3) * TH:(bb * 3 + 1) * TH], in_=ys_d[et * 128:(et + 1) * 128, :]).then_inc(p.sem["gin"], 16)
            g.dma_start(out=yst[:, (bb * 3 + 1) * TH:(bb * 3 + 2) * TH], in_=u_d[et * 128:(et + 1) * 128, :]).then_inc(p.sem["gin"], 16)
            g.dma_start(out=yst[:, (bb * 3 + 2) * TH:(bb * 3 + 3) * TH], in_=z_d[et * 128:(et + 1) * 128, :]).then_inc(p.sem["gin"], 16)
            p.n["gin"] += 48
        p.sec("gpsimd", ld)
        gin_t = p.n["gin"]

        def vt(v, et=et, bb=bb):
            v.wait_ge(p.sem["gin"], gin_t)
            p.wait(v, "din", din0)
            ins = v.scalar_tensor_tensor(tmp[:, bb * TH:(bb + 1) * TH], yst[:, (bb * 3 + 1) * TH:(bb * 3 + 2) * TH],
                                         dsk[:, et:et + 1], yst[:, (bb * 3) * TH:(bb * 3 + 1) * TH], ALU.mult, ALU.add)
            p.inc(ins, "v")
        p.sec("vector", vt)
        v_tmp = p.n["v"]

        def st(sc, et=et, bb=bb):
            sc.wait_ge(p.sem["gin"], gin_t)
            ins = sc.activation(szt[:, bb * TH:(bb + 1) * TH], yst[:, (bb * 3 + 2) * TH:(bb * 3 + 3) * TH], AF.Silu)
            p.inc(ins, "sc")
        p.sec("scalar", st)
        sc_z = p.n["sc"]

        def vm(v, et=et, bb=bb, sc_z=sc_z):
            v.wait_ge(p.sem["sc"], sc_z)
            ins = v.tensor_tensor(ybuf[:, et * TH:(et + 1) * TH], tmp[:, bb * TH:(bb + 1) * TH],
                                  szt[:, bb * TH:(bb + 1) * TH], ALU.mult)
            p.inc(ins, "v")
        p.sec("vector", vm)
        vready = p.n["v"]
    v_y = p.n["v"]

    pool = Pool(p, 4)
    jobs = []
    for mt in range(8):
        for ng in range(4):
            c0 = ng * 512
            mm_list = [(wob[:, k * 1024 + mt * 128: k * 1024 + (mt + 1) * 128],
                        ybuf[:, k * TH + c0: k * TH + c0 + 512]) for k in range(16)]

            def ev(env, ps, pred, mt=mt, c0=c0):
                ins = env.tensor_tensor(xres[:, mt * TH + c0: mt * TH + c0 + 512],
                                        ps[:, :512], xres[:, mt * TH + c0: mt * TH + c0 + 512], ALU.add)
                p.inc(ins, "v")
            jobs.append((mm_list, 128, 512, ev))
    emit_mm(p, pool, jobs, ("vector", "v"), wait_before=[("v", v_y), ("gin", gin_w), ("din", din0)])
    v_s = p.n["v"]
    sc1.close()

    emit_postln(p, xres, xres, TH, co, ready=[("v", v_s)])
    v_ln = p.n["v"]

    def dx(s):
        s.wait_ge(p.sem["v"], v_ln)
        for mt in range(8):
            p.inc(s.dma_start(out=x1_d[mt * 128:(mt + 1) * 128, :], in_=xres[:, mt * TH:(mt + 1) * TH]), "dout", 16)
        p.wait(s, "dout")
    p.sec("sync", dx)
    p.close()
    return p.nc


def host_A3_inmaps(inputs, co_np, a1, a2):
    WoutT = np.ascontiguousarray(inputs["L0_attn_W_out"].T)
    Dsk = inputs["L0_attn_D_skip"].reshape(16, 128).T.copy()
    x = inputs["L0_attn_x"]
    base = const_inmap(co_np)
    maps = []
    for c in range(NCORE):
        b, i = c // 2, c % 2
        t0 = i * TH
        ysl = np.concatenate([np.asarray(a2[2 * b]["ys"])[:, t0:t0 + TH],
                              np.asarray(a2[2 * b + 1]["ys"])[:, t0:t0 + TH]], axis=0)
        m = dict(base)
        m.update(ysl=np.ascontiguousarray(ysl), u=np.asarray(a1[c]["u"]), z=np.asarray(a1[c]["z"]),
                 Dsk=Dsk, WoutT=WoutT, xres=np.ascontiguousarray(x[b, t0:t0 + TH].T))
        maps.append(m)
    return maps


def build_A4(co_np, p=None, split_in=False):
    if p is None:
        p = P("A4")
    TX = TH + 1
    if split_in:
        x1o_d = p.din("x1own", [1024, TH])
        pc_d = p.din("pcol", [1024, 1])
    else:
        xh_d = p.din("xh", [1024, TX])
    mur_d = p.din("mur", [128, 8])
    muk_d = p.din("muk", [128, 8])
    wr_d = p.din("WrT", [1024, 1024])
    wk_d = p.din("WkT", [1024, 1024])
    wv_d = p.din("WvT", [1024, 1024])
    x2_d = p.dout("x2", [1024, TH])

    p.start()
    co = emit_consts(p, co_np)
    mur = p.sb("mur", [128, 8])
    muk = p.sb("muk", [128, 8])
    dma_in(p, mur, mur_d)
    dma_in(p, muk, muk_d)
    xh = p.sb("xh", [128, 8 * TX])
    if split_in:
        for kt in range(8):
            dma_in(p, xh[:, kt * TX + 1: kt * TX + 1 + TH], x1o_d[kt * 128:(kt + 1) * 128, :], cast=False)
            dma_in(p, xh[:, kt * TX: kt * TX + 1], pc_d[kt * 128:(kt + 1) * 128, 0:1], cast=False)
    else:
        load_fm(p, xh, xh_d, 1024, TX, cast=False)
    din0 = p.n["din"]

    sc1 = ExitStack()
    X = p.sb("X", [128, 8 * TH], BF16, sc1)
    rbuf = p.sb("rbuf", [128, 8 * TH], BF16, sc1)
    kabuf = p.sb("kabuf", [128, 8 * TH], BF16, sc1)
    wbuf = p.sb("wbuf", [128, 8 * 1024], BF16, sc1)
    dsc = p.sb("dsc", [128, TH], F32, sc1)
    ssc = p.sb("ssc", [128, 2 * 512], F32, sc1)
    pool = Pool(p, 4)

    def mk_mix(mu, which_v):
        for mt in range(8):
            def vm(v, mt=mt, mu=mu):
                if mt == 0:
                    p.wait(v, "din", din0)
                    p.wait(v, "mm")
                v.tensor_tensor(dsc[:, :], xh[:, mt * TX + 1: mt * TX + 1 + TH],
                                xh[:, mt * TX: mt * TX + TH], ALU.subtract)
                ins = v.scalar_tensor_tensor(X[:, mt * TH:(mt + 1) * TH], dsc[:, :],
                                             mu[:, mt:mt + 1], xh[:, mt * TX: mt * TX + TH], ALU.mult, ALU.add)
                p.inc(ins, "v")
            p.sec("vector", vm)

    def load_w(wd):
        def lw(g):
            p.wait(g, "mm")
            for kt in range(8):
                p.inc(g.dma_start(out=wbuf[:, kt * 1024:(kt + 1) * 1024], in_=wd[kt * 128:(kt + 1) * 128, :]), "gin", 16)
        p.sec("gpsimd", lw)
        return p.n["gin"]

    mk_mix(mur, None)
    v_xr = p.n["v"]
    g_wr = load_w(wr_d)
    jobs = []
    for mt in range(8):
        for ng in range(4):
            c0 = ng * 512
            mm_list = [(wbuf[:, k * 1024 + mt * 128: k * 1024 + (mt + 1) * 128],
                        X[:, k * TH + c0: k * TH + c0 + 512]) for k in range(8)]
            def ev(env, ps, pred, mt=mt, c0=c0):
                ins = env.activation(rbuf[:, mt * TH + c0: mt * TH + c0 + 512], ps[:, :512], AF.Sigmoid)
                p.inc(ins, "sc")
            jobs.append((mm_list, 128, 512, ev))
    emit_mm(p, pool, jobs, ("scalar", "sc"), wait_before=[("v", v_xr), ("gin", g_wr)])

    mk_mix(muk, None)
    v_xk = p.n["v"]
    g_wk = load_w(wk_d)
    jobs = []
    for mt in range(8):
        for ng in range(4):
            c0 = ng * 512
            mm_list = [(wbuf[:, k * 1024 + mt * 128: k * 1024 + (mt + 1) * 128],
                        X[:, k * TH + c0: k * TH + c0 + 512]) for k in range(8)]
            def ev(env, ps, pred, mt=mt, c0=c0):
                env.activation(ssc[:, :512], ps[:, :512], AF.Relu)
                ins = env.activation(kabuf[:, mt * TH + c0: mt * TH + c0 + 512], ssc[:, :512], AF.Square)
                p.inc(ins, "sc")
            jobs.append((mm_list, 128, 512, ev))
    emit_mm(p, pool, jobs, ("scalar", "sc"), wait_before=[("v", v_xk), ("gin", g_wk)])
    sc_ka = p.n["sc"]

    g_wv = load_w(wv_d)
    jobs = []
    for mt in range(8):
        for ng in range(4):
            c0 = ng * 512
            mm_list = [(wbuf[:, k * 1024 + mt * 128: k * 1024 + (mt + 1) * 128],
                        kabuf[:, k * TH + c0: k * TH + c0 + 512]) for k in range(8)]
            def ev(env, ps, pred, mt=mt, c0=c0):
                env.tensor_tensor(ssc[:, 512:1024], ps[:, :512], rbuf[:, mt * TH + c0: mt * TH + c0 + 512], ALU.mult)
                ins = env.tensor_tensor(xh[:, mt * TX + 1 + c0: mt * TX + 1 + c0 + 512],
                                        ssc[:, 512:1024], xh[:, mt * TX + 1 + c0: mt * TX + 1 + c0 + 512], ALU.add)
                p.inc(ins, "v")
            jobs.append((mm_list, 128, 512, ev))
    emit_mm(p, pool, jobs, ("vector", "v"), wait_before=[("sc", sc_ka), ("gin", g_wv)])
    v_s = p.n["v"]
    sc1.close()

    xout = p.sb("xout", [128, 8 * TH])
    emit_postln(p, xh, xout, TH, co, ready=[("v", v_s)], cstride=TX, col0=1)
    v_ln = p.n["v"]

    def dx(s):
        s.wait_ge(p.sem["v"], v_ln)
        for mt in range(8):
            p.inc(s.dma_start(out=x2_d[mt * 128:(mt + 1) * 128, :], in_=xout[:, mt * TH:(mt + 1) * TH]), "dout", 16)
        p.wait(s, "dout")
    p.sec("sync", dx)
    p.close()
    return p.nc


def host_A4_inmaps(inputs, co_np, x1_parts):
    WrT = np.ascontiguousarray(inputs["L0_ffn_W_r"].T)
    WkT = np.ascontiguousarray(inputs["L0_ffn_W_k"].T)
    WvT = np.ascontiguousarray(inputs["L0_ffn_W_v"].T)
    mur = inputs["L0_ffn_mu_r"].reshape(8, 128).T.copy()
    muk = inputs["L0_ffn_mu_k"].reshape(8, 128).T.copy()
    base = const_inmap(co_np)
    maps = []
    for c in range(NCORE):
        b, i = c // 2, c % 2
        xh = np.zeros((1024, TH + 1), dtype=np.float32)
        xh[:, 1:] = x1_parts[c]
        if i == 1:
            xh[:, 0] = x1_parts[2 * b][:, -1]
        m = dict(base)
        m.update(xh=xh, mur=mur, muk=muk, WrT=WrT, WkT=WkT, WvT=WvT)
        maps.append(m)
    return maps


def build_attn(co_np, biased=True, name="A5", npairs=4, nts=32, phases=9, dbg=False, p=None, fused_out=False):
    if p is None:
        p = P(name)
    HP = npairs
    W = 512
    xf_d = p.din("xf", [1024, T])
    wq_d = p.din("WQ", [1024, W])
    wk_d = p.din("WK", [1024, W])
    wv_d = p.din("WV", [1024, W])
    bq_d = p.din("bQ", [128, HP])
    bk_d = p.din("bK", [128, HP])
    bv_d = p.din("bV", [128, HP])
    msk_d = p.din("MSK", [4 * 128, 512], BF16)
    z_d = p.dout("z", [2 * W, TH] if fused_out else [W, T], BF16)
    if dbg:
        dbg_rows_d = p.dout("dbg_rows", [1, 2 * T], BF16)
        dbg_mhc_d = p.dout("dbg_mhc", [128, 128])
        dbg_ds_d = p.dout("dbg_ds", [128, 16])
        dbg_psb_d = p.dout("dbg_psb", [128, T], BF16)

    p.start()
    co = emit_consts(p, co_np)
    bq = p.sb("bq", [128, HP]); bk = p.sb("bk", [128, HP]); bv = p.sb("bv", [128, HP])
    dma_in(p, bq, bq_d); dma_in(p, bk, bk_d); dma_in(p, bv, bv_d)
    msk = p.sb("msk", [128, 4 * 512], BF16)
    for mI in range(4):
        dma_in(p, msk[:, mI * 512:(mI + 1) * 512], msk_d[mI * 128:(mI + 1) * 128, :])
    zout = p.sb("zout", [128, T], BF16)
    Psb = p.sb("Psb", [128, T], BF16)
    q2 = p.sb("q2", [128, T], BF16)
    rowsc = p.sb("rowsc", [1, 2 * T], BF16)
    mx1 = p.sb("mx1", [1, 4])
    mhc = p.sb("mhc", [128, 128])
    dsums = p.sb("dsums", [128, 16])
    PTsb = p.sb("PTsb", [128, 512], BF16)
    zsc = p.sb("zsc", [128, 128])
    rrow = p.sb("rrow", [1, 128])

    big = ExitStack()
    xf = p.sb("xf", [128, 8 * T], BF16, big)
    wqb = p.sb("wqb", [128, 8 * W], BF16, big)
    wkb = p.sb("wkb", [128, 8 * W], BF16, big)
    wvb = p.sb("wvb", [128, 8 * W], BF16, big)
    qp = p.sb("qp", [128, T], BF16, big)
    kp = p.sb("kp", [128, T], BF16, big)
    vp = p.sb("vp", [128, T], BF16, big)
    vT = p.sb("vT", [128, 2 * T], BF16, big)

    load_fm(p, xf, xf_d, 1024, T)
    load_fm(p, wqb, wq_d, 1024, W)
    load_fm(p, wkb, wk_d, 1024, W)
    load_fm(p, wvb, wv_d, 1024, W)
    gin0, din0 = p.n["gin"], p.n["din"]

    pool = Pool(p, 3)
    ps_pt = p.ps("pspt", [128, 512], BF16)
    ps_z = p.ps("psz", [128, 128])
    ps_sm = p.ps("pssm", [128, 512])
    ps_sm2 = p.ps("pssm2", [128, 512])

    NT = nts
    esem = "v"
    zdma_n = 0
    mm_psb_free = 0
    v_misc = 0
    v_ptevac = 0
    v_zevac = 0
    sc_track = {}

    for hp in range(HP):
        jobs = []
        for pi, (wb, bcol, dst) in enumerate(((wqb, bq, qp), (wkb, bk, kp), (wvb, bv, vp))):
            for ng in range(8):
                c0 = ng * 512
                mm_list = [(wb[:, k * W + hp * 128: k * W + (hp + 1) * 128],
                            xf[:, k * T + c0: k * T + c0 + 512]) for k in range(8)]
                if biased:
                    def ev(env, ps, pred, bcol=bcol, dst=dst, c0=c0, hp=hp):
                        ins = env.tensor_scalar(dst[:, c0:c0 + 512], ps[:, :512], bcol[:, hp:hp + 1], None, ALU.add)
                        p.inc(ins, "v")
                else:
                    def ev(env, ps, pred, dst=dst, c0=c0):
                        ins = env.tensor_copy(dst[:, c0:c0 + 512], ps[:, :512])
                        p.inc(ins, "v")
                jobs.append((mm_list, 128, 512, ev))
        wb4 = [("gin", gin0), ("din", din0)]
        if hp > 0:
            wb4 += [("zd", zdma_n), ("mm", mm_psb_free), ("v", v_zevac)]
        emit_mm(p, pool, jobs, ("vector", "v"), wait_before=wb4)
        sc_qkv = p.n["sc"]; v_qkv = p.n["v"]

        for hl in range(2):
            if phases < 2:
                continue
            h = 2 * hp + hl
            hb = hl * 64

            for grp in range(4):
                def tvt(t, grp=grp, hb=hb, hl=hl):
                    if grp == 0 and hl == 0:
                        p.wait(t, "v", v_qkv)
                    if v_ptevac > 0:
                        t.wait_ge(p.sem["v"], v_ptevac)
                    for j in range(8):
                        kt = grp * 8 + j
                        ins = t.transpose(ps_pt[:, j * 64: j * 64 + 64],
                                          vp[hb:hb + 64, kt * 128:(kt + 1) * 128], co["ident"][hb:hb + 64, hb:hb + 64])
                    p.inc(ins, "mm")
                p.sec("tensor", tvt)
                mmv = p.n["mm"]

                def vvt(v, grp=grp, hl=hl, mmv=mmv):
                    v.wait_ge(p.sem["mm"], mmv)
                    ins = v.tensor_copy(vT[:, hl * T + grp * 512:hl * T + (grp + 1) * 512], ps_pt[:, :])
                    p.inc(ins, "v")
                p.sec("vector", vvt)
                v_ptevac = p.n["v"]

            if phases < 3:
                continue
            def sq2(sc, hb=hb, vcur=0):
                sc.wait_ge(p.sem["v"], v_qkv)
                p.wait(sc, "v")
                ins = sc.activation(q2[hb:hb + 64, :], qp[hb:hb + 64, :], AF.Square)
                p.inc(ins, "sc")
            p.sec("scalar", sq2)
            sc_q2 = p.n["sc"]

            v_qn = {}
            for ng in range(8):
                pssm = ps_sm if ng % 2 == 0 else ps_sm2
                gprev = v_qn[ng - 2] if ng >= 2 else 0
                def tqn(t, ng=ng, pssm=pssm, gprev=gprev):
                    t.wait_ge(p.sem["sc"], sc_q2)
                    if ng == 0:
                        p.wait(t, "v")
                    if gprev:
                        t.wait_ge(p.sem["v"], gprev)
                    ins = t.matmul(pssm[0:1, :512], co["ones_bf"][hb:hb + 64, :],
                                   q2[hb:hb + 64, ng * 512:(ng + 1) * 512], start=True, stop=True)
                    p.inc(ins, "mm")
                p.sec("tensor", tqn)
                mmq = p.n["mm"]

                def vqn(v, ng=ng, mmq=mmq, pssm=pssm):
                    v.wait_ge(p.sem["mm"], mmq)
                    ins = v.tensor_copy(rowsc[0:1, ng * 512:(ng + 1) * 512], pssm[0:1, :512])
                    p.inc(ins, "v")
                p.sec("vector", vqn)
                v_qn[ng] = p.n["v"]
                v_misc = p.n["v"]

            def sk2(sc, hb=hb):
                p.wait(sc, "v", v_misc)
                ins = sc.activation(q2[hb:hb + 64, :], kp[hb:hb + 64, :], AF.Square)
                p.inc(ins, "sc")
            p.sec("scalar", sk2)
            sc_k2 = p.n["sc"]

            v_kn = {}
            for ng in range(8):
                pssm = ps_sm if ng % 2 == 0 else ps_sm2
                gprev = v_kn[ng - 2] if ng >= 2 else 0
                def tkn(t, ng=ng, pssm=pssm, gprev=gprev):
                    t.wait_ge(p.sem["sc"], sc_k2)
                    if gprev:
                        t.wait_ge(p.sem["v"], gprev)
                    ins = t.matmul(pssm[0:1, :512], co["ones_bf"][hb:hb + 64, :],
                                   q2[hb:hb + 64, ng * 512:(ng + 1) * 512], start=True, stop=True)
                    p.inc(ins, "mm")
                p.sec("tensor", tkn)
                mmk = p.n["mm"]

                def vkn(v, ng=ng, mmk=mmk, pssm=pssm):
                    v.wait_ge(p.sem["mm"], mmk)
                    ins = v.tensor_copy(rowsc[0:1, T + ng * 512: T + (ng + 1) * 512], pssm[0:1, :512])
                    p.inc(ins, "v")
                p.sec("vector", vkn)
                v_kn[ng] = p.n["v"]
                v_misc = p.n["v"]

            def vmh(v):
                p.wait(v, "v")
                i1 = p.inc(v.tensor_reduce(mx1[0:1, 0:1], rowsc[0:1, T:2 * T], AX.X, ALU.max), "v")
                v.wait_ge(p.sem["v"], i1)
                ins = v.tensor_scalar(rowsc[0:1, 0:T], rowsc[0:1, 0:T],
                                      mx1[0:1, 0:1], None, ALU.mult)
                p.inc(ins, "v")
            p.sec("vector", vmh)
            v_mh = p.n["v"]

            def smh(sc):
                sc.wait_ge(p.sem["v"], v_mh)
                ins = sc.activation(rowsc[0:1, 0:T], rowsc[0:1, 0:T], AF.Sqrt)
                p.inc(ins, "sc")
            p.sec("scalar", smh)
            sc_mh = p.n["sc"]

            def tmh(t):
                t.wait_ge(p.sem["sc"], sc_mh)
                if v_ptevac:
                    t.wait_ge(p.sem["v"], v_ptevac)
                for ts in range(NT):
                    ins = t.transpose(ps_pt[:, 2 * ts:2 * ts + 1], rowsc[0:1, ts * 128:(ts + 1) * 128],
                                      co["ident"][0:1, 0:1])
                p.inc(ins, "mm")
            p.sec("tensor", tmh)
            mm_mh = p.n["mm"]

            def vmh2(v, mm_mh=mm_mh):
                v.wait_ge(p.sem["mm"], mm_mh)
                i1 = p.inc(v.tensor_copy(mhc[:, 0:64], ps_pt[:, 0:64]), "v")
                v.wait_ge(p.sem["v"], i1)
                ins = v.tensor_scalar(mhc[:, 64:128], mhc[:, 0:64], -0.125, None, ALU.mult)
                p.inc(ins, "v")
            p.sec("vector", vmh2)
            v_misc = p.n["v"]
            v_ptevac = p.n["v"]

            if phases < 4:
                continue
            for ts in range(NT):
                nkg = ts // 4 + 1
                mI = ts % 4
                splan = []

                scbase = p.n["sc"]
                def tS(t, ts=ts, nkg=nkg, hb=hb, scbase=scbase):
                    if ts == 0:
                        t.wait_ge(p.sem["v"], v_misc)
                    for kg in range(nkg):
                        j = pool.rr % pool.nb
                        pool.rr += 1
                        if pool.freed[j] is not None:
                            t.wait_ge(p.sem[pool.freed[j][0]], pool.freed[j][1])
                        pool.freed[j] = ("sc", scbase + kg + 1)
                        ins = t.matmul(pool.tiles[j][:, :512], qp[hb:hb + 64, ts * 128:(ts + 1) * 128],
                                       kp[hb:hb + 64, kg * 512:(kg + 1) * 512], start=True, stop=True)
                        cnt = p.inc(ins, "mm")
                        splan.append((kg, j, cnt))
                p.sec("tensor", tS)

                def sexp(sc, ts=ts, nkg=nkg):
                    sc.wait_ge(p.sem["mm"], mm_psb_free)
                    sc.wait_ge(p.sem["v"], v_zevac) if v_zevac else None
                    for kg, j, cnt in splan:
                        sc.wait_ge(p.sem["mm"], cnt)
                        kwargs = {}
                        if kg < nkg - 1:
                            kwargs["accum_out"] = dsums[:, kg:kg + 1]
                        ins = sc.activation(Psb[:, kg * 512:(kg + 1) * 512], pool.tiles[j][:, :512],
                                            AF.Exp, scale=0.125, bias=mhc[:, 64 + 2 * ts:65 + 2 * ts], **kwargs)
                        cnt2 = p.inc(ins, "sc")
                p.sec("scalar", sexp)
                sc_exp = p.n["sc"]
                if phases < 6:
                    mm_psb_free = p.n["mm"]
                    continue

                def vdiag(v, ts=ts, nkg=nkg, mI=mI, sc_exp=sc_exp):
                    v.wait_ge(p.sem["sc"], sc_exp)
                    dg0 = (nkg - 1) * 512
                    i1 = p.inc(v.tensor_tensor(Psb[:, dg0:dg0 + 512], Psb[:, dg0:dg0 + 512],
                                               msk[:, mI * 512:(mI + 1) * 512], ALU.mult), "v")
                    v.wait_ge(p.sem["v"], i1)
                    i2 = p.inc(v.tensor_reduce(dsums[:, nkg - 1:nkg], Psb[:, dg0:dg0 + 512], AX.X, ALU.add), "v")
                    v.wait_ge(p.sem["v"], i2)
                    i3 = p.inc(v.tensor_reduce(dsums[:, 15:16], dsums[:, 0:nkg], AX.X, ALU.add), "v")
                    v.wait_ge(p.sem["v"], i3)
                    ins = v.reciprocal(dsums[:, 14:15], dsums[:, 15:16])
                    p.inc(ins, "v")
                p.sec("vector", vdiag)
                v_diag = p.n["v"]
                if phases < 7:
                    mm_psb_free = p.n["mm"]
                    v_zevac = p.n["v"]
                    continue

                def trr(t, v_diag=v_diag):
                    t.wait_ge(p.sem["v"], v_diag)
                    ins = t.transpose(ps_sm[:1, 0:128], dsums[:, 14:15], co["identf"][:, :])
                    p.inc(ins, "mm")
                p.sec("tensor", trr)
                mm_rr = p.n["mm"]

                def vrr(v, mm_rr=mm_rr):
                    v.wait_ge(p.sem["mm"], mm_rr)
                    ins = v.tensor_copy(rrow[0:1, :], ps_sm[0:1, 0:128])
                    p.inc(ins, "v")
                p.sec("vector", vrr)
                v_rr = p.n["v"]

                def tdb(t, v_rr=v_rr, hb=hb):
                    t.wait_ge(p.sem["v"], v_rr)
                    ins = t.matmul(ps_sm[hb:hb + 64, 128:256], co["onesrow"][0:1, 0:64], rrow[0:1, :],
                                   start=True, stop=True)
                    p.inc(ins, "mm")
                p.sec("tensor", tdb)
                mm_db = p.n["mm"]
                if phases < 8:
                    mm_psb_free = p.n["mm"]
                    v_zevac = p.n["v"]
                    continue

                for kg in range(nkg):
                    def tpt(t, kg=kg, ts=ts, v_diag=v_diag):
                        if kg == nkg - 1:
                            t.wait_ge(p.sem["v"], v_diag)
                        if v_ptevac:
                            t.wait_ge(p.sem["v"], v_ptevac)
                        for j in range(4):
                            ins = t.transpose(ps_pt[:, j * 128:(j + 1) * 128],
                                              Psb[:, kg * 512 + j * 128: kg * 512 + (j + 1) * 128],
                                              co["ident"][:, :])
                        p.inc(ins, "mm")
                    p.sec("tensor", tpt)
                    mm_pt = p.n["mm"]

                    def vpt(v, mm_pt=mm_pt):
                        v.wait_ge(p.sem["mm"], mm_pt)
                        ins = v.tensor_copy(PTsb[:, :], ps_pt[:, :])
                        p.inc(ins, "v")
                    p.sec("vector", vpt)
                    v_ptevac = p.n["v"]

                    def tav(t, kg=kg, ts=ts, hl=hl, v_ptevac=v_ptevac):
                        t.wait_ge(p.sem["v"], v_ptevac)
                        if kg == 0 and v_zevac:
                            t.wait_ge(p.sem["v"], v_zevac)
                        hb2 = hl * 64
                        for j in range(4):
                            kt = kg * 4 + j
                            ins = t.matmul(ps_z[hb2:hb2 + 64, :], vT[:, hl * T + kt * 64: hl * T + (kt + 1) * 64],
                                           PTsb[:, j * 128:(j + 1) * 128],
                                           start=(kg == 0 and j == 0), stop=(kg == nkg - 1 and j == 3))
                        if kg == nkg - 1:
                            p.inc(ins, "mm")
                    p.sec("tensor", tav)
                mm_av = p.n["mm"]
                mm_psb_free = mm_av
                if phases < 9:
                    v_zevac = p.n["v"]
                    continue

                def vz(v, ts=ts, hp=hp, hl=hl, mm_av=mm_av):
                    v.wait_ge(p.sem["mm"], mm_av)
                    hb2 = hl * 64
                    i1 = p.inc(v.tensor_copy(zsc[hb2:hb2 + 64, :], ps_z[hb2:hb2 + 64, :]), "v")
                    v.wait_ge(p.sem["v"], i1)
                    if ts == 0 and hl == 0 and hp > 0:
                        p.wait(v, "zd", (32 if fused_out else 16) * hp)
                    ins = v.tensor_tensor(zout[hl * 64:hl * 64 + 64, ts * 128:(ts + 1) * 128],
                                          zsc[hb2:hb2 + 64, :], ps_sm[hb2:hb2 + 64, 128:256], ALU.mult)
                    p.inc(ins, "v")
                p.sec("vector", vz)
                v_zevac = p.n["v"]

        if dbg and hp == 0:
            def ddbg(s, v_zevac=v_zevac):
                s.wait_ge(p.sem["v"], v_zevac)
                p.inc(s.dma_start(out=dbg_rows_d, in_=rowsc[0:1, :]), "dout", 16)
                p.inc(s.dma_start(out=dbg_mhc_d, in_=mhc[:, :]), "dout", 16)
                p.inc(s.dma_start(out=dbg_ds_d, in_=dsums[:, :]), "dout", 16)
                p.inc(s.dma_start(out=dbg_psb_d, in_=Psb[:, :]), "dout", 16)
            p.sec("sync", ddbg)

        def dz(s, hp=hp, v_zevac=v_zevac):
            s.wait_ge(p.sem["v"], v_zevac)
            if fused_out:
                p.inc(s.dma_start(out=z_d[hp * 128:(hp + 1) * 128, :], in_=zout[:, 0:TH]), "zd", 16)
                p.inc(s.dma_start(out=z_d[W + hp * 128: W + (hp + 1) * 128, :], in_=zout[:, TH:T]), "zd", 16)
            else:
                p.inc(s.dma_start(out=z_d[hp * 128:(hp + 1) * 128, :], in_=zout[:, :]), "zd", 16)
        p.sec("sync", dz)
        zdma_n = p.n["zd"]

    def fin(s):
        p.wait(s, "zd")
    p.sec("sync", fin)
    big.close()
    p.close()
    return p.nc


def moe_e8():
    E8 = np.zeros((8, 8 * 128), dtype=BF)
    for e in range(8):
        E8[e, e * 128:(e + 1) * 128] = 1
    return E8


def attn_masks():
    MSK = np.zeros((4, 128, 512), dtype=BF)
    for m in range(4):
        for pp in range(128):
            MSK[m, pp, :m * 128 + pp + 1] = 1
    return MSK.reshape(4 * 128, 512)


def host_A5_inmaps(inputs, co_np, x2_parts):
    MSK = attn_masks()
    WQ = inputs["L1_attn_W_Q"]; WK = inputs["L1_attn_W_K"]; WV = inputs["L1_attn_W_V"]
    bQ = inputs["L1_attn_b_Q"]; bK = inputs["L1_attn_b_K"]; bV = inputs["L1_attn_b_V"]
    base = const_inmap(co_np)
    maps = []
    for c in range(NCORE):
        b, j = c // 2, c % 2
        h0 = j * 8
        xf = np.concatenate([x2_parts[2 * b], x2_parts[2 * b + 1]], axis=1)
        def wslice(Wt):
            return np.concatenate([Wt[h] for h in range(h0, h0 + 8)], axis=1)
        def bslice(bt):
            return bt[h0:h0 + 8].reshape(4, 128).T.copy()
        m = dict(base)
        m.update(xf=np.ascontiguousarray(xf), WQ=np.ascontiguousarray(wslice(WQ)),
                 WK=np.ascontiguousarray(wslice(WK)), WV=np.ascontiguousarray(wslice(WV)),
                 bQ=bslice(bQ), bK=bslice(bK), bV=bslice(bV), MSK=MSK)
        maps.append(m)
    return maps


def build_A6(co_np, nexp=8, p=None):
    if p is None:
        p = P("A6")
    z_d = p.din("zsl", [1024, TH], BF16)
    wo_d = p.din("WO", [1024, 1024])
    bo_d = p.din("bO", [128, 8])
    x2_d = p.din("x2", [1024, TH])
    rmsw_d = p.din("rmsw", [128, 8])
    gw_d = p.din("gwT", [1024, 8])
    w1_d = p.din("W1all", [8 * 1024, 4096])
    b1_d = p.din("b1c", [128, 8 * 32])
    w2_d = p.din("W2all", [8 * 4096, 1024])
    b2_d = p.din("b2c", [128, 8 * 8])
    e8_d = p.din("E8", [8, 8 * 128], BF16)
    x4_d = p.dout("x4", [1024, TH])
    x3_d = p.dint(p.pre + "x3scr", [1024, TH], F32)
    acc_d = p.dint(p.pre + "accscr", [1024, TH], F32)

    p.start()
    co = emit_consts(p, co_np)
    bo = p.sb("bo", [128, 8]); rmsw = p.sb("rmsw", [128, 8])
    b1c = p.sb("b1c", [128, 8 * 32]); b2c = p.sb("b2c", [128, 8 * 8])
    dma_in(p, bo, bo_d); dma_in(p, rmsw, rmsw_d); dma_in(p, b1c, b1_d); dma_in(p, b2c, b2_d)
    xn = p.sb("xn", [128, 8 * TH], BF16)
    lgtile = p.sb("lgtile", [128, 8])
    mx8 = p.sb("mx8", [128, 8])
    et8 = p.sb("et8", [128, 8])
    idx8 = p.sb("idx8", [128, 8], U32)
    mT8 = p.sb("mT8", [128, 8])
    scol = p.sb("scol", [128, 2])
    maskR = p.sb("maskR", [8, TH], BF16)
    e8c = p.sb("e8c", [8, 8 * 128], BF16)
    dma_in(p, e8c, e8_d)
    scoreR = p.sb("scoreR", [1, TH], BF16)
    pool = Pool(p, 4)

    sc1 = ExitStack()
    x2b = p.sb("x2b", [128, 8 * TH], F32, sc1)
    zb = p.sb("zb", [128, 8 * TH], BF16, sc1)
    wob = p.sb("wob", [128, 8 * 1024], BF16, sc1)
    gwb = p.sb("gwb", [128, 8 * 8], F32, sc1)
    gw2 = p.sb("gw2", [128, 8 * 8], F32, sc1)
    rstc = p.sb("rstc", [128, 1], F32, sc1)
    gf = ExitStack()
    lgf = p.sb("lgf", [8, TH], scope=gf)
    rmsrow = p.sb("rmsrow", [1, TH], scope=gf)
    load_fm(p, x2b, x2_d, 1024, TH, cast=False)
    load_fm(p, zb, z_d, 1024, TH)
    load_fm(p, wob, wo_d, 1024, 1024)
    load_fm(p, gwb, gw_d, 1024, 8, cast=False)
    gin0, din0 = p.n["gin"], p.n["din"]

    jobs = []
    for mt in range(8):
        for ng in range(4):
            c0 = ng * 512
            mm_list = [(wob[:, k * 1024 + mt * 128: k * 1024 + (mt + 1) * 128],
                        zb[:, k * TH + c0: k * TH + c0 + 512]) for k in range(8)]
            def ev(env, ps, pred, mt=mt, c0=c0):
                i1 = env.tensor_scalar(ps[:, :512], ps[:, :512], bo[:, mt:mt + 1], None, ALU.add)
                ins = env.tensor_tensor(x2b[:, mt * TH + c0: mt * TH + c0 + 512], ps[:, :512],
                                        x2b[:, mt * TH + c0: mt * TH + c0 + 512], ALU.add)
                p.inc(ins, "v")
            jobs.append((mm_list, 128, 512, ev))
    emit_mm(p, pool, jobs, ("vector", "v"), wait_before=[("gin", gin0), ("din", din0)])
    v_s = p.n["v"]
    emit_postln(p, x2b, x2b, TH, co, ready=[("v", v_s)])
    v_x3 = p.n["v"]

    lnsc = ExitStack()
    sqf = p.sb("rsq", [128, 8 * 512], BF16, lnsc)
    r_t2 = p.sb("rt2", [1, 512], F32, lnsc)
    ps_a = p.ps("rpa", [128, 512], F32, lnsc)
    ps_mu = p.ps("rpmu", [128, 512], F32, lnsc)
    mm_prev2 = 0
    for ng in range(4):
        c0 = ng * 512
        def rsq(sc, c0=c0, ng=ng, mm_prev2=mm_prev2):
            if ng == 0:
                sc.wait_ge(p.sem["v"], v_x3)
            else:
                sc.wait_ge(p.sem["mm"], mm_prev2)
            for mt in range(8):
                p.inc(sc.activation(sqf[:, mt * 512:(mt + 1) * 512], x2b[:, mt * TH + c0: mt * TH + c0 + 512], AF.Square), "sc")
        p.sec("scalar", rsq)
        sc_sq = p.n["sc"]

        def rst(t, c0=c0, ng=ng):
            t.wait_ge(p.sem["sc"], sc_sq)
            if ng > 0:
                p.wait(t, "v")
            for mt in range(8):
                ins = t.matmul(ps_a[0:1, :512], co["ones_bf"][:, :], sqf[:, mt * 512:(mt + 1) * 512],
                               start=(mt == 0), stop=(mt == 7))
            p.inc(ins, "mm")
        p.sec("tensor", rst)
        mm_st = p.n["mm"]

        def rvs(v, c0=c0, mm_st=mm_st):
            v.wait_ge(p.sem["mm"], mm_st)
            ins = v.tensor_scalar(r_t2[0:1, :512], ps_a[0:1, :512], 1.0 / 1024, 1e-6, ALU.mult, ALU.add)
            p.inc(ins, "v")
        p.sec("vector", rvs)
        v_t2 = p.n["v"]

        def rss(sc, c0=c0, v_t2=v_t2):
            sc.wait_ge(p.sem["v"], v_t2)
            i1 = p.inc(sc.activation(r_t2[0:1, :512], r_t2[0:1, :512], AF.Ln), "sc")
            sc.wait_ge(p.sem["sc"], i1)
            ins = sc.activation(rmsrow[0:1, c0:c0 + 512], r_t2[0:1, :512], AF.Exp, scale=-0.5)
            p.inc(ins, "sc")
        p.sec("scalar", rss)
        sc_rms = p.n["sc"]

        def rbc(t, c0=c0, sc_rms=sc_rms):
            t.wait_ge(p.sem["sc"], sc_rms)
            ins = t.matmul(ps_mu[:, :512], co["onesrow"][:, :], rmsrow[0:1, c0:c0 + 512], start=True, stop=True)
            p.inc(ins, "mm")
        p.sec("tensor", rbc)
        mm_prev2 = p.n["mm"]
        mm_bc = mm_prev2

        def rxn(v, c0=c0, mm_bc=mm_bc):
            v.wait_ge(p.sem["mm"], mm_bc)
            for mt in range(8):
                ins = v.scalar_tensor_tensor(xn[:, mt * TH + c0: mt * TH + c0 + 512],
                                             x2b[:, mt * TH + c0: mt * TH + c0 + 512],
                                             rmsw[:, mt:mt + 1], ps_mu[:, :512], ALU.mult, ALU.mult)
                p.inc(ins, "v")
        p.sec("vector", rxn)
    v_xn = p.n["v"]
    lnsc.close()

    def dx3(s):
        s.wait_ge(p.sem["v"], v_xn)
        for mt in range(8):
            p.inc(s.dma_start(out=x3_d[mt * 128:(mt + 1) * 128, :], in_=x2b[:, mt * TH:(mt + 1) * TH]), "zd", 16)
    p.sec("sync", dx3)
    zd_x3 = p.n["zd"]

    gsc = ExitStack()
    ps_g = p.ps("psg", [128, 512], F32, gsc)
    def vgw2(v):
        p.wait(v, "din", din0)
        for k in range(8):
            p.inc(v.tensor_scalar(gw2[:, k * 8:(k + 1) * 8], gwb[:, k * 8:(k + 1) * 8],
                                  rmsw[:, k:k + 1], None, ALU.mult), "v")
    p.sec("vector", vgw2)
    for ng in range(4):
        def tgate(t, ng=ng):
            p.wait(t, "v")
            for k in range(8):
                ins = t.matmul(ps_g[0:8, 0:512], gw2[:, k * 8:(k + 1) * 8],
                               x2b[:, k * TH + ng * 512: k * TH + ng * 512 + 512],
                               start=(k == 0), stop=(k == 7))
            p.inc(ins, "mm")
        p.sec("tensor", tgate)
        mm_g = p.n["mm"]

        def vge(v, ng=ng, mm_g=mm_g):
            v.wait_ge(p.sem["mm"], mm_g)
            ins = v.tensor_copy(lgf[0:8, ng * 512:(ng + 1) * 512], ps_g[0:8, 0:512])
            p.inc(ins, "v")
        p.sec("vector", vge)
    v_lg = p.n["v"]

    for tt in range(16):
        def tlt(t, tt=tt):
            t.wait_ge(p.sem["v"], p.n["v"])
            t.transpose(ps_g[:, 0:8], lgf[0:8, tt * 128:(tt + 1) * 128], co["identf"][0:8, 0:8])
            ins = t.transpose(ps_g[:, 8:9], rmsrow[0:1, tt * 128:(tt + 1) * 128], co["identf"][0:1, 0:1])
            p.inc(ins, "mm")
        p.sec("tensor", tlt)
        mm_lt = p.n["mm"]

        def vgt(v, tt=tt, mm_lt=mm_lt):
            v.wait_ge(p.sem["mm"], mm_lt)
            ia = p.inc(v.tensor_copy(rstc[:, 0:1], ps_g[:, 8:9]), "v")
            v.wait_ge(p.sem["v"], ia)
            i0 = p.inc(v.tensor_scalar(lgtile[:, :], ps_g[:, 0:8], rstc[:, 0:1], None, ALU.mult), "v")
            v.wait_ge(p.sem["v"], i0)
            i1 = p.inc(v.max(mx8[:, :], lgtile[:, :]), "v")
            v.wait_ge(p.sem["v"], i1)
            i2 = p.inc(v.max_index(idx8[:, :], mx8[:, :], lgtile[:, :]), "v")
            v.wait_ge(p.sem["v"], i2)
            ins = v.tensor_scalar(scol[:, 0:1], mx8[:, 0:1], -1.0, None, ALU.mult)
            p.inc(ins, "v")
        p.sec("vector", vgt)
        v_gt = p.n["v"]

        def sgt(sc, tt=tt, v_gt=v_gt):
            sc.wait_ge(p.sem["v"], v_gt)
            ins = sc.activation(et8[:, :], lgtile[:, :], AF.Exp, bias=scol[:, 0:1])
            p.inc(ins, "sc")
        p.sec("scalar", sgt)
        sc_e8 = p.n["sc"]

        def vms(v, tt=tt, sc_e8=sc_e8):
            v.wait_ge(p.sem["sc"], sc_e8)
            i1 = p.inc(v.tensor_reduce(scol[:, 1:2], et8[:, :], AX.X, ALU.add), "v")
            v.wait_ge(p.sem["v"], i1)
            i2 = p.inc(v.reciprocal(scol[:, 0:1], scol[:, 1:2]), "v")
            v.wait_ge(p.sem["v"], i2)
            for e in range(8):
                ins = v.tensor_scalar(mT8[:, e:e + 1], idx8[:, 0:1], float(e), None, ALU.is_equal)
            p.inc(ins, "v")
        p.sec("vector", vms)
        v_ms = p.n["v"]

        def tmt(t, tt=tt, v_ms=v_ms):
            t.wait_ge(p.sem["v"], v_ms)
            t.transpose(ps_g[0:8, 128:256], mT8[:, :], co["identf"][:, :])
            ins = t.transpose(ps_g[0:1, 256:384], scol[:, 0:1], co["identf"][:, :])
            p.inc(ins, "mm")
        p.sec("tensor", tmt)
        mm_mt = p.n["mm"]

        def vmr(v, tt=tt, mm_mt=mm_mt):
            v.wait_ge(p.sem["mm"], mm_mt)
            v.tensor_copy(maskR[0:8, tt * 128:(tt + 1) * 128], ps_g[0:8, 128:256])
            ins = v.tensor_copy(scoreR[0:1, tt * 128:(tt + 1) * 128], ps_g[0:1, 256:384])
            p.inc(ins, "v")
        p.sec("vector", vmr)
    v_gate = p.n["v"]
    gsc.close()
    gf.close()
    sc1.close()

    ex = ExitStack()
    w1b = p.sb("w1b", [128, 8 * 4096], BF16, ex)
    hb = p.sb("hb", [128, 32 * 1024], BF16, ex)
    w2b = p.sb("w2b", [128, 2 * 32 * 128], BF16, ex)
    mkb = p.sb("mkb", [128, TH], BF16, ex)
    tmpe = p.sb("tmpe", [128, 2 * 1024], F32, ex)
    ps_mk = p.ps("psmk", [128, 512], F32, ex)
    zd_region = {}
    w1tally = {}
    w1free_at = {}
    w2free = {0: 0, 1: 0}
    w2tally = {}
    w2ctr = [0]

    def emit_lw1(e, first):
        def lw1(g, e=e, first=first):
            if first:
                p.wait(g, "mm")
                p.wait(g, "v", v_xn)
            else:
                g.wait_ge(p.sem["mm"], w1free_at[e])
            for kt in range(8):
                p.inc(g.dma_start(out=w1b[:, kt * 4096:(kt + 1) * 4096],
                                  in_=w1_d[e * 1024 + kt * 128: e * 1024 + (kt + 1) * 128, :]), "gin", 16)
        p.sec("gpsimd", lw1)
        w1tally[e] = p.n["gin"]

    def emit_lw2b(e, half, dt_):
        sl = w2ctr[0] % 2
        w2ctr[0] += 1
        def lw2b(g, e=e, dt_=dt_, sl=sl, guard=w2free[sl]):
            if guard:
                g.wait_ge(p.sem["mm"], guard)
            p.inc(g.dma_start(out=w2b[:, sl * 4096:(sl + 1) * 4096]
                              .rearrange("p (a m) -> p a m", a=32),
                              in_=w2_d[e * 4096:(e + 1) * 4096, dt_ * 128:(dt_ + 1) * 128]
                              .rearrange("(a p) m -> p a m", p=128)), "gin", 16)
        p.sec("gpsimd", lw2b)
        w2tally[(e, half, dt_)] = p.n["gin"]
        return sl

    emit_lw1(0, True)
    for e in range(nexp):
        for ng in range(4):
            def tmk(t, e=e, ng=ng):
                if e == 0 and ng == 0:
                    t.wait_ge(p.sem["v"], v_gate)
                p.wait(t, "v")
                ins = t.matmul(ps_mk[:, :512], e8c[0:8, e * 128:(e + 1) * 128],
                               maskR[0:8, ng * 512:(ng + 1) * 512], start=True, stop=True)
                p.inc(ins, "mm")
            p.sec("tensor", tmk)
            mm_mk = p.n["mm"]

            def vmk(v, ng=ng, mm_mk=mm_mk):
                v.wait_ge(p.sem["mm"], mm_mk)
                ins = v.tensor_copy(mkb[:, ng * 512:(ng + 1) * 512], ps_mk[:, :512])
                p.inc(ins, "v")
            p.sec("vector", vmk)
        v_mk = p.n["v"]

        for half in range(2):
            tc0 = half * 1024
            jobs = []
            for ft in range(32):
                for ng in range(2):
                    c0 = tc0 + ng * 512
                    mm_list = [(w1b[:, k * 4096 + ft * 128: k * 4096 + (ft + 1) * 128],
                                xn[:, k * TH + c0: k * TH + c0 + 512]) for k in range(8)]
                    def ev(env, ps, pred, e=e, ft=ft, ng=ng):
                        ins = env.activation(hb[:, ft * 1024 + ng * 512: ft * 1024 + (ng + 1) * 512],
                                             ps[:, :512], AF.Silu, bias=b1c[:, e * 32 + ft: e * 32 + ft + 1])
                        p.inc(ins, "sc")
                    jobs.append((mm_list, 128, 512, ev))
            emit_mm(p, pool, jobs, ("scalar", "sc"), wait_before=[("gin", w1tally[e]), ("v", v_mk)])
            sc_h = p.n["sc"]
            if half == 1:
                w1free_at[e + 1] = p.n["mm"]

            slots = {0: emit_lw2b(e, half, 0)}
            for dt_ in range(8):
                sl = slots[dt_]
                jobs = []
                for ng in range(2):
                    c0l = ng * 512
                    mm_list = [(w2b[:, sl * 4096 + kk * 128: sl * 4096 + (kk + 1) * 128],
                                hb[:, kk * 1024 + c0l: kk * 1024 + c0l + 512]) for kk in range(32)]
                    def ev(env, ps, pred, e=e, dt_=dt_, ng=ng, half=half, c0l=c0l, sl=sl):
                        ins = env.scalar_tensor_tensor(tmpe[:, sl * 1024 + ng * 512: sl * 1024 + (ng + 1) * 512],
                                                       ps[:, :512],
                                                       b2c[:, e * 8 + dt_: e * 8 + dt_ + 1],
                                                       mkb[:, half * 1024 + c0l: half * 1024 + c0l + 512],
                                                       ALU.add, ALU.mult)
                        p.inc(ins, "v")
                    jobs.append((mm_list, 128, 512, ev))
                prev_dma = zd_region.get(("tmpe", sl), 0)
                emit_mm(p, pool, jobs, ("vector", "v"),
                        wait_before=[("gin", w2tally[(e, half, dt_)]), ("sc", sc_h)]
                        + ([("zg", prev_dma)] if prev_dma else []))
                w2free[sl] = p.n["mm"]
                v_tmp = p.n["v"]

                if dt_ + 1 < 8:
                    slots[dt_ + 1] = emit_lw2b(e, half, dt_ + 1)
                if half == 1 and dt_ == 1 and e + 1 < nexp:
                    emit_lw1(e + 1, False)

                def dacc(g, e=e, dt_=dt_, half=half, sl=sl, v_tmp=v_tmp):
                    g.wait_ge(p.sem["v"], v_tmp)
                    rkey = (dt_, half)
                    if rkey in zd_region:
                        g.wait_ge(p.sem["zg"], zd_region[rkey])
                    ins = g.dma_start(out=acc_d[dt_ * 128:(dt_ + 1) * 128, half * 1024:(half + 1) * 1024],
                                      in_=tmpe[:, sl * 1024:(sl + 1) * 1024],
                                      **({} if e == 0 else {"accum_op": ALU.add}))
                    p.inc(ins, "zg", 16)
                    zd_region[rkey] = p.n["zg"]
                    zd_region[("tmpe", sl)] = p.n["zg"]
                p.sec("gpsimd", dacc)
    zd_all = p.n["zg"]
    ex.close()

    fin = ExitStack()
    x3b = p.sb("x3b", [128, 8 * TH], F32, fin)
    accs = p.sb("accs", [128, TH], F32, fin)
    scb = p.sb("scb", [128, TH], BF16, fin)
    ps_sc = pool.tiles[0]
    def lx3(s):
        p.wait(s, "zg", zd_all)
        for mt in range(8):
            p.inc(s.dma_start(out=x3b[:, mt * TH:(mt + 1) * TH], in_=x3_d[mt * 128:(mt + 1) * 128, :]), "din", 16)
    p.sec("sync", lx3)
    din_x3 = p.n["din"]

    for ng in range(4):
        def tsc(t, ng=ng):
            p.wait(t, "v")
            ins = t.matmul(ps_sc[:, :512], co["onesrow_bf"][:, :], scoreR[0:1, ng * 512:(ng + 1) * 512],
                           start=True, stop=True)
            p.inc(ins, "mm")
        p.sec("tensor", tsc)
        mm_sc = p.n["mm"]

        def vsc2(v, ng=ng, mm_sc=mm_sc):
            v.wait_ge(p.sem["mm"], mm_sc)
            ins = v.tensor_copy(scb[:, ng * 512:(ng + 1) * 512], ps_sc[:, :512])
            p.inc(ins, "v")
        p.sec("vector", vsc2)
    v_scb = p.n["v"]

    for mt in range(8):
        def lacc(g, mt=mt):
            if mt >= 1:
                p.wait(g, "v")
            else:
                p.wait(g, "zg", zd_all)
                p.wait(g, "v")
            p.inc(g.dma_start(out=accs[:, :], in_=acc_d[mt * 128:(mt + 1) * 128, :]), "gin", 16)
        p.sec("gpsimd", lacc)
        gin_acc = p.n["gin"]

        def vfin(v, mt=mt, gin_acc=gin_acc):
            v.wait_ge(p.sem["gin"], gin_acc)
            v.wait_ge(p.sem["v"], v_scb)
            v.wait_ge(p.sem["din"], din_x3)
            i1 = p.inc(v.tensor_tensor(accs[:, :], accs[:, :], scb[:, :], ALU.mult), "v")
            v.wait_ge(p.sem["v"], i1)
            ins = v.tensor_tensor(x3b[:, mt * TH:(mt + 1) * TH], x3b[:, mt * TH:(mt + 1) * TH], accs[:, :], ALU.add)
            p.inc(ins, "v")
        p.sec("vector", vfin)
    v_fin = p.n["v"]

    emit_postln(p, x3b, x3b, TH, co, ready=[("v", v_fin)])
    v_ln2 = p.n["v"]

    def dx4(s):
        s.wait_ge(p.sem["v"], v_ln2)
        for mt in range(8):
            p.inc(s.dma_start(out=x4_d[mt * 128:(mt + 1) * 128, :], in_=x3b[:, mt * TH:(mt + 1) * TH]), "dout", 16)
        p.wait(s, "dout")
    p.sec("sync", dx4)
    fin.close()
    p.close()
    return p.nc


def host_A6_inmaps(inputs, co_np, z_parts, x2_parts):
    WO = np.ascontiguousarray(inputs["L1_attn_W_O"].reshape(1024, 1024))
    bO = inputs["L1_attn_b_O"].reshape(8, 128).T.copy()
    rmsw = inputs["L1_ffn_rms_w"].reshape(8, 128).T.copy()
    gwT = np.ascontiguousarray(inputs["L1_ffn_gate_w"].T)
    W1 = np.ascontiguousarray(inputs["L1_ffn_W1"].reshape(8 * 1024, 4096))
    W2 = np.ascontiguousarray(inputs["L1_ffn_W2"].reshape(8 * 4096, 1024))
    b1c = np.ascontiguousarray(inputs["L1_ffn_b1"].reshape(8, 32, 128).transpose(2, 0, 1).reshape(128, 8 * 32))
    b2c = np.ascontiguousarray(inputs["L1_ffn_b2"].reshape(8, 8, 128).transpose(2, 0, 1).reshape(128, 8 * 8))
    base = const_inmap(co_np)
    maps = []
    for c in range(NCORE):
        b, i = c // 2, c % 2
        t0 = i * TH
        zsl = np.concatenate([np.asarray(z_parts[2 * b])[:, t0:t0 + TH],
                              np.asarray(z_parts[2 * b + 1])[:, t0:t0 + TH]], axis=0)
        m = dict(base)
        m.update(zsl=np.ascontiguousarray(zsl), WO=WO, bO=bO, x2=x2_parts[c], rmsw=rmsw,
                 gwT=gwT, W1all=W1, b1c=b1c, W2all=W2, b2c=b2c, E8=moe_e8())
        maps.append(m)
    return maps


def build_A7(co_np, p=None):
    if p is None:
        p = P("A7")
    x_d = p.din("x", [1024, TH])
    wk_d = p.din("Wk", [1024, 512]); wv_d = p.din("Wv", [1024, 512])
    wr_d = p.din("Wr", [1024, 512]); wg_d = p.din("Wg", [1024, 512])
    wsel_d = p.din("Wsel", [1024, 160])
    dtw_d = p.din("dtW", [32, 512])
    dtb_d = p.din("dtbc", [128, 4])
    kk_d = p.dout("k", [512, TH], BF16)
    vv_d = p.dout("v", [512, TH], BF16)
    rr_d = p.dout("r", [512, TH], BF16)
    gg_d = p.dout("g", [512, TH], BF16)
    dd_d = p.dout("delta", [512, TH], BF16)
    bs_d = p.dout("bscale", [1, TH])

    p.start()
    co = emit_consts(p, co_np)
    dtbc = p.sb("dtbc", [128, 4])
    dma_in(p, dtbc, dtb_d)
    xb = p.sb("xb", [128, 8 * TH], BF16)
    load_fm(p, xb, x_d, 1024, TH)
    wb = p.sb("wb", [128, 8 * 512], BF16)
    wsb = p.sb("wsb", [128, 8 * 160], BF16)
    dtwb = p.sb("dtwb", [96, 512], BF16)
    dma_in(p, dtwb[64:96, :], dtw_d, cast=True)
    load_fm(p, wsb, wsel_d, 1024, 160)
    projb = p.sb("projb", [96, TH], BF16)
    outb = p.sb("outb", [128, 4 * TH], BF16)
    bsr = p.sb("bsr", [1, TH])
    pool = Pool(p, 4)
    gin0, din0 = p.n["gin"], p.n["din"]

    jobs = []
    for ng in range(4):
        c0 = ng * 512
        mm_list = [(wsb[:, k * 160: k * 160 + 96], xb[:, k * TH + c0: k * TH + c0 + 512]) for k in range(8)]
        def ev(env, ps, pred, c0=c0):
            ins = env.tensor_copy(projb[:, c0:c0 + 512], ps[:96, :512])
            p.inc(ins, "v")
        jobs.append((mm_list, 96, 512, ev))
    emit_mm(p, pool, jobs, ("vector", "v"), wait_before=[("gin", gin0), ("din", din0)])
    v_proj = p.n["v"]

    ps_b = p.ps("psb", [128, 512])
    for ng in range(4):
        c0 = ng * 512
        def tbs(t, c0=c0, ng=ng):
            if ng == 0:
                t.wait_ge(p.sem["v"], v_proj)
            else:
                p.wait(t, "v")
            ins = t.matmul(ps_b[0:1, :512], co["ones_bf"][0:64, :], projb[0:64, c0:c0 + 512],
                           start=True, stop=True)
            p.inc(ins, "mm")
        p.sec("tensor", tbs)
        mm_b = p.n["mm"]

        def vbs(v, c0=c0, mm_b=mm_b):
            v.wait_ge(p.sem["mm"], mm_b)
            ins = v.tensor_scalar(bsr[0:1, c0:c0 + 512], ps_b[0:1, :512], 1.0 / 64, None, ALU.mult)
            p.inc(ins, "v")
        p.sec("vector", vbs)
    v_bs = p.n["v"]

    spt2 = p.sb("spt2", [128, 512])
    jobs = []
    for mt in range(4):
        for ng in range(4):
            c0 = ng * 512
            mm_list = [(dtwb[64:96, mt * 128:(mt + 1) * 128], projb[64:96, c0:c0 + 512])]
            def ev(env, ps, pred, mt=mt, c0=c0):
                i1 = env.activation(spt2[:, :], ps[:, :512], AF.Exp, bias=dtbc[:, mt:mt + 1])
                ins = env.activation(outb[:, mt * TH + c0: mt * TH + c0 + 512], spt2[:, :], AF.Ln, bias=1.0)
                p.inc(ins, "sc")
            jobs.append((mm_list, 128, 512, ev))
    emit_mm(p, pool, jobs, ("scalar", "sc"), wait_before=[("v", v_proj)])
    sc_dt = p.n["sc"]

    def ddma(s):
        s.wait_ge(p.sem["sc"], sc_dt)
        for mt in range(4):
            p.inc(s.dma_start(out=dd_d[mt * 128:(mt + 1) * 128, :], in_=outb[:, mt * TH:(mt + 1) * TH]), "dout", 16)
        s.wait_ge(p.sem["v"], v_bs)
        p.inc(s.dma_start(out=bs_d, in_=bsr[0:1, :]), "dout", 16)
    p.sec("sync", ddma)
    d_prev = p.n["dout"]

    for wi, (wd, od) in enumerate(((wk_d, kk_d), (wv_d, vv_d), (wr_d, rr_d), (wg_d, gg_d))):
        def lw(g, wd=wd, wi=wi):
            p.wait(g, "mm")
            for kt in range(8):
                p.inc(g.dma_start(out=wb[:, kt * 512:(kt + 1) * 512], in_=wd[kt * 128:(kt + 1) * 128, :]), "gin", 16)
        p.sec("gpsimd", lw)
        gin_w = p.n["gin"]

        jobs = []
        for mt in range(4):
            for ng in range(4):
                c0 = ng * 512
                mm_list = [(wb[:, k * 512 + mt * 128: k * 512 + (mt + 1) * 128],
                            xb[:, k * TH + c0: k * TH + c0 + 512]) for k in range(8)]
                def ev(env, ps, pred, mt=mt, c0=c0, wi=wi):
                    if wi >= 1:
                        p.wait(env, "dout", d_prev + 16 * 4 * (wi - 1))
                    ins = env.tensor_copy(outb[:, mt * TH + c0: mt * TH + c0 + 512], ps[:, :512])
                    p.inc(ins, "v")
                jobs.append((mm_list, 128, 512, ev))
        emit_mm(p, pool, jobs, ("vector", "v"), wait_before=[("gin", gin_w)])
        v_w = p.n["v"]

        def odma(s, od=od, v_w=v_w):
            s.wait_ge(p.sem["v"], v_w)
            for mt in range(4):
                p.inc(s.dma_start(out=od[mt * 128:(mt + 1) * 128, :], in_=outb[:, mt * TH:(mt + 1) * TH]), "dout", 16)
        p.sec("sync", odma)
    def fin(s):
        p.wait(s, "dout")
    p.sec("sync", fin)
    p.close()
    return p.nc


def host_A7_inmaps(inputs, co_np, x4_parts):
    dtbc = inputs["L2_attn_dt_b"].reshape(4, 128).T.copy()
    base = const_inmap(co_np)
    maps = []
    for c in range(NCORE):
        m = dict(base)
        Wsel_perm = np.ascontiguousarray(np.concatenate(
            [inputs["L2_attn_W_sel"][:, 32:96], inputs["L2_attn_W_sel"][:, 0:32],
             inputs["L2_attn_W_sel"][:, 96:160]], axis=1))
        m.update(x=x4_parts[c], Wk=inputs["L2_attn_Wk"], Wv=inputs["L2_attn_Wv"],
                 Wr=inputs["L2_attn_Wr"], Wg=inputs["L2_attn_Wg"], Wsel=Wsel_perm,
                 dtW=inputs["L2_attn_dt_W"], dtbc=dtbc)
        maps.append(m)
    return maps


def build_A9(co_np, p=None):
    if p is None:
        p = P("A9")
    ys_d = p.din("ysl", [512, TH], BF16)
    wo_d = p.din("Wo", [512, 1024])
    x4_d = p.din("x4", [1024, TH])
    w1_d = p.din("W1", [1024, 4096])
    b1_d = p.din("b1c", [128, 32])
    w2_d = p.din("W2", [4096, 1024])
    b2_d = p.din("b2c", [128, 8])
    x6_d = p.dout("x6", [1024, TH])

    p.start()
    co = emit_consts(p, co_np)
    b1c = p.sb("b1c", [128, 32]); b2c = p.sb("b2c", [128, 8])
    dma_in(p, b1c, b1_d); dma_in(p, b2c, b2_d)
    xr = p.sb("xr", [128, 8 * TH], F32)
    load_fm(p, xr, x4_d, 1024, TH, cast=False)
    din0 = p.n["din"]
    pool = Pool(p, 4)

    s1 = ExitStack()
    ysb = p.sb("ysb", [128, 4 * TH], BF16, s1)
    wob = p.sb("wob", [128, 4 * 1024], BF16, s1)
    load_fm(p, ysb, ys_d, 512, TH)
    load_fm(p, wob, wo_d, 512, 1024)
    gin0 = p.n["gin"]

    jobs = []
    for mt in range(8):
        for ng in range(4):
            c0 = ng * 512
            mm_list = [(wob[:, k * 1024 + mt * 128: k * 1024 + (mt + 1) * 128],
                        ysb[:, k * TH + c0: k * TH + c0 + 512]) for k in range(4)]
            def ev(env, ps, pred, mt=mt, c0=c0):
                ins = env.tensor_tensor(xr[:, mt * TH + c0: mt * TH + c0 + 512], ps[:, :512],
                                        xr[:, mt * TH + c0: mt * TH + c0 + 512], ALU.add)
                p.inc(ins, "v")
            jobs.append((mm_list, 128, 512, ev))
    emit_mm(p, pool, jobs, ("vector", "v"), wait_before=[("gin", gin0), ("din", din0)])
    v_s = p.n["v"]
    emit_postln(p, xr, xr, TH, co, ready=[("v", v_s)])
    v_x5 = p.n["v"]
    s1.close()

    s2 = ExitStack()
    x5b = p.sb("x5b", [128, 8 * TH], BF16, s2)
    w1b = p.sb("w1b", [128, 8 * 4096], BF16, s2)
    hbb = p.sb("hbb", [128, 32 * 512], BF16, s2)
    w2b = p.sb("w2b", [128, 32 * 128], BF16, s2)

    def vx5(v):
        v.wait_ge(p.sem["v"], v_x5)
        for mt in range(8):
            p.inc(v.tensor_copy(x5b[:, mt * TH:(mt + 1) * TH], xr[:, mt * TH:(mt + 1) * TH]), "v")
    p.sec("vector", vx5)
    v_xb = p.n["v"]

    def lw1(g):
        p.wait(g, "mm")
        for kt in range(8):
            p.inc(g.dma_start(out=w1b[:, kt * 4096:(kt + 1) * 4096], in_=w1_d[kt * 128:(kt + 1) * 128, :]), "gin", 16)
    p.sec("gpsimd", lw1)
    gin_w1 = p.n["gin"]

    for qt in range(4):
        tc0 = qt * 512
        jobs = []
        for ft in range(32):
            mm_list = [(w1b[:, k * 4096 + ft * 128: k * 4096 + (ft + 1) * 128],
                        x5b[:, k * TH + tc0: k * TH + tc0 + 512]) for k in range(8)]
            def ev(env, ps, pred, ft=ft):
                ins = env.activation(hbb[:, ft * 512:(ft + 1) * 512],
                                     ps[:, :512], AF.Relu, bias=b1c[:, ft: ft + 1])
                p.inc(ins, "sc")
            jobs.append((mm_list, 128, 512, ev))
        emit_mm(p, pool, jobs, ("scalar", "sc"), wait_before=[("gin", gin_w1), ("v", v_xb)])
        sc_h = p.n["sc"]

        for dt_ in range(8):
            def lw2(g, dt_=dt_):
                p.wait(g, "mm")
                p.inc(g.dma_start(out=w2b[:, :].rearrange("p (a m) -> p a m", a=32),
                                  in_=w2_d[:, dt_ * 128:(dt_ + 1) * 128].rearrange("(a p) m -> p a m", p=128)), "gin", 16)
            p.sec("gpsimd", lw2)
            gin_w2 = p.n["gin"]

            jobs = []
            mm_list = [(w2b[:, kk * 128:(kk + 1) * 128],
                        hbb[:, kk * 512: kk * 512 + 512]) for kk in range(32)]
            def ev(env, ps, pred, dt_=dt_, tc0=tc0):
                env.tensor_scalar(ps[:, :512], ps[:, :512], b2c[:, dt_:dt_ + 1], None, ALU.add)
                ins = env.tensor_tensor(xr[:, dt_ * TH + tc0: dt_ * TH + tc0 + 512], ps[:, :512],
                                        xr[:, dt_ * TH + tc0: dt_ * TH + tc0 + 512], ALU.add)
                p.inc(ins, "v")
            jobs.append((mm_list, 128, 512, ev))
            emit_mm(p, pool, jobs, ("vector", "v"), wait_before=[("gin", gin_w2), ("sc", sc_h)])
    v_s6 = p.n["v"]
    s2.close()

    emit_postln(p, xr, xr, TH, co, ready=[("v", v_s6)])
    v_ln = p.n["v"]

    def dx(s):
        s.wait_ge(p.sem["v"], v_ln)
        for mt in range(8):
            p.inc(s.dma_start(out=x6_d[mt * 128:(mt + 1) * 128, :], in_=xr[:, mt * TH:(mt + 1) * TH]), "dout", 16)
        p.wait(s, "dout")
    p.sec("sync", dx)
    p.close()
    return p.nc


def host_A9_inmaps(inputs, co_np, ys_parts, x4_parts):
    b1c = inputs["L2_ffn_b1"].reshape(32, 128).T.copy()
    b2c = inputs["L2_ffn_b2"].reshape(8, 128).T.copy()
    base = const_inmap(co_np)
    maps = []
    for c in range(NCORE):
        b, i = c // 2, c % 2
        t0 = i * TH
        ysl = np.concatenate([np.asarray(ys_parts[2 * b])[:, t0:t0 + TH],
                              np.asarray(ys_parts[2 * b + 1])[:, t0:t0 + TH]], axis=0)
        m = dict(base)
        m.update(ysl=np.ascontiguousarray(ysl), Wo=inputs["L2_attn_Wo"], x4=x4_parts[c],
                 W1=inputs["L2_ffn_W1"], b1c=b1c, W2=inputs["L2_ffn_W2"], b2c=b2c)
        maps.append(m)
    return maps


def build_A8(co_np, p=None, fused_out=False):
    if p is None:
        p = P("A8")
    NH = 4
    k_d = p.din("kf", [NH * 64, T], BF16)
    v_d = p.din("vf", [NH * 64, T], BF16)
    r_d = p.din("rf", [NH * 64, T], BF16)
    g_d = p.din("gf", [NH * 64, T], BF16)
    dl_d = p.din("df", [NH * 64, T], BF16)
    bs_d = p.din("bscale", [1, T])
    bdc_d = p.din("bdc", [128, 2])
    tfc_d = p.din("tfc", [128, 2])
    m16_d = p.din("MSK16", [16, 128], mybir.dt.uint8)
    ys_d = p.dout("ys", [2 * NH * 64, TH] if fused_out else [NH * 64, T], BF16)

    p.start()
    co = emit_consts(p, co_np)
    bdc = p.sb("bdc", [128, 2]); tfc = p.sb("tfc", [128, 2])
    m16 = p.sb("m16", [16, 128], mybir.dt.uint8)
    dma_in(p, bdc, bdc_d); dma_in(p, tfc, tfc_d); dma_in(p, m16, m16_d)
    bsr = p.sb("bsr", [1, T])
    dma_in(p, bsr, bs_d)
    ones64 = p.sb("ones64", [64, 512], BF16)
    zer64 = p.sb("zer64", [64, 64])
    zer16 = p.sb("zer16", [16, 128], BF16)
    bscB = p.sb("bscB", [64, T], BF16)
    kf = p.sb("kf", [64, T], BF16)
    vf = p.sb("vf", [64, T], BF16)
    rf = p.sb("rf", [64, T], BF16)
    gf = p.sb("gf", [64, T], BF16)
    df = p.sb("df", [64, T], BF16)
    Lb = p.sb("Lb", [64, T], F32)
    Efl = p.sb("Efl", [64, T], F32)
    mfl = p.sb("mfl", [64, T], F32)
    Erow = p.sb("Erow", [64, 256], F32)
    Mrow = p.sb("Mrow", [64, 256], F32)
    lam = p.sb("lam", [64, 256], F32)
    fsc = p.sb("fsc", [64, T], F32)
    ktil = p.sb("ktil", [64, T], BF16)
    kap = p.sb("kap", [64, T], BF16)
    kin = p.sb("kin", [64, T], BF16)
    kpr = p.sb("kpr", [64, T], BF16)
    EProw = p.sb("EProw", [64, 256], F32)
    vtl = p.sb("vtl", [64, T], BF16)
    wkv = p.sb("wkv", [64, T], F32)
    kTg = p.sb("kTg", [16, 512], BF16)
    vTg = p.sb("vTg", [16, 512], BF16)
    Pm = p.sb("Pm", [16, 128], BF16)
    Ub = p.sb("Ub", [64, 512])
    Sg = p.sb("Sg", [64, 512], BF16)
    Scar = p.sb("Scar", [64, 64], BF16)

    ps_tr = p.ps("pstr", [16, 512], BF16)
    ps_u = p.ps("psu", [64, 512], F32)
    ps_p = p.ps("psp", [16, 128], F32)
    ps_w = p.ps("psw", [64, 128], F32)
    ps_bc = p.ps("psbc", [64, 512], F32)

    def ginit(g):
        g.memset(ones64[:, :], 1.0)
        g.memset(zer64[:, :], 0.0)
        g.memset(Scar[:, :], 0.0)
        ins = g.memset(zer16[:, :], 0.0)
        p.inc(ins, "gp")
    p.sec("gpsimd", ginit)
    gp0 = p.n["gp"]
    din0, gin0 = p.n["din"], p.n["gin"]

    for ng in range(8):
        def tb(t, ng=ng):
            if ng == 0:
                p.wait(t, "din", din0)
            else:
                p.wait(t, "v")
            ins = t.matmul(ps_bc[:, :512], co["onesrow"][0:1, 0:64], bsr[0:1, ng * 512:(ng + 1) * 512],
                           start=True, stop=True)
            p.inc(ins, "mm")
        p.sec("tensor", tb)
        mm_b = p.n["mm"]

        def vb(v, ng=ng, mm_b=mm_b):
            v.wait_ge(p.sem["mm"], mm_b)
            ins = v.tensor_copy(bscB[:, ng * 512:(ng + 1) * 512], ps_bc[:, :512])
            p.inc(ins, "v")
        p.sec("vector", vb)
    v_bsc = p.n["v"]

    NG8 = T // 512
    for h in range(NH):
        hcol = h // 2
        hrow = (h % 2) * 64
        def lh(g, h=h):
            p.wait(g, "v")
            p.wait(g, "sc")
            p.wait(g, "zd")
            for src, dst in ((k_d, kf), (v_d, vf), (r_d, rf), (g_d, gf), (dl_d, df)):
                p.inc(g.dma_start(out=dst[:, :], in_=src[h * 64:(h + 1) * 64, :]), "gin", 16)
        p.sec("gpsimd", lh)
        gin_h = p.n["gin"]

        def va(v, h=h, hcol=hcol, hrow=hrow):
            v.wait_ge(p.sem["gin"], gin_h)
            p.wait(v, "gp", gp0)
            i1 = p.inc(v.tensor_scalar(fsc[:, :], df[:, :], bdc[hrow:hrow + 64, hcol:hcol + 1], None, ALU.mult), "v")
            v.wait_ge(p.sem["v"], i1)
            last = None
            for cg in range(8):
                c0 = cg * 512
                init = 0.0 if cg == 0 else Lb[:, c0 - 1: c0]
                i2 = p.inc(v.tensor_tensor_scan(Lb[:, c0:c0 + 512], ones64[:, :], fsc[:, c0:c0 + 512],
                                                init, ALU.mult, ALU.add), "v")
                v.wait_ge(p.sem["v"], i2)
            Lv = Lb[:, :].rearrange("p (c s) -> p c s", s=16)
            i3 = p.inc(v.tensor_copy(Erow[:, :], Lv[:, :, 15]), "v")
            v.wait_ge(p.sem["v"], i3)
            v.tensor_scalar(lam[:, 0:1], Erow[:, 0:1], -1.0, None, ALU.mult)
            i4 = p.inc(v.tensor_tensor(lam[:, 1:256], Erow[:, 0:255], Erow[:, 1:256], ALU.subtract), "v")
            v.wait_ge(p.sem["v"], i4)
            v.tensor_scalar(Mrow[:, 0:1], Erow[:, 0:1], 0.5, None, ALU.mult)
            i5 = p.inc(v.tensor_tensor(Mrow[:, 1:256], Erow[:, 0:255], Erow[:, 1:256], ALU.add), "v")
            v.wait_ge(p.sem["v"], i5)
            i6 = p.inc(v.tensor_scalar(Mrow[:, 1:256], Mrow[:, 1:256], 0.5, None, ALU.mult), "v")
            v.wait_ge(p.sem["v"], i6)
            v.memset(EProw[:, 0:1], 0.0)
            i7 = p.inc(v.tensor_copy(EProw[:, 1:256], Erow[:, 0:255]), "v")
            v.wait_ge(p.sem["v"], i7)
            Ev = Efl[:, :].rearrange("p (c s) -> p c s", s=16)
            Mv = mfl[:, :].rearrange("p (c s) -> p c s", s=16)
            for sI in range(16):
                v.tensor_copy(Ev[:, :, sI], Erow[:, :])
                ins = v.tensor_copy(Mv[:, :, sI], Mrow[:, :])
            p.inc(ins, "v")
        p.sec("vector", va)
        v_L = p.n["v"]

        def sl(sc, h=h):
            sc.wait_ge(p.sem["v"], v_L)
            ins = sc.activation(lam[:, :], lam[:, :], AF.Exp)
            p.inc(ins, "sc")
        p.sec("scalar", sl)
        sc_lam = p.n["sc"]

        def vargs(v, h=h):
            v.wait_ge(p.sem["v"], v_L)
            ins = v.tensor_tensor(fsc[:, :], mfl[:, :], Lb[:, :], ALU.subtract)
            p.inc(ins, "v")
        p.sec("vector", vargs)
        v_a1 = p.n["v"]

        def sexps(sc, h=h):
            sc.wait_ge(p.sem["v"], v_a1)
            i1 = p.inc(sc.activation(ktil[:, :], fsc[:, :], AF.Exp), "sc")
            ins = sc.activation(kap[:, :], fsc[:, :], AF.Exp, scale=-1.0)
            p.inc(ins, "sc")
        p.sec("scalar", sexps)
        sc_e1 = p.n["sc"]

        def varg2(v, h=h):
            v.wait_ge(p.sem["sc"], sc_e1)
            i1 = p.inc(v.tensor_tensor(fsc[:, :], Lb[:, :], Efl[:, :], ALU.subtract), "v")
            v.wait_ge(p.sem["v"], i1)
            EPv = Efl[:, :].rearrange("p (c s) -> p c s", s=16)
            for sI in range(16):
                ins = v.tensor_copy(EPv[:, :, sI], EProw[:, :])
            i2 = p.inc(ins, "v")
            v.wait_ge(p.sem["v"], i2)
            ins = v.tensor_tensor(Efl[:, :], Efl[:, :], Lb[:, :], ALU.subtract)
            p.inc(ins, "v")
        p.sec("vector", varg2)
        v_a2 = p.n["v"]

        def sexp2(sc, h=h):
            sc.wait_ge(p.sem["v"], v_a2)
            i1 = p.inc(sc.activation(fsc[:, :], fsc[:, :], AF.Exp), "sc")
            ins = sc.activation(kin[:, :], Efl[:, :], AF.Exp)
            p.inc(ins, "sc")
        p.sec("scalar", sexp2)
        sc_e2 = p.n["sc"]

        def vfac(v, h=h):
            v.wait_ge(p.sem["sc"], sc_e2)
            v.wait_ge(p.sem["v"], v_bsc)
            p.inc(v.tensor_tensor(ktil[:, :], ktil[:, :], kf[:, :], ALU.mult), "v")
            p.inc(v.tensor_tensor(kap[:, :], kap[:, :], kf[:, :], ALU.mult), "v")
            p.inc(v.tensor_tensor(kin[:, :], kin[:, :], kf[:, :], ALU.mult), "v")
            p.inc(v.tensor_tensor(kpr[:, :], fsc[:, :], kf[:, :], ALU.mult), "v")
            ins = v.tensor_tensor(vtl[:, :], vf[:, :], bscB[:, :], ALU.mult)
            p.inc(ins, "v")
        p.sec("vector", vfac)
        v_fac = p.n["v"]

        for g8 in range(32):
            t0 = g8 * 128
            def ttr(t, g8=g8, t0=t0):
                if g8 == 0:
                    t.wait_ge(p.sem["v"], v_fac)
                p.wait(t, "v")
                for c8 in range(8):
                    c0 = t0 + c8 * 16
                    ins = t.transpose(ps_tr[:, c8 * 64:(c8 + 1) * 64], kpr[:, c0:c0 + 16], co["ident"][0:64, 0:64])
                p.inc(ins, "mm")
            p.sec("tensor", ttr)
            mm_t1 = p.n["mm"]

            def vtr(v, mm_t1=mm_t1):
                v.wait_ge(p.sem["mm"], mm_t1)
                ins = v.tensor_copy(kTg[:, :], ps_tr[:, :])
                p.inc(ins, "v")
            p.sec("vector", vtr)
            v_t1 = p.n["v"]

            def ttr2(t, g8=g8, t0=t0, v_t1=v_t1):
                t.wait_ge(p.sem["v"], v_t1)
                for c8 in range(8):
                    c0 = t0 + c8 * 16
                    ins = t.transpose(ps_tr[:, c8 * 64:(c8 + 1) * 64], vtl[:, c0:c0 + 16], co["ident"][0:64, 0:64])
                p.inc(ins, "mm")
            p.sec("tensor", ttr2)
            mm_t2 = p.n["mm"]

            def vtr2(v, mm_t2=mm_t2):
                v.wait_ge(p.sem["mm"], mm_t2)
                ins = v.tensor_copy(vTg[:, :], ps_tr[:, :])
                p.inc(ins, "v")
            p.sec("vector", vtr2)
            v_t2 = p.n["v"]

            def tu(t, g8=g8, t0=t0, v_t2=v_t2):
                t.wait_ge(p.sem["v"], v_t2)
                for c8 in range(8):
                    t.matmul(ps_u[:, c8 * 64:(c8 + 1) * 64], kTg[:, c8 * 64:(c8 + 1) * 64],
                             vTg[:, c8 * 64:(c8 + 1) * 64], start=True, stop=True)
                for c8 in range(8):
                    c0 = t0 + c8 * 16
                    ins = t.matmul(ps_p[:, c8 * 16:(c8 + 1) * 16], kap[:, c0:c0 + 16],
                                   ktil[:, c0:c0 + 16], start=True, stop=True)
                p.inc(ins, "mm")
            p.sec("tensor", tu)
            mm_u = p.n["mm"]

            def vu(v, g8=g8, mm_u=mm_u):
                v.wait_ge(p.sem["mm"], mm_u)
                i1 = p.inc(v.tensor_copy(Ub[:, :], ps_u[:, :]), "v")
                i2 = p.inc(v.select(Pm[:, :], m16[:, :], ps_p[:, :], zer16[:, :]), "v")
                v.wait_ge(p.sem["v"], i1)
                for c8 in range(8):
                    cI = g8 * 8 + c8
                    prev = Scar[:, :] if c8 == 0 else Sg[:, (c8 - 1) * 64: c8 * 64]
                    i3 = p.inc(v.scalar_tensor_tensor(Sg[:, c8 * 64:(c8 + 1) * 64], prev,
                                                      lam[:, cI:cI + 1], Ub[:, c8 * 64:(c8 + 1) * 64],
                                                      ALU.mult, ALU.add), "v")
                    v.wait_ge(p.sem["v"], i3)
            p.sec("vector", vu)
            v_ch = p.n["v"]

            def tw(t, g8=g8, t0=t0, v_ch=v_ch):
                t.wait_ge(p.sem["v"], v_ch)
                for c8 in range(8):
                    c0 = t0 + c8 * 16
                    Sprev = Scar[:, :] if c8 == 0 else Sg[:, (c8 - 1) * 64: c8 * 64]
                    t.matmul(ps_w[:, c8 * 16:(c8 + 1) * 16], Sprev, kin[:, c0:c0 + 16], start=True, stop=False)
                    ins = t.matmul(ps_w[:, c8 * 16:(c8 + 1) * 16], vTg[:, c8 * 64:(c8 + 1) * 64],
                                   Pm[:, c8 * 16:(c8 + 1) * 16], start=False, stop=True)
                p.inc(ins, "mm")
            p.sec("tensor", tw)
            mm_w = p.n["mm"]

            def vw(v, g8=g8, t0=t0, mm_w=mm_w):
                v.wait_ge(p.sem["mm"], mm_w)
                i1 = p.inc(v.tensor_copy(wkv[:, t0:t0 + 128], ps_w[:, :]), "v")
                ins = v.tensor_copy(Scar[:, :], Sg[:, 7 * 64: 8 * 64])
                p.inc(ins, "v")
            p.sec("vector", vw)

        v_wkv = p.n["v"]

        def sep(sc, h=h):
            sc.wait_ge(p.sem["v"], v_wkv)
            i1 = p.inc(sc.activation(rf[:, :], rf[:, :], AF.Sigmoid), "sc")
            ins = sc.activation(gf[:, :], gf[:, :], AF.Sigmoid)
            p.inc(ins, "sc")
        p.sec("scalar", sep)
        sc_sig = p.n["sc"]

        def vep(v, h=h, hcol=hcol, hrow=hrow, sc_sig=sc_sig):
            v.wait_ge(p.sem["sc"], sc_sig)
            i1 = p.inc(v.tensor_scalar(fsc[:, :], kf[:, :], tfc[hrow:hrow + 64, hcol:hcol + 1], None, ALU.mult), "v")
            v.wait_ge(p.sem["v"], i1)
            i2 = p.inc(v.tensor_tensor(fsc[:, :], fsc[:, :], vf[:, :], ALU.mult), "v")
            v.wait_ge(p.sem["v"], i2)
            i3 = p.inc(v.tensor_tensor(wkv[:, :], wkv[:, :], fsc[:, :], ALU.add), "v")
            v.wait_ge(p.sem["v"], i3)
            i4 = p.inc(v.tensor_tensor(wkv[:, :], wkv[:, :], rf[:, :], ALU.mult), "v")
            v.wait_ge(p.sem["v"], i4)
            ins = v.tensor_tensor(ktil[:, :], wkv[:, :], gf[:, :], ALU.mult)
            p.inc(ins, "v")
        p.sec("vector", vep)
        v_ys = p.n["v"]

        def dys(s, h=h, v_ys=v_ys):
            s.wait_ge(p.sem["v"], v_ys)
            if fused_out:
                p.inc(s.dma_start(out=ys_d[h * 64:(h + 1) * 64, :], in_=ktil[:, 0:TH]), "zd", 16)
                p.inc(s.dma_start(out=ys_d[256 + h * 64: 256 + (h + 1) * 64, :], in_=ktil[:, TH:T]), "zd", 16)
            else:
                p.inc(s.dma_start(out=ys_d[h * 64:(h + 1) * 64, :], in_=ktil[:, :]), "zd", 16)
        p.sec("sync", dys)

        def gz(g, h=h):
            p.wait(g, "mm")
            ins = g.memset(Scar[:, :], 0.0)
            p.inc(ins, "gp")
        p.sec("gpsimd", gz)
        gp0 = p.n["gp"]

    def fin(s):
        p.wait(s, "zd")
    p.sec("sync", fin)
    p.close()
    return p.nc


def host_A8_inmaps(inputs, co_np, l2parts):
    MSK16 = np.zeros((16, 128), dtype=np.uint8)
    for pp in range(16):
        for c in range(8):
            MSK16[pp, c * 16 + pp: c * 16 + 16] = 1
    bd = inputs["L2_attn_base_decay"]
    tf = inputs["L2_attn_time_first"]
    base = const_inmap(co_np)
    maps = []
    for c in range(NCORE):
        b, j = c // 2, c % 2
        h0 = j * 4
        def cat(key):
            return np.ascontiguousarray(np.concatenate([np.asarray(l2parts[2 * b][key])[h0 * 64:(h0 + 4) * 64, :],
                                                        np.asarray(l2parts[2 * b + 1][key])[h0 * 64:(h0 + 4) * 64, :]], axis=1))
        bdc = np.zeros((128, 2), np.float32)
        tfc = np.zeros((128, 2), np.float32)
        for hl in range(4):
            bdc[(hl % 2) * 64:(hl % 2) * 64 + 64, hl // 2] = bd[h0 + hl]
            tfc[(hl % 2) * 64:(hl % 2) * 64 + 64, hl // 2] = tf[h0 + hl]
        bs = np.concatenate([np.asarray(l2parts[2 * b]["bscale"]), np.asarray(l2parts[2 * b + 1]["bscale"])], axis=1)
        m = dict(base)
        m.update(kf=cat("k"), vf=cat("v"), rf=cat("r"), gf=cat("g"), df=cat("delta"),
                 bscale=np.ascontiguousarray(bs), bdc=bdc, tfc=tfc, MSK16=MSK16)
        maps.append(m)
    return maps


def host_A10_inmaps(inputs, co_np, x6_parts):
    MSK = attn_masks()
    WqT = np.ascontiguousarray(inputs["L3_attn_Wq"].T)
    WkT = np.ascontiguousarray(inputs["L3_attn_Wk"].T)
    WvT = np.ascontiguousarray(inputs["L3_attn_Wv"].T)
    zb = np.zeros((128, 4), np.float32)
    base = const_inmap(co_np)
    maps = []
    for c in range(NCORE):
        b, j = c // 2, c % 2
        h0 = j * 8
        xf = np.concatenate([x6_parts[2 * b], x6_parts[2 * b + 1]], axis=1)
        m = dict(base)
        m.update(xf=np.ascontiguousarray(xf),
                 WQ=np.ascontiguousarray(WqT[:, h0 * 64:(h0 + 8) * 64]),
                 WK=np.ascontiguousarray(WkT[:, h0 * 64:(h0 + 8) * 64]),
                 WV=np.ascontiguousarray(WvT[:, h0 * 64:(h0 + 8) * 64]),
                 bQ=zb, bK=zb, bV=zb, MSK=MSK)
        maps.append(m)
    return maps


def build_A11(co_np, p=None):
    if p is None:
        p = P("A11")
    z_d = p.din("zsl", [1024, TH], BF16)
    wo_d = p.din("WoT", [1024, 1024])
    x6_d = p.din("x6", [1024, TH])
    wr_d = p.din("wR", [1024, 1024])
    wi_d = p.din("wI", [1024, 1024])
    fr_d = p.din("FR", [1024, 1024])
    fi_d = p.din("FI", [1024, 1024])
    gr_d = p.din("GR", [1024, 1024])
    gin_d = p.din("GIn", [1024, 1024])
    bias_d = p.din("biasc", [128, 8])
    x8_d = p.dout("x8", [1024, TH])

    p.start()
    co = emit_consts(p, co_np)
    biasc = p.sb("biasc", [128, 8])
    dma_in(p, biasc, bias_d)
    xr = p.sb("xr", [128, 8 * TH], F32)
    load_fm(p, xr, x6_d, 1024, TH, cast=False)
    din0 = p.n["din"]
    T1R = p.sb("T1R", [128, 8 * 1024], BF16)
    T1I = p.sb("T1I", [128, 8 * 1024], BF16)
    Mb = p.sb("Mb", [128, 8 * 1024], BF16)
    pool = Pool(p, 4)

    s1 = ExitStack()
    zb = p.sb("zb", [128, 8 * TH], BF16, s1)
    wob = p.sb("wob", [128, 8 * 1024], BF16, s1)
    load_fm(p, zb, z_d, 1024, TH)
    load_fm(p, wob, wo_d, 1024, 1024)
    gin1 = p.n["gin"]
    jobs = []
    for mt in range(8):
        for ng in range(4):
            c0 = ng * 512
            mm_list = [(wob[:, k * 1024 + mt * 128: k * 1024 + (mt + 1) * 128],
                        zb[:, k * TH + c0: k * TH + c0 + 512]) for k in range(8)]
            def ev(env, ps, pred, mt=mt, c0=c0):
                ins = env.tensor_tensor(xr[:, mt * TH + c0: mt * TH + c0 + 512], ps[:, :512],
                                        xr[:, mt * TH + c0: mt * TH + c0 + 512], ALU.add)
                p.inc(ins, "v")
            jobs.append((mm_list, 128, 512, ev))
    emit_mm(p, pool, jobs, ("vector", "v"), wait_before=[("gin", gin1), ("din", din0)])
    v_s = p.n["v"]
    emit_postln(p, xr, xr, TH, co, ready=[("v", v_s)])
    v_x7 = p.n["v"]
    s1.close()

    s2 = ExitStack()
    wRb = p.sb("wRb", [128, 8 * 1024], BF16, s2)
    wIb = p.sb("wIb", [128, 8 * 1024], BF16, s2)
    FRb = p.sb("FRb", [128, 8 * 1024], BF16, s2)
    FIb = p.sb("FIb", [128, 8 * 1024], BF16, s2)
    tA = p.sb("tA", [128, 512], F32, s2)
    def _g2(g):
        g.wait_ge(p.sem["v"], v_x7)
    p.sec("gpsimd", _g2)
    load_fm(p, wRb, wr_d, 1024, 1024)
    load_fm(p, wIb, wi_d, 1024, 1024)
    load_fm(p, FRb, fr_d, 1024, 1024)
    load_fm(p, FIb, fi_d, 1024, 1024)
    gin2 = p.n["gin"]

    for part in range(2):
        dstbuf = T1R if part == 0 else T1I
        jobs = []
        for mt in range(8):
            for ng in range(2):
                c0 = ng * 512
                if part == 0:
                    mmA = [(wRb[:, k * 1024 + mt * 128: k * 1024 + (mt + 1) * 128],
                            FRb[:, k * 1024 + c0: k * 1024 + c0 + 512]) for k in range(8)]
                    mmB = [(wIb[:, k * 1024 + mt * 128: k * 1024 + (mt + 1) * 128],
                            FIb[:, k * 1024 + c0: k * 1024 + c0 + 512]) for k in range(8)]
                else:
                    mmA = [(wRb[:, k * 1024 + mt * 128: k * 1024 + (mt + 1) * 128],
                            FIb[:, k * 1024 + c0: k * 1024 + c0 + 512]) for k in range(8)]
                    mmB = [(wIb[:, k * 1024 + mt * 128: k * 1024 + (mt + 1) * 128],
                            FRb[:, k * 1024 + c0: k * 1024 + c0 + 512]) for k in range(8)]
                def evA(env, ps, pred, mt=mt, c0=c0):
                    ins = env.tensor_copy(tA[:, :], ps[:, :512])
                    p.inc(ins, "v")
                jobs.append((mmA, 128, 512, evA))
                def evB(env, ps, pred, mt=mt, c0=c0, part=part, dstbuf=dstbuf):
                    op = ALU.subtract if part == 0 else ALU.add
                    ins = env.tensor_tensor(dstbuf[:, mt * 1024 + c0: mt * 1024 + c0 + 512],
                                            tA[:, :], ps[:, :512], op)
                    p.inc(ins, "v")
                jobs.append((mmB, 128, 512, evB))
        emit_mm(p, pool, jobs, ("vector", "v"), wait_before=[("gin", gin2)])
    v_t1 = p.n["v"]
    s2.close()

    s3 = ExitStack()
    GRb = p.sb("GRb", [128, 8 * 1024], BF16, s3)
    GInb = p.sb("GInb", [128, 8 * 1024], BF16, s3)
    def _g3(g):
        g.wait_ge(p.sem["v"], v_t1)
    p.sec("gpsimd", _g3)
    load_fm(p, GRb, gr_d, 1024, 1024)
    load_fm(p, GInb, gin_d, 1024, 1024)
    gin3 = p.n["gin"]
    jobs = []
    for mt in range(8):
        for ng in range(2):
            c0 = ng * 512
            mm_list = [(T1R[:, k * 1024 + mt * 128: k * 1024 + (mt + 1) * 128],
                        GRb[:, k * 1024 + c0: k * 1024 + c0 + 512]) for k in range(8)] + \
                      [(T1I[:, k * 1024 + mt * 128: k * 1024 + (mt + 1) * 128],
                        GInb[:, k * 1024 + c0: k * 1024 + c0 + 512]) for k in range(8)]
            def ev(env, ps, pred, mt=mt, c0=c0):
                ins = env.tensor_copy(Mb[:, mt * 1024 + c0: mt * 1024 + c0 + 512], ps[:, :512])
                p.inc(ins, "v")
            jobs.append((mm_list, 128, 512, ev))
    emit_mm(p, pool, jobs, ("vector", "v"), wait_before=[("gin", gin3), ("v", v_t1)])
    v_m = p.n["v"]
    s3.close()

    s4 = ExitStack()
    x7b = p.sb("x7b", [128, 8 * TH], BF16, s4)
    def vx7(v):
        v.wait_ge(p.sem["v"], v_m)
        for mt in range(8):
            p.inc(v.tensor_copy(x7b[:, mt * TH:(mt + 1) * TH], xr[:, mt * TH:(mt + 1) * TH]), "v")
    p.sec("vector", vx7)
    v_x7b = p.n["v"]

    jobs = []
    for mt in range(8):
        for ng in range(4):
            c0 = ng * 512
            mm_list = [(Mb[:, k * 1024 + mt * 128: k * 1024 + (mt + 1) * 128],
                        x7b[:, k * TH + c0: k * TH + c0 + 512]) for k in range(8)]
            def ev(env, ps, pred, mt=mt, c0=c0):
                env.tensor_scalar(ps[:, :512], ps[:, :512], biasc[:, mt:mt + 1], None, ALU.add)
                ins = env.tensor_tensor(xr[:, mt * TH + c0: mt * TH + c0 + 512], ps[:, :512],
                                        xr[:, mt * TH + c0: mt * TH + c0 + 512], ALU.add)
                p.inc(ins, "v")
            jobs.append((mm_list, 128, 512, ev))
    emit_mm(p, pool, jobs, ("vector", "v"), wait_before=[("v", v_x7b)])
    v_s8 = p.n["v"]
    emit_postln(p, xr, xr, TH, co, ready=[("v", v_s8)])
    v_ln = p.n["v"]
    s4.close()

    def dx(s):
        s.wait_ge(p.sem["v"], v_ln)
        for mt in range(8):
            p.inc(s.dma_start(out=x8_d[mt * 128:(mt + 1) * 128, :], in_=xr[:, mt * TH:(mt + 1) * TH]), "dout", 16)
        p.wait(s, "dout")
    p.sec("sync", dx)
    p.close()
    return p.nc


def dft_consts():
    D = 1024
    jk = np.outer(np.arange(D), np.arange(D)) % D
    ang = 2.0 * np.pi * jk / D
    FR = np.cos(ang).astype(np.float32)
    FI = (-np.sin(ang)).astype(np.float32)
    GR = (np.cos(ang) / D).astype(np.float32)
    GIn = (-np.sin(ang) / D).astype(np.float32)
    return FR, FI, GR, GIn


def host_A11_inmaps(inputs, co_np, z_parts, x6_parts):
    FR, FI, GR, GIn = dft_consts()
    WoT = np.ascontiguousarray(inputs["L3_attn_Wo"].T)
    biasc = inputs["L3_ffn_bias"].reshape(8, 128).T.copy()
    base = const_inmap(co_np)
    maps = []
    for c in range(NCORE):
        b, i = c // 2, c % 2
        t0 = i * TH
        zsl = np.concatenate([np.asarray(z_parts[2 * b])[:, t0:t0 + TH],
                              np.asarray(z_parts[2 * b + 1])[:, t0:t0 + TH]], axis=0)
        m = dict(base)
        m.update(zsl=np.ascontiguousarray(zsl), WoT=WoT, x6=x6_parts[c],
                 wR=inputs["L3_ffn_w_freq_real"], wI=inputs["L3_ffn_w_freq_imag"],
                 FR=FR, FI=FI, GR=GR, GIn=GIn, biasc=biasc)
        maps.append(m)
    return maps


def _prog(name, builder):
    if name not in _prog_cache:
        _prog_cache[name] = builder()
    return _prog_cache[name]


def kernel(**inputs):
    return kernel_fused(**inputs)


def kernel_unfused(**inputs):
    inputs = {k: np.ascontiguousarray(np.asarray(v, dtype=np.float32)) for k, v in inputs.items()}
    co_np = host_consts()
    cores = list(range(NCORE))

    r1 = run_bass_kernel_spmd(_prog("A1", lambda: build_A1(co_np)), host_A1_inmaps(inputs, co_np), core_ids=cores).results
    r2 = run_bass_kernel_spmd(_prog("A2", lambda: build_A2(co_np)), host_A2_inmaps(inputs, co_np, r1), core_ids=cores).results
    r3 = run_bass_kernel_spmd(_prog("A3", lambda: build_A3(co_np)), host_A3_inmaps(inputs, co_np, r1, r2), core_ids=cores).results
    x1p = [np.asarray(r3[c]["x1"], np.float32) for c in cores]
    r4 = run_bass_kernel_spmd(_prog("A4", lambda: build_A4(co_np)), host_A4_inmaps(inputs, co_np, x1p), core_ids=cores).results
    x2p = [np.asarray(r4[c]["x2"], np.float32) for c in cores]
    r5 = run_bass_kernel_spmd(_prog("A5", lambda: build_attn(co_np, biased=True, name="A5")),
                              host_A5_inmaps(inputs, co_np, x2p), core_ids=cores).results
    zp = [np.asarray(r5[c]["z"]) for c in cores]
    r6 = run_bass_kernel_spmd(_prog("A6", lambda: build_A6(co_np)), host_A6_inmaps(inputs, co_np, zp, x2p), core_ids=cores).results
    x4p = [np.asarray(r6[c]["x4"], np.float32) for c in cores]
    r7 = run_bass_kernel_spmd(_prog("A7", lambda: build_A7(co_np)), host_A7_inmaps(inputs, co_np, x4p), core_ids=cores).results
    r8 = run_bass_kernel_spmd(_prog("A8", lambda: build_A8(co_np)), host_A8_inmaps(inputs, co_np, r7), core_ids=cores).results
    ysp = [np.asarray(r8[c]["ys"]) for c in cores]
    r9 = run_bass_kernel_spmd(_prog("A9", lambda: build_A9(co_np)), host_A9_inmaps(inputs, co_np, ysp, x4p), core_ids=cores).results
    x6p = [np.asarray(r9[c]["x6"], np.float32) for c in cores]
    r10 = run_bass_kernel_spmd(_prog("A10", lambda: build_attn(co_np, biased=True, name="A10")),
                               host_A10_inmaps(inputs, co_np, x6p), core_ids=cores).results
    z3p = [np.asarray(r10[c]["z"]) for c in cores]
    r11 = run_bass_kernel_spmd(_prog("A11", lambda: build_A11(co_np)), host_A11_inmaps(inputs, co_np, z3p, x6p), core_ids=cores).results

    out = np.zeros((B, T, D), dtype=np.float32)
    for c in cores:
        b, i = c // 2, c % 2
        out[b, i * TH:(i + 1) * TH] = np.asarray(r11[c]["x8"], np.float32).T
    return out


RG2 = [[0, 1], [2, 3], [4, 5], [6, 7]]


def _barrier(p):
    sems = ["din", "gin", "mm", "v", "sc", "zd", "zg", "dout", "cc"]
    def gb(g):
        for sm in sems:
            p.wait(g, sm)
    p.sec("gpsimd", gb)
    def sb_(s):
        for sm in sems:
            p.wait(s, sm)
    p.sec("sync", sb_)


def _coll(p, kind, in_ap, out_ap, op=ALU.bypass):
    def g(g):
        for sm in ("dout", "din", "zd", "zg"):
            p.wait(g, sm)
        ins = g.collective_compute(kind, op, replica_groups=RG2,
                                   ins=[in_ap.opt()], outs=[out_ap.opt()])
        p.inc(ins, "cc", 1)
        g.wait_ge(p.sem["cc"], p.n["cc"])
    p.sec("gpsimd", g)


def _dd(p, dst_ap, src_ap, slow=False):
    def s(s):
        p.wait(s, "cc")
        p.inc(s.dma_start(out=dst_ap, in_=src_ap, allow_slow_non_contiguous=slow),
              "din", 16)
        p.wait(s, "din")
    p.sec("sync", s)


def emit_xchg(p, src, ybuf, dst, S, N, dt, m0, m1):
    tps = S // 128
    sc = ExitStack()
    nsl = 2
    ts = [p.sb("xgt", [128, N], dt, scope=sc) for _ in range(nsl)]
    y0s = [p.sb("xgy0", [128, N], dt, scope=sc) for _ in range(nsl)]
    y1s = [p.sb("xgy1", [128, N], dt, scope=sc) for _ in range(nsl)]
    tg = [0] * nsl
    y0g = [0] * nsl
    y1g = [0] * nsl
    for r in range(2 * tps):
        sl = r % nsl
        j, rr = r // tps, r % tps
        def ld(sy, r=r, sl=sl, g=tg[sl]):
            if g:
                sy.wait_ge(p.sem["v"], g)
            p.inc(sy.dma_start(out=ts[sl][:, :], in_=src[r * 128:(r + 1) * 128, :]),
                  "din", 16)
        p.sec("sync", ld)
        dv = p.n["din"]
        def v0(v, sl=sl, dv=dv, g=y0g[sl]):
            v.wait_ge(p.sem["din"], dv)
            if g:
                v.wait_ge(p.sem["dout"], g)
            p.inc(v.tensor_scalar(y0s[sl][:, :], ts[sl][:, :], m0[:, :], None, ALU.mult),
                  "v", 1)
        p.sec("vector", v0)
        vv0 = p.n["v"]
        def s0(sy, sl=sl, vv=vv0, row=(2 * j) * S + rr * 128):
            sy.wait_ge(p.sem["v"], vv)
            p.inc(sy.dma_start(out=ybuf[row:row + 128, :], in_=y0s[sl][:, :]), "dout", 16)
        p.sec("sync", s0)
        y0g[sl] = p.n["dout"]
        def v1(v, sl=sl, g=y1g[sl]):
            if g:
                v.wait_ge(p.sem["dout"], g)
            p.inc(v.tensor_scalar(y1s[sl][:, :], ts[sl][:, :], m1[:, :], None, ALU.mult),
                  "v", 1)
        p.sec("vector", v1)
        tg[sl] = p.n["v"]
        vv1 = p.n["v"]
        def s1(sy, sl=sl, vv=vv1, row=(2 * j + 1) * S + rr * 128):
            sy.wait_ge(p.sem["v"], vv)
            p.inc(sy.dma_start(out=ybuf[row:row + 128, :], in_=y1s[sl][:, :]), "dout", 16)
        p.sec("sync", s1)
        y1g[sl] = p.n["dout"]
    sc.close()
    _coll(p, "ReduceScatter", ybuf, dst, op=ALU.add)


def build_fused(co_np):
    p = P("FUSED")
    p.fused = True
    p.start()
    emit_consts(p, co_np)

    d = p.dint
    u_i = d("i_u", [2048, TH], BF16)
    z_i = d("i_z", [2048, TH], BF16)
    dbc_i = d("i_dbc", [96, TH], F32)
    y_u = d("y_u", [4096, TH], BF16)
    ucc = d("c_u", [2048, TH], BF16)
    dbcg = d("c_dbc", [192, TH], F32)
    uh_i = d("i_uh", [1024, T], BF16)
    dbcf_i = d("i_dbcf", [96, T], F32)
    ys2_i = d("i_ys2", [2048, TH], BF16)
    y_ys2 = d("y_ys2", [4096, TH], BF16)
    ysl3_i = d("c_ysl3", [2048, TH], BF16)
    x1_i = d("i_x1", [1024, TH], F32)
    hal_in = d("i_halin", [1024, 1], F32)
    halg = d("c_halg", [2048, 1], F32)
    pcol_d = d("i_pcol", [1024, 1], F32)
    x2_i = d("i_x2", [1024, TH], F32)
    x2g = d("c_x2g", [2048, TH], F32)
    xf5_i = d("i_xf5", [1024, T], F32)
    z5_i = d("i_z5", [1024, TH], BF16)
    y_z5 = d("y_z5", [2048, TH], BF16)
    zsl6_i = d("c_zsl6", [1024, TH], BF16)
    x4_i = d("i_x4", [1024, TH], F32)
    l2names = ["k", "v", "r", "g", "delta"]
    l2_i = {nm: d(f"i_l2{nm}", [512, TH], BF16) for nm in l2names}
    y_l2 = {nm: d(f"y_l2{nm}", [1024, TH], BF16) for nm in l2names}
    l2cc = {nm: d(f"c_l2{nm}", [512, TH], BF16) for nm in l2names}
    l2f = {nm: d(f"i_l2f{nm}", [256, T], BF16) for nm in l2names}
    bs_i = d("i_bs", [1, TH], F32)
    bsg = d("c_bsg", [2, TH], F32)
    bsf_i = d("i_bsf", [1, T], F32)
    ys8_i = d("i_ys8", [512, TH], BF16)
    y_ys8 = d("y_ys8", [1024, TH], BF16)
    ysl9_i = d("c_ysl9", [512, TH], BF16)
    x6_i = d("i_x6", [1024, TH], F32)
    x6g = d("c_x6g", [2048, TH], F32)
    xf10_i = d("i_xf10", [1024, T], F32)
    z10_i = d("i_z10", [1024, TH], BF16)
    y_z10 = d("y_z10", [2048, TH], BF16)
    zsl11_i = d("c_zsl11", [1024, TH], BF16)

    m0h = p.din("G_m0", [128, 1], F32)
    m1h = p.din("G_m1", [128, 1], F32)
    m0b = p.sb("m0b", [128, 1], F32)
    m1b = p.sb("m1b", [128, 1], F32)
    m1f = m1b
    dma_in(p, m0b, m0h, cast=False)
    dma_in(p, m1b, m1h, cast=False)
    def vw(v):
        p.wait(v, "gin")
        p.wait(v, "din")
    p.sec("vector", vw)

    def stage(pre, io, fn):
        _barrier(p)
        p.pre = pre
        p.io_map = {pre + k: v for k, v in io.items()}
        p.push_frame()
        with p.nc.named_scope(pre.rstrip("_")):
            fn()
        p.pop_frame()
        p.pre = ""
        p.io_map = {}
        _barrier(p)

    stage("A1_", {"u": u_i, "z": z_i, "dbc": dbc_i}, lambda: build_A1(co_np, p=p))
    emit_xchg(p, u_i, y_u, ucc, 1024, TH, BF16, m0b, m1b)
    _dd(p, uh_i[:, 0:TH], ucc[0:1024, :])
    _dd(p, uh_i[:, TH:T], ucc[1024:2048, :])
    _coll(p, "AllGather", dbc_i, dbcg)
    _dd(p, dbcf_i[:, 0:TH], dbcg[0:96, :])
    _dd(p, dbcf_i[:, TH:T], dbcg[96:192, :])

    stage("A2_", {"dbc": dbcf_i, "u": uh_i, "ys": ys2_i},
          lambda: build_A2(co_np, p=p, fused_out=True))
    emit_xchg(p, ys2_i, y_ys2, ysl3_i, 1024, TH, BF16, m0b, m1b)

    stage("A3_", {"ysl": ysl3_i, "u": u_i, "z": z_i, "x1": x1_i},
          lambda: build_A3(co_np, p=p))
    _dd(p, hal_in, x1_i[:, TH - 1:TH], slow=True)
    _barrier(p)
    _coll(p, "AllGather", hal_in, halg)
    hsc = ExitStack()
    ht = p.sb("halt", [128, 8], F32, scope=hsc)
    hy = p.sb("haly", [128, 8], F32, scope=hsc)
    def hl(sy):
        p.wait(sy, "cc")
        p.inc(sy.dma_start(out=ht[:, :], in_=halg[0:1024, :]), "din", 16)
    p.sec("sync", hl)
    hdv = p.n["din"]
    def hv(v, hdv=hdv):
        v.wait_ge(p.sem["din"], hdv)
        p.inc(v.tensor_scalar(hy[:, :], ht[:, :], m1f[:, :], None, ALU.mult), "v", 1)
    p.sec("vector", hv)
    hvv = p.n["v"]
    def hs(sy, hvv=hvv):
        sy.wait_ge(p.sem["v"], hvv)
        p.inc(sy.dma_start(out=pcol_d, in_=hy[:, :]), "dout", 16)
    p.sec("sync", hs)
    hsc.close()

    stage("A4_", {"x1own": x1_i, "pcol": pcol_d, "x2": x2_i},
          lambda: build_A4(co_np, p=p, split_in=True))
    _coll(p, "AllGather", x2_i, x2g)
    _dd(p, xf5_i[:, 0:TH], x2g[0:1024, :])
    _dd(p, xf5_i[:, TH:T], x2g[1024:2048, :])

    stage("A5_", {"xf": xf5_i, "z": z5_i},
          lambda: build_attn(co_np, biased=True, name="A5", p=p, fused_out=True))
    emit_xchg(p, z5_i, y_z5, zsl6_i, 512, TH, BF16, m0b, m1b)

    stage("A6_", {"zsl": zsl6_i, "x2": x2_i, "x4": x4_i},
          lambda: build_A6(co_np, p=p))

    stage("A7_", dict({nm: l2_i[nm] for nm in l2names}, x=x4_i, bscale=bs_i),
          lambda: build_A7(co_np, p=p))
    for nm in l2names:
        emit_xchg(p, l2_i[nm], y_l2[nm], l2cc[nm], 256, TH, BF16, m0b, m1b)
        _dd(p, l2f[nm][:, 0:TH], l2cc[nm][0:256, :])
        _dd(p, l2f[nm][:, TH:T], l2cc[nm][256:512, :])
    _coll(p, "AllGather", bs_i, bsg)
    _dd(p, bsf_i[0:1, 0:TH], bsg[0:1, :])
    _dd(p, bsf_i[0:1, TH:T], bsg[1:2, :])

    stage("A8_", {"kf": l2f["k"], "vf": l2f["v"], "rf": l2f["r"], "gf": l2f["g"],
                  "df": l2f["delta"], "bscale": bsf_i, "ys": ys8_i},
          lambda: build_A8(co_np, p=p, fused_out=True))
    emit_xchg(p, ys8_i, y_ys8, ysl9_i, 256, TH, BF16, m0b, m1b)

    stage("A9_", {"ysl": ysl9_i, "x4": x4_i, "x6": x6_i},
          lambda: build_A9(co_np, p=p))
    _coll(p, "AllGather", x6_i, x6g)
    _dd(p, xf10_i[:, 0:TH], x6g[0:1024, :])
    _dd(p, xf10_i[:, TH:T], x6g[1024:2048, :])

    stage("A10_", {"xf": xf10_i, "z": z10_i},
          lambda: build_attn(co_np, biased=True, name="A10", p=p, fused_out=True))
    emit_xchg(p, z10_i, y_z10, zsl11_i, 512, TH, BF16, m0b, m1b)

    stage("A11_", {"zsl": zsl11_i, "x6": x6_i},
          lambda: build_A11(co_np, p=p))

    def fin(s):
        for sm in ("dout", "zd", "zg", "din", "gin"):
            p.wait(s, sm)
    p.sec("sync", fin)
    p.stack.close()
    return p.nc


_FUSE_KEYS = {
    "A1": ["xT", "WinT", "convw", "convb", "WxT"],
    "A2": ["WdtT", "bdt", "AL", "PD4", "PBq", "PCq", "SEL"],
    "A3": ["Dsk", "WoutT", "xres"],
    "A4": ["mur", "muk", "WrT", "WkT", "WvT"],
    "A5": ["WQ", "WK", "WV", "bQ", "bK", "bV", "MSK"],
    "A6": ["WO", "bO", "rmsw", "gwT", "W1all", "b1c", "W2all", "b2c", "E8"],
    "A7": ["Wk", "Wv", "Wr", "Wg", "Wsel", "dtW", "dtbc"],
    "A8": ["bdc", "tfc", "MSK16"],
    "A9": ["Wo", "W1", "b1c", "W2", "b2c"],
    "A10": ["WQ", "WK", "WV", "bQ", "bK", "bV", "MSK"],
    "A11": ["WoT", "wR", "wI", "FR", "FI", "GR", "GIn", "biasc"],
}


def host_fused_inmaps(inputs, co_np):
    dz8 = [np.zeros((1024, TH), np.float32) for _ in range(NCORE)]
    dzb = [{"k": np.zeros((512, TH), BF), "v": np.zeros((512, TH), BF), "r": np.zeros((512, TH), BF),
            "g": np.zeros((512, TH), BF), "delta": np.zeros((512, TH), BF),
            "bscale": np.zeros((1, TH), np.float32), "u": np.zeros((2048, TH), BF),
            "z": np.zeros((2048, TH), BF), "dbc": np.zeros((96, TH), np.float32),
            "ys": np.zeros((1024, TH), BF), "x1": dz8[0]} for _ in range(NCORE)]
    per_stage = {
        "A1": host_A1_inmaps(inputs, co_np),
        "A2": host_A2_inmaps(inputs, co_np, dzb),
        "A3": host_A3_inmaps(inputs, co_np, dzb, dzb),
        "A4": host_A4_inmaps(inputs, co_np, dz8),
        "A5": host_A5_inmaps(inputs, co_np, dz8),
        "A6": host_A6_inmaps(inputs, co_np, [a["ys"] for a in dzb], dz8),
        "A7": host_A7_inmaps(inputs, co_np, dz8),
        "A8": host_A8_inmaps(inputs, co_np, dzb),
        "A9": host_A9_inmaps(inputs, co_np, [a["ys"][:512] for a in dzb], dz8),
        "A10": host_A10_inmaps(inputs, co_np, dz8),
        "A11": host_A11_inmaps(inputs, co_np, [a["ys"] for a in dzb], dz8),
    }
    base = const_inmap(co_np)
    maps = []
    for c in range(NCORE):
        m = dict(base)
        i = c % 2
        m["G_m0"] = np.full((128, 1), 1.0 - i, np.float32)
        m["G_m1"] = np.full((128, 1), float(i), np.float32)
        for st_, keys in _FUSE_KEYS.items():
            sm = per_stage[st_][c]
            for k in keys:
                m[st_ + "_" + k] = sm[k]
        maps.append(m)
    return maps


def kernel_fused(**inputs):
    inputs = {k: np.ascontiguousarray(np.asarray(v, dtype=np.float32)) for k, v in inputs.items()}
    co_np = host_consts()
    nc = _prog("FUSED", lambda: build_fused(co_np))
    maps = host_fused_inmaps(inputs, co_np)
    res = run_bass_kernel_spmd(nc, maps, core_ids=list(range(NCORE))).results
    out = np.zeros((B, T, D), dtype=np.float32)
    for c in range(NCORE):
        b, i = c // 2, c % 2
        out[b, i * TH:(i + 1) * TH] = np.asarray(res[c]["A11_x8"], np.float32).T
    return out
